# version 01
# baseline (speedup 1.0000x reference)
import os
import sys
import numpy as np

sys.path.insert(0, "/opt/trn_rl_repo")

import concourse.bass as bass
import concourse.mybir as mybir
from concourse.bass_utils import run_bass_kernel_spmd

F32 = mybir.dt.float32
BF16 = mybir.dt.bfloat16
I32 = mybir.dt.int32
AF = mybir.ActivationFunctionType
ALU = mybir.AluOpType
AX = mybir.AxisListType

B = 2
S = 8192
D = 2048
NCORE = 8
TPC = 2048
NTC = 4
TC = 512
NFC = 16
HEADS = 16
HD = 128
GROUPS = 4
NEXP = 8
HEXP = 1024
HSH = 2048
EPS = 1e-5
SM_SCALE = float(HD) ** -0.5
NEG = -1.0e9


def _gchunk(c):
    return [4 * j + c for j in range(NTC)]


def _phys_chunk(w):
    g = w // 4
    return (g % 4) * 16 + (g // 4) * 4 + (w % 4)



def _shard_x(x):
    shards = []
    for core in range(NCORE):
        s, c = divmod(core, 4)
        toks = np.concatenate([x[s, g * TC:(g + 1) * TC, :] for g in _gchunk(c)], axis=0)
        shards.append(np.ascontiguousarray(toks.T))
    return shards


def _unshard_x(shards):
    out = np.empty((B, S, D), dtype=np.float32)
    for core in range(NCORE):
        s, c = divmod(core, 4)
        tok = np.ascontiguousarray(np.asarray(shards[core]).T)
        for j, g in enumerate(_gchunk(c)):
            out[s, g * TC:(g + 1) * TC, :] = tok[j * TC:(j + 1) * TC, :]
    return out


def _host_consts():
    tri = np.zeros((4, 128, TC), dtype=np.float32)
    ql = np.arange(TC)[None, :]
    kl = np.arange(128)[:, None]
    for t in range(4):
        tri[t] = (ql >= 128 * t + kl).astype(np.float32)
    return {"tri01": tri.reshape(512, TC), "ident": np.eye(128, dtype=np.float32)}


def _host_tables(core):
    import ml_dtypes
    s, c = divmod(core, 4)
    ind = np.zeros((128, 64), dtype=np.float32)
    tri = np.zeros((4, 128, TC), dtype=np.float32)
    ql = np.arange(TC)[None, :]
    kl = np.arange(128)[:, None]
    for t in range(4):
        tri[t] = (ql >= 128 * t + kl).astype(np.float32)
    bmask = np.zeros((NTC, 16, 128, TC), dtype=np.float32)
    for j in range(NTC):
        for u in range(16):
            col = j * 16 + u
            if u < 4 * c:
                bmask[j, u] = 1.0
            elif u < 4 * c + 4:
                bmask[j, u] = tri[u - 4 * c]
            else:
                ind[:, col] = NEG
    bmask = bmask.reshape(NTC * 16 * 128, TC).astype(ml_dtypes.bfloat16)
    return {"ind": ind, "bmask": bmask}


def _prep_inputs(inputs):
    a = {k: np.asarray(v) for k, v in inputs.items()}
    T = lambda m: np.ascontiguousarray(np.asarray(m).T)
    shared = {
        "wq_t": T(a["L0_attn_a_Wq"]),
        "wk_t": T(a["L0_attn_a_Wk"]),
        "wv_t": T(a["L0_attn_a_Wv"]),
        "wo_t": T(a["L0_attn_a_Wo"]),
        "bwq": np.ascontiguousarray(a["L0_attn_b_w_q"]),
        "bwkv": np.ascontiguousarray(a["L0_attn_b_w_kv"]),
        "bwd": np.ascontiguousarray(a["L0_attn_b_w_dense"]),
        "ggate": T(a["L0_ffn_gate_w"]),
        "wg_t": np.ascontiguousarray(np.transpose(a["L0_ffn_w_gate"], (0, 2, 1))),
        "wu_t": np.ascontiguousarray(np.transpose(a["L0_ffn_w_up"], (0, 2, 1))),
        "wd_t": np.ascontiguousarray(np.transpose(a["L0_ffn_w_down"], (0, 2, 1))),
        "shg_t": T(a["L0_ffn_sh_gate"]),
        "shu_t": T(a["L0_ffn_sh_up"]),
        "shd_t": T(a["L0_ffn_sh_down"]),
        "se_t": T(a["L0_ffn_se_gate_w"]),
    }
    sel = np.zeros((8, 8 * 128), dtype=np.float32)
    for e in range(8):
        sel[e, e * 128:(e + 1) * 128] = 1.0
    shared["selmat"] = sel
    shared.update(_host_consts())
    return shared



class RSem:

    def __init__(self, h, b):
        self.h = h
        self.b = b
        self.t = 0

    def tick(self, n):
        self.t += n
        return (self.h, n)


def W(eng, s, v):
    if isinstance(s, RSem):
        eng.wait_ge(s.h, s.b + v)
    else:
        eng.wait_ge(s, v)


class Prog:
    def __init__(self, nc, upto, dbg):
        import contextlib
        self.nc = nc
        self.upto = upto
        self.dbg = set(dbg)
        self.ctx = contextlib.ExitStack()
        self.t = {}
        self._marks = {}
        self._nsem = 0
        self._pool = {}
        self._phase = 0
        self._rentals = {}

    def phase_begin(self):
        self._phase += 1
        for ph in list(self._rentals):
            if ph <= self._phase - 2:
                for klass, rs in self._rentals.pop(ph):
                    self._pool.setdefault(klass, []).append((rs.h, rs.b + rs.t))

    def _raw_sem(self, name):
        self._nsem += 1
        return self.ctx.enter_context(self.nc.semaphore(f"{name}_{self._nsem}"))

    def new_sem(self, name, klass="c"):
        pool = self._pool.setdefault(klass, [])
        if pool:
            h, b = pool.pop()
            rs = RSem(h, b)
        else:
            rs = RSem(self._raw_sem(klass + name), 0)
        self._rentals.setdefault(self._phase, []).append((klass, rs))
        return rs

    def mark(self, name, inst, n=1):
        if name not in self._marks or self._marks[name][2]:
            self._marks[name] = [self._raw_sem("mk_" + name), 0, False]
        m = self._marks[name]
        inst.then_inc(m[0], n)
        m[1] += n

    def alias_mark(self, name, sem, total):
        if isinstance(sem, RSem):
            self._marks[name] = [sem.h, sem.b + total, True]
        else:
            self._marks[name] = [sem, total, True]

    def wait_mark(self, eng, name):
        m = self._marks[name]
        eng.wait_ge(m[0], m[1])

    def slots(self, name, n, klass="s"):
        return [self.new_sem(f"{name}{k}", klass) for k in range(n)]

    def arena(self, name, shape, dtype, region, off=0):
        base = {"XN": 159744, "BIG": 94208, "SCR": 36864}[region]
        return self.nc.alloc_sbuf_tensor_at(name, shape, dtype, offset=base + off)



def ph_stats(p, xsrc, tag, wait_marks=()):
    p.phase_begin()
    nc = p.nc
    xr = p.t["xring"]
    ones = p.t["ones1f"]
    row = p.t["rowtmp"]
    stat = p.t["stat"]
    sx = p.slots(tag + "x", 4)
    sq_done = p.new_sem(tag + "q")
    mm = p.new_sem(tag + "m")
    spr = p.new_sem(tag + "p")
    sv = p.new_sem(tag + "v")
    ss = p.new_sem(tag + "s")
    with (
        nc.psum_tensor("ps" + tag, [1, 2, TC], F32) as ps,
        nc.sbuf_tensor("sq" + tag, [128, 2, TC], F32) as sq,
        nc.Block() as blk,
    ):
        @blk.sync
        def _(sync):
            for mname in wait_marks:
                p.wait_mark(sync, mname)
            for i in range(NTC * NFC):
                if i >= 4:
                    W(sync, mm, 2 * (i - 3))
                sync.dma_start(xr[:, i % 4, :],
                               xsrc[(i % NFC) * 128:(i % NFC + 1) * 128,
                                    (i // NFC) * TC:(i // NFC + 1) * TC]).then_inc(*sx[i % 4].tick(16))
            for k in range(4):
                W(sync, sx[k], 16 * (NTC * NFC // 4))

        @blk.tensor
        def _(tensor):
            p.wait_mark(tensor, "const")
            for i in range(NTC * NFC):
                tc, fc = divmod(i, NFC)
                W(tensor, sx[i % 4], 16 * (i // 4 + 1))
                if fc == 0 and tc >= 1:
                    W(tensor, spr, tc)
                tensor.matmul(ps[:, 0, :], ones[:, :], xr[:, i % 4, :],
                              start=(fc == 0), stop=(fc == NFC - 1))
                W(tensor, sq_done, i + 1)
                tensor.matmul(ps[:, 1, :], ones[:, :], sq[:, i % 2, :],
                              start=(fc == 0), stop=(fc == NFC - 1)).then_inc(*mm.tick(2))

        @blk.vector
        def _(vector):
            for tc in range(NTC):
                W(vector, mm, 2 * NFC * (tc + 1))
                sl = slice(tc * TC, (tc + 1) * TC)
                mean, ex2, var = row[:, 0, :], row[:, 1, :], row[:, 2, :]
                vector.tensor_scalar(mean, ps[:, 0, :], 1.0 / D, None, ALU.mult)
                vector.tensor_scalar(ex2, ps[:, 1, :], 1.0 / D, None, ALU.mult).then_inc(*spr.tick(1))
                vector.drain()
                vector.tensor_mul(var, mean, mean)
                vector.drain()
                vector.scalar_tensor_tensor(var, ex2, EPS, var,
                                            ALU.add, ALU.subtract).then_inc(*sv.tick(1))
                W(vector, ss, tc + 1)
                inst = vector.tensor_mul(stat[:, 0, sl], mean, stat[:, 1, sl])
                p.mark("stat", inst, 1)
                vector.drain()

        @blk.scalar
        def _(scalar):
            for tc in range(NTC):
                for fc in range(NFC):
                    i = tc * NFC + fc
                    W(scalar, sx[i % 4], 16 * (i // 4 + 1))
                    if i >= 2:
                        W(scalar, mm, 2 * (i - 1))
                    scalar.activation(sq[:, i % 2, :], xr[:, i % 4, :],
                                      AF.Square).then_inc(*sq_done.tick(1))
                sl = slice(tc * TC, (tc + 1) * TC)
                W(scalar, sv, tc + 1)
                scalar.drain()
                scalar.activation(row[:, 3, :], row[:, 2, :], AF.Ln)
                scalar.drain()
                scalar.activation(stat[:, 1, sl], row[:, 3, :], AF.Exp,
                                  scale=-0.5).then_inc(*ss.tick(1))


def ph_xn(p, xsrc, tag, wait_marks=()):
    p.phase_begin()
    nc = p.nc
    xbig = p.arena("xnb" + tag, [128, 2, NFC, TC], F32, "BIG")
    xn = p.t["xn"]
    onesr = p.t["onesr"]
    stat = p.t["stat"]
    bb = p.t["bcb"]
    sx = p.slots(tag + "x", 2)
    mm = p.new_sem(tag + "m")
    sv = p.new_sem(tag + "v")
    with (
        nc.psum_tensor("ps" + tag, [128, 2, TC], F32) as ps,
        nc.Block() as blk,
    ):
        @blk.sync
        def _(sync):
            for mname in wait_marks:
                p.wait_mark(sync, mname)
            for tc in range(NTC):
                for fc in range(NFC):
                    if tc >= 2 and fc == 0:
                        W(sync, sv, tc - 1)
                    sync.dma_start(xbig[:, tc % 2, fc, :],
                                   xsrc[fc * 128:(fc + 1) * 128,
                                        tc * TC:(tc + 1) * TC]).then_inc(*sx[tc % 2].tick(16))
            for k in range(2):
                W(sync, sx[k], 16 * NFC * (NTC // 2))

        @blk.tensor
        def _(tensor):
            p.wait_mark(tensor, "stat")
            for tc in range(NTC):
                if tc >= 1:
                    W(tensor, sv, tc)
                tensor.matmul(ps[:, 0, :], onesr[:, :], stat[:, 1, tc * TC:(tc + 1) * TC],
                              start=True, stop=True)
                tensor.matmul(ps[:, 1, :], onesr[:, :], stat[:, 0, tc * TC:(tc + 1) * TC],
                              start=True, stop=True).then_inc(*mm.tick(1))

        @blk.vector
        def _(vector):
            for tc in range(NTC):
                W(vector, mm, tc + 1)
                if tc >= 1:
                    vector.drain()
                vector.tensor_copy(bb[:, 0, :], ps[:, 0, :])
                vector.tensor_copy(bb[:, 1, :], ps[:, 1, :])
                vector.drain()
                W(vector, sx[tc % 2], 16 * NFC * (tc // 2 + 1))
                for fc in range(NFC):
                    vector.tensor_mul(xbig[:, tc % 2, fc, :], xbig[:, tc % 2, fc, :],
                                      bb[:, 0, :])
                vector.drain()
                for fc in range(NFC - 1):
                    vector.tensor_sub(xn[:, fc, tc * TC:(tc + 1) * TC],
                                      xbig[:, tc % 2, fc, :], bb[:, 1, :])
                vector.drain()
                vector.tensor_sub(xn[:, NFC - 1, tc * TC:(tc + 1) * TC],
                                  xbig[:, tc % 2, NFC - 1, :], bb[:, 1, :]).then_inc(*sv.tick(1))
            p.alias_mark("xn", sv, NTC)


def _dbg_rows(p, srcname, outname, w, marks=()):
    p.phase_begin()
    nc = p.nc
    out = nc.dram_tensor(outname, [128, w], F32, kind="ExternalOutput")
    sd = p.new_sem("dbgr", "g")
    with nc.Block() as blk:
        @blk.gpsimd
        def _(g):
            for m in marks:
                p.wait_mark(g, m)
            g.dma_start(out[:, :], p.t[srcname][:, :]).then_inc(*sd.tick(16))
            W(g, sd, 16)


def _dbg_dump_fm(p, srcname, outname, nfc=NFC, width=TPC, marks=("xn",)):
    p.phase_begin()
    nc = p.nc
    src = p.t[srcname]
    out = p.t.get(outname)
    if out is None:
        out = nc.dram_tensor(outname, [nfc * 128, width], F32, kind="ExternalOutput")
        p.t[outname] = out
    sv = p.new_sem("dmp")
    sx = p.slots("dmpx", 2)
    with (
        nc.sbuf_tensor("dmp" + outname, [128, 2, width], F32) as tmp,
        nc.Block() as blk,
    ):
        @blk.vector
        def _(vector):
            for m in marks:
                p.wait_mark(vector, m)
            for fc in range(nfc):
                if fc >= 2:
                    W(vector, sx[fc % 2], 16 * ((fc - 2) // 2 + 1))
                vector.tensor_copy(tmp[:, fc % 2, :], src[:, fc, :]).then_inc(*sv.tick(1))

        @blk.sync
        def _(sync):
            for fc in range(nfc):
                W(sync, sv, fc + 1)
                sync.dma_start(out[fc * 128:(fc + 1) * 128, :],
                               tmp[:, fc % 2, :]).then_inc(*sx[fc % 2].tick(16))
            for k in range(2):
                W(sync, sx[k], 16 * (nfc // 2))



def ph_proj(p, tag, wt, kind, n_oc, *, dst, dst_col0=0, dst_row0=0,
            norm_row=None, norm_col0=0, wt_col0=0, n_dc=NFC):
    p.phase_begin()
    nc = p.nc
    xn = p.t["xn"]
    nrows = p.t.get("nrows")
    row = p.t["rowtmp"]
    sw = p.slots(tag + "w", 2)
    swc = p.new_sem(tag + "wc")
    smm = p.new_sem(tag + "mm")
    sev = p.new_sem(tag + "ev")
    ssq = p.new_sem(tag + "sq")
    snm = p.new_sem(tag + "nm")
    snr = p.new_sem(tag + "nr")
    svt = p.new_sem(tag + "vt")
    svc = p.new_sem(tag + "vc")
    skd = p.slots(tag + "kd", 2, "g")
    if "nonorm" in p.dbg:
        norm_row = None
    normed = norm_row is not None

    wrf = p.arena("wrf" + tag, [128, 2, n_dc, 128], F32, "SCR", 16384)
    wrb = p.arena("wrb" + tag, [128, 2, n_dc, 128], BF16, "SCR", 32768)
    stg = p.arena("stg" + tag, [128, 2, TPC], BF16, "SCR", 40960)
    vstg = p.arena("vsg" + tag, [128, 2, NFC, 128], BF16, "SCR", 49152)
    with (
        nc.sbuf_tensor("sqn" + tag, [128, TPC], BF16) as sqn,
        nc.psum_tensor("pp" + tag, [128, TPC], F32) as pp,
        nc.psum_tensor("vt0" + tag, [128, 128], BF16) as vtp0,
        nc.psum_tensor("vt1" + tag, [128, 128], BF16) as vtp1,
        nc.psum_tensor("np" + tag, [1, 2, TC], F32) as nps,
        nc.Block() as blk,
    ):
        @blk.sync
        def _(sync):
            for oc in range(n_oc):
                if oc >= 2:
                    W(sync, swc, oc - 1)
                c0 = wt_col0 + oc * 128
                sync.dma_start(
                    wrf[:, oc % 2, :, :],
                    wt[:, c0:c0 + 128].rearrange("(dc p) o -> p dc o", p=128),
                ).then_inc(*sw[oc % 2].tick(16))
            for k in range(2):
                W(sync, sw[k], 16 * ((n_oc - k + 1) // 2))

        @blk.tensor
        def _(tensor):
            p.wait_mark(tensor, "xn")
            p.wait_mark(tensor, "const")
            for oc in range(n_oc):
                W(tensor, swc, oc + 1)
                if oc >= 1:
                    W(tensor, sev, oc)
                    if normed:
                        W(tensor, ssq, oc)
                inst = None
                for dc in range(n_dc):
                    for tc in range(NTC):
                        inst = tensor.matmul(pp[:, tc * TC:(tc + 1) * TC],
                                             wrb[:, oc % 2, dc, :],
                                             xn[:, dc, tc * TC:(tc + 1) * TC],
                                             start=(dc == 0), stop=(dc == n_dc - 1))
                inst.then_inc(*smm.tick(1))
                if normed and "nnm" not in p.dbg:
                    W(tensor, ssq, oc + 1)
                    for tc in range(NTC):
                        g = 4 * oc + tc
                        if g >= 2:
                            W(tensor, snr, g - 1)
                        tensor.matmul(nps[:, tc % 2, :], p.t["ones1b"][:, :],
                                      sqn[:, tc * TC:(tc + 1) * TC],
                                      start=True, stop=True).then_inc(*snm.tick(1))
                if kind == "vt":
                    W(tensor, sev, oc + 1)
                    for j in range(NFC):
                        m = oc * NFC + j
                        if m >= 2:
                            W(tensor, svc, m - 1)
                        vtp = vtp0 if m % 2 == 0 else vtp1
                        tensor.transpose(vtp[:, :],
                                         stg[:, oc % 2, j * 128:(j + 1) * 128],
                                         p.t["identb"][:, :]).then_inc(*svt.tick(1))

        @blk.scalar
        def _(scalar):
            if normed:
                for oc in range(n_oc):
                    W(scalar, sev, oc + 1)
                    if oc >= 1 and "nnm" not in p.dbg:
                        W(scalar, snm, 4 * oc)
                    scalar.activation(sqn[:, :], pp[:, :], AF.Square).then_inc(*ssq.tick(1))

        @blk.vector
        def _(vector):
            for oc in range(n_oc):
                W(vector, sw[oc % 2], 16 * (oc // 2 + 1))
                if oc >= 2:
                    W(vector, smm, oc - 1)
                vector.tensor_copy(wrb[:, oc % 2, :, :], wrf[:, oc % 2, :, :]).then_inc(*swc.tick(1))
                W(vector, smm, oc + 1)
                if kind == "q":
                    vector.tensor_copy(dst[:, dst_col0 + oc, :], pp[:, :]).then_inc(*sev.tick(1))
                else:
                    if oc >= 2:
                        if kind == "kf":
                            W(vector, skd[oc % 2], 16 * (oc // 2))
                        else:
                            W(vector, svt, 16 * (oc - 1))
                    vector.tensor_copy(stg[:, oc % 2, :], pp[:, :]).then_inc(*sev.tick(1))
                if normed and "nnm" not in p.dbg:
                    for tc in range(NTC):
                        g = 4 * oc + tc
                        W(vector, snm, g + 1)
                        vector.tensor_reduce(row[:, 3, tc:tc + 1], nps[:, tc % 2, :],
                                             AX.X, ALU.max).then_inc(*snr.tick(1))
                    vector.drain()
                    vector.tensor_reduce(row[:, 2, 0:1], row[:, 3, 0:4], AX.X, ALU.max)
                    vector.drain()
                    inst = vector.tensor_copy(nrows[:, norm_row, norm_col0 + oc:norm_col0 + oc + 1],
                                              row[:, 2, 0:1])
                    p.mark(tag + "_nrm", inst, 1)
                    vector.drain()
                if kind == "vt":
                    for j in range(NFC):
                        m = oc * NFC + j
                        W(vector, svt, m + 1)
                        if oc >= 2 and j == 0:
                            W(vector, skd[oc % 2], 16 * (oc // 2))
                        vector.tensor_copy(vstg[:, oc % 2, j, :],
                                           (vtp0 if m % 2 == 0 else vtp1)[:, :]).then_inc(*svc.tick(1))

        @blk.gpsimd
        def _(g):
            for oc in range(n_oc):
                if kind == "kf":
                    W(g, sev, oc + 1)
                    r0 = dst_row0 + oc * 128
                    g.dma_start(dst[r0:r0 + 128, :], stg[:, oc % 2, :]).then_inc(*skd[oc % 2].tick(16))
                elif kind == "vt":
                    W(g, svc, 16 * (oc + 1))
                    c0 = dst_col0 + oc * 128
                    g.dma_start(dst.ap().rearrange("(j p) w -> p j w", p=128)[:, :, c0:c0 + 128],
                                vstg[:, oc % 2, :, :]).then_inc(*skd[oc % 2].tick(16))
            if kind in ("kf", "vt"):
                for k in range(2):
                    W(g, skd[k], 16 * ((n_oc - k + 1) // 2))

    if kind == "q":
        p.alias_mark(tag + "_ev", sev, n_oc)


def ph_cc(p, tag, pairs, qrow, krow, grow, scale, nh, negM, q_nrm_mark, k_nrm_mark,
          nin, nout, group_of_head=None):
    p.phase_begin()
    nc = p.nc
    scc = p.new_sem(tag + "cc", "g")
    sd = p.slots(tag + "cd", 2, "g")
    s1 = p.new_sem(tag + "c1")
    s2 = p.new_sem(tag + "c2")
    s3 = p.new_sem(tag + "c3")
    s4 = p.new_sem(tag + "c4")
    nrows = p.t["nrows"]
    row = p.t["rowtmp"]
    grp = [[0, 1, 2, 3], [4, 5, 6, 7]]
    nk = nin.shape[1]
    with (
        nc.psum_tensor("bps" + tag, [128, 128], F32) as bps,
        nc.Block() as blk,
    ):
        @blk.gpsimd
        def _(g):
            p.wait_mark(g, k_nrm_mark)
            g.dma_start(nin[:, :], nrows[:, krow, 0:nk]).then_inc(*sd[0].tick(16))
            W(g, sd[0], 16)
            for (ci, co) in pairs:
                g.collective_compute("AllGather", ALU.bypass, grp,
                                     [p.t[ci].ap().opt()],
                                     [p.t[co].ap().opt()]).then_inc(*scc.tick(1))
            g.collective_compute("AllReduce", ALU.max, grp,
                                 [nin.ap().opt()], [nout.ap().opt()]).then_inc(*scc.tick(1))
            W(g, scc, len(pairs) + 1)
            g.dma_start(nrows[:, grow, 0:nk], nout[:, :]).then_inc(*sd[1].tick(16))
        p.alias_mark("gath_" + tag, scc, len(pairs) + 1)

        @blk.vector
        def _(vector):
            p.wait_mark(vector, q_nrm_mark)
            W(vector, sd[1], 16)
            if group_of_head is None:
                vector.tensor_copy(row[:, 3, 0:nh], nrows[:, grow, 0:nh])
            else:
                for h in range(nh):
                    vector.tensor_copy(row[:, 3, h:h + 1],
                                       nrows[:, grow, h // GROUPS:h // GROUPS + 1])
            vector.drain()
            vector.tensor_mul(row[:, 0, 0:nh], nrows[:, qrow, 0:nh],
                              row[:, 3, 0:nh]).then_inc(*s1.tick(1))
            W(vector, s2, 1)
            vector.tensor_scalar(row[:, 2, 0:nh], row[:, 1, 0:nh],
                                 -scale, None, ALU.mult).then_inc(*s3.tick(1))
            W(vector, s4, 1)
            inst = vector.tensor_copy(negM[:, 0:nh], bps[:, 0:nh])
            p.mark("negm_" + tag, inst, 1)

        @blk.scalar
        def _(scalar):
            W(scalar, s1, 1)
            scalar.activation(row[:, 1, 0:nh], row[:, 0, 0:nh], AF.Sqrt).then_inc(*s2.tick(1))

        @blk.tensor
        def _(tensor):
            W(tensor, s3, 1)
            tensor.matmul(bps[:, 0:nh], p.t["onesr"][:, :], row[:, 2, 0:nh],
                          start=True, stop=True).then_inc(*s4.tick(1))


def _dbg_dram(p, src_name, out_name, shape, dtype=F32, n_rows=None):
    p.phase_begin()
    nc = p.nc
    out = nc.dram_tensor(out_name, shape, dtype, kind="ExternalOutput")
    r = shape[0] if n_rows is None else n_rows
    sd = p.new_sem("dbgd", "g")
    with nc.Block() as blk:
        @blk.gpsimd
        def _(g):
            g.dma_start(out[:r, :], p.t[src_name][:r, :]).then_inc(*sd.tick(16))
            W(g, sd, 16)



def ph_attn_a(p):
    p.phase_begin()
    nc = p.nc
    Q = p.t["Q"]
    Vsb = p.arena("a3v", [128, 2, 64, 256], BF16, "XN", 0)
    pt = p.arena("a3p", [128, 2, 1024], BF16, "SCR", 0)
    lbsb = p.arena("a3lb", [128, 1024], F32, "SCR", 4096)
    ctile = p.arena("a3c", [128, 2, 1024], BF16, "SCR", 8192)
    lrow = p.arena("a3lr", [1, 1024], F32, "SCR", 12288)
    Ksb = p.arena("a3k", [128, 2, 4 * TPC], BF16, "SCR", 16384)
    negM = p.t["negMa"]
    kcc, vcc, ctxd = p.t["kcc_out"], p.t["vcc_out"], p.t["ctxd"]
    skv = p.slots("a3kv", 2, "g")
    svv = p.slots("a3vv", 2, "g")
    sctx = p.slots("a3cx", 2, "g")
    s_st = p.new_sem("a3st")
    s_xp = p.new_sem("a3xp")
    s_pv = p.new_sem("a3pv")
    s_rc = p.new_sem("a3rc")
    s_bc = p.new_sem("a3bc")
    s_ev = p.new_sem("a3ev")
    NH = 2 if "mini" in p.dbg else HEADS
    NH2 = NH * 2

    with (
        nc.psum_tensor("a3stp", [128, 2, 1024], F32) as st,
        nc.psum_tensor("a3op", [128, 1024], F32) as ops,
        nc.psum_tensor("a3lp", [1, 1024], F32) as lps,
        nc.Block() as blk,
    ):
        @blk.gpsimd
        def _(g):
            def store_ctx(hh):
                h, half = hh // 2, hh % 2
                W(g, s_ev, hh + 1)
                g.dma_start(ctxd[h * 128:(h + 1) * 128, half * 1024:(half + 1) * 1024],
                            ctile[:, hh % 2, :]).then_inc(*sctx[hh % 2].tick(16))

            for h in range(NH):
                if h >= 2:
                    store_ctx(2 * (h - 2))
                    store_ctx(2 * (h - 2) + 1)
                sl = h % 2
                if h >= 2:
                    W(g, s_st, 128 * (h - 1))
                for r in range(4):
                    g.dma_start(Ksb[:, sl, r * TPC:(r + 1) * TPC],
                                kcc[r * D + h * 128:r * D + (h + 1) * 128, :]
                                ).then_inc(*skv[sl].tick(16))
                if h % 2 == 0:
                    P = h // 2
                    vs = P % 2
                    if P >= 2:
                        W(g, s_pv, 256 * (P - 1))
                    vv = vcc.ap().rearrange("(kb pp) f -> pp kb f", pp=128)
                    for kq in range(8):
                        g.dma_start(
                            Vsb[:, vs, kq * 8:(kq + 1) * 8, :],
                            vv[:, kq * 8:(kq + 1) * 8, h * 128:(h + 2) * 128],
                        ).then_inc(*svv[vs].tick(16))
            for hh in range(max(0, NH2 - 4), NH2):
                store_ctx(hh)
            for k in range(2):
                W(g, sctx[k], 16 * (NH2 // 2))

        @blk.tensor
        def _(tensor):
            p.wait_mark(tensor, "wq_ev")
            for hh in range(NH2):
                h, half = hh // 2, hh % 2
                hp = h % 2
                P = h // 2
                q0 = half * 1024
                if half == 0:
                    W(tensor, skv[h % 2], 64 * (h // 2 + 1))
                    if hp == 0:
                        W(tensor, svv[P % 2], 128 * (P // 2 + 1))

                def pv(j):
                    m = hh * 64 + j
                    W(tensor, s_xp, m + 1)
                    if j == 0:
                        W(tensor, s_ev, hh)
                        W(tensor, s_rc, hh)
                    st_, sp_ = (j == 0), (j == 63)
                    tensor.matmul(ops[:, 0:512], Vsb[:, P % 2, j, hp * 128:(hp + 1) * 128],
                                  pt[:, m % 2, 0:512], start=st_, stop=sp_)
                    tensor.matmul(ops[:, 512:1024], Vsb[:, P % 2, j, hp * 128:(hp + 1) * 128],
                                  pt[:, m % 2, 512:1024], start=st_, stop=sp_)
                    tensor.matmul(lps[:, 0:512], p.t["ones1b"][:, :],
                                  pt[:, m % 2, 0:512], start=st_, stop=sp_)
                    tensor.matmul(lps[:, 512:1024], p.t["ones1b"][:, :],
                                  pt[:, m % 2, 512:1024], start=st_, stop=sp_).then_inc(*s_pv.tick(1))

                for kb in range(64):
                    m = hh * 64 + kb
                    if m >= 2:
                        W(tensor, s_xp, m - 1)
                    if kb == 0 and hh >= 1:
                        W(tensor, s_ev, hh)
                    tensor.matmul(st[:, m % 2, 0:512], Ksb[:, h % 2, kb * 128:(kb + 1) * 128],
                                  Q[:, h, q0:q0 + 512], start=True, stop=True)
                    tensor.matmul(st[:, m % 2, 512:1024], Ksb[:, h % 2, kb * 128:(kb + 1) * 128],
                                  Q[:, h, q0 + 512:q0 + 1024], start=True, stop=True).then_inc(*s_st.tick(1))
                    if kb >= 1:
                        pv(kb - 1)
                pv(63)
                W(tensor, s_rc, hh + 1)
                tensor.matmul(st[:, 0, 0:512], p.t["onesr"][:, :],
                              lrow[:, 0:512], start=True, stop=True)
                tensor.matmul(st[:, 0, 512:1024], p.t["onesr"][:, :],
                              lrow[:, 512:1024], start=True, stop=True).then_inc(*s_bc.tick(1))

        @blk.scalar
        def _(scalar):
            p.wait_mark(scalar, "negm_a")
            for hh in range(NH2):
                h = hh // 2
                for kb in range(64):
                    m = hh * 64 + kb
                    W(scalar, s_st, m + 1)
                    if m >= 2:
                        W(scalar, s_pv, m - 1)
                    scalar.activation(pt[:, m % 2, :], st[:, m % 2, :], AF.Exp,
                                      bias=negM[:, h:h + 1], scale=SM_SCALE).then_inc(*s_xp.tick(1))

        @blk.vector
        def _(vector):
            for hh in range(NH2):
                W(vector, s_pv, 64 * (hh + 1))
                vector.reciprocal(lrow[:, :], lps[:, :]).then_inc(*s_rc.tick(1))
                W(vector, s_bc, hh + 1)
                vector.tensor_copy(lbsb[:, :], st[:, 0, :])
                vector.drain()
                if hh >= 2:
                    W(vector, sctx[hh % 2], 16 * (hh // 2))
                vector.tensor_mul(ctile[:, hh % 2, :], ops[:, :], lbsb[:, :]).then_inc(*s_ev.tick(1))




def ph_attn_b(p):
    p.phase_begin()
    nc = p.nc
    Q = p.t["Q"]
    Kg = p.arena("b3k", [128, 2, 4 * TPC], BF16, "XN", 0)
    Vg = p.arena("b3v", [128, 2, 64, 128], BF16, "XN", 32768)
    pt = p.arena("b3p", [128, 2, TC], BF16, "SCR", 0)
    lbsb = p.arena("b3lb", [128, TC], F32, "SCR", 2048)
    ctile = p.arena("b3c", [128, 2, TC], BF16, "SCR", 4096)
    lrow = p.arena("b3lr", [1, TC], F32, "SCR", 6144)
    btab = p.arena("b3bt", [128, HEADS, 64], F32, "SCR", 8192)
    indsb = p.arena("b3in", [128, 64], F32, "SCR", 12544)
    bms = p.arena("b3bm", [128, 2, 16, TC], BF16, "SCR", 16384)
    negM = p.t["negMb"]
    kcc, vcc, ctxd = p.t["kbcc_out"], p.t["vbcc_out"], p.t["ctxd"]

    skv = p.slots("b3kv", 2, "g")
    svv = p.slots("b3vv", 2, "g")
    sbm = p.slots("b3bm", 2)
    sctx = p.slots("b3cx", 2, "g")
    scst = p.slots("b3cs", 2)
    s_st = p.new_sem("b3st")
    s_xp = p.new_sem("b3xp")
    s_tri = p.new_sem("b3tr")
    s_pv = p.new_sem("b3pv")
    s_rc = p.new_sem("b3rc")
    s_bc = p.new_sem("b3bc")
    s_ev = p.new_sem("b3ev")
    s_bt = p.new_sem("b3btv")

    blocks = []
    for g in range(GROUPS):
        for j in range(NTC):
            for h4 in range(4):
                blocks.append((g, j, g * 4 + h4))
    nw_of = lambda j: 16 * j + 16
    m_start = []
    m = 0
    for (g, j, h) in blocks:
        m_start.append(m)
        m += nw_of(j)
    ntri_before = []
    nt = 0
    for bi, (g, j, h) in enumerate(blocks):
        for w in range(nw_of(j)):
            ntri_before.append(nt)
            if w >= 16 * j:
                nt += 1

    def phys_win(j, u):
        return (u // 4) * 16 + j * 4 + (u % 4)

    def slab_of(bi):
        g, j, h = blocks[bi]
        return g * NTC + j

    tris_through_slab = {}
    cnt = 0
    for bi, (g, j, h) in enumerate(blocks):
        cnt += 16
        tris_through_slab[g * NTC + j] = cnt

    with (
        nc.psum_tensor("b3st0", [128, TC], F32) as st0,
        nc.psum_tensor("b3st1", [128, TC], F32) as st1,
        nc.psum_tensor("b3op", [128, TC], F32) as ops,
        nc.psum_tensor("b3lp", [1, TC], F32) as lps,
        nc.Block() as blk,
    ):
        @blk.sync
        def _(sync):
            sync.dma_start(indsb[:, :], p.inp("ind")[:, :]).then_inc(*scst[0].tick(16))
            bm = p.inp("bmask").ap().rearrange("(ju pp) q -> pp ju q", pp=128)
            for si in range(GROUPS * NTC):
                g, j = divmod(si, NTC)
                if si >= 2:
                    W(sync, s_tri, tris_through_slab[si - 2])
                sync.dma_start(bms[:, si % 2, :, :],
                               bm[:, j * 16:(j + 1) * 16, :]).then_inc(*sbm[si % 2].tick(16))

        @blk.vector
        def _(vector):
            p.wait_mark(vector, "negm_b")
            W(vector, scst[0], 16)
            for h in range(HEADS):
                vector.tensor_scalar(btab[:, h, :], indsb[:, :], negM[:, h:h + 1],
                                     None, ALU.add)
            vector.drain()
            vector.tensor_copy(btab[:, 0, 0:1], btab[:, 0, 0:1]).then_inc(*s_bt.tick(1))
            m = 0
            for bi, (g, j, h) in enumerate(blocks):
                nw = nw_of(j)
                si = g * NTC + j
                if h % 4 == 0:
                    W(vector, sbm[si % 2], 16 * (si // 2 + 1))
                for w in range(nw):
                    if w >= 16 * j:
                        W(vector, s_xp, m + 1)
                        vector.tensor_mul(pt[:, m % 2, :], pt[:, m % 2, :],
                                          bms[:, si % 2, w - 16 * j, :]
                                          ).then_inc(*s_tri.tick(1))
                    m += 1
                W(vector, s_pv, m_start[bi] + nw)
                vector.reciprocal(lrow[:, :], lps[:, :]).then_inc(*s_rc.tick(1))
                W(vector, s_bc, bi + 1)
                vector.tensor_copy(lbsb[:, :], st0[:, :])
                vector.drain()
                if bi >= 2:
                    W(vector, sctx[bi % 2], 16 * (bi // 2))
                vector.tensor_mul(ctile[:, bi % 2, :], ops[:, :],
                                  lbsb[:, :]).then_inc(*s_ev.tick(1))

        @blk.gpsimd
        def _(g_):
            for gi in range(GROUPS):
                if gi >= 1:
                    for bi in range((gi - 1) * 16 - 4 if gi >= 2 else 0, gi * 16 - 4):
                        _b3_store(p, g_, bi, blocks, ctile, ctxd, s_ev, sctx)
                sl = gi % 2
                if gi >= 2:
                    W(g_, s_st, m_start[(gi - 1) * 16])
                for r in range(4):
                    g_.dma_start(Kg[:, sl, r * TPC:(r + 1) * TPC],
                                 kcc[r * 512 + gi * 128:r * 512 + (gi + 1) * 128, :]
                                 ).then_inc(*skv[sl].tick(16))
                if gi >= 2:
                    W(g_, s_pv, m_start[(gi - 1) * 16])
                vv = vcc.ap().rearrange("(kb pp) f -> pp kb f", pp=128)
                for kq in range(8):
                    g_.dma_start(Vg[:, sl, kq * 8:(kq + 1) * 8, :],
                                 vv[:, kq * 8:(kq + 1) * 8, gi * 128:(gi + 1) * 128]
                                 ).then_inc(*svv[sl].tick(16))
            for bi in range(3 * 16 - 4, 4 * 16):
                _b3_store(p, g_, bi, blocks, ctile, ctxd, s_ev, sctx)
            for k in range(2):
                W(g_, sctx[k], 16 * (len(blocks) // 2))
            p.mark("ctxb", g_.nop(), 1)

        @blk.tensor
        def _(tensor):
            p.wait_mark(tensor, "bq_ev")
            m = 0
            for bi, (g, j, h) in enumerate(blocks):
                nw = nw_of(j)
                sl = g % 2
                if j == 0 and h % 4 == 0:
                    W(tensor, skv[sl], 64 * (g // 2 + 1))
                    W(tensor, svv[sl], 128 * (g // 2 + 1))

                def pv(mm, w):
                    if w >= 16 * j:
                        W(tensor, s_tri, ntri_before[mm] + 1)
                    else:
                        W(tensor, s_xp, mm + 1)
                    if w == 0:
                        W(tensor, s_ev, bi)
                        W(tensor, s_rc, bi)
                    P = _phys_chunk(w) if w < 16 * j else phys_win(j, w - 16 * j)
                    st_, sp_ = (w == 0), (w == nw - 1)
                    tensor.matmul(ops[:, :], Vg[:, sl, P, :], pt[:, mm % 2, :],
                                  start=st_, stop=sp_)
                    tensor.matmul(lps[:, :], p.t["ones1b"][:, :], pt[:, mm % 2, :],
                                  start=st_, stop=sp_).then_inc(*s_pv.tick(1))

                for w in range(nw):
                    if m >= 2:
                        W(tensor, s_xp, m - 1)
                    if w == 0 and bi >= 1:
                        W(tensor, s_ev, bi)
                    P = _phys_chunk(w) if w < 16 * j else phys_win(j, w - 16 * j)
                    kt = Kg[:, sl, P * 128:(P + 1) * 128]
                    stp = st0 if m % 2 == 0 else st1
                    tensor.matmul(stp[:, :], kt, Q[:, h, j * TC:(j + 1) * TC],
                                  start=True, stop=True).then_inc(*s_st.tick(1))
                    if w >= 1:
                        pv(m - 1, w - 1)
                    m += 1
                pv(m - 1, nw - 1)
                W(tensor, s_rc, bi + 1)
                tensor.matmul(st0[:, :], p.t["onesr"][:, :], lrow[:, :],
                              start=True, stop=True).then_inc(*s_bc.tick(1))

        @blk.scalar
        def _(scalar):
            p.wait_mark(scalar, "negm_b")
            W(scalar, s_bt, 1)
            m = 0
            for bi, (g, j, h) in enumerate(blocks):
                nw = nw_of(j)
                for w in range(nw):
                    W(scalar, s_st, m + 1)
                    if m >= 2:
                        W(scalar, s_pv, m - 1)
                    stp = st0 if m % 2 == 0 else st1
                    if w < 16 * j:
                        bias = negM[:, h:h + 1]
                    else:
                        col = j * 16 + (w - 16 * j)
                        bias = btab[:, h, col:col + 1]
                    scalar.activation(pt[:, m % 2, :], stp[:, :], AF.Exp,
                                      bias=bias, scale=SM_SCALE).then_inc(*s_xp.tick(1))
                    m += 1


def _b3_store(p, g_, bi, blocks, ctile, ctxd, s_ev, sctx):
    (g, j, h) = blocks[bi]
    W(g_, s_ev, bi + 1)
    g_.dma_start(ctxd[h * 128:(h + 1) * 128, j * TC:(j + 1) * TC],
                 ctile[:, bi % 2, :]).then_inc(*sctx[bi % 2].tick(16))



def ph_wcast(p):
    p.phase_begin()
    nc = p.nc
    jobs = [(p.inp("wg_t").ap().rearrange("e d h -> (e d) h"), p.t["wgb"], NEXP * D, HEXP),
            (p.inp("wu_t").ap().rearrange("e d h -> (e d) h"), p.t["wub"], NEXP * D, HEXP),
            (p.inp("wd_t").ap().rearrange("e h d -> (e h) d"), p.t["wdb"], NEXP * HEXP, D),
            (p.inp("shg_t").ap(), p.t["shgb"], D, HSH),
            (p.inp("shu_t").ap(), p.t["shub"], D, HSH),
            (p.inp("shd_t").ap(), p.t["shdb"], HSH, D)]
    sf = p.slots("wcf", 4)
    sb = p.slots("wcb", 2, "g")
    sv = p.new_sem("wcv")
    buf_f = p.arena("wcff", [128, 4, D], F32, "SCR", 16384)
    buf_b = p.arena("wcbb", [128, 2, D], BF16, "SCR", 0)
    with nc.Block() as blk:
        tiles = []
        for src, dst, R, C in jobs:
            for r in range(R // 128):
                tiles.append((src, dst, r, C))

        @blk.sync
        def _(sync):
            for i, (src, dst, r, C) in enumerate(tiles):
                if i >= 4:
                    W(sync, sv, i - 3)
                sync.dma_start(buf_f[:, i % 4, 0:C],
                               src[r * 128:(r + 1) * 128, :]).then_inc(*sf[i % 4].tick(16))

        @blk.vector
        def _(vector):
            for i, (src, dst, r, C) in enumerate(tiles):
                W(vector, sf[i % 4], 16 * (i // 4 + 1))
                if i >= 2:
                    W(vector, sb[i % 2], 16 * (i // 2))
                vector.tensor_copy(buf_b[:, i % 2, 0:C], buf_f[:, i % 4, 0:C]).then_inc(*sv.tick(1))

        @blk.gpsimd
        def _(g):
            for i, (src, dst, r, C) in enumerate(tiles):
                W(g, sv, i + 1)
                g.dma_start(dst[r * 128:(r + 1) * 128, :], buf_b[:, i % 2, 0:C]).then_inc(*sb[i % 2].tick(16))
            n = len(tiles)
            for k in range(2):
                W(g, sb[k], 16 * ((n - k + 1) // 2))
            p.mark("wcast", g.nop(), 1)


def ph_gconst(p):
    p.phase_begin()
    nc = p.nc
    p.t["ggb"] = nc.alloc_sbuf_tensor("ggb", [128, NFC, 8], BF16)
    p.t["seb"] = nc.alloc_sbuf_tensor("seb", [128, NFC, 1], BF16)
    p.t["selb"] = nc.alloc_sbuf_tensor("selb", [8, 8 * 128], BF16)
    p.t["sg"] = p.arena("sg", [1, TPC], F32, "SCR", 49152)
    gf = p.arena("gcf", [128, NFC, 8], F32, "SCR", 16384)
    sef = p.arena("gce", [128, NFC, 1], F32, "SCR", 22528)
    self_sel = p.arena("gcs", [8, 8 * 128], F32, "SCR", 24576)
    sd = p.slots("gcd", 3)
    with nc.Block() as blk:
        @blk.sync
        def _(sync):
            sync.dma_start(gf[:, :, :],
                           p.inp("ggate").ap().rearrange("(dc p) e -> p dc e", p=128)
                           ).then_inc(*sd[0].tick(16))
            sync.dma_start(sef[:, :, :],
                           p.inp("se_t").ap().rearrange("(dc p) e -> p dc e", p=128),
                           allow_slow_non_contiguous=True).then_inc(*sd[1].tick(16))
            sync.dma_start(self_sel[:, :], p.inp("selmat")[:, :]).then_inc(*sd[2].tick(16))

        @blk.vector
        def _(vector):
            W(vector, sd[0], 16)
            vector.tensor_copy(p.t["ggb"][:, :, :], gf[:, :, :])
            W(vector, sd[1], 16)
            vector.tensor_copy(p.t["seb"][:, :, :], sef[:, :, :])
            W(vector, sd[2], 16)
            vector.drain()
            inst = vector.tensor_copy(p.t["selb"][:, :], self_sel[:, :])
            p.mark("gconst", inst, 1)


def ph_ffn_gate(p, tag):
    p.phase_begin()
    nc = p.nc
    xn = p.t["xn"]
    comb_bc = p.arena("cbc" + tag, [128, NEXP, TPC], BF16, "BIG", 0)
    p.t["comb_bc"] = comb_bc
    sg = p.t["sg"]
    ggb = p.t["ggb"]
    seb = p.t["seb"]
    selb = p.t["selb"]
    comb_tm = p.arena("ctm" + tag, [128, NFC, 8], F32, "SCR", 0)
    comb_T = p.arena("cT" + tag, [8, NFC, 128], BF16, "SCR", 8192)
    row8 = p.arena("r8" + tag, [128, 8, 8], F32, "SCR", 12288)
    s_mm = p.new_sem(tag + "mm")
    s_cp = p.new_sem(tag + "cp")
    s_d = p.new_sem(tag + "d")
    s_w1 = p.new_sem(tag + "w1")
    s_cb = p.new_sem(tag + "cb")
    s_tr = p.new_sem(tag + "tr")
    s_tc = p.new_sem(tag + "tc")
    s_bc = p.new_sem(tag + "bc")
    s_bv = p.new_sem(tag + "bv")
    s_sgm = p.new_sem(tag + "sgm")
    s_sgs = p.new_sem(tag + "sgs")
    with (
        nc.psum_tensor(tag + "sc0", [128, 8], F32) as sc0,
        nc.psum_tensor(tag + "sc1", [128, 8], F32) as sc1,
        nc.psum_tensor(tag + "tr0", [8, 128], F32) as tr0,
        nc.psum_tensor(tag + "tr1", [8, 128], F32) as tr1,
        nc.psum_tensor(tag + "bc0", [128, 128], F32) as bc0,
        nc.psum_tensor(tag + "bc1", [128, 128], F32) as bc1,
        nc.psum_tensor(tag + "sgp", [1, TC], F32) as sgp,
        nc.Block() as blk,
    ):
        @blk.tensor
        def _(tensor):
            p.wait_mark(tensor, "xn")
            p.wait_mark(tensor, "gconst")
            for tt in range(NFC):
                if tt >= 2:
                    W(tensor, s_cp, tt - 1)
                sc = sc0 if tt % 2 == 0 else sc1
                inst = None
                for dc in range(NFC):
                    inst = tensor.matmul(sc[:, :], xn[:, dc, tt * 128:(tt + 1) * 128],
                                         ggb[:, dc, :], start=(dc == 0),
                                         stop=(dc == NFC - 1))
                inst.then_inc(*s_mm.tick(1))
            for tc in range(NTC):
                if tc >= 1:
                    W(tensor, s_sgs, tc)
                for dc in range(NFC):
                    inst = tensor.matmul(sgp[:, :], seb[:, dc, :],
                                         xn[:, dc, tc * TC:(tc + 1) * TC],
                                         start=(dc == 0), stop=(dc == NFC - 1))
                inst.then_inc(*s_sgm.tick(1))
            for tt in range(NFC):
                W(tensor, s_cb, tt + 1)
                if tt >= 2:
                    W(tensor, s_tc, tt - 1)
                tr = tr0 if tt % 2 == 0 else tr1
                tensor.transpose(tr[:, :], comb_tm[:, tt, :], p.t["identf"][:, :]).then_inc(*s_tr.tick(1))
            for e in range(NEXP):
                for tt in range(NFC):
                    i = e * NFC + tt
                    W(tensor, s_tc, NFC)
                    if i >= 2:
                        W(tensor, s_bv, i - 1)
                    bc = bc0 if i % 2 == 0 else bc1
                    tensor.matmul(bc[:, :], selb[:, e * 128:(e + 1) * 128],
                                  comb_T[:, tt, :], start=True, stop=True).then_inc(*s_bc.tick(1))

        @blk.vector
        def _(vector):
            for tt in range(NFC):
                W(vector, s_mm, tt + 1)
                sc = sc0 if tt % 2 == 0 else sc1
                s8 = row8[:, 0, :]
                m1 = row8[:, 1, 0:1]
                eq1 = row8[:, 2, :]
                s2 = row8[:, 3, :]
                m2 = row8[:, 4, 0:1]
                eq2 = row8[:, 5, :]
                dd = row8[:, 6, 0:1]
                w2 = row8[:, 7, 0:1]
                vector.tensor_copy(s8, sc[:, :]).then_inc(*s_cp.tick(1))
                vector.drain()
                vector.tensor_reduce(m1, s8, AX.X, ALU.max)
                vector.drain()
                vector.tensor_scalar(eq1, s8, m1, None, ALU.is_ge)
                vector.drain()
                vector.scalar_tensor_tensor(s2, eq1, NEG, s8, ALU.mult, ALU.add)
                vector.drain()
                vector.tensor_reduce(m2, s2, AX.X, ALU.max)
                vector.drain()
                vector.tensor_scalar(eq2, s2, m2, None, ALU.is_ge)
                vector.drain()
                vector.tensor_sub(dd, m1, m2).then_inc(*s_d.tick(1))
                W(vector, s_w1, tt + 1)
                vector.tensor_scalar(w2, row8[:, 1, 1:2], -1.0, 1.0, ALU.mult, ALU.add)
                vector.drain()
                vector.tensor_scalar(eq1, eq1, row8[:, 1, 1:2], None, ALU.mult)
                vector.drain()
                vector.scalar_tensor_tensor(comb_tm[:, tt, :], eq2, w2, eq1,
                                            ALU.mult, ALU.add).then_inc(*s_cb.tick(1))
                vector.drain()
            for tt in range(NFC):
                W(vector, s_tr, tt + 1)
                tr = tr0 if tt % 2 == 0 else tr1
                vector.tensor_copy(comb_T[:, tt, :], tr[:, :]).then_inc(*s_tc.tick(1))
            for e in range(NEXP):
                for tt in range(NFC):
                    i = e * NFC + tt
                    W(vector, s_bc, i + 1)
                    bc = bc0 if i % 2 == 0 else bc1
                    vector.tensor_copy(comb_bc[:, e, tt * 128:(tt + 1) * 128],
                                       bc[:, :]).then_inc(*s_bv.tick(1))
            vector.drain()
            inst = vector.tensor_copy(row8[:, 0, 0:1], row8[:, 0, 0:1])
            p.mark(tag + "_comb", inst, 1)

        @blk.scalar
        def _(scalar):
            for tt in range(NFC):
                W(scalar, s_d, tt + 1)
                scalar.activation(row8[:, 1, 1:2], row8[:, 6, 0:1], AF.Sigmoid).then_inc(*s_w1.tick(1))
            for tc in range(NTC):
                W(scalar, s_sgm, tc + 1)
                scalar.activation(sg[:, tc * TC:(tc + 1) * TC], sgp[:, :],
                                  AF.Sigmoid).then_inc(*s_sgs.tick(1))
            p.mark(tag + "_sg", scalar.nop(), 1)


def ph_ffn_hidden(p, tag, wgsrc, wusrc, n_hc, row0, comb_e, cache, crow0, comb_mark=None):
    p.phase_begin()
    nc = p.nc
    xn = p.t["xn"]
    wrb = p.arena("fhw" + tag, [128, 2, NFC, 128], BF16, "SCR", 0)
    hgs = p.arena("fhg" + tag, [128, TPC], BF16, "SCR", 8192)
    t1 = p.arena("fht" + tag, [128, 2, TPC], BF16, "SCR", 12288)
    sw = p.slots(tag + "w", 2)
    shd = p.slots(tag + "hd", 2, "g")
    s_g = p.new_sem(tag + "g")
    s_u = p.new_sem(tag + "u")
    s_sl = p.new_sem(tag + "sl")
    s_t = p.new_sem(tag + "t")
    with (
        nc.psum_tensor("fhpg" + tag, [128, TPC], F32) as pg,
        nc.psum_tensor("fhpu" + tag, [128, TPC], F32) as pu,
        nc.Block() as blk,
    ):
        @blk.sync
        def _(sync):
            p.wait_mark(sync, "wcast")
            for hc in range(n_hc):
                for half, src in ((0, wgsrc), (1, wusrc)):
                    if hc >= 1:
                        W(sync, s_g if half == 0 else s_u, hc)
                    sync.dma_start(
                        wrb[:, half, :, :],
                        src[row0:row0 + D, hc * 128:(hc + 1) * 128]
                        .rearrange("(dc p) o -> p dc o", p=128),
                    ).then_inc(*sw[half].tick(16))

        @blk.tensor
        def _(tensor):
            p.wait_mark(tensor, "xn")
            for hc in range(n_hc):
                W(tensor, sw[0], 16 * (hc + 1))
                if hc >= 1:
                    W(tensor, s_sl, hc)
                inst = None
                for dc in range(NFC):
                    for tc in range(NTC):
                        inst = tensor.matmul(pg[:, tc * TC:(tc + 1) * TC],
                                             wrb[:, 0, dc, :],
                                             xn[:, dc, tc * TC:(tc + 1) * TC],
                                             start=(dc == 0), stop=(dc == NFC - 1))
                inst.then_inc(*s_g.tick(1))
                W(tensor, sw[1], 16 * (hc + 1))
                if hc >= 1:
                    W(tensor, s_t, hc)
                for dc in range(NFC):
                    for tc in range(NTC):
                        inst = tensor.matmul(pu[:, tc * TC:(tc + 1) * TC],
                                             wrb[:, 1, dc, :],
                                             xn[:, dc, tc * TC:(tc + 1) * TC],
                                             start=(dc == 0), stop=(dc == NFC - 1))
                inst.then_inc(*s_u.tick(1))

        @blk.scalar
        def _(scalar):
            for hc in range(n_hc):
                W(scalar, s_g, hc + 1)
                if hc >= 1:
                    W(scalar, s_t, hc)
                scalar.activation(hgs[:, :], pg[:, :], AF.Silu).then_inc(*s_sl.tick(1))

        @blk.vector
        def _(vector):
            if comb_mark is not None:
                p.wait_mark(vector, comb_mark)
            for hc in range(n_hc):
                W(vector, s_u, hc + 1)
                W(vector, s_sl, hc + 1)
                if hc >= 2:
                    W(vector, shd[hc % 2], 16 * (hc // 2))
                vector.tensor_mul(t1[:, hc % 2, :], pu[:, :], hgs[:, :])
                vector.drain()
                if comb_e is not None:
                    vector.tensor_mul(t1[:, hc % 2, :], t1[:, hc % 2, :], comb_e)
                    vector.drain()
                vector.tensor_copy(t1[:, hc % 2, 0:1],
                                   t1[:, hc % 2, 0:1]).then_inc(*s_t.tick(1))

        @blk.gpsimd
        def _(g):
            for hc in range(n_hc):
                W(g, s_t, hc + 1)
                g.dma_start(cache[crow0 + hc * 128:crow0 + (hc + 1) * 128, :],
                            t1[:, hc % 2, :]).then_inc(*shd[hc % 2].tick(16))
            for k in range(2):
                W(g, shd[k], 16 * ((n_hc - k + 1) // 2))
            p.mark("hc_" + tag, g.nop(), 1)


def ph_ffn_down(p, tag, xprev, xnext, ready_mark, hmark, shmark, sgmark):
    p.phase_begin()
    nc = p.nc
    hsl = p.arena("fd_h" + tag, [128, NEXP, 8, TC], BF16, "BIG", 0)
    shsl = p.arena("fd_s" + tag, [128, NFC, TC], BF16, "XN", 0)
    wdr = p.arena("fd_w" + tag, [128, 2, NEXP, 8, 128], BF16, "XN", 16384)
    sdr = p.arena("fd_sw" + tag, [128, 2, NFC, 128], BF16, "XN", 49152)
    sgb = p.arena("fd_sg" + tag, [128, TC], F32, "SCR", 16384)
    xpr = p.arena("fd_xp" + tag, [128, 4, TC], F32, "SCR", 20480)
    xnw = p.arena("fd_xn" + tag, [128, 2, TC], F32, "SCR", 28672)
    tsh = p.arena("fd_t" + tag, [128, 2, TC], F32, "SCR", 32768)
    sg = p.t["sg"]
    wdb, shdb = p.t["wdb"], p.t["shdb"]
    sh_sl = p.slots(tag + "hs", 2, "g")
    ss_sl = p.slots(tag + "ss", 2, "g")
    sw_sl = p.slots(tag + "ws", 2)
    sd_sl = p.slots(tag + "ds", 2)
    sxp = p.slots(tag + "xp", 4)
    sxn = p.slots(tag + "xn", 2, "g")
    s_rm = p.new_sem(tag + "rm")
    s_sm = p.new_sem(tag + "sm")
    s_ev = p.new_sem(tag + "ev")
    s_sgb = p.new_sem(tag + "sgb")
    s_sgc = p.new_sem(tag + "sgc")
    with (
        nc.psum_tensor("fdr0" + tag, [128, TC], F32) as pr0,
        nc.psum_tensor("fdr1" + tag, [128, TC], F32) as pr1,
        nc.psum_tensor("fds0" + tag, [128, TC], F32) as ps0,
        nc.psum_tensor("fds1" + tag, [128, TC], F32) as ps1,
        nc.psum_tensor("fdsg" + tag, [128, TC], F32) as psg,
        nc.Block() as blk,
    ):
        @blk.gpsimd
        def _(g):
            p.wait_mark(g, hmark)
            p.wait_mark(g, shmark)
            def store_tile(i):
                W(g, s_ev, i + 1)
                tc_, oc_ = divmod(i, NFC)
                g.dma_start(xnext[oc_ * 128:(oc_ + 1) * 128, tc_ * TC:(tc_ + 1) * TC],
                            xnw[:, i % 2, :]).then_inc(*sxn[i % 2].tick(16))

            for tc in range(NTC):
                if tc >= 1:
                    for i in range((tc - 1) * NFC, tc * NFC):
                        store_tile(i)
                if tc >= 1:
                    W(g, s_rm, NFC * tc)
                    W(g, s_sm, NFC * tc)
                for e in range(NEXP):
                    g.dma_start(
                        hsl[:, e, :, :],
                        p.t["hcache"][e * HEXP:(e + 1) * HEXP, tc * TC:(tc + 1) * TC]
                        .rearrange("(hc pp) t -> pp hc t", pp=128),
                    ).then_inc(*sh_sl[tc % 2].tick(16))
                g.dma_start(
                    shsl[:, :, :],
                    p.t["shcache"][:, tc * TC:(tc + 1) * TC]
                    .rearrange("(hc pp) t -> pp hc t", pp=128),
                ).then_inc(*ss_sl[tc % 2].tick(16))
            for i in range((NTC - 1) * NFC, NTC * NFC):
                store_tile(i)
            for k in range(2):
                W(g, sxn[k], 16 * (NTC * NFC // 2))
            p.mark(ready_mark, g.nop(), 1)

        @blk.sync
        def _(sync):
            p.wait_mark(sync, "wcast")
            for tc in range(NTC):
                for oc in range(NFC):
                    i = tc * NFC + oc
                    if i >= 2:
                        W(sync, s_rm, i - 1)
                    for e in range(NEXP):
                        sync.dma_start(
                            wdr[:, i % 2, e, :, :],
                            wdb[e * HEXP:(e + 1) * HEXP, oc * 128:(oc + 1) * 128]
                            .rearrange("(hc pp) o -> pp hc o", pp=128),
                        ).then_inc(*sw_sl[i % 2].tick(16))
                    if i >= 2:
                        W(sync, s_sm, i - 1)
                    sync.dma_start(
                        sdr[:, i % 2, :, :],
                        shdb[:, oc * 128:(oc + 1) * 128]
                        .rearrange("(hc pp) o -> pp hc o", pp=128),
                    ).then_inc(*sd_sl[i % 2].tick(16))
                    if i >= 4:
                        W(sync, s_ev, i - 3)
                    sync.dma_start(xpr[:, i % 4, :],
                                   xprev[oc * 128:(oc + 1) * 128, tc * TC:(tc + 1) * TC]
                                   ).then_inc(*sxp[i % 4].tick(16))

        @blk.tensor
        def _(tensor):
            p.wait_mark(tensor, sgmark)
            for tc in range(NTC):
                if tc >= 1:
                    W(tensor, s_sgc, tc)
                tensor.matmul(psg[:, :], p.t["onesr"][:, :], sg[:, tc * TC:(tc + 1) * TC],
                              start=True, stop=True).then_inc(*s_sgb.tick(1))
                W(tensor, sh_sl[tc % 2], 128 * (tc // 2 + 1))
                W(tensor, ss_sl[tc % 2], 16 * (tc // 2 + 1))
                for oc in range(NFC):
                    i = tc * NFC + oc
                    W(tensor, sw_sl[i % 2], 128 * (i // 2 + 1))
                    if i >= 2:
                        W(tensor, s_ev, i - 1)
                    pr = pr0 if i % 2 == 0 else pr1
                    ps = ps0 if i % 2 == 0 else ps1
                    inst = None
                    for e in range(NEXP):
                        for hc in range(8):
                            inst = tensor.matmul(pr[:, :], wdr[:, i % 2, e, hc, :],
                                                 hsl[:, e, hc, :],
                                                 start=(e == 0 and hc == 0),
                                                 stop=(e == NEXP - 1 and hc == 7))
                    inst.then_inc(*s_rm.tick(1))
                    W(tensor, sd_sl[i % 2], 16 * (i // 2 + 1))
                    for hc in range(NFC):
                        inst = tensor.matmul(ps[:, :], sdr[:, i % 2, hc, :],
                                             shsl[:, hc, :],
                                             start=(hc == 0), stop=(hc == NFC - 1))
                    inst.then_inc(*s_sm.tick(1))

        @blk.vector
        def _(vector):
            for tc in range(NTC):
                W(vector, s_sgb, tc + 1)
                vector.tensor_copy(sgb[:, :], psg[:, :]).then_inc(*s_sgc.tick(1))
                vector.drain()
                for oc in range(NFC):
                    i = tc * NFC + oc
                    W(vector, s_sm, i + 1)
                    if "nosgate" in p.dbg:
                        vector.tensor_copy(tsh[:, i % 2, :], (ps0 if i % 2 == 0 else ps1)[:, :])
                    else:
                        vector.tensor_mul(tsh[:, i % 2, :], (ps0 if i % 2 == 0 else ps1)[:, :],
                                          sgb[:, :])
                    if "noshared" in p.dbg:
                        vector.drain()
                        vector.memset(tsh[:, i % 2, :], 0.0)
                    vector.drain()
                    W(vector, s_rm, i + 1)
                    if "norouted" not in p.dbg:
                        vector.tensor_add(tsh[:, i % 2, :], (pr0 if i % 2 == 0 else pr1)[:, :],
                                          tsh[:, i % 2, :])
                        vector.drain()
                    W(vector, sxp[i % 4], 16 * (i // 4 + 1))
                    if i >= 2:
                        W(vector, sxn[i % 2], 16 * (i // 2))
                    vector.tensor_add(xnw[:, i % 2, :], tsh[:, i % 2, :],
                                      xpr[:, i % 4, :]).then_inc(*s_ev.tick(1))


def ph_attn_out(p, tag, wt, xprev, xnext, ready_mark):
    p.phase_begin()
    nc = p.nc
    ctxsb = p.arena("cxs" + tag, [128, NFC, TPC], BF16, "BIG")
    wrf = p.arena("owf" + tag, [128, 2, NFC, 128], F32, "SCR", 16384)
    wrb = p.arena("owb" + tag, [128, 2, NFC, 128], BF16, "SCR", 32768)
    xpr = p.arena("oxp" + tag, [128, 4, TC], F32, "SCR", 40960)
    xnw = p.arena("oxn" + tag, [128, 2, TC], F32, "SCR", 49152)
    ctxd = p.t["ctxd"]
    sw = p.slots(tag + "w", 2)
    scx = p.slots(tag + "cx", 2, "g")
    sxp = p.slots(tag + "xp", 4)
    sxn = p.slots(tag + "xn", 2, "g")
    swc = p.new_sem(tag + "wc")
    smm = p.new_sem(tag + "mm")
    sev = p.new_sem(tag + "ev")

    with (
        nc.psum_tensor("po" + tag, [128, TPC], F32) as pp,
        nc.Block() as blk,
    ):
        @blk.gpsimd
        def _(g):
            for fc in range(NFC):
                if fc >= 2:
                    W(g, scx[fc % 2], 16 * (fc // 2))
                g.dma_start(ctxsb[:, fc, :], ctxd[fc * 128:(fc + 1) * 128, :]
                            ).then_inc(*scx[fc % 2].tick(16))
            for i in range(NFC * NTC):
                W(g, sev, i + 1)
                oc, tc = divmod(i, NTC)
                g.dma_start(xnext[oc * 128:(oc + 1) * 128, tc * TC:(tc + 1) * TC],
                            xnw[:, i % 2, :]).then_inc(*sxn[i % 2].tick(16))
            for k in range(2):
                W(g, sxn[k], 16 * (NFC * NTC // 2))
            inst = g.nop()
            p.mark(ready_mark, inst, 1)

        @blk.sync
        def _(sync):
            for oc in range(NFC):
                if oc >= 2:
                    W(sync, swc, oc - 1)
                sync.dma_start(
                    wrf[:, oc % 2, :, :],
                    wt[:, oc * 128:(oc + 1) * 128].rearrange("(dc p) o -> p dc o", p=128),
                ).then_inc(*sw[oc % 2].tick(16))
                for tc in range(NTC):
                    i = oc * NTC + tc
                    if i >= 4:
                        W(sync, sev, i - 3)
                    sync.dma_start(xpr[:, i % 4, :],
                                   xprev[oc * 128:(oc + 1) * 128, tc * TC:(tc + 1) * TC]
                                   ).then_inc(*sxp[i % 4].tick(16))
            for k in range(2):
                W(sync, sw[k], 16 * (NFC // 2))

        @blk.vector
        def _(vector):
            for oc in range(NFC):
                W(vector, sw[oc % 2], 16 * (oc // 2 + 1))
                if oc >= 2:
                    W(vector, smm, oc - 1)
                vector.tensor_copy(wrb[:, oc % 2, :, :], wrf[:, oc % 2, :, :]).then_inc(*swc.tick(1))
                W(vector, smm, oc + 1)
                for tc in range(NTC):
                    i = oc * NTC + tc
                    W(vector, sxp[i % 4], 16 * (i // 4 + 1))
                    if i >= 2:
                        W(vector, sxn[i % 2], 16 * (i // 2))
                    vector.tensor_add(xnw[:, i % 2, :], pp[:, tc * TC:(tc + 1) * TC],
                                      xpr[:, i % 4, :]).then_inc(*sev.tick(1))

        @blk.tensor
        def _(tensor):
            for oc in range(NFC):
                W(tensor, swc, oc + 1)
                if oc == 0:
                    for k in range(2):
                        W(tensor, scx[k], 16 * (NFC // 2))
                if oc >= 1:
                    W(tensor, sev, 4 * oc)
                inst = None
                for dc in range(NFC):
                    for tc in range(NTC):
                        inst = tensor.matmul(pp[:, tc * TC:(tc + 1) * TC],
                                             wrb[:, oc % 2, dc, :],
                                             ctxsb[:, dc, tc * TC:(tc + 1) * TC],
                                             start=(dc == 0), stop=(dc == NFC - 1))
                inst.then_inc(*smm.tick(1))



def _sub_ffn(p, tag, xprev, xnext, prev_mark, ready_mark, first):
    u = p.upto
    ph_stats(p, xprev, tag + "st", wait_marks=(prev_mark,))
    ph_xn(p, xprev, tag + "xn")
    if first:
        ph_gconst(p)
        ph_wcast(p)
    if u < 2.4:
        return False
    ph_ffn_gate(p, tag + "g")
    if "comb" in p.dbg:
        _dbg_dump_fm(p, "comb_bc", "dbg_comb", nfc=NEXP, width=TPC,
                     marks=(tag + "g_comb",))
    if u < 2.6:
        return False
    nexp = 2 if "emini" in p.dbg else NEXP
    for e in range(nexp):
        ph_ffn_hidden(p, tag + "h%d" % e, p.t["wgb"], p.t["wub"], 8, e * D,
                      p.t["comb_bc"][:, e, :],
                      p.t["hcache"], e * HEXP, comb_mark=tag + "g_comb")
    if "hc" in p.dbg:
        _dbg_dram(p, "hcache", "dbg_hc", [NEXP * HEXP, TPC], BF16,
                  n_rows=nexp * HEXP)
    if u < 2.8:
        return False
    ph_ffn_hidden(p, tag + "sh", p.t["shgb"], p.t["shub"], NFC, 0, None,
                  p.t["shcache"], 0)
    if "shc" in p.dbg:
        _dbg_dram(p, "shcache", "dbg_shc", [HSH, TPC], BF16)
    if u < 3:
        return False
    ph_ffn_down(p, tag + "d", xprev, xnext, ready_mark,
                "hc_" + tag + "h%d" % (NEXP - 1), "hc_" + tag + "sh",
                tag + "g_sg")
    return True



def build_program(upto=99, dbg=()):
    nc = bass.Bass("TRN2", target_bir_lowering=False)
    p = Prog(nc, upto, dbg)

    x0 = nc.dram_tensor("x0", [D, TPC], F32, kind="ExternalInput")
    xout = nc.dram_tensor("xout", [D, TPC], F32, kind="ExternalOutput")
    p.t["x0"], p.t["xout"] = x0, xout
    p.used_inputs = {"x0"}
    _in_shapes = {"wq_t": [D, D], "wk_t": [D, D], "wv_t": [D, D],
                  "wo_t": [D, D], "bwq": [D, D], "bwkv": [D, 1024],
                  "bwd": [D, D], "ggate": [D, 8],
                  "wg_t": [NEXP, D, HEXP], "wu_t": [NEXP, D, HEXP],
                  "wd_t": [NEXP, HEXP, D], "shg_t": [D, HSH],
                  "shu_t": [D, HSH], "shd_t": [HSH, D], "se_t": [D, 1],
                  "tri01": [512, TC], "ident": [128, 128], "ind": [128, 64],
                  "selmat": [8, 1024], "bmask": [NTC * 16 * 128, TC]}

    def inp(nm):
        if nm not in p.t:
            dt = BF16 if nm == "bmask" else F32
            p.t[nm] = nc.dram_tensor(nm, _in_shapes[nm], dt, kind="ExternalInput")
            p.used_inputs.add(nm)
        return p.t[nm]

    p.inp = inp

    p.t["xd1"] = nc.dram_tensor("xd1", [D, TPC], F32)
    p.t["xd2"] = nc.dram_tensor("xd2", [D, TPC], F32)
    p.t["kcc_in"] = nc.dram_tensor("kcc_in", [D, TPC], BF16)
    p.t["kcc_out"] = nc.dram_tensor("kcc_out", [4 * D, TPC], BF16)
    p.t["vcc_in"] = nc.dram_tensor("vcc_in", [TPC, D], BF16)
    p.t["vcc_out"] = nc.dram_tensor("vcc_out", [4 * TPC, D], BF16)
    p.t["nrma_in"] = nc.dram_tensor("nrma_in", [1, 16], F32)
    p.t["nrma_out"] = nc.dram_tensor("nrma_out", [1, 16], F32)
    p.t["nrmb_in"] = nc.dram_tensor("nrmb_in", [1, 4], F32)
    p.t["nrmb_out"] = nc.dram_tensor("nrmb_out", [1, 4], F32)
    p.t["kbcc_in"] = nc.dram_tensor("kbcc_in", [512, TPC], BF16)
    p.t["kbcc_out"] = nc.dram_tensor("kbcc_out", [2048, TPC], BF16)
    p.t["vbcc_in"] = nc.dram_tensor("vbcc_in", [TPC, 512], BF16)
    p.t["vbcc_out"] = nc.dram_tensor("vbcc_out", [4 * TPC, 512], BF16)
    p.t["ctxd"] = nc.dram_tensor("ctxd", [D, TPC], BF16)
    p.t["hcache"] = nc.dram_tensor("hcache", [NEXP * HEXP, TPC], BF16)
    p.t["shcache"] = nc.dram_tensor("shcache", [HSH, TPC], BF16)
    p.t["wgb"] = nc.dram_tensor("wgb", [NEXP * D, HEXP], BF16)
    p.t["wub"] = nc.dram_tensor("wub", [NEXP * D, HEXP], BF16)
    p.t["wdb"] = nc.dram_tensor("wdb", [NEXP * HEXP, D], BF16)
    p.t["shgb"] = nc.dram_tensor("shgb", [D, HSH], BF16)
    p.t["shub"] = nc.dram_tensor("shub", [D, HSH], BF16)
    p.t["shdb"] = nc.dram_tensor("shdb", [HSH, D], BF16)

    if "xn" in p.dbg:
        p.t["dbg_xn"] = nc.dram_tensor("dbg_xn", [D, TPC], F32, kind="ExternalOutput")

    p.t["xn"] = p.arena("xn", [128, NFC, TPC], BF16, "XN")
    p.t["stat"] = p.arena("stat", [1, 2, TPC], F32, "SCR", 0)
    p.t["xring"] = p.arena("xring", [128, 4, TC], F32, "SCR", 16384)
    p.t["bcb"] = p.arena("bcb", [128, 2, TC], F32, "SCR", 24576)
    p.t["ones1f"] = nc.alloc_sbuf_tensor("ones1f", [128, 1], F32)
    p.t["ones1b"] = nc.alloc_sbuf_tensor("ones1b", [128, 1], BF16)
    p.t["onesr"] = nc.alloc_sbuf_tensor("onesr", [1, 128], F32)
    p.t["rowtmp"] = nc.alloc_sbuf_tensor("rowtmp", [1, 4, TC], F32)
    p.t["nrows"] = nc.alloc_sbuf_tensor("nrows", [1, 8, NFC], F32)
    p.t["negMa"] = nc.alloc_sbuf_tensor("negMa", [128, NFC], F32)
    p.t["negMb"] = nc.alloc_sbuf_tensor("negMb", [128, NFC], F32)
    p.t["identf"] = nc.alloc_sbuf_tensor("identf", [128, 128], F32)
    p.t["identb"] = nc.alloc_sbuf_tensor("identb", [128, 128], BF16)

    sid = p.new_sem("cinit", "s")
    with nc.Block() as blk:
        @blk.sync
        def _(sync):
            sync.dma_start(p.t["identf"][:, :], p.inp("ident")[:, :]).then_inc(*sid.tick(16))

        @blk.vector
        def _(vector):
            p.mark("const", vector.memset(p.t["ones1f"][:, :], 1.0), 1)
            p.mark("const", vector.memset(p.t["ones1b"][:, :], 1.0), 1)
            p.mark("const", vector.memset(p.t["onesr"][:, :], 1.0), 1)
            W(vector, sid, 16)
            p.mark("const", vector.tensor_copy(p.t["identb"][:, :], p.t["identf"][:, :]), 1)

    ph_stats(p, x0, "st0")
    ph_xn(p, x0, "xn0")

    if "xn" in p.dbg:
        _dbg_dump_fm(p, "xn", "dbg_xn")

    if upto >= 1:
        Q = p.arena("Q", [128, NFC, TPC], BF16, "BIG")
        p.t["Q"] = Q
        ph_proj(p, "wq", p.inp("wq_t"), "q", NFC, dst=Q, norm_row=0)
    if upto >= 1.1:
        ph_proj(p, "wk", p.inp("wk_t"), "kf", NFC, dst=p.t["kcc_in"], norm_row=1)
        if "kin" in p.dbg:
            _dbg_dram(p, "kcc_in", "dbg_kin", [D, TPC], BF16)
    if upto >= 1.15:
        ph_proj(p, "wv", p.inp("wv_t"), "vt", NFC, dst=p.t["vcc_in"])
        if "vin" in p.dbg:
            _dbg_dram(p, "vcc_in", "dbg_vin", [TPC, D], BF16)
    if upto >= 1.2:
        ph_cc(p, "a", [("kcc_in", "kcc_out"), ("vcc_in", "vcc_out")],
              qrow=0, krow=1, grow=2, scale=SM_SCALE, nh=HEADS,
              negM=p.t["negMa"], q_nrm_mark="wq_nrm", k_nrm_mark="wk_nrm",
              nin=p.t["nrma_in"], nout=p.t["nrma_out"])
    if upto >= 1.5:
        ph_attn_a(p)
        if "ctx" in p.dbg:
            _dbg_dram(p, "ctxd", "dbg_ctx", [D, TPC], BF16,
                      n_rows=(256 if "mini" in p.dbg else None))
    cur_x, cur_mark = x0, None
    if upto >= 2:
        ph_attn_out(p, "wo", p.inp("wo_t"), x0, p.t["xd1"], "x1rdy")
        cur_x, cur_mark = p.t["xd1"], "x1rdy"
    if upto >= 2.2:
        done = _sub_ffn(p, "f1", p.t["xd1"], p.t["xd2"], "x1rdy", "x2rdy", first=True)
        if done:
            cur_x, cur_mark = p.t["xd2"], "x2rdy"
    if upto >= 3.2:
        ph_stats(p, p.t["xd2"], "bst", wait_marks=("x2rdy",))
        ph_xn(p, p.t["xd2"], "bxn")
        ph_proj(p, "bq", p.inp("bwq"), "q", NFC, dst=p.t["Q"], norm_row=3)
        ph_proj(p, "bk", p.inp("bwkv"), "kf", 4, dst=p.t["kbcc_in"], norm_row=4)
        ph_proj(p, "bv", p.inp("bwkv"), "vt", 4, dst=p.t["vbcc_in"], wt_col0=512)
        ph_cc(p, "b", [("kbcc_in", "kbcc_out"), ("vbcc_in", "vbcc_out")],
              qrow=3, krow=4, grow=5, scale=SM_SCALE, nh=HEADS,
              negM=p.t["negMb"], q_nrm_mark="bq_nrm", k_nrm_mark="bk_nrm",
              nin=p.t["nrmb_in"], nout=p.t["nrmb_out"], group_of_head=True)
        ph_attn_b(p)
        if "ctxb" in p.dbg:
            _dbg_dram(p, "ctxd", "dbg_ctxb", [D, TPC], BF16)
    if upto >= 4:
        ph_attn_out(p, "bo", p.inp("bwd"), p.t["xd2"], p.t["xd1"], "x3rdy")
        cur_x, cur_mark = p.t["xd1"], "x3rdy"
    if upto >= 5:
        done = _sub_ffn(p, "f2", p.t["xd1"], xout, "x3rdy", "x4rdy", first=False)
        if done:
            cur_x, cur_mark = xout, "x4rdy"

    if "q" in p.dbg:
        _dbg_dump_fm(p, "Q", "dbg_q", marks=("wq_ev",))
    if "kv" in p.dbg:
        _dbg_dram(p, "kcc_out", "dbg_k", [4 * D, TPC], BF16)
        _dbg_dram(p, "vcc_out", "dbg_v", [4 * TPC, D], BF16)
    if "negm" in p.dbg:
        _dbg_rows(p, "negMa", "dbg_negm", NFC, marks=("negm_a",))

    if cur_x is not xout:
        sg = p.new_sem("out", "g")
        with nc.Block() as blk:
            @blk.gpsimd
            def _(g):
                if cur_mark is not None:
                    p.wait_mark(g, cur_mark)
                g.dma_start(p.t["xout"][:, :], cur_x[:, :]).then_inc(*sg.tick(16))
                W(g, sg, 16)

    p.ctx.close()
    nc.finalize()
    return nc, set(p.used_inputs)



def _run(inputs, upto=99, dbg=(), _nc_out=None):
    x = np.asarray(inputs["L0_attn_a_x"], dtype=np.float32)
    shards = _shard_x(x)
    shared = _prep_inputs(inputs)
    nc, used = build_program(upto=upto, dbg=dbg)
    if _nc_out is not None:
        _nc_out.append(nc)
    in_maps = []
    for core in range(NCORE):
        m = dict(shared)
        m.update(_host_tables(core))
        m["x0"] = shards[core]
        in_maps.append({k: v for k, v in m.items() if k in used})
    return run_bass_kernel_spmd(nc, in_maps, core_ids=list(range(NCORE)))


def kernel(**inputs):
    res = _run(inputs)
    outs = [res.results[c]["xout"] for c in range(NCORE)]
    return _unshard_x(outs)


# version 02
# speedup vs baseline: 1.0178x; 1.0178x over previous
import os
import sys
import numpy as np

sys.path.insert(0, "/opt/trn_rl_repo")

import concourse.bass as bass
import concourse.mybir as mybir
from concourse.bass_utils import run_bass_kernel_spmd

F32 = mybir.dt.float32
BF16 = mybir.dt.bfloat16
I32 = mybir.dt.int32
AF = mybir.ActivationFunctionType
ALU = mybir.AluOpType
AX = mybir.AxisListType

B = 2
S = 8192
D = 2048
NCORE = 8
TPC = 2048
NTC = 4
TC = 512
NFC = 16
HEADS = 16
HD = 128
GROUPS = 4
NEXP = 8
HEXP = 1024
HSH = 2048
EPS = 1e-5
SM_SCALE = float(HD) ** -0.5
NEG = -1.0e9


def _gchunk(c):
    return [4 * j + c for j in range(NTC)]


def _phys_chunk(w):
    g = w // 4
    return (g % 4) * 16 + (g // 4) * 4 + (w % 4)



def _shard_x(x):
    shards = []
    for core in range(NCORE):
        s, c = divmod(core, 4)
        toks = np.concatenate([x[s, g * TC:(g + 1) * TC, :] for g in _gchunk(c)], axis=0)
        shards.append(np.ascontiguousarray(toks.T))
    return shards


def _unshard_x(shards):
    out = np.empty((B, S, D), dtype=np.float32)
    for core in range(NCORE):
        s, c = divmod(core, 4)
        tok = np.ascontiguousarray(np.asarray(shards[core]).T)
        for j, g in enumerate(_gchunk(c)):
            out[s, g * TC:(g + 1) * TC, :] = tok[j * TC:(j + 1) * TC, :]
    return out


def _host_consts():
    tri = np.zeros((4, 128, TC), dtype=np.float32)
    ql = np.arange(TC)[None, :]
    kl = np.arange(128)[:, None]
    for t in range(4):
        tri[t] = (ql >= 128 * t + kl).astype(np.float32)
    return {"tri01": tri.reshape(512, TC), "ident": np.eye(128, dtype=np.float32)}


def _host_tables(core):
    import ml_dtypes
    s, c = divmod(core, 4)
    ind = np.zeros((128, 64), dtype=np.float32)
    tri = np.zeros((4, 128, TC), dtype=np.float32)
    ql = np.arange(TC)[None, :]
    kl = np.arange(128)[:, None]
    for t in range(4):
        tri[t] = (ql >= 128 * t + kl).astype(np.float32)
    bmask = np.zeros((NTC, 16, 128, TC), dtype=np.float32)
    for j in range(NTC):
        for u in range(16):
            col = j * 16 + u
            if u < 4 * c:
                bmask[j, u] = 1.0
            elif u < 4 * c + 4:
                bmask[j, u] = tri[u - 4 * c]
            else:
                ind[:, col] = NEG
    bmask = bmask.reshape(NTC * 16 * 128, TC).astype(ml_dtypes.bfloat16)
    return {"ind": ind, "bmask": bmask}


def _prep_inputs(inputs):
    a = {k: np.asarray(v) for k, v in inputs.items()}
    T = lambda m: np.ascontiguousarray(np.asarray(m).T)
    shared = {
        "wq_t": T(a["L0_attn_a_Wq"]),
        "wk_t": T(a["L0_attn_a_Wk"]),
        "wv_t": T(a["L0_attn_a_Wv"]),
        "wo_t": T(a["L0_attn_a_Wo"]),
        "bwq": np.ascontiguousarray(a["L0_attn_b_w_q"]),
        "bwkv": np.ascontiguousarray(a["L0_attn_b_w_kv"]),
        "bwd": np.ascontiguousarray(a["L0_attn_b_w_dense"]),
        "ggate": T(a["L0_ffn_gate_w"]),
        "wg_t": np.ascontiguousarray(np.transpose(a["L0_ffn_w_gate"], (0, 2, 1))),
        "wu_t": np.ascontiguousarray(np.transpose(a["L0_ffn_w_up"], (0, 2, 1))),
        "wd_t": np.ascontiguousarray(np.transpose(a["L0_ffn_w_down"], (0, 2, 1))),
        "shg_t": T(a["L0_ffn_sh_gate"]),
        "shu_t": T(a["L0_ffn_sh_up"]),
        "shd_t": T(a["L0_ffn_sh_down"]),
        "se_t": T(a["L0_ffn_se_gate_w"]),
    }
    sel = np.zeros((8, 8 * 128), dtype=np.float32)
    for e in range(8):
        sel[e, e * 128:(e + 1) * 128] = 1.0
    shared["selmat"] = sel
    shared.update(_host_consts())
    return shared



class RSem:

    def __init__(self, h, b):
        self.h = h
        self.b = b
        self.t = 0

    def tick(self, n):
        self.t += n
        return (self.h, n)


def W(eng, s, v):
    if isinstance(s, RSem):
        eng.wait_ge(s.h, s.b + v)
    else:
        eng.wait_ge(s, v)


class Prog:
    def __init__(self, nc, upto, dbg):
        import contextlib
        self.nc = nc
        self.upto = upto
        self.dbg = set(dbg)
        self.ctx = contextlib.ExitStack()
        self.t = {}
        self._marks = {}
        self._nsem = 0
        self._pool = {}
        self._phase = 0
        self._rentals = {}

    def phase_begin(self):
        self._phase += 1
        for ph in list(self._rentals):
            if ph <= self._phase - 2:
                for klass, rs in self._rentals.pop(ph):
                    self._pool.setdefault(klass, []).append((rs.h, rs.b + rs.t))

    def _raw_sem(self, name):
        self._nsem += 1
        return self.ctx.enter_context(self.nc.semaphore(f"{name}_{self._nsem}"))

    def new_sem(self, name, klass="c"):
        pool = self._pool.setdefault(klass, [])
        if pool:
            h, b = pool.pop()
            rs = RSem(h, b)
        else:
            rs = RSem(self._raw_sem(klass + name), 0)
        self._rentals.setdefault(self._phase, []).append((klass, rs))
        return rs

    def mark(self, name, inst, n=1):
        if name not in self._marks or self._marks[name][2]:
            self._marks[name] = [self._raw_sem("mk_" + name), 0, False]
        m = self._marks[name]
        inst.then_inc(m[0], n)
        m[1] += n

    def alias_mark(self, name, sem, total):
        if isinstance(sem, RSem):
            self._marks[name] = [sem.h, sem.b + total, True]
        else:
            self._marks[name] = [sem, total, True]

    def wait_mark(self, eng, name):
        m = self._marks[name]
        eng.wait_ge(m[0], m[1])

    def slots(self, name, n, klass="s"):
        return [self.new_sem(f"{name}{k}", klass) for k in range(n)]

    def arena(self, name, shape, dtype, region, off=0):
        base = {"XN": 159744, "BIG": 94208, "SCR": 36864}[region]
        return self.nc.alloc_sbuf_tensor_at(name, shape, dtype, offset=base + off)



def ph_stats(p, xsrc, tag, wait_marks=()):
    p.phase_begin()
    nc = p.nc
    xr = p.t["xring"]
    ones = p.t["ones1f"]
    row = p.t["rowtmp"]
    stat = p.t["stat"]
    sx = p.slots(tag + "x", 4)
    sq_done = p.new_sem(tag + "q")
    mm = p.new_sem(tag + "m")
    spr = p.new_sem(tag + "p")
    sv = p.new_sem(tag + "v")
    ss = p.new_sem(tag + "s")
    with (
        nc.psum_tensor("ps" + tag, [1, 2, TC], F32) as ps,
        nc.sbuf_tensor("sq" + tag, [128, 2, TC], F32) as sq,
        nc.Block() as blk,
    ):
        @blk.sync
        def _(sync):
            for mname in wait_marks:
                p.wait_mark(sync, mname)
            for i in range(NTC * NFC):
                if i >= 4:
                    W(sync, mm, 2 * (i - 3))
                sync.dma_start(xr[:, i % 4, :],
                               xsrc[(i % NFC) * 128:(i % NFC + 1) * 128,
                                    (i // NFC) * TC:(i // NFC + 1) * TC]).then_inc(*sx[i % 4].tick(16))
            for k in range(4):
                W(sync, sx[k], 16 * (NTC * NFC // 4))

        @blk.tensor
        def _(tensor):
            p.wait_mark(tensor, "const")
            for i in range(NTC * NFC):
                tc, fc = divmod(i, NFC)
                W(tensor, sx[i % 4], 16 * (i // 4 + 1))
                if fc == 0 and tc >= 1:
                    W(tensor, spr, tc)
                tensor.matmul(ps[:, 0, :], ones[:, :], xr[:, i % 4, :],
                              start=(fc == 0), stop=(fc == NFC - 1))
                W(tensor, sq_done, i + 1)
                tensor.matmul(ps[:, 1, :], ones[:, :], sq[:, i % 2, :],
                              start=(fc == 0), stop=(fc == NFC - 1)).then_inc(*mm.tick(2))

        @blk.vector
        def _(vector):
            for tc in range(NTC):
                W(vector, mm, 2 * NFC * (tc + 1))
                sl = slice(tc * TC, (tc + 1) * TC)
                mean, ex2, var = row[:, 0, :], row[:, 1, :], row[:, 2, :]
                vector.tensor_scalar(mean, ps[:, 0, :], 1.0 / D, None, ALU.mult)
                vector.tensor_scalar(ex2, ps[:, 1, :], 1.0 / D, None, ALU.mult).then_inc(*spr.tick(1))
                vector.drain()
                vector.tensor_mul(var, mean, mean)
                vector.drain()
                vector.scalar_tensor_tensor(var, ex2, EPS, var,
                                            ALU.add, ALU.subtract).then_inc(*sv.tick(1))
                W(vector, ss, tc + 1)
                inst = vector.tensor_mul(stat[:, 0, sl], mean, stat[:, 1, sl])
                p.mark("stat", inst, 1)
                vector.drain()

        @blk.scalar
        def _(scalar):
            for tc in range(NTC):
                for fc in range(NFC):
                    i = tc * NFC + fc
                    W(scalar, sx[i % 4], 16 * (i // 4 + 1))
                    if i >= 2:
                        W(scalar, mm, 2 * (i - 1))
                    scalar.activation(sq[:, i % 2, :], xr[:, i % 4, :],
                                      AF.Square).then_inc(*sq_done.tick(1))
                sl = slice(tc * TC, (tc + 1) * TC)
                W(scalar, sv, tc + 1)
                scalar.drain()
                scalar.activation(row[:, 3, :], row[:, 2, :], AF.Ln)
                scalar.drain()
                scalar.activation(stat[:, 1, sl], row[:, 3, :], AF.Exp,
                                  scale=-0.5).then_inc(*ss.tick(1))


def ph_xn(p, xsrc, tag, wait_marks=()):
    p.phase_begin()
    nc = p.nc
    xbig = p.arena("xnb" + tag, [128, 2, NFC, TC], F32, "BIG")
    xn = p.t["xn"]
    onesr = p.t["onesr"]
    stat = p.t["stat"]
    bb = p.t["bcb"]
    sx = p.slots(tag + "x", 2)
    mm = p.new_sem(tag + "m")
    sv = p.new_sem(tag + "v")
    with (
        nc.psum_tensor("ps" + tag, [128, 2, TC], F32) as ps,
        nc.Block() as blk,
    ):
        @blk.sync
        def _(sync):
            for mname in wait_marks:
                p.wait_mark(sync, mname)
            for tc in range(NTC):
                for fc in range(NFC):
                    if tc >= 2 and fc == 0:
                        W(sync, sv, tc - 1)
                    sync.dma_start(xbig[:, tc % 2, fc, :],
                                   xsrc[fc * 128:(fc + 1) * 128,
                                        tc * TC:(tc + 1) * TC]).then_inc(*sx[tc % 2].tick(16))
            for k in range(2):
                W(sync, sx[k], 16 * NFC * (NTC // 2))

        @blk.tensor
        def _(tensor):
            p.wait_mark(tensor, "stat")
            for tc in range(NTC):
                if tc >= 1:
                    W(tensor, sv, tc)
                tensor.matmul(ps[:, 0, :], onesr[:, :], stat[:, 1, tc * TC:(tc + 1) * TC],
                              start=True, stop=True)
                tensor.matmul(ps[:, 1, :], onesr[:, :], stat[:, 0, tc * TC:(tc + 1) * TC],
                              start=True, stop=True).then_inc(*mm.tick(1))

        @blk.vector
        def _(vector):
            for tc in range(NTC):
                W(vector, mm, tc + 1)
                if tc >= 1:
                    vector.drain()
                vector.tensor_copy(bb[:, 0, :], ps[:, 0, :])
                vector.tensor_copy(bb[:, 1, :], ps[:, 1, :])
                vector.drain()
                W(vector, sx[tc % 2], 16 * NFC * (tc // 2 + 1))
                for fc in range(NFC):
                    vector.tensor_mul(xbig[:, tc % 2, fc, :], xbig[:, tc % 2, fc, :],
                                      bb[:, 0, :])
                vector.drain()
                for fc in range(NFC - 1):
                    vector.tensor_sub(xn[:, fc, tc * TC:(tc + 1) * TC],
                                      xbig[:, tc % 2, fc, :], bb[:, 1, :])
                vector.drain()
                vector.tensor_sub(xn[:, NFC - 1, tc * TC:(tc + 1) * TC],
                                  xbig[:, tc % 2, NFC - 1, :], bb[:, 1, :]).then_inc(*sv.tick(1))
            p.alias_mark("xn", sv, NTC)


def _dbg_rows(p, srcname, outname, w, marks=()):
    p.phase_begin()
    nc = p.nc
    out = nc.dram_tensor(outname, [128, w], F32, kind="ExternalOutput")
    sd = p.new_sem("dbgr", "g")
    with nc.Block() as blk:
        @blk.gpsimd
        def _(g):
            for m in marks:
                p.wait_mark(g, m)
            g.dma_start(out[:, :], p.t[srcname][:, :]).then_inc(*sd.tick(16))
            W(g, sd, 16)


def _dbg_dump_fm(p, srcname, outname, nfc=NFC, width=TPC, marks=("xn",)):
    p.phase_begin()
    nc = p.nc
    src = p.t[srcname]
    out = p.t.get(outname)
    if out is None:
        out = nc.dram_tensor(outname, [nfc * 128, width], F32, kind="ExternalOutput")
        p.t[outname] = out
    sv = p.new_sem("dmp")
    sx = p.slots("dmpx", 2)
    with (
        nc.sbuf_tensor("dmp" + outname, [128, 2, width], F32) as tmp,
        nc.Block() as blk,
    ):
        @blk.vector
        def _(vector):
            for m in marks:
                p.wait_mark(vector, m)
            for fc in range(nfc):
                if fc >= 2:
                    W(vector, sx[fc % 2], 16 * ((fc - 2) // 2 + 1))
                vector.tensor_copy(tmp[:, fc % 2, :], src[:, fc, :]).then_inc(*sv.tick(1))

        @blk.sync
        def _(sync):
            for fc in range(nfc):
                W(sync, sv, fc + 1)
                sync.dma_start(out[fc * 128:(fc + 1) * 128, :],
                               tmp[:, fc % 2, :]).then_inc(*sx[fc % 2].tick(16))
            for k in range(2):
                W(sync, sx[k], 16 * (nfc // 2))



def ph_proj(p, tag, wt, kind, n_oc, *, dst, dst_col0=0, dst_row0=0,
            norm_row=None, norm_col0=0, wt_col0=0, n_dc=NFC):
    p.phase_begin()
    nc = p.nc
    xn = p.t["xn"]
    nrows = p.t.get("nrows")
    row = p.t["rowtmp"]
    sw = p.slots(tag + "w", 2)
    swc = p.new_sem(tag + "wc")
    smm = p.new_sem(tag + "mm")
    sev = p.new_sem(tag + "ev")
    ssq = p.new_sem(tag + "sq")
    snm = p.new_sem(tag + "nm")
    snr = p.new_sem(tag + "nr")
    svt = p.new_sem(tag + "vt")
    svc = p.new_sem(tag + "vc")
    skd = p.slots(tag + "kd", 2, "g")
    if "nonorm" in p.dbg:
        norm_row = None
    normed = norm_row is not None

    wrf = p.arena("wrf" + tag, [128, 2, n_dc, 128], F32, "SCR", 16384)
    wrb = p.arena("wrb" + tag, [128, 2, n_dc, 128], BF16, "SCR", 32768)
    stg = p.arena("stg" + tag, [128, 2, TPC], BF16, "SCR", 40960)
    vstg = p.arena("vsg" + tag, [128, 2, NFC, 128], BF16, "SCR", 49152)
    with (
        nc.sbuf_tensor("sqn" + tag, [128, TPC], BF16) as sqn,
        nc.psum_tensor("pp" + tag, [128, TPC], F32) as pp,
        nc.psum_tensor("vt0" + tag, [128, 128], BF16) as vtp0,
        nc.psum_tensor("vt1" + tag, [128, 128], BF16) as vtp1,
        nc.psum_tensor("np" + tag, [1, 2, TC], F32) as nps,
        nc.Block() as blk,
    ):
        @blk.sync
        def _(sync):
            for oc in range(n_oc):
                if oc >= 2:
                    W(sync, swc, oc - 1)
                c0 = wt_col0 + oc * 128
                sync.dma_start(
                    wrf[:, oc % 2, :, :],
                    wt[:, c0:c0 + 128].rearrange("(dc p) o -> p dc o", p=128),
                ).then_inc(*sw[oc % 2].tick(16))
            for k in range(2):
                W(sync, sw[k], 16 * ((n_oc - k + 1) // 2))

        @blk.tensor
        def _(tensor):
            p.wait_mark(tensor, "xn")
            p.wait_mark(tensor, "const")
            for oc in range(n_oc):
                W(tensor, swc, oc + 1)
                if oc >= 1:
                    W(tensor, sev, oc)
                    if normed:
                        W(tensor, ssq, oc)
                inst = None
                for dc in range(n_dc):
                    for tc in range(NTC):
                        inst = tensor.matmul(pp[:, tc * TC:(tc + 1) * TC],
                                             wrb[:, oc % 2, dc, :],
                                             xn[:, dc, tc * TC:(tc + 1) * TC],
                                             start=(dc == 0), stop=(dc == n_dc - 1))
                inst.then_inc(*smm.tick(1))
                if normed and "nnm" not in p.dbg:
                    W(tensor, ssq, oc + 1)
                    for tc in range(NTC):
                        g = 4 * oc + tc
                        if g >= 2:
                            W(tensor, snr, g - 1)
                        tensor.matmul(nps[:, tc % 2, :], p.t["ones1b"][:, :],
                                      sqn[:, tc * TC:(tc + 1) * TC],
                                      start=True, stop=True).then_inc(*snm.tick(1))
                if kind == "vt":
                    W(tensor, sev, oc + 1)
                    for j in range(NFC):
                        m = oc * NFC + j
                        if m >= 2:
                            W(tensor, svc, m - 1)
                        vtp = vtp0 if m % 2 == 0 else vtp1
                        tensor.transpose(vtp[:, :],
                                         stg[:, oc % 2, j * 128:(j + 1) * 128],
                                         p.t["identb"][:, :]).then_inc(*svt.tick(1))

        @blk.scalar
        def _(scalar):
            if normed:
                for oc in range(n_oc):
                    W(scalar, sev, oc + 1)
                    if oc >= 1 and "nnm" not in p.dbg:
                        W(scalar, snm, 4 * oc)
                    scalar.activation(sqn[:, :], pp[:, :], AF.Square).then_inc(*ssq.tick(1))

        @blk.vector
        def _(vector):
            for oc in range(n_oc):
                W(vector, sw[oc % 2], 16 * (oc // 2 + 1))
                if oc >= 2:
                    W(vector, smm, oc - 1)
                vector.tensor_copy(wrb[:, oc % 2, :, :], wrf[:, oc % 2, :, :]).then_inc(*swc.tick(1))
                W(vector, smm, oc + 1)
                if kind == "q":
                    vector.tensor_copy(dst[:, dst_col0 + oc, :], pp[:, :]).then_inc(*sev.tick(1))
                else:
                    if oc >= 2:
                        if kind == "kf":
                            W(vector, skd[oc % 2], 16 * (oc // 2))
                        else:
                            W(vector, svt, 16 * (oc - 1))
                    vector.tensor_copy(stg[:, oc % 2, :], pp[:, :]).then_inc(*sev.tick(1))
                if normed and "nnm" not in p.dbg:
                    for tc in range(NTC):
                        g = 4 * oc + tc
                        W(vector, snm, g + 1)
                        vector.tensor_reduce(row[:, 3, tc:tc + 1], nps[:, tc % 2, :],
                                             AX.X, ALU.max).then_inc(*snr.tick(1))
                    vector.drain()
                    vector.tensor_reduce(row[:, 2, 0:1], row[:, 3, 0:4], AX.X, ALU.max)
                    vector.drain()
                    inst = vector.tensor_copy(nrows[:, norm_row, norm_col0 + oc:norm_col0 + oc + 1],
                                              row[:, 2, 0:1])
                    p.mark(tag + "_nrm", inst, 1)
                    vector.drain()
                if kind == "vt":
                    for j in range(NFC):
                        m = oc * NFC + j
                        W(vector, svt, m + 1)
                        if oc >= 2 and j == 0:
                            W(vector, skd[oc % 2], 16 * (oc // 2))
                        vector.tensor_copy(vstg[:, oc % 2, j, :],
                                           (vtp0 if m % 2 == 0 else vtp1)[:, :]).then_inc(*svc.tick(1))

        @blk.gpsimd
        def _(g):
            for oc in range(n_oc):
                if kind == "kf":
                    W(g, sev, oc + 1)
                    r0 = dst_row0 + oc * 128
                    g.dma_start(dst[r0:r0 + 128, :], stg[:, oc % 2, :]).then_inc(*skd[oc % 2].tick(16))
                elif kind == "vt":
                    W(g, svc, 16 * (oc + 1))
                    c0 = dst_col0 + oc * 128
                    g.dma_start(dst.ap().rearrange("(j p) w -> p j w", p=128)[:, :, c0:c0 + 128],
                                vstg[:, oc % 2, :, :]).then_inc(*skd[oc % 2].tick(16))
            if kind in ("kf", "vt"):
                for k in range(2):
                    W(g, skd[k], 16 * ((n_oc - k + 1) // 2))

    if kind == "q":
        p.alias_mark(tag + "_ev", sev, n_oc)


def ph_cc(p, tag, pairs, krow, grow, nin, nout):
    p.phase_begin()
    nc = p.nc
    s_nrm = p.new_sem(tag + "nr", "g")
    scc = p.new_sem(tag + "cc", "g")
    sd = p.slots(tag + "cd", 2, "g")
    nrows = p.t["nrows"]
    grp = [[0, 1, 2, 3], [4, 5, 6, 7]]
    nk = nin.shape[1]
    with nc.Block() as blk:
        @blk.gpsimd
        def _(g):
            g.dma_start(nin[:, :], nrows[:, krow, 0:nk]).then_inc(*sd[0].tick(16))
            W(g, sd[0], 16)
            g.collective_compute("AllReduce", ALU.max, grp,
                                 [nin.ap().opt()], [nout.ap().opt()]).then_inc(*s_nrm.tick(1))
            for (ci, co) in pairs:
                g.collective_compute("AllGather", ALU.bypass, grp,
                                     [p.t[ci].ap().opt()],
                                     [p.t[co].ap().opt()]).then_inc(*scc.tick(1))
            W(g, s_nrm, 1)
            g.dma_start(nrows[:, grow, 0:nk], nout[:, :]).then_inc(*sd[1].tick(16))
            W(g, sd[1], 16)
            p.mark("nrmg_" + tag, g.nop(), 1)
        p.alias_mark("gath_" + tag, scc, len(pairs))


def _negm_emit_vector(p, tag, vector, qrow, grow, scale, nh, negM,
                      q_nrm_mark, bps, s1, s2, s3, s4, s5, group_of_head=None):
    nrows = p.t["nrows"]
    row = p.t["rowtmp"]
    p.wait_mark(vector, q_nrm_mark)
    p.wait_mark(vector, "nrmg_" + tag)
    if group_of_head is None:
        vector.tensor_copy(row[:, 3, 0:nh], nrows[:, grow, 0:nh])
    else:
        for h in range(nh):
            vector.tensor_copy(row[:, 3, h:h + 1],
                               nrows[:, grow, h // GROUPS:h // GROUPS + 1])
    vector.drain()
    vector.tensor_mul(row[:, 0, 0:nh], nrows[:, qrow, 0:nh],
                      row[:, 3, 0:nh]).then_inc(*s1.tick(1))
    W(vector, s2, 1)
    vector.tensor_scalar(row[:, 2, 0:nh], row[:, 1, 0:nh],
                         -scale, None, ALU.mult).then_inc(*s3.tick(1))
    W(vector, s4, 1)
    vector.tensor_copy(negM[:, 0:nh], bps[:, 0:nh]).then_inc(*s5.tick(1))
    vector.drain()


def _negm_emit_scalar(p, scalar, s1, s2, nh):
    row = p.t["rowtmp"]
    W(scalar, s1, 1)
    scalar.activation(row[:, 1, 0:nh], row[:, 0, 0:nh], AF.Sqrt).then_inc(*s2.tick(1))
    scalar.drain()


def _negm_emit_tensor(p, tensor, s3, s4, bps, nh):
    row = p.t["rowtmp"]
    W(tensor, s3, 1)
    tensor.matmul(bps[:, 0:nh], p.t["onesr"][:, :], row[:, 2, 0:nh],
                  start=True, stop=True).then_inc(*s4.tick(1))


def _dbg_dram(p, src_name, out_name, shape, dtype=F32, n_rows=None):
    p.phase_begin()
    nc = p.nc
    out = nc.dram_tensor(out_name, shape, dtype, kind="ExternalOutput")
    r = shape[0] if n_rows is None else n_rows
    sd = p.new_sem("dbgd", "g")
    with nc.Block() as blk:
        @blk.gpsimd
        def _(g):
            g.dma_start(out[:r, :], p.t[src_name][:r, :]).then_inc(*sd.tick(16))
            W(g, sd, 16)



def ph_attn_a(p):
    p.phase_begin()
    nc = p.nc
    Q = p.t["Q"]
    Vsb = p.arena("a3v", [128, 2, 64, 256], BF16, "XN", 0)
    pt = p.arena("a3p", [128, 2, 1024], BF16, "SCR", 0)
    lbsb = p.arena("a3lb", [128, 1024], F32, "SCR", 4096)
    ctile = p.arena("a3c", [128, 2, 1024], BF16, "SCR", 8192)
    lrow = p.arena("a3lr", [1, 1024], F32, "SCR", 12288)
    Ksb = p.arena("a3k", [128, 2, 4 * TPC], BF16, "SCR", 16384)
    negM = p.t["negMa"]
    kcc, vcc, ctxd = p.t["kcc_out"], p.t["vcc_out"], p.t["ctxd"]
    skv = p.slots("a3kv", 2, "g")
    svv = p.slots("a3vv", 2, "g")
    sctx = p.slots("a3cx", 2, "g")
    s_st = p.new_sem("a3st")
    s_xp = p.new_sem("a3xp")
    s_pv = p.new_sem("a3pv")
    s_rc = p.new_sem("a3rc")
    s_bc = p.new_sem("a3bc")
    s_ev = p.new_sem("a3ev")
    sn1, sn2, sn3, sn4, sn5 = (p.new_sem("a3n1"), p.new_sem("a3n2"),
                               p.new_sem("a3n3"), p.new_sem("a3n4"),
                               p.new_sem("a3n5"))
    NH = 2 if "mini" in p.dbg else HEADS
    NH2 = NH * 2

    with (
        nc.psum_tensor("a3stp", [128, 2, 1024], F32) as st,
        nc.psum_tensor("a3op", [128, 1024], F32) as ops,
        nc.psum_tensor("a3lp", [1, 1024], F32) as lps,
        nc.Block() as blk,
    ):
        @blk.gpsimd
        def _(g):
            p.wait_mark(g, "gath_a")

            def store_ctx(hh):
                h, half = hh // 2, hh % 2
                W(g, s_ev, hh + 1)
                g.dma_start(ctxd[h * 128:(h + 1) * 128, half * 1024:(half + 1) * 1024],
                            ctile[:, hh % 2, :]).then_inc(*sctx[hh % 2].tick(16))

            for h in range(NH):
                if h >= 2:
                    store_ctx(2 * (h - 2))
                    store_ctx(2 * (h - 2) + 1)
                sl = h % 2
                if h >= 2:
                    W(g, s_st, 128 * (h - 1))
                for r in range(4):
                    g.dma_start(Ksb[:, sl, r * TPC:(r + 1) * TPC],
                                kcc[r * D + h * 128:r * D + (h + 1) * 128, :]
                                ).then_inc(*skv[sl].tick(16))
                if h % 2 == 0:
                    P = h // 2
                    vs = P % 2
                    if P >= 2:
                        W(g, s_pv, 256 * (P - 1))
                    vv = vcc.ap().rearrange("(kb pp) f -> pp kb f", pp=128)
                    for kq in range(8):
                        g.dma_start(
                            Vsb[:, vs, kq * 8:(kq + 1) * 8, :],
                            vv[:, kq * 8:(kq + 1) * 8, h * 128:(h + 2) * 128],
                        ).then_inc(*svv[vs].tick(16))
            for hh in range(max(0, NH2 - 4), NH2):
                store_ctx(hh)
            for k in range(2):
                W(g, sctx[k], 16 * (NH2 // 2))

        @blk.tensor
        def _(tensor):
            _negm_emit_tensor(p, tensor, sn3, sn4, ops[:, 0:128], HEADS)
            p.wait_mark(tensor, "wq_ev")
            for hh in range(NH2):
                h, half = hh // 2, hh % 2
                hp = h % 2
                P = h // 2
                q0 = half * 1024
                if half == 0:
                    W(tensor, skv[h % 2], 64 * (h // 2 + 1))
                    if hp == 0:
                        W(tensor, svv[P % 2], 128 * (P // 2 + 1))

                def pv(j):
                    m = hh * 64 + j
                    W(tensor, s_xp, m + 1)
                    if j == 0:
                        W(tensor, s_ev, hh)
                        W(tensor, s_rc, hh)
                    st_, sp_ = (j == 0), (j == 63)
                    tensor.matmul(ops[:, 0:512], Vsb[:, P % 2, j, hp * 128:(hp + 1) * 128],
                                  pt[:, m % 2, 0:512], start=st_, stop=sp_)
                    tensor.matmul(ops[:, 512:1024], Vsb[:, P % 2, j, hp * 128:(hp + 1) * 128],
                                  pt[:, m % 2, 512:1024], start=st_, stop=sp_)
                    tensor.matmul(lps[:, 0:512], p.t["ones1b"][:, :],
                                  pt[:, m % 2, 0:512], start=st_, stop=sp_)
                    tensor.matmul(lps[:, 512:1024], p.t["ones1b"][:, :],
                                  pt[:, m % 2, 512:1024], start=st_, stop=sp_).then_inc(*s_pv.tick(1))

                for kb in range(64):
                    m = hh * 64 + kb
                    if m >= 2:
                        W(tensor, s_xp, m - 1)
                    if kb == 0 and hh >= 1:
                        W(tensor, s_ev, hh)
                    tensor.matmul(st[:, m % 2, 0:512], Ksb[:, h % 2, kb * 128:(kb + 1) * 128],
                                  Q[:, h, q0:q0 + 512], start=True, stop=True)
                    tensor.matmul(st[:, m % 2, 512:1024], Ksb[:, h % 2, kb * 128:(kb + 1) * 128],
                                  Q[:, h, q0 + 512:q0 + 1024], start=True, stop=True).then_inc(*s_st.tick(1))
                    if kb >= 1:
                        pv(kb - 1)
                pv(63)
                W(tensor, s_rc, hh + 1)
                tensor.matmul(st[:, 0, 0:512], p.t["onesr"][:, :],
                              lrow[:, 0:512], start=True, stop=True)
                tensor.matmul(st[:, 0, 512:1024], p.t["onesr"][:, :],
                              lrow[:, 512:1024], start=True, stop=True).then_inc(*s_bc.tick(1))

        @blk.scalar
        def _(scalar):
            _negm_emit_scalar(p, scalar, sn1, sn2, HEADS)
            W(scalar, sn5, 1)
            for hh in range(NH2):
                h = hh // 2
                for kb in range(64):
                    m = hh * 64 + kb
                    W(scalar, s_st, m + 1)
                    if m >= 2:
                        W(scalar, s_pv, m - 1)
                    scalar.activation(pt[:, m % 2, :], st[:, m % 2, :], AF.Exp,
                                      bias=negM[:, h:h + 1], scale=SM_SCALE).then_inc(*s_xp.tick(1))

        @blk.vector
        def _(vector):
            _negm_emit_vector(p, "a", vector, 0, 2, SM_SCALE, HEADS, p.t["negMa"],
                              "wq_nrm", ops[:, 0:128], sn1, sn2, sn3, sn4, sn5)
            for hh in range(NH2):
                W(vector, s_pv, 64 * (hh + 1))
                vector.reciprocal(lrow[:, :], lps[:, :]).then_inc(*s_rc.tick(1))
                W(vector, s_bc, hh + 1)
                vector.tensor_copy(lbsb[:, :], st[:, 0, :])
                vector.drain()
                if hh >= 2:
                    W(vector, sctx[hh % 2], 16 * (hh // 2))
                vector.tensor_mul(ctile[:, hh % 2, :], ops[:, :], lbsb[:, :]).then_inc(*s_ev.tick(1))




def ph_attn_b(p):
    p.phase_begin()
    nc = p.nc
    Q = p.t["Q"]
    Kg = p.arena("b3k", [128, 2, 4 * TPC], BF16, "XN", 0)
    Vg = p.arena("b3v", [128, 2, 64, 128], BF16, "XN", 32768)
    pt = p.arena("b3p", [128, 2, TC], BF16, "SCR", 0)
    lbsb = p.arena("b3lb", [128, TC], F32, "SCR", 2048)
    ctile = p.arena("b3c", [128, 2, TC], BF16, "SCR", 4096)
    lrow = p.arena("b3lr", [1, TC], F32, "SCR", 6144)
    btab = p.arena("b3bt", [128, HEADS, 64], F32, "SCR", 8192)
    indsb = p.arena("b3in", [128, 64], F32, "SCR", 12544)
    bms = p.arena("b3bm", [128, 2, 16, TC], BF16, "SCR", 16384)
    negM = p.t["negMb"]
    kcc, vcc, ctxd = p.t["kbcc_out"], p.t["vbcc_out"], p.t["ctxd"]

    skv = p.slots("b3kv", 2, "g")
    svv = p.slots("b3vv", 2, "g")
    sbm = p.slots("b3bm", 2)
    sctx = p.slots("b3cx", 2, "g")
    scst = p.slots("b3cs", 2)
    s_st = p.new_sem("b3st")
    s_xp = p.new_sem("b3xp")
    s_tri = p.new_sem("b3tr")
    s_pv = p.new_sem("b3pv")
    s_rc = p.new_sem("b3rc")
    s_bc = p.new_sem("b3bc")
    s_ev = p.new_sem("b3ev")
    s_bt = p.new_sem("b3btv")
    bn1, bn2, bn3, bn4, bn5 = (p.new_sem("b3n1"), p.new_sem("b3n2"),
                               p.new_sem("b3n3"), p.new_sem("b3n4"),
                               p.new_sem("b3n5"))

    blocks = []
    for g in range(GROUPS):
        for j in range(NTC):
            for h4 in range(4):
                blocks.append((g, j, g * 4 + h4))
    nw_of = lambda j: 16 * j + 16
    m_start = []
    m = 0
    for (g, j, h) in blocks:
        m_start.append(m)
        m += nw_of(j)
    ntri_before = []
    nt = 0
    for bi, (g, j, h) in enumerate(blocks):
        for w in range(nw_of(j)):
            ntri_before.append(nt)
            if w >= 16 * j:
                nt += 1

    def phys_win(j, u):
        return (u // 4) * 16 + j * 4 + (u % 4)

    def slab_of(bi):
        g, j, h = blocks[bi]
        return g * NTC + j

    tris_through_slab = {}
    cnt = 0
    for bi, (g, j, h) in enumerate(blocks):
        cnt += 16
        tris_through_slab[g * NTC + j] = cnt

    with (
        nc.psum_tensor("b3st0", [128, TC], F32) as st0,
        nc.psum_tensor("b3st1", [128, TC], F32) as st1,
        nc.psum_tensor("b3op", [128, TC], F32) as ops,
        nc.psum_tensor("b3lp", [1, TC], F32) as lps,
        nc.Block() as blk,
    ):
        @blk.sync
        def _(sync):
            sync.dma_start(indsb[:, :], p.inp("ind")[:, :]).then_inc(*scst[0].tick(16))
            bm = p.inp("bmask").ap().rearrange("(ju pp) q -> pp ju q", pp=128)
            for si in range(GROUPS * NTC):
                g, j = divmod(si, NTC)
                if si >= 2:
                    W(sync, s_tri, tris_through_slab[si - 2])
                sync.dma_start(bms[:, si % 2, :, :],
                               bm[:, j * 16:(j + 1) * 16, :]).then_inc(*sbm[si % 2].tick(16))

        @blk.vector
        def _(vector):
            _negm_emit_vector(p, "b", vector, 3, 5, SM_SCALE, HEADS, p.t["negMb"],
                              "bq_nrm", ops[:, 0:128], bn1, bn2, bn3, bn4, bn5,
                              group_of_head=True)
            W(vector, scst[0], 16)
            for h in range(HEADS):
                vector.tensor_scalar(btab[:, h, :], indsb[:, :], negM[:, h:h + 1],
                                     None, ALU.add)
            vector.drain()
            vector.tensor_copy(btab[:, 0, 0:1], btab[:, 0, 0:1]).then_inc(*s_bt.tick(1))
            m = 0
            for bi, (g, j, h) in enumerate(blocks):
                nw = nw_of(j)
                si = g * NTC + j
                if h % 4 == 0:
                    W(vector, sbm[si % 2], 16 * (si // 2 + 1))
                for w in range(nw):
                    if w >= 16 * j:
                        W(vector, s_xp, m + 1)
                        vector.tensor_mul(pt[:, m % 2, :], pt[:, m % 2, :],
                                          bms[:, si % 2, w - 16 * j, :]
                                          ).then_inc(*s_tri.tick(1))
                    m += 1
                W(vector, s_pv, m_start[bi] + nw)
                vector.reciprocal(lrow[:, :], lps[:, :]).then_inc(*s_rc.tick(1))
                W(vector, s_bc, bi + 1)
                vector.tensor_copy(lbsb[:, :], st0[:, :])
                vector.drain()
                if bi >= 2:
                    W(vector, sctx[bi % 2], 16 * (bi // 2))
                vector.tensor_mul(ctile[:, bi % 2, :], ops[:, :],
                                  lbsb[:, :]).then_inc(*s_ev.tick(1))

        @blk.gpsimd
        def _(g_):
            p.wait_mark(g_, "gath_b")
            for gi in range(GROUPS):
                if gi >= 1:
                    for bi in range((gi - 1) * 16 - 4 if gi >= 2 else 0, gi * 16 - 4):
                        _b3_store(p, g_, bi, blocks, ctile, ctxd, s_ev, sctx)
                sl = gi % 2
                if gi >= 2:
                    W(g_, s_st, m_start[(gi - 1) * 16])
                for r in range(4):
                    g_.dma_start(Kg[:, sl, r * TPC:(r + 1) * TPC],
                                 kcc[r * 512 + gi * 128:r * 512 + (gi + 1) * 128, :]
                                 ).then_inc(*skv[sl].tick(16))
                if gi >= 2:
                    W(g_, s_pv, m_start[(gi - 1) * 16])
                vv = vcc.ap().rearrange("(kb pp) f -> pp kb f", pp=128)
                for kq in range(8):
                    g_.dma_start(Vg[:, sl, kq * 8:(kq + 1) * 8, :],
                                 vv[:, kq * 8:(kq + 1) * 8, gi * 128:(gi + 1) * 128]
                                 ).then_inc(*svv[sl].tick(16))
            for bi in range(3 * 16 - 4, 4 * 16):
                _b3_store(p, g_, bi, blocks, ctile, ctxd, s_ev, sctx)
            for k in range(2):
                W(g_, sctx[k], 16 * (len(blocks) // 2))
            p.mark("ctxb", g_.nop(), 1)

        @blk.tensor
        def _(tensor):
            _negm_emit_tensor(p, tensor, bn3, bn4, ops[:, 0:128], HEADS)
            p.wait_mark(tensor, "bq_ev")
            m = 0
            for bi, (g, j, h) in enumerate(blocks):
                nw = nw_of(j)
                sl = g % 2
                if j == 0 and h % 4 == 0:
                    W(tensor, skv[sl], 64 * (g // 2 + 1))
                    W(tensor, svv[sl], 128 * (g // 2 + 1))

                def pv(mm, w):
                    if w >= 16 * j:
                        W(tensor, s_tri, ntri_before[mm] + 1)
                    else:
                        W(tensor, s_xp, mm + 1)
                    if w == 0:
                        W(tensor, s_ev, bi)
                        W(tensor, s_rc, bi)
                    P = _phys_chunk(w) if w < 16 * j else phys_win(j, w - 16 * j)
                    st_, sp_ = (w == 0), (w == nw - 1)
                    tensor.matmul(ops[:, :], Vg[:, sl, P, :], pt[:, mm % 2, :],
                                  start=st_, stop=sp_)
                    tensor.matmul(lps[:, :], p.t["ones1b"][:, :], pt[:, mm % 2, :],
                                  start=st_, stop=sp_).then_inc(*s_pv.tick(1))

                for w in range(nw):
                    if m >= 2:
                        W(tensor, s_xp, m - 1)
                    if w == 0 and bi >= 1:
                        W(tensor, s_ev, bi)
                    P = _phys_chunk(w) if w < 16 * j else phys_win(j, w - 16 * j)
                    kt = Kg[:, sl, P * 128:(P + 1) * 128]
                    stp = st0 if m % 2 == 0 else st1
                    tensor.matmul(stp[:, :], kt, Q[:, h, j * TC:(j + 1) * TC],
                                  start=True, stop=True).then_inc(*s_st.tick(1))
                    if w >= 1:
                        pv(m - 1, w - 1)
                    m += 1
                pv(m - 1, nw - 1)
                W(tensor, s_rc, bi + 1)
                tensor.matmul(st0[:, :], p.t["onesr"][:, :], lrow[:, :],
                              start=True, stop=True).then_inc(*s_bc.tick(1))

        @blk.scalar
        def _(scalar):
            _negm_emit_scalar(p, scalar, bn1, bn2, HEADS)
            W(scalar, s_bt, 1)
            m = 0
            for bi, (g, j, h) in enumerate(blocks):
                nw = nw_of(j)
                for w in range(nw):
                    W(scalar, s_st, m + 1)
                    if m >= 2:
                        W(scalar, s_pv, m - 1)
                    stp = st0 if m % 2 == 0 else st1
                    if w < 16 * j:
                        bias = negM[:, h:h + 1]
                    else:
                        col = j * 16 + (w - 16 * j)
                        bias = btab[:, h, col:col + 1]
                    scalar.activation(pt[:, m % 2, :], stp[:, :], AF.Exp,
                                      bias=bias, scale=SM_SCALE).then_inc(*s_xp.tick(1))
                    m += 1


def _b3_store(p, g_, bi, blocks, ctile, ctxd, s_ev, sctx):
    (g, j, h) = blocks[bi]
    W(g_, s_ev, bi + 1)
    g_.dma_start(ctxd[h * 128:(h + 1) * 128, j * TC:(j + 1) * TC],
                 ctile[:, bi % 2, :]).then_inc(*sctx[bi % 2].tick(16))



def ph_wcast(p):
    p.phase_begin()
    nc = p.nc
    jobs = [(p.inp("wd_t").ap().rearrange("e h d -> (e h) d"), p.t["wdb"], NEXP * HEXP, D),
            (p.inp("shd_t").ap(), p.t["shdb"], HSH, D)]
    sf = p.slots("wcf", 4)
    sb = p.slots("wcb", 2, "g")
    sv = p.new_sem("wcv")
    buf_f = p.arena("wcff", [128, 4, D], F32, "SCR", 16384)
    buf_b = p.arena("wcbb", [128, 2, D], BF16, "SCR", 0)
    with nc.Block() as blk:
        tiles = []
        for src, dst, R, C in jobs:
            for r in range(R // 128):
                tiles.append((src, dst, r, C))

        @blk.sync
        def _(sync):
            for i, (src, dst, r, C) in enumerate(tiles):
                if i >= 4:
                    W(sync, sv, i - 3)
                sync.dma_start(buf_f[:, i % 4, 0:C],
                               src[r * 128:(r + 1) * 128, :]).then_inc(*sf[i % 4].tick(16))

        @blk.vector
        def _(vector):
            for i, (src, dst, r, C) in enumerate(tiles):
                W(vector, sf[i % 4], 16 * (i // 4 + 1))
                if i >= 2:
                    W(vector, sb[i % 2], 16 * (i // 2))
                vector.tensor_copy(buf_b[:, i % 2, 0:C], buf_f[:, i % 4, 0:C]).then_inc(*sv.tick(1))

        @blk.gpsimd
        def _(g):
            for i, (src, dst, r, C) in enumerate(tiles):
                W(g, sv, i + 1)
                g.dma_start(dst[r * 128:(r + 1) * 128, :], buf_b[:, i % 2, 0:C]).then_inc(*sb[i % 2].tick(16))
            n = len(tiles)
            for k in range(2):
                W(g, sb[k], 16 * ((n - k + 1) // 2))
            p.mark("wcast", g.nop(), 1)


def ph_gconst(p):
    p.phase_begin()
    nc = p.nc
    p.t["ggb"] = nc.alloc_sbuf_tensor("ggb", [128, NFC, 8], BF16)
    p.t["seb"] = nc.alloc_sbuf_tensor("seb", [128, NFC, 1], BF16)
    p.t["selb"] = nc.alloc_sbuf_tensor("selb", [8, 8 * 128], BF16)
    p.t["sg"] = p.arena("sg", [1, TPC], F32, "SCR", 49152)
    gf = p.arena("gcf", [128, NFC, 8], F32, "SCR", 16384)
    sef = p.arena("gce", [128, NFC, 1], F32, "SCR", 22528)
    self_sel = p.arena("gcs", [8, 8 * 128], F32, "SCR", 24576)
    sd = p.slots("gcd", 3)
    with nc.Block() as blk:
        @blk.sync
        def _(sync):
            sync.dma_start(gf[:, :, :],
                           p.inp("ggate").ap().rearrange("(dc p) e -> p dc e", p=128)
                           ).then_inc(*sd[0].tick(16))
            sync.dma_start(sef[:, :, :],
                           p.inp("se_t").ap().rearrange("(dc p) e -> p dc e", p=128),
                           allow_slow_non_contiguous=True).then_inc(*sd[1].tick(16))
            sync.dma_start(self_sel[:, :], p.inp("selmat")[:, :]).then_inc(*sd[2].tick(16))

        @blk.vector
        def _(vector):
            W(vector, sd[0], 16)
            vector.tensor_copy(p.t["ggb"][:, :, :], gf[:, :, :])
            W(vector, sd[1], 16)
            vector.tensor_copy(p.t["seb"][:, :, :], sef[:, :, :])
            W(vector, sd[2], 16)
            vector.drain()
            inst = vector.tensor_copy(p.t["selb"][:, :], self_sel[:, :])
            p.mark("gconst", inst, 1)


def ph_ffn_gate(p, tag):
    p.phase_begin()
    nc = p.nc
    xn = p.t["xn"]
    comb_bc = p.arena("cbc" + tag, [128, NEXP, TPC], BF16, "BIG", 0)
    p.t["comb_bc"] = comb_bc
    sg = p.t["sg"]
    ggb = p.t["ggb"]
    seb = p.t["seb"]
    selb = p.t["selb"]
    comb_tm = p.arena("ctm" + tag, [128, NFC, 8], F32, "SCR", 0)
    comb_T = p.arena("cT" + tag, [8, NFC, 128], BF16, "SCR", 8192)
    row8 = p.arena("r8" + tag, [128, 8, 8], F32, "SCR", 12288)
    s_mm = p.new_sem(tag + "mm")
    s_cp = p.new_sem(tag + "cp")
    s_d = p.new_sem(tag + "d")
    s_w1 = p.new_sem(tag + "w1")
    s_cb = p.new_sem(tag + "cb")
    s_tr = p.new_sem(tag + "tr")
    s_tc = p.new_sem(tag + "tc")
    s_bc = p.new_sem(tag + "bc")
    s_bv = p.new_sem(tag + "bv")
    s_sgm = p.new_sem(tag + "sgm")
    s_sgs = p.new_sem(tag + "sgs")
    with (
        nc.psum_tensor(tag + "sc0", [128, 8], F32) as sc0,
        nc.psum_tensor(tag + "sc1", [128, 8], F32) as sc1,
        nc.psum_tensor(tag + "tr0", [8, 128], F32) as tr0,
        nc.psum_tensor(tag + "tr1", [8, 128], F32) as tr1,
        nc.psum_tensor(tag + "bc0", [128, 128], F32) as bc0,
        nc.psum_tensor(tag + "bc1", [128, 128], F32) as bc1,
        nc.psum_tensor(tag + "sgp", [1, TC], F32) as sgp,
        nc.Block() as blk,
    ):
        @blk.tensor
        def _(tensor):
            p.wait_mark(tensor, "xn")
            p.wait_mark(tensor, "gconst")
            for tt in range(NFC):
                if tt >= 2:
                    W(tensor, s_cp, tt - 1)
                sc = sc0 if tt % 2 == 0 else sc1
                inst = None
                for dc in range(NFC):
                    inst = tensor.matmul(sc[:, :], xn[:, dc, tt * 128:(tt + 1) * 128],
                                         ggb[:, dc, :], start=(dc == 0),
                                         stop=(dc == NFC - 1))
                inst.then_inc(*s_mm.tick(1))
            for tc in range(NTC):
                if tc >= 1:
                    W(tensor, s_sgs, tc)
                for dc in range(NFC):
                    inst = tensor.matmul(sgp[:, :], seb[:, dc, :],
                                         xn[:, dc, tc * TC:(tc + 1) * TC],
                                         start=(dc == 0), stop=(dc == NFC - 1))
                inst.then_inc(*s_sgm.tick(1))
            for tt in range(NFC):
                W(tensor, s_cb, tt + 1)
                if tt >= 2:
                    W(tensor, s_tc, tt - 1)
                tr = tr0 if tt % 2 == 0 else tr1
                tensor.transpose(tr[:, :], comb_tm[:, tt, :], p.t["identf"][:, :]).then_inc(*s_tr.tick(1))
            for e in range(NEXP):
                for tt in range(NFC):
                    i = e * NFC + tt
                    W(tensor, s_tc, NFC)
                    if i >= 2:
                        W(tensor, s_bv, i - 1)
                    bc = bc0 if i % 2 == 0 else bc1
                    tensor.matmul(bc[:, :], selb[:, e * 128:(e + 1) * 128],
                                  comb_T[:, tt, :], start=True, stop=True).then_inc(*s_bc.tick(1))

        @blk.vector
        def _(vector):
            for tt in range(NFC):
                W(vector, s_mm, tt + 1)
                sc = sc0 if tt % 2 == 0 else sc1
                s8 = row8[:, 0, :]
                m1 = row8[:, 1, 0:1]
                eq1 = row8[:, 2, :]
                s2 = row8[:, 3, :]
                m2 = row8[:, 4, 0:1]
                eq2 = row8[:, 5, :]
                dd = row8[:, 6, 0:1]
                w2 = row8[:, 7, 0:1]
                vector.tensor_copy(s8, sc[:, :]).then_inc(*s_cp.tick(1))
                vector.drain()
                vector.tensor_reduce(m1, s8, AX.X, ALU.max)
                vector.drain()
                vector.tensor_scalar(eq1, s8, m1, None, ALU.is_ge)
                vector.drain()
                vector.scalar_tensor_tensor(s2, eq1, NEG, s8, ALU.mult, ALU.add)
                vector.drain()
                vector.tensor_reduce(m2, s2, AX.X, ALU.max)
                vector.drain()
                vector.tensor_scalar(eq2, s2, m2, None, ALU.is_ge)
                vector.drain()
                vector.tensor_sub(dd, m1, m2).then_inc(*s_d.tick(1))
                W(vector, s_w1, tt + 1)
                vector.tensor_scalar(w2, row8[:, 1, 1:2], -1.0, 1.0, ALU.mult, ALU.add)
                vector.drain()
                vector.tensor_scalar(eq1, eq1, row8[:, 1, 1:2], None, ALU.mult)
                vector.drain()
                vector.scalar_tensor_tensor(comb_tm[:, tt, :], eq2, w2, eq1,
                                            ALU.mult, ALU.add).then_inc(*s_cb.tick(1))
                vector.drain()
            for tt in range(NFC):
                W(vector, s_tr, tt + 1)
                tr = tr0 if tt % 2 == 0 else tr1
                vector.tensor_copy(comb_T[:, tt, :], tr[:, :]).then_inc(*s_tc.tick(1))
            for e in range(NEXP):
                for tt in range(NFC):
                    i = e * NFC + tt
                    W(vector, s_bc, i + 1)
                    bc = bc0 if i % 2 == 0 else bc1
                    vector.tensor_copy(comb_bc[:, e, tt * 128:(tt + 1) * 128],
                                       bc[:, :]).then_inc(*s_bv.tick(1))
            vector.drain()
            inst = vector.tensor_copy(row8[:, 0, 0:1], row8[:, 0, 0:1])
            p.mark(tag + "_comb", inst, 1)

        @blk.scalar
        def _(scalar):
            for tt in range(NFC):
                W(scalar, s_d, tt + 1)
                scalar.activation(row8[:, 1, 1:2], row8[:, 6, 0:1], AF.Sigmoid).then_inc(*s_w1.tick(1))
            for tc in range(NTC):
                W(scalar, s_sgm, tc + 1)
                scalar.activation(sg[:, tc * TC:(tc + 1) * TC], sgp[:, :],
                                  AF.Sigmoid).then_inc(*s_sgs.tick(1))
            p.mark(tag + "_sg", scalar.nop(), 1)


def ph_ffn_hidden(p, tag, wgsrc, wusrc, n_hc, row0, comb_e, cache, crow0, comb_mark=None):
    p.phase_begin()
    nc = p.nc
    xn = p.t["xn"]
    wrb = p.arena("fhw" + tag, [128, 2, NFC, 128], BF16, "SCR", 0)
    hgs = p.arena("fhg" + tag, [128, TPC], BF16, "SCR", 8192)
    t1 = p.arena("fht" + tag, [128, 2, TPC], BF16, "SCR", 12288)
    wrf = p.arena("fhwf" + tag, [128, 2, NFC, 128], F32, "SCR", 20480)
    sw = p.slots(tag + "w", 2)
    shd = p.slots(tag + "hd", 2, "g")
    swc = p.new_sem(tag + "wc")
    s_g = p.new_sem(tag + "g")
    s_u = p.new_sem(tag + "u")
    s_sl = p.new_sem(tag + "sl")
    s_t = p.new_sem(tag + "t")
    with (
        nc.psum_tensor("fhpg" + tag, [128, TPC], F32) as pg,
        nc.psum_tensor("fhpu" + tag, [128, TPC], F32) as pu,
        nc.Block() as blk,
    ):
        @blk.sync
        def _(sync):
            for hc in range(n_hc):
                for half, src in ((0, wgsrc), (1, wusrc)):
                    i = 2 * hc + half
                    if i >= 2:
                        W(sync, swc, i - 1)
                    sync.dma_start(
                        wrf[:, half, :, :],
                        src[row0:row0 + D, hc * 128:(hc + 1) * 128]
                        .rearrange("(dc p) o -> p dc o", p=128),
                    ).then_inc(*sw[half].tick(16))

        @blk.tensor
        def _(tensor):
            p.wait_mark(tensor, "xn")
            for hc in range(n_hc):
                W(tensor, swc, 2 * hc + 1)
                if hc >= 1:
                    W(tensor, s_sl, hc)
                inst = None
                for dc in range(NFC):
                    for tc in range(NTC):
                        inst = tensor.matmul(pg[:, tc * TC:(tc + 1) * TC],
                                             wrb[:, 0, dc, :],
                                             xn[:, dc, tc * TC:(tc + 1) * TC],
                                             start=(dc == 0), stop=(dc == NFC - 1))
                inst.then_inc(*s_g.tick(1))
                W(tensor, swc, 2 * hc + 2)
                if hc >= 1:
                    W(tensor, s_t, hc)
                for dc in range(NFC):
                    for tc in range(NTC):
                        inst = tensor.matmul(pu[:, tc * TC:(tc + 1) * TC],
                                             wrb[:, 1, dc, :],
                                             xn[:, dc, tc * TC:(tc + 1) * TC],
                                             start=(dc == 0), stop=(dc == NFC - 1))
                inst.then_inc(*s_u.tick(1))

        @blk.scalar
        def _(scalar):
            for hc in range(n_hc):
                W(scalar, s_g, hc + 1)
                if hc >= 1:
                    W(scalar, s_t, hc)
                scalar.activation(hgs[:, :], pg[:, :], AF.Silu).then_inc(*s_sl.tick(1))

        @blk.vector
        def _(vector):
            if comb_mark is not None:
                p.wait_mark(vector, comb_mark)
            for hc in range(n_hc):
                for half in (0, 1):
                    i = 2 * hc + half
                    W(vector, sw[half], 16 * (hc + 1))
                    if i >= 4:
                        W(vector, s_g if half == 0 else s_u, hc - 1)
                    vector.tensor_copy(wrb[:, half, :, :],
                                       wrf[:, half, :, :]).then_inc(*swc.tick(1))
                W(vector, s_u, hc + 1)
                W(vector, s_sl, hc + 1)
                if hc >= 2:
                    W(vector, shd[hc % 2], 16 * (hc // 2))
                vector.tensor_mul(t1[:, hc % 2, :], pu[:, :], hgs[:, :])
                vector.drain()
                if comb_e is not None:
                    vector.tensor_mul(t1[:, hc % 2, :], t1[:, hc % 2, :], comb_e)
                    vector.drain()
                vector.tensor_copy(t1[:, hc % 2, 0:1],
                                   t1[:, hc % 2, 0:1]).then_inc(*s_t.tick(1))

        @blk.gpsimd
        def _(g):
            for hc in range(n_hc):
                W(g, s_t, hc + 1)
                g.dma_start(cache[crow0 + hc * 128:crow0 + (hc + 1) * 128, :],
                            t1[:, hc % 2, :]).then_inc(*shd[hc % 2].tick(16))
            for k in range(2):
                W(g, shd[k], 16 * ((n_hc - k + 1) // 2))
            p.mark("hc_" + tag, g.nop(), 1)


def ph_ffn_down(p, tag, xprev, xnext, ready_mark, hmark, shmark, sgmark):
    p.phase_begin()
    nc = p.nc
    hsl = p.arena("fd_h" + tag, [128, NEXP, 8, TC], BF16, "BIG", 0)
    shsl = p.arena("fd_s" + tag, [128, NFC, TC], BF16, "XN", 0)
    wdr = p.arena("fd_w" + tag, [128, 2, NEXP, 8, 128], BF16, "XN", 16384)
    sdr = p.arena("fd_sw" + tag, [128, 2, NFC, 128], BF16, "XN", 49152)
    sgb = p.arena("fd_sg" + tag, [128, TC], F32, "SCR", 16384)
    xpr = p.arena("fd_xp" + tag, [128, 4, TC], F32, "SCR", 20480)
    xnw = p.arena("fd_xn" + tag, [128, 2, TC], F32, "SCR", 28672)
    tsh = p.arena("fd_t" + tag, [128, 2, TC], F32, "SCR", 32768)
    sg = p.t["sg"]
    wdb, shdb = p.t["wdb"], p.t["shdb"]
    sh_sl = p.slots(tag + "hs", 2, "g")
    ss_sl = p.slots(tag + "ss", 2, "g")
    sw_sl = p.slots(tag + "ws", 2)
    sd_sl = p.slots(tag + "ds", 2)
    sxp = p.slots(tag + "xp", 4)
    sxn = p.slots(tag + "xn", 2, "g")
    s_rm = p.new_sem(tag + "rm")
    s_sm = p.new_sem(tag + "sm")
    s_ev = p.new_sem(tag + "ev")
    s_sgb = p.new_sem(tag + "sgb")
    s_sgc = p.new_sem(tag + "sgc")
    with (
        nc.psum_tensor("fdr0" + tag, [128, TC], F32) as pr0,
        nc.psum_tensor("fdr1" + tag, [128, TC], F32) as pr1,
        nc.psum_tensor("fds0" + tag, [128, TC], F32) as ps0,
        nc.psum_tensor("fds1" + tag, [128, TC], F32) as ps1,
        nc.psum_tensor("fdsg" + tag, [128, TC], F32) as psg,
        nc.Block() as blk,
    ):
        @blk.gpsimd
        def _(g):
            p.wait_mark(g, hmark)
            p.wait_mark(g, shmark)
            def store_tile(i):
                W(g, s_ev, i + 1)
                tc_, oc_ = divmod(i, NFC)
                g.dma_start(xnext[oc_ * 128:(oc_ + 1) * 128, tc_ * TC:(tc_ + 1) * TC],
                            xnw[:, i % 2, :]).then_inc(*sxn[i % 2].tick(16))

            for tc in range(NTC):
                if tc >= 1:
                    for i in range((tc - 1) * NFC, tc * NFC):
                        store_tile(i)
                if tc >= 1:
                    W(g, s_rm, NFC * tc)
                    W(g, s_sm, NFC * tc)
                for e in range(NEXP):
                    g.dma_start(
                        hsl[:, e, :, :],
                        p.t["hcache"][e * HEXP:(e + 1) * HEXP, tc * TC:(tc + 1) * TC]
                        .rearrange("(hc pp) t -> pp hc t", pp=128),
                    ).then_inc(*sh_sl[tc % 2].tick(16))
                g.dma_start(
                    shsl[:, :, :],
                    p.t["shcache"][:, tc * TC:(tc + 1) * TC]
                    .rearrange("(hc pp) t -> pp hc t", pp=128),
                ).then_inc(*ss_sl[tc % 2].tick(16))
            for i in range((NTC - 1) * NFC, NTC * NFC):
                store_tile(i)
            for k in range(2):
                W(g, sxn[k], 16 * (NTC * NFC // 2))
            p.mark(ready_mark, g.nop(), 1)

        @blk.sync
        def _(sync):
            p.wait_mark(sync, "wcast")
            for tc in range(NTC):
                for oc in range(NFC):
                    i = tc * NFC + oc
                    if i >= 2:
                        W(sync, s_rm, i - 1)
                    for e in range(NEXP):
                        sync.dma_start(
                            wdr[:, i % 2, e, :, :],
                            wdb[e * HEXP:(e + 1) * HEXP, oc * 128:(oc + 1) * 128]
                            .rearrange("(hc pp) o -> pp hc o", pp=128),
                        ).then_inc(*sw_sl[i % 2].tick(16))
                    if i >= 2:
                        W(sync, s_sm, i - 1)
                    sync.dma_start(
                        sdr[:, i % 2, :, :],
                        shdb[:, oc * 128:(oc + 1) * 128]
                        .rearrange("(hc pp) o -> pp hc o", pp=128),
                    ).then_inc(*sd_sl[i % 2].tick(16))
                    if i >= 4:
                        W(sync, s_ev, i - 3)
                    sync.dma_start(xpr[:, i % 4, :],
                                   xprev[oc * 128:(oc + 1) * 128, tc * TC:(tc + 1) * TC]
                                   ).then_inc(*sxp[i % 4].tick(16))

        @blk.tensor
        def _(tensor):
            p.wait_mark(tensor, sgmark)
            for tc in range(NTC):
                if tc >= 1:
                    W(tensor, s_sgc, tc)
                tensor.matmul(psg[:, :], p.t["onesr"][:, :], sg[:, tc * TC:(tc + 1) * TC],
                              start=True, stop=True).then_inc(*s_sgb.tick(1))
                W(tensor, sh_sl[tc % 2], 128 * (tc // 2 + 1))
                W(tensor, ss_sl[tc % 2], 16 * (tc // 2 + 1))
                for oc in range(NFC):
                    i = tc * NFC + oc
                    W(tensor, sw_sl[i % 2], 128 * (i // 2 + 1))
                    if i >= 2:
                        W(tensor, s_ev, i - 1)
                    pr = pr0 if i % 2 == 0 else pr1
                    ps = ps0 if i % 2 == 0 else ps1
                    inst = None
                    for e in range(NEXP):
                        for hc in range(8):
                            inst = tensor.matmul(pr[:, :], wdr[:, i % 2, e, hc, :],
                                                 hsl[:, e, hc, :],
                                                 start=(e == 0 and hc == 0),
                                                 stop=(e == NEXP - 1 and hc == 7))
                    inst.then_inc(*s_rm.tick(1))
                    W(tensor, sd_sl[i % 2], 16 * (i // 2 + 1))
                    for hc in range(NFC):
                        inst = tensor.matmul(ps[:, :], sdr[:, i % 2, hc, :],
                                             shsl[:, hc, :],
                                             start=(hc == 0), stop=(hc == NFC - 1))
                    inst.then_inc(*s_sm.tick(1))

        @blk.vector
        def _(vector):
            for tc in range(NTC):
                W(vector, s_sgb, tc + 1)
                vector.tensor_copy(sgb[:, :], psg[:, :]).then_inc(*s_sgc.tick(1))
                vector.drain()
                for oc in range(NFC):
                    i = tc * NFC + oc
                    W(vector, s_sm, i + 1)
                    if "nosgate" in p.dbg:
                        vector.tensor_copy(tsh[:, i % 2, :], (ps0 if i % 2 == 0 else ps1)[:, :])
                    else:
                        vector.tensor_mul(tsh[:, i % 2, :], (ps0 if i % 2 == 0 else ps1)[:, :],
                                          sgb[:, :])
                    if "noshared" in p.dbg:
                        vector.drain()
                        vector.memset(tsh[:, i % 2, :], 0.0)
                    vector.drain()
                    W(vector, s_rm, i + 1)
                    if "norouted" not in p.dbg:
                        vector.tensor_add(tsh[:, i % 2, :], (pr0 if i % 2 == 0 else pr1)[:, :],
                                          tsh[:, i % 2, :])
                        vector.drain()
                    W(vector, sxp[i % 4], 16 * (i // 4 + 1))
                    if i >= 2:
                        W(vector, sxn[i % 2], 16 * (i // 2))
                    vector.tensor_add(xnw[:, i % 2, :], tsh[:, i % 2, :],
                                      xpr[:, i % 4, :]).then_inc(*s_ev.tick(1))


def ph_attn_out(p, tag, wt, xprev, xnext, ready_mark):
    p.phase_begin()
    nc = p.nc
    ctxsb = p.arena("cxs" + tag, [128, NFC, TPC], BF16, "BIG")
    wrf = p.arena("owf" + tag, [128, 2, NFC, 128], F32, "SCR", 16384)
    wrb = p.arena("owb" + tag, [128, 2, NFC, 128], BF16, "SCR", 32768)
    xpr = p.arena("oxp" + tag, [128, 4, TC], F32, "SCR", 40960)
    xnw = p.arena("oxn" + tag, [128, 2, TC], F32, "SCR", 49152)
    ctxd = p.t["ctxd"]
    sw = p.slots(tag + "w", 2)
    scx = p.slots(tag + "cx", 2, "g")
    sxp = p.slots(tag + "xp", 4)
    sxn = p.slots(tag + "xn", 2, "g")
    swc = p.new_sem(tag + "wc")
    smm = p.new_sem(tag + "mm")
    sev = p.new_sem(tag + "ev")

    with (
        nc.psum_tensor("po" + tag, [128, TPC], F32) as pp,
        nc.Block() as blk,
    ):
        @blk.gpsimd
        def _(g):
            for fc in range(NFC):
                if fc >= 2:
                    W(g, scx[fc % 2], 16 * (fc // 2))
                g.dma_start(ctxsb[:, fc, :], ctxd[fc * 128:(fc + 1) * 128, :]
                            ).then_inc(*scx[fc % 2].tick(16))
            for i in range(NFC * NTC):
                W(g, sev, i + 1)
                oc, tc = divmod(i, NTC)
                g.dma_start(xnext[oc * 128:(oc + 1) * 128, tc * TC:(tc + 1) * TC],
                            xnw[:, i % 2, :]).then_inc(*sxn[i % 2].tick(16))
            for k in range(2):
                W(g, sxn[k], 16 * (NFC * NTC // 2))
            inst = g.nop()
            p.mark(ready_mark, inst, 1)

        @blk.sync
        def _(sync):
            for oc in range(NFC):
                if oc >= 2:
                    W(sync, swc, oc - 1)
                sync.dma_start(
                    wrf[:, oc % 2, :, :],
                    wt[:, oc * 128:(oc + 1) * 128].rearrange("(dc p) o -> p dc o", p=128),
                ).then_inc(*sw[oc % 2].tick(16))
                for tc in range(NTC):
                    i = oc * NTC + tc
                    if i >= 4:
                        W(sync, sev, i - 3)
                    sync.dma_start(xpr[:, i % 4, :],
                                   xprev[oc * 128:(oc + 1) * 128, tc * TC:(tc + 1) * TC]
                                   ).then_inc(*sxp[i % 4].tick(16))
            for k in range(2):
                W(sync, sw[k], 16 * (NFC // 2))

        @blk.vector
        def _(vector):
            for oc in range(NFC):
                W(vector, sw[oc % 2], 16 * (oc // 2 + 1))
                if oc >= 2:
                    W(vector, smm, oc - 1)
                vector.tensor_copy(wrb[:, oc % 2, :, :], wrf[:, oc % 2, :, :]).then_inc(*swc.tick(1))
                W(vector, smm, oc + 1)
                for tc in range(NTC):
                    i = oc * NTC + tc
                    W(vector, sxp[i % 4], 16 * (i // 4 + 1))
                    if i >= 2:
                        W(vector, sxn[i % 2], 16 * (i // 2))
                    vector.tensor_add(xnw[:, i % 2, :], pp[:, tc * TC:(tc + 1) * TC],
                                      xpr[:, i % 4, :]).then_inc(*sev.tick(1))

        @blk.tensor
        def _(tensor):
            for oc in range(NFC):
                W(tensor, swc, oc + 1)
                if oc == 0:
                    for k in range(2):
                        W(tensor, scx[k], 16 * (NFC // 2))
                if oc >= 1:
                    W(tensor, sev, 4 * oc)
                inst = None
                for dc in range(NFC):
                    for tc in range(NTC):
                        inst = tensor.matmul(pp[:, tc * TC:(tc + 1) * TC],
                                             wrb[:, oc % 2, dc, :],
                                             ctxsb[:, dc, tc * TC:(tc + 1) * TC],
                                             start=(dc == 0), stop=(dc == NFC - 1))
                inst.then_inc(*smm.tick(1))



def _sub_ffn(p, tag, xprev, xnext, prev_mark, ready_mark, first):
    u = p.upto
    ph_stats(p, xprev, tag + "st", wait_marks=(prev_mark,))
    ph_xn(p, xprev, tag + "xn")
    if first:
        ph_gconst(p)
        ph_wcast(p)
    if u < 2.4:
        return False
    ph_ffn_gate(p, tag + "g")
    if "comb" in p.dbg:
        _dbg_dump_fm(p, "comb_bc", "dbg_comb", nfc=NEXP, width=TPC,
                     marks=(tag + "g_comb",))
    if u < 2.6:
        return False
    nexp = 2 if "emini" in p.dbg else NEXP
    wgf = p.inp("wg_t").ap().rearrange("e d h -> (e d) h")
    wuf = p.inp("wu_t").ap().rearrange("e d h -> (e d) h")
    for e in range(nexp):
        ph_ffn_hidden(p, tag + "h%d" % e, wgf, wuf, 8, e * D,
                      p.t["comb_bc"][:, e, :],
                      p.t["hcache"], e * HEXP, comb_mark=tag + "g_comb")
    if "hc" in p.dbg:
        _dbg_dram(p, "hcache", "dbg_hc", [NEXP * HEXP, TPC], BF16,
                  n_rows=nexp * HEXP)
    if u < 2.8:
        return False
    ph_ffn_hidden(p, tag + "sh", p.inp("shg_t").ap(), p.inp("shu_t").ap(),
                  NFC, 0, None, p.t["shcache"], 0)
    if "shc" in p.dbg:
        _dbg_dram(p, "shcache", "dbg_shc", [HSH, TPC], BF16)
    if u < 3:
        return False
    ph_ffn_down(p, tag + "d", xprev, xnext, ready_mark,
                "hc_" + tag + "h%d" % (NEXP - 1), "hc_" + tag + "sh",
                tag + "g_sg")
    return True



def build_program(upto=99, dbg=()):
    nc = bass.Bass("TRN2", target_bir_lowering=False)
    p = Prog(nc, upto, dbg)

    x0 = nc.dram_tensor("x0", [D, TPC], F32, kind="ExternalInput")
    xout = nc.dram_tensor("xout", [D, TPC], F32, kind="ExternalOutput")
    p.t["x0"], p.t["xout"] = x0, xout
    p.used_inputs = {"x0"}
    _in_shapes = {"wq_t": [D, D], "wk_t": [D, D], "wv_t": [D, D],
                  "wo_t": [D, D], "bwq": [D, D], "bwkv": [D, 1024],
                  "bwd": [D, D], "ggate": [D, 8],
                  "wg_t": [NEXP, D, HEXP], "wu_t": [NEXP, D, HEXP],
                  "wd_t": [NEXP, HEXP, D], "shg_t": [D, HSH],
                  "shu_t": [D, HSH], "shd_t": [HSH, D], "se_t": [D, 1],
                  "tri01": [512, TC], "ident": [128, 128], "ind": [128, 64],
                  "selmat": [8, 1024], "bmask": [NTC * 16 * 128, TC]}

    def inp(nm):
        if nm not in p.t:
            dt = BF16 if nm == "bmask" else F32
            p.t[nm] = nc.dram_tensor(nm, _in_shapes[nm], dt, kind="ExternalInput")
            p.used_inputs.add(nm)
        return p.t[nm]

    p.inp = inp

    p.t["xd1"] = nc.dram_tensor("xd1", [D, TPC], F32)
    p.t["xd2"] = nc.dram_tensor("xd2", [D, TPC], F32)
    p.t["kcc_in"] = nc.dram_tensor("kcc_in", [D, TPC], BF16)
    p.t["kcc_out"] = nc.dram_tensor("kcc_out", [4 * D, TPC], BF16)
    p.t["vcc_in"] = nc.dram_tensor("vcc_in", [TPC, D], BF16)
    p.t["vcc_out"] = nc.dram_tensor("vcc_out", [4 * TPC, D], BF16)
    p.t["nrma_in"] = nc.dram_tensor("nrma_in", [1, 16], F32)
    p.t["nrma_out"] = nc.dram_tensor("nrma_out", [1, 16], F32)
    p.t["nrmb_in"] = nc.dram_tensor("nrmb_in", [1, 4], F32)
    p.t["nrmb_out"] = nc.dram_tensor("nrmb_out", [1, 4], F32)
    p.t["kbcc_in"] = nc.dram_tensor("kbcc_in", [512, TPC], BF16)
    p.t["kbcc_out"] = nc.dram_tensor("kbcc_out", [2048, TPC], BF16)
    p.t["vbcc_in"] = nc.dram_tensor("vbcc_in", [TPC, 512], BF16)
    p.t["vbcc_out"] = nc.dram_tensor("vbcc_out", [4 * TPC, 512], BF16)
    p.t["ctxd"] = nc.dram_tensor("ctxd", [D, TPC], BF16)
    p.t["hcache"] = nc.dram_tensor("hcache", [NEXP * HEXP, TPC], BF16)
    p.t["shcache"] = nc.dram_tensor("shcache", [HSH, TPC], BF16)
    p.t["wdb"] = nc.dram_tensor("wdb", [NEXP * HEXP, D], BF16)
    p.t["shdb"] = nc.dram_tensor("shdb", [HSH, D], BF16)

    if "xn" in p.dbg:
        p.t["dbg_xn"] = nc.dram_tensor("dbg_xn", [D, TPC], F32, kind="ExternalOutput")

    p.t["xn"] = p.arena("xn", [128, NFC, TPC], BF16, "XN")
    p.t["stat"] = p.arena("stat", [1, 2, TPC], F32, "SCR", 0)
    p.t["xring"] = p.arena("xring", [128, 4, TC], F32, "SCR", 16384)
    p.t["bcb"] = p.arena("bcb", [128, 2, TC], F32, "SCR", 24576)
    p.t["ones1f"] = nc.alloc_sbuf_tensor("ones1f", [128, 1], F32)
    p.t["ones1b"] = nc.alloc_sbuf_tensor("ones1b", [128, 1], BF16)
    p.t["onesr"] = nc.alloc_sbuf_tensor("onesr", [1, 128], F32)
    p.t["rowtmp"] = nc.alloc_sbuf_tensor("rowtmp", [1, 4, TC], F32)
    p.t["nrows"] = nc.alloc_sbuf_tensor("nrows", [1, 8, NFC], F32)
    p.t["negMa"] = nc.alloc_sbuf_tensor("negMa", [128, NFC], F32)
    p.t["negMb"] = nc.alloc_sbuf_tensor("negMb", [128, NFC], F32)
    p.t["identf"] = nc.alloc_sbuf_tensor("identf", [128, 128], F32)
    p.t["identb"] = nc.alloc_sbuf_tensor("identb", [128, 128], BF16)

    sid = p.new_sem("cinit", "s")
    with nc.Block() as blk:
        @blk.sync
        def _(sync):
            sync.dma_start(p.t["identf"][:, :], p.inp("ident")[:, :]).then_inc(*sid.tick(16))

        @blk.vector
        def _(vector):
            p.mark("const", vector.memset(p.t["ones1f"][:, :], 1.0), 1)
            p.mark("const", vector.memset(p.t["ones1b"][:, :], 1.0), 1)
            p.mark("const", vector.memset(p.t["onesr"][:, :], 1.0), 1)
            W(vector, sid, 16)
            p.mark("const", vector.tensor_copy(p.t["identb"][:, :], p.t["identf"][:, :]), 1)

    ph_stats(p, x0, "st0")
    ph_xn(p, x0, "xn0")

    if "xn" in p.dbg:
        _dbg_dump_fm(p, "xn", "dbg_xn")

    if upto >= 1:
        Q = p.arena("Q", [128, NFC, TPC], BF16, "BIG")
        p.t["Q"] = Q
    if upto >= 1.1:
        ph_proj(p, "wk", p.inp("wk_t"), "kf", NFC, dst=p.t["kcc_in"], norm_row=1)
        if "kin" in p.dbg:
            _dbg_dram(p, "kcc_in", "dbg_kin", [D, TPC], BF16)
    if upto >= 1.15:
        ph_proj(p, "wv", p.inp("wv_t"), "vt", NFC, dst=p.t["vcc_in"])
        if "vin" in p.dbg:
            _dbg_dram(p, "vcc_in", "dbg_vin", [TPC, D], BF16)
    if upto >= 1.2:
        ph_cc(p, "a", [("kcc_in", "kcc_out"), ("vcc_in", "vcc_out")],
              krow=1, grow=2, nin=p.t["nrma_in"], nout=p.t["nrma_out"])
    if upto >= 1:
        ph_proj(p, "wq", p.inp("wq_t"), "q", NFC, dst=Q, norm_row=0)
    if upto >= 1.5:
        ph_attn_a(p)
        if "ctx" in p.dbg:
            _dbg_dram(p, "ctxd", "dbg_ctx", [D, TPC], BF16,
                      n_rows=(256 if "mini" in p.dbg else None))
    cur_x, cur_mark = x0, None
    if upto >= 2:
        ph_attn_out(p, "wo", p.inp("wo_t"), x0, p.t["xd1"], "x1rdy")
        cur_x, cur_mark = p.t["xd1"], "x1rdy"
    if upto >= 2.2:
        done = _sub_ffn(p, "f1", p.t["xd1"], p.t["xd2"], "x1rdy", "x2rdy", first=True)
        if done:
            cur_x, cur_mark = p.t["xd2"], "x2rdy"
    if upto >= 3.2:
        ph_stats(p, p.t["xd2"], "bst", wait_marks=("x2rdy",))
        ph_xn(p, p.t["xd2"], "bxn")
        ph_proj(p, "bk", p.inp("bwkv"), "kf", 4, dst=p.t["kbcc_in"], norm_row=4)
        ph_proj(p, "bv", p.inp("bwkv"), "vt", 4, dst=p.t["vbcc_in"], wt_col0=512)
        ph_cc(p, "b", [("kbcc_in", "kbcc_out"), ("vbcc_in", "vbcc_out")],
              krow=4, grow=5, nin=p.t["nrmb_in"], nout=p.t["nrmb_out"])
        ph_proj(p, "bq", p.inp("bwq"), "q", NFC, dst=p.t["Q"], norm_row=3)
        ph_attn_b(p)
        if "ctxb" in p.dbg:
            _dbg_dram(p, "ctxd", "dbg_ctxb", [D, TPC], BF16)
    if upto >= 4:
        ph_attn_out(p, "bo", p.inp("bwd"), p.t["xd2"], p.t["xd1"], "x3rdy")
        cur_x, cur_mark = p.t["xd1"], "x3rdy"
    if upto >= 5:
        done = _sub_ffn(p, "f2", p.t["xd1"], xout, "x3rdy", "x4rdy", first=False)
        if done:
            cur_x, cur_mark = xout, "x4rdy"

    if "q" in p.dbg:
        _dbg_dump_fm(p, "Q", "dbg_q", marks=("wq_ev",))
    if "kv" in p.dbg:
        _dbg_dram(p, "kcc_out", "dbg_k", [4 * D, TPC], BF16)
        _dbg_dram(p, "vcc_out", "dbg_v", [4 * TPC, D], BF16)
    if "negm" in p.dbg:
        _dbg_rows(p, "negMa", "dbg_negm", NFC, marks=("negm_a",))

    if cur_x is not xout:
        sg = p.new_sem("out", "g")
        with nc.Block() as blk:
            @blk.gpsimd
            def _(g):
                if cur_mark is not None:
                    p.wait_mark(g, cur_mark)
                g.dma_start(p.t["xout"][:, :], cur_x[:, :]).then_inc(*sg.tick(16))
                W(g, sg, 16)

    p.ctx.close()
    nc.finalize()
    return nc, set(p.used_inputs)



def _run(inputs, upto=99, dbg=(), _nc_out=None):
    x = np.asarray(inputs["L0_attn_a_x"], dtype=np.float32)
    shards = _shard_x(x)
    shared = _prep_inputs(inputs)
    nc, used = build_program(upto=upto, dbg=dbg)
    if _nc_out is not None:
        _nc_out.append(nc)
    in_maps = []
    for core in range(NCORE):
        m = dict(shared)
        m.update(_host_tables(core))
        m["x0"] = shards[core]
        in_maps.append({k: v for k, v in m.items() if k in used})
    return run_bass_kernel_spmd(nc, in_maps, core_ids=list(range(NCORE)))


def kernel(**inputs):
    res = _run(inputs)
    outs = [res.results[c]["xout"] for c in range(NCORE)]
    return _unshard_x(outs)


# version 03
# speedup vs baseline: 1.0430x; 1.0248x over previous
import os
import sys
import numpy as np

sys.path.insert(0, "/opt/trn_rl_repo")

import concourse.bass as bass
import concourse.mybir as mybir
from concourse.bass_utils import run_bass_kernel_spmd

F32 = mybir.dt.float32
BF16 = mybir.dt.bfloat16
I32 = mybir.dt.int32
AF = mybir.ActivationFunctionType
ALU = mybir.AluOpType
AX = mybir.AxisListType

B = 2
S = 8192
D = 2048
NCORE = 8
TPC = 2048
NTC = 4
TC = 512
NFC = 16
HEADS = 16
HD = 128
GROUPS = 4
NEXP = 8
HEXP = 1024
HSH = 2048
EPS = 1e-5
SM_SCALE = float(HD) ** -0.5
NEG = -1.0e9


def _gchunk(c):
    return [4 * j + c for j in range(NTC)]


def _phys_chunk(w):
    g = w // 4
    return (g % 4) * 16 + (g // 4) * 4 + (w % 4)



def _shard_x(x):
    shards = []
    for core in range(NCORE):
        s, c = divmod(core, 4)
        toks = np.concatenate([x[s, g * TC:(g + 1) * TC, :] for g in _gchunk(c)], axis=0)
        shards.append(np.ascontiguousarray(toks.T))
    return shards


def _unshard_x(shards):
    out = np.empty((B, S, D), dtype=np.float32)
    for core in range(NCORE):
        s, c = divmod(core, 4)
        tok = np.ascontiguousarray(np.asarray(shards[core]).T)
        for j, g in enumerate(_gchunk(c)):
            out[s, g * TC:(g + 1) * TC, :] = tok[j * TC:(j + 1) * TC, :]
    return out


def _host_consts():
    tri = np.zeros((4, 128, TC), dtype=np.float32)
    ql = np.arange(TC)[None, :]
    kl = np.arange(128)[:, None]
    for t in range(4):
        tri[t] = (ql >= 128 * t + kl).astype(np.float32)
    sel4f = np.zeros((128, 16), dtype=np.float32)
    sel4t = np.zeros((4, 512), dtype=np.float32)
    for tc in range(4):
        sel4f[:, tc * 4 + tc] = 1.0
        sel4t[tc, tc * 128:(tc + 1) * 128] = 1.0
    return {"tri01": tri.reshape(512, TC), "ident": np.eye(128, dtype=np.float32),
            "sel4f": sel4f, "sel4t": sel4t}


def _host_tables(core):
    import ml_dtypes
    s, c = divmod(core, 4)
    ind = np.zeros((128, 64), dtype=np.float32)
    tri = np.zeros((4, 128, TC), dtype=np.float32)
    ql = np.arange(TC)[None, :]
    kl = np.arange(128)[:, None]
    for t in range(4):
        tri[t] = (ql >= 128 * t + kl).astype(np.float32)
    bmask = np.zeros((NTC, 16, 128, TC), dtype=np.float32)
    for j in range(NTC):
        for u in range(16):
            col = j * 16 + u
            if u < 4 * c:
                bmask[j, u] = 1.0
            elif u < 4 * c + 4:
                bmask[j, u] = tri[u - 4 * c]
            else:
                ind[:, col] = NEG
    bmask = bmask.reshape(NTC * 16 * 128, TC).astype(ml_dtypes.bfloat16)
    return {"ind": ind, "bmask": bmask}


def _prep_inputs(inputs):
    a = {k: np.asarray(v) for k, v in inputs.items()}
    T = lambda m: np.ascontiguousarray(np.asarray(m).T)
    shared = {
        "wq_t": T(a["L0_attn_a_Wq"]),
        "wk_t": T(a["L0_attn_a_Wk"]),
        "wv_t": T(a["L0_attn_a_Wv"]),
        "wo_t": T(a["L0_attn_a_Wo"]),
        "bwq": np.ascontiguousarray(a["L0_attn_b_w_q"]),
        "bwkv": np.ascontiguousarray(a["L0_attn_b_w_kv"]),
        "bwd": np.ascontiguousarray(a["L0_attn_b_w_dense"]),
        "ggate": T(a["L0_ffn_gate_w"]),
        "wg_t": np.ascontiguousarray(np.transpose(a["L0_ffn_w_gate"], (0, 2, 1))),
        "wu_t": np.ascontiguousarray(np.transpose(a["L0_ffn_w_up"], (0, 2, 1))),
        "wd_t": np.ascontiguousarray(np.transpose(a["L0_ffn_w_down"], (0, 2, 1))),
        "shg_t": T(a["L0_ffn_sh_gate"]),
        "shu_t": T(a["L0_ffn_sh_up"]),
        "shd_t": T(a["L0_ffn_sh_down"]),
        "se_t": T(a["L0_ffn_se_gate_w"]),
    }
    sel = np.zeros((8, 8 * 128), dtype=np.float32)
    for e in range(8):
        sel[e, e * 128:(e + 1) * 128] = 1.0
    shared["selmat"] = sel
    shared.update(_host_consts())
    return shared



class RSem:

    def __init__(self, h, b):
        self.h = h
        self.b = b
        self.t = 0

    def tick(self, n):
        self.t += n
        return (self.h, n)


def W(eng, s, v):
    if isinstance(s, RSem):
        eng.wait_ge(s.h, s.b + v)
    else:
        eng.wait_ge(s, v)


class Prog:
    def __init__(self, nc, upto, dbg):
        import contextlib
        self.nc = nc
        self.upto = upto
        self.dbg = set(dbg)
        self.ctx = contextlib.ExitStack()
        self.t = {}
        self._marks = {}
        self._nsem = 0
        self._pool = {}
        self._phase = 0
        self._rentals = {}

    def phase_begin(self):
        self._phase += 1
        for ph in list(self._rentals):
            if ph <= self._phase - 2:
                for klass, rs in self._rentals.pop(ph):
                    self._pool.setdefault(klass, []).append((rs.h, rs.b + rs.t))

    def _raw_sem(self, name):
        self._nsem += 1
        return self.ctx.enter_context(self.nc.semaphore(f"{name}_{self._nsem}"))

    def new_sem(self, name, klass="c"):
        pool = self._pool.setdefault(klass, [])
        if "nopool" in self.dbg:
            pool = []
        if pool:
            h, b = pool.pop()
            rs = RSem(h, b)
        else:
            rs = RSem(self._raw_sem(klass + name), 0)
        self._rentals.setdefault(self._phase, []).append((klass, rs))
        return rs

    def mark(self, name, inst, n=1):
        if name not in self._marks or self._marks[name][2]:
            self._marks[name] = [self._raw_sem("mk_" + name), 0, False]
        m = self._marks[name]
        inst.then_inc(m[0], n)
        m[1] += n

    def alias_mark(self, name, sem, total):
        if isinstance(sem, RSem):
            self._marks[name] = [sem.h, sem.b + total, True]
        else:
            self._marks[name] = [sem, total, True]

    def wait_mark(self, eng, name):
        m = self._marks[name]
        eng.wait_ge(m[0], m[1])

    def slots(self, name, n, klass="s"):
        return [self.new_sem(f"{name}{k}", klass) for k in range(n)]

    def arena(self, name, shape, dtype, region, off=0):
        base = {"XN": 159744, "BIG": 94208, "SCR": 36864}[region]
        return self.nc.alloc_sbuf_tensor_at(name, shape, dtype, offset=base + off)



def _stats_fin_v1(p, vector, ps4, sv):
    r4 = p.t["r4"]
    mean, ex2, var = r4[:, 0, :], r4[:, 1, :], r4[:, 2, :]
    vector.tensor_scalar(mean, ps4[:, 0, :], 1.0 / D, None, ALU.mult)
    vector.tensor_scalar(ex2, ps4[:, 1, :], 1.0 / D, None, ALU.mult)
    vector.drain()
    vector.tensor_mul(var, mean, mean)
    vector.drain()
    vector.scalar_tensor_tensor(var, ex2, EPS, var, ALU.add,
                                ALU.subtract).then_inc(*sv.tick(1))


def _stats_fin_s(p, scalar, sv, ss):
    r4 = p.t["r4"]
    stat = p.t["stat"]
    W(scalar, sv, 1)
    scalar.drain()
    scalar.activation(r4[:, 3, :], r4[:, 2, :], AF.Ln)
    scalar.drain()
    scalar.activation(stat[:, 1, :], r4[:, 3, :], AF.Exp,
                      scale=-0.5).then_inc(*ss.tick(1))


def _stats_fin_v2(p, vector, ss):
    r4 = p.t["r4"]
    stat = p.t["stat"]
    W(vector, ss, 1)
    inst = vector.tensor_mul(stat[:, 0, :], r4[:, 0, :], stat[:, 1, :])
    p.mark("stat", inst, 1)
    vector.drain()


def ph_stats(p, xsrc, tag, wait_marks=()):
    p.phase_begin()
    nc = p.nc
    xr = p.t["xring"]
    sel4f = p.t["sel4fs"]
    sx = p.slots(tag + "x", 4)
    sq_done = p.new_sem(tag + "q")
    mm = p.new_sem(tag + "m")
    sv = p.new_sem(tag + "v")
    ss = p.new_sem(tag + "s")
    with (
        nc.psum_tensor("ps" + tag, [4, 2, TC], F32) as ps4,
        nc.sbuf_tensor("sq" + tag, [128, 2, TC], F32) as sq,
        nc.Block() as blk,
    ):
        @blk.sync
        def _(sync):
            for mname in wait_marks:
                p.wait_mark(sync, mname)
            for i in range(NTC * NFC):
                if i >= 4:
                    W(sync, mm, 2 * (i - 3))
                sync.dma_start(xr[:, i % 4, :],
                               xsrc[(i % NFC) * 128:(i % NFC + 1) * 128,
                                    (i // NFC) * TC:(i // NFC + 1) * TC]).then_inc(*sx[i % 4].tick(16))
            for k in range(4):
                W(sync, sx[k], 16 * (NTC * NFC // 4))

        @blk.tensor
        def _(tensor):
            p.wait_mark(tensor, "const")
            for i in range(NTC * NFC):
                tc = i // NFC
                W(tensor, sx[i % 4], 16 * (i // 4 + 1))
                tensor.matmul(ps4[:, 0, :], sel4f[:, tc * 4:(tc + 1) * 4],
                              xr[:, i % 4, :], start=(i == 0), stop=(i == NTC * NFC - 1))
                W(tensor, sq_done, i + 1)
                tensor.matmul(ps4[:, 1, :], sel4f[:, tc * 4:(tc + 1) * 4],
                              sq[:, i % 2, :], start=(i == 0),
                              stop=(i == NTC * NFC - 1)).then_inc(*mm.tick(2))

        @blk.scalar
        def _(scalar):
            for i in range(NTC * NFC):
                W(scalar, sx[i % 4], 16 * (i // 4 + 1))
                if i >= 2:
                    W(scalar, mm, 2 * (i - 1))
                scalar.activation(sq[:, i % 2, :], xr[:, i % 4, :],
                                  AF.Square).then_inc(*sq_done.tick(1))
            _stats_fin_s(p, scalar, sv, ss)

        @blk.vector
        def _(vector):
            W(vector, mm, 2 * NTC * NFC)
            _stats_fin_v1(p, vector, ps4, sv)
            _stats_fin_v2(p, vector, ss)


def ph_xn(p, xsrc, tag, wait_marks=()):
    p.phase_begin()
    nc = p.nc
    xbig = p.arena("xnb" + tag, [128, 2, NFC, TC], F32, "BIG")
    xn = p.t["xn"]
    stat = p.t["stat"]
    bb = p.t["bcb"]
    sx = p.slots(tag + "x", 2)
    mm = p.new_sem(tag + "m")
    sv = p.new_sem(tag + "v")
    with (
        nc.psum_tensor("ps" + tag, [128, 2, TC], F32) as ps,
        nc.Block() as blk,
    ):
        @blk.sync
        def _(sync):
            for mname in wait_marks:
                p.wait_mark(sync, mname)
            for tc in range(NTC):
                for fc in range(NFC):
                    if tc >= 2 and fc == 0:
                        W(sync, sv, tc - 1)
                    sync.dma_start(xbig[:, tc % 2, fc, :],
                                   xsrc[fc * 128:(fc + 1) * 128,
                                        tc * TC:(tc + 1) * TC]).then_inc(*sx[tc % 2].tick(16))
            for k in range(2):
                W(sync, sx[k], 16 * NFC * (NTC // 2))

        @blk.tensor
        def _(tensor):
            p.wait_mark(tensor, "stat")
            sel4t = p.t["sel4ts"]
            for tc in range(NTC):
                if tc >= 1:
                    W(tensor, sv, tc)
                tensor.matmul(ps[:, 0, :], sel4t[:, tc * 128:(tc + 1) * 128],
                              stat[:, 1, :], start=True, stop=True)
                tensor.matmul(ps[:, 1, :], sel4t[:, tc * 128:(tc + 1) * 128],
                              stat[:, 0, :], start=True, stop=True).then_inc(*mm.tick(1))

        @blk.vector
        def _(vector):
            for tc in range(NTC):
                W(vector, mm, tc + 1)
                if tc >= 1:
                    vector.drain()
                vector.tensor_copy(bb[:, 0, :], ps[:, 0, :])
                vector.tensor_copy(bb[:, 1, :], ps[:, 1, :])
                vector.drain()
                W(vector, sx[tc % 2], 16 * NFC * (tc // 2 + 1))
                for fc in range(NFC):
                    vector.tensor_mul(xbig[:, tc % 2, fc, :], xbig[:, tc % 2, fc, :],
                                      bb[:, 0, :])
                vector.drain()
                for fc in range(NFC - 1):
                    vector.tensor_sub(xn[:, fc, tc * TC:(tc + 1) * TC],
                                      xbig[:, tc % 2, fc, :], bb[:, 1, :])
                vector.drain()
                vector.tensor_sub(xn[:, NFC - 1, tc * TC:(tc + 1) * TC],
                                  xbig[:, tc % 2, NFC - 1, :], bb[:, 1, :]).then_inc(*sv.tick(1))
            p.alias_mark("xn", sv, NTC)


def _dbg_rows(p, srcname, outname, w, marks=()):
    p.phase_begin()
    nc = p.nc
    out = nc.dram_tensor(outname, [128, w], F32, kind="ExternalOutput")
    sd = p.new_sem("dbgr", "g")
    with nc.Block() as blk:
        @blk.gpsimd
        def _(g):
            for m in marks:
                p.wait_mark(g, m)
            g.dma_start(out[:, :], p.t[srcname][:, :]).then_inc(*sd.tick(16))
            W(g, sd, 16)


def _dbg_dump_fm(p, srcname, outname, nfc=NFC, width=TPC, marks=("xn",)):
    p.phase_begin()
    nc = p.nc
    src = p.t[srcname]
    out = p.t.get(outname)
    if out is None:
        out = nc.dram_tensor(outname, [nfc * 128, width], F32, kind="ExternalOutput")
        p.t[outname] = out
    sv = p.new_sem("dmp")
    sx = p.slots("dmpx", 2)
    with (
        nc.sbuf_tensor("dmp" + outname, [128, 2, width], F32) as tmp,
        nc.Block() as blk,
    ):
        @blk.vector
        def _(vector):
            for m in marks:
                p.wait_mark(vector, m)
            for fc in range(nfc):
                if fc >= 2:
                    W(vector, sx[fc % 2], 16 * ((fc - 2) // 2 + 1))
                vector.tensor_copy(tmp[:, fc % 2, :], src[:, fc, :]).then_inc(*sv.tick(1))

        @blk.sync
        def _(sync):
            for fc in range(nfc):
                W(sync, sv, fc + 1)
                sync.dma_start(out[fc * 128:(fc + 1) * 128, :],
                               tmp[:, fc % 2, :]).then_inc(*sx[fc % 2].tick(16))
            for k in range(2):
                W(sync, sx[k], 16 * (nfc // 2))



def ph_proj(p, tag, wt, kind, n_oc, *, dst, dst_col0=0, dst_row0=0,
            norm_row=None, norm_col0=0, wt_col0=0, n_dc=NFC):
    p.phase_begin()
    nc = p.nc
    xn = p.t["xn"]
    nrows = p.t.get("nrows")
    row = p.t["rowtmp"]
    sw = p.slots(tag + "w", 2)
    swc = p.new_sem(tag + "wc")
    smm = p.new_sem(tag + "mm")
    sev = p.new_sem(tag + "ev")
    ssq = p.new_sem(tag + "sq")
    snm = p.new_sem(tag + "nm")
    snr = p.new_sem(tag + "nr")
    svt = p.new_sem(tag + "vt")
    svc = p.new_sem(tag + "vc")
    skd = p.slots(tag + "kd", 2, "g")
    if "nonorm" in p.dbg:
        norm_row = None
    normed = norm_row is not None

    wrf = p.arena("wrf" + tag, [128, 2, n_dc, 128], F32, "SCR", 16384)
    wrb = p.arena("wrb" + tag, [128, 2, n_dc, 128], BF16, "SCR", 32768)
    stg = p.arena("stg" + tag, [128, 2, TPC], BF16, "SCR", 40960)
    vstg = p.arena("vsg" + tag, [128, 2, NFC, 128], BF16, "SCR", 49152)
    with (
        nc.sbuf_tensor("sqn" + tag, [128, TPC], BF16) as sqn,
        nc.psum_tensor("pp" + tag, [128, TPC], F32) as pp,
        nc.psum_tensor("vt0" + tag, [128, 128], BF16) as vtp0,
        nc.psum_tensor("vt1" + tag, [128, 128], BF16) as vtp1,
        nc.psum_tensor("np" + tag, [1, 2, TC], F32) as nps,
        nc.Block() as blk,
    ):
        @blk.sync
        def _(sync):
            for oc in range(n_oc):
                if oc >= 2:
                    W(sync, swc, oc - 1)
                c0 = wt_col0 + oc * 128
                sync.dma_start(
                    wrf[:, oc % 2, :, :],
                    wt[:, c0:c0 + 128].rearrange("(dc p) o -> p dc o", p=128),
                ).then_inc(*sw[oc % 2].tick(16))
            for k in range(2):
                W(sync, sw[k], 16 * ((n_oc - k + 1) // 2))

        @blk.tensor
        def _(tensor):
            p.wait_mark(tensor, "xn")
            p.wait_mark(tensor, "const")
            for oc in range(n_oc):
                W(tensor, swc, oc + 1)
                if oc >= 1:
                    W(tensor, sev, oc)
                    if normed:
                        W(tensor, ssq, oc)
                inst = None
                for dc in range(n_dc):
                    for tc in range(NTC):
                        inst = tensor.matmul(pp[:, tc * TC:(tc + 1) * TC],
                                             wrb[:, oc % 2, dc, :],
                                             xn[:, dc, tc * TC:(tc + 1) * TC],
                                             start=(dc == 0), stop=(dc == n_dc - 1))
                inst.then_inc(*smm.tick(1))
                if normed and "nnm" not in p.dbg:
                    W(tensor, ssq, oc + 1)
                    for tc in range(NTC):
                        g = 4 * oc + tc
                        if g >= 2:
                            W(tensor, snr, g - 1)
                        tensor.matmul(nps[:, tc % 2, :], p.t["ones1b"][:, :],
                                      sqn[:, tc * TC:(tc + 1) * TC],
                                      start=True, stop=True).then_inc(*snm.tick(1))
                if kind == "vt":
                    W(tensor, sev, oc + 1)
                    for j in range(NFC):
                        m = oc * NFC + j
                        if m >= 2:
                            W(tensor, svc, m - 1)
                        vtp = vtp0 if m % 2 == 0 else vtp1
                        tensor.transpose(vtp[:, :],
                                         stg[:, oc % 2, j * 128:(j + 1) * 128],
                                         p.t["identb"][:, :]).then_inc(*svt.tick(1))

        @blk.scalar
        def _(scalar):
            if normed:
                for oc in range(n_oc):
                    W(scalar, sev, oc + 1)
                    if oc >= 1 and "nnm" not in p.dbg:
                        W(scalar, snm, 4 * oc)
                    scalar.activation(sqn[:, :], pp[:, :], AF.Square).then_inc(*ssq.tick(1))

        @blk.vector
        def _(vector):
            for oc in range(n_oc):
                W(vector, sw[oc % 2], 16 * (oc // 2 + 1))
                if oc >= 2:
                    W(vector, smm, oc - 1)
                vector.tensor_copy(wrb[:, oc % 2, :, :], wrf[:, oc % 2, :, :]).then_inc(*swc.tick(1))
                W(vector, smm, oc + 1)
                if kind == "q":
                    vector.tensor_copy(dst[:, dst_col0 + oc, :], pp[:, :]).then_inc(*sev.tick(1))
                else:
                    if oc >= 2:
                        if kind == "kf":
                            W(vector, skd[oc % 2], 16 * (oc // 2))
                        else:
                            W(vector, svt, 16 * (oc - 1))
                    vector.tensor_copy(stg[:, oc % 2, :], pp[:, :]).then_inc(*sev.tick(1))
                if normed and "nnm" not in p.dbg:
                    for tc in range(NTC):
                        g = 4 * oc + tc
                        W(vector, snm, g + 1)
                        vector.tensor_reduce(row[:, 3, tc:tc + 1], nps[:, tc % 2, :],
                                             AX.X, ALU.max).then_inc(*snr.tick(1))
                    vector.drain()
                    vector.tensor_reduce(row[:, 2, 0:1], row[:, 3, 0:4], AX.X, ALU.max)
                    vector.drain()
                    inst = vector.tensor_copy(nrows[:, norm_row, norm_col0 + oc:norm_col0 + oc + 1],
                                              row[:, 2, 0:1])
                    p.mark(tag + "_nrm", inst, 1)
                    vector.drain()
                if kind == "vt":
                    for j in range(NFC):
                        m = oc * NFC + j
                        W(vector, svt, m + 1)
                        if oc >= 2 and j == 0:
                            W(vector, skd[oc % 2], 16 * (oc // 2))
                        vector.tensor_copy(vstg[:, oc % 2, j, :],
                                           (vtp0 if m % 2 == 0 else vtp1)[:, :]).then_inc(*svc.tick(1))

        @blk.gpsimd
        def _(g):
            for oc in range(n_oc):
                if kind == "kf":
                    W(g, sev, oc + 1)
                    r0 = dst_row0 + oc * 128
                    g.dma_start(dst[r0:r0 + 128, :], stg[:, oc % 2, :]).then_inc(*skd[oc % 2].tick(16))
                elif kind == "vt":
                    W(g, svc, 16 * (oc + 1))
                    c0 = dst_col0 + oc * 128
                    g.dma_start(dst.ap().rearrange("(j p) w -> p j w", p=128)[:, :, c0:c0 + 128],
                                vstg[:, oc % 2, :, :]).then_inc(*skd[oc % 2].tick(16))
            if kind in ("kf", "vt"):
                for k in range(2):
                    W(g, skd[k], 16 * ((n_oc - k + 1) // 2))

    if kind == "q":
        p.alias_mark(tag + "_ev", sev, n_oc)


def ph_cc(p, tag, pairs, krow, grow, nin, nout):
    p.phase_begin()
    nc = p.nc
    s_nrm = p.new_sem(tag + "nr", "g")
    scc = p.new_sem(tag + "cc", "g")
    sd = p.slots(tag + "cd", 2, "g")
    nrows = p.t["nrows"]
    grp = [[0, 1, 2, 3], [4, 5, 6, 7]]
    nk = nin.shape[1]
    with nc.Block() as blk:
        @blk.gpsimd
        def _(g):
            g.dma_start(nin[:, :], nrows[:, krow, 0:nk]).then_inc(*sd[0].tick(16))
            W(g, sd[0], 16)
            g.collective_compute("AllReduce", ALU.max, grp,
                                 [nin.ap().opt()], [nout.ap().opt()]).then_inc(*s_nrm.tick(1))
            for (ci, co) in pairs:
                g.collective_compute("AllGather", ALU.bypass, grp,
                                     [p.t[ci].ap().opt()],
                                     [p.t[co].ap().opt()]).then_inc(*scc.tick(1))
            W(g, s_nrm, 1)
            g.dma_start(nrows[:, grow, 0:nk], nout[:, :]).then_inc(*sd[1].tick(16))
            W(g, sd[1], 16)
            p.mark("nrmg_" + tag, g.nop(), 1)
        p.alias_mark("gath_" + tag, scc, len(pairs))


def _negm_emit_vector(p, tag, vector, qrow, grow, scale, nh, negM,
                      q_nrm_mark, bps, s1, s2, s3, s4, s5, group_of_head=None):
    nrows = p.t["nrows"]
    row = p.t["rowtmp"]
    p.wait_mark(vector, q_nrm_mark)
    p.wait_mark(vector, "nrmg_" + tag)
    if group_of_head is None:
        vector.tensor_copy(row[:, 3, 0:nh], nrows[:, grow, 0:nh])
    else:
        for h in range(nh):
            vector.tensor_copy(row[:, 3, h:h + 1],
                               nrows[:, grow, h // GROUPS:h // GROUPS + 1])
    vector.drain()
    vector.tensor_mul(row[:, 0, 0:nh], nrows[:, qrow, 0:nh],
                      row[:, 3, 0:nh]).then_inc(*s1.tick(1))
    W(vector, s2, 1)
    vector.tensor_scalar(row[:, 2, 0:nh], row[:, 1, 0:nh],
                         -scale, None, ALU.mult).then_inc(*s3.tick(1))
    W(vector, s4, 1)
    vector.tensor_copy(negM[:, 0:nh], bps[:, 0:nh]).then_inc(*s5.tick(1))
    vector.drain()


def _negm_emit_scalar(p, scalar, s1, s2, nh):
    row = p.t["rowtmp"]
    W(scalar, s1, 1)
    scalar.activation(row[:, 1, 0:nh], row[:, 0, 0:nh], AF.Sqrt).then_inc(*s2.tick(1))
    scalar.drain()


def _negm_emit_tensor(p, tensor, s3, s4, bps, nh):
    row = p.t["rowtmp"]
    W(tensor, s3, 1)
    tensor.matmul(bps[:, 0:nh], p.t["onesr"][:, :], row[:, 2, 0:nh],
                  start=True, stop=True).then_inc(*s4.tick(1))


def _dbg_dram(p, src_name, out_name, shape, dtype=F32, n_rows=None):
    p.phase_begin()
    nc = p.nc
    out = nc.dram_tensor(out_name, shape, dtype, kind="ExternalOutput")
    r = shape[0] if n_rows is None else n_rows
    sd = p.new_sem("dbgd", "g")
    with nc.Block() as blk:
        @blk.gpsimd
        def _(g):
            g.dma_start(out[:r, :], p.t[src_name][:r, :]).then_inc(*sd.tick(16))
            W(g, sd, 16)



def ph_attn_a(p):
    p.phase_begin()
    nc = p.nc
    Q = p.t["Q"]
    Vsb = p.arena("a3v", [128, 2, 64, 256], BF16, "XN", 0)
    pt = p.arena("a3p", [128, 2, 1024], BF16, "SCR", 0)
    lbsb = p.arena("a3lb", [128, 1024], F32, "SCR", 4096)
    ctile = p.arena("a3c", [128, 2, 1024], BF16, "SCR", 8192)
    lrow = p.arena("a3lr", [1, 1024], F32, "SCR", 12288)
    Ksb = p.arena("a3k", [128, 2, 4 * TPC], BF16, "SCR", 16384)
    negM = p.t["negMa"]
    kcc, vcc, ctxd = p.t["kcc_out"], p.t["vcc_out"], p.t["ctxd"]
    skv = p.slots("a3kv", 2, "g")
    svv = p.slots("a3vv", 2, "g")
    sctx = p.slots("a3cx", 2, "g")
    s_st = p.new_sem("a3st")
    s_xp = p.new_sem("a3xp")
    s_pv = p.new_sem("a3pv")
    s_rc = p.new_sem("a3rc")
    s_bc = p.new_sem("a3bc")
    s_ev = p.new_sem("a3ev")
    sn1, sn2, sn3, sn4, sn5 = (p.new_sem("a3n1"), p.new_sem("a3n2"),
                               p.new_sem("a3n3"), p.new_sem("a3n4"),
                               p.new_sem("a3n5"))
    NH = 2 if "mini" in p.dbg else HEADS
    NH2 = NH * 2

    with (
        nc.psum_tensor("a3stp", [128, 2, 1024], F32) as st,
        nc.psum_tensor("a3op", [128, 1024], F32) as ops,
        nc.psum_tensor("a3lp", [1, 1024], F32) as lps,
        nc.Block() as blk,
    ):
        @blk.gpsimd
        def _(g):
            p.wait_mark(g, "gath_a")

            def store_ctx(hh):
                h, half = hh // 2, hh % 2
                W(g, s_ev, hh + 1)
                g.dma_start(ctxd[h * 128:(h + 1) * 128, half * 1024:(half + 1) * 1024],
                            ctile[:, hh % 2, :]).then_inc(*sctx[hh % 2].tick(16))

            for h in range(NH):
                if h >= 2:
                    store_ctx(2 * (h - 2))
                    store_ctx(2 * (h - 2) + 1)
                sl = h % 2
                if h >= 2:
                    W(g, s_st, 128 * (h - 1))
                for r in range(4):
                    g.dma_start(Ksb[:, sl, r * TPC:(r + 1) * TPC],
                                kcc[r * D + h * 128:r * D + (h + 1) * 128, :]
                                ).then_inc(*skv[sl].tick(16))
                if h % 2 == 0:
                    P = h // 2
                    vs = P % 2
                    if P >= 2:
                        W(g, s_pv, 256 * (P - 1))
                    vv = vcc.ap().rearrange("(kb pp) f -> pp kb f", pp=128)
                    for kq in range(8):
                        g.dma_start(
                            Vsb[:, vs, kq * 8:(kq + 1) * 8, :],
                            vv[:, kq * 8:(kq + 1) * 8, h * 128:(h + 2) * 128],
                        ).then_inc(*svv[vs].tick(16))
            for hh in range(max(0, NH2 - 4), NH2):
                store_ctx(hh)
            for k in range(2):
                W(g, sctx[k], 16 * (NH2 // 2))

        @blk.tensor
        def _(tensor):
            _negm_emit_tensor(p, tensor, sn3, sn4, ops[:, 0:128], HEADS)
            p.wait_mark(tensor, "wq_ev")
            for hh in range(NH2):
                h, half = hh // 2, hh % 2
                hp = h % 2
                P = h // 2
                q0 = half * 1024
                if half == 0:
                    W(tensor, skv[h % 2], 64 * (h // 2 + 1))
                    if hp == 0:
                        W(tensor, svv[P % 2], 128 * (P // 2 + 1))

                def pv(j):
                    m = hh * 64 + j
                    W(tensor, s_xp, m + 1)
                    if j == 0:
                        W(tensor, s_ev, hh)
                        W(tensor, s_rc, hh)
                    st_, sp_ = (j == 0), (j == 63)
                    tensor.matmul(ops[:, 0:512], Vsb[:, P % 2, j, hp * 128:(hp + 1) * 128],
                                  pt[:, m % 2, 0:512], start=st_, stop=sp_)
                    tensor.matmul(ops[:, 512:1024], Vsb[:, P % 2, j, hp * 128:(hp + 1) * 128],
                                  pt[:, m % 2, 512:1024], start=st_, stop=sp_)
                    tensor.matmul(lps[:, 0:512], p.t["ones1b"][:, :],
                                  pt[:, m % 2, 0:512], start=st_, stop=sp_)
                    tensor.matmul(lps[:, 512:1024], p.t["ones1b"][:, :],
                                  pt[:, m % 2, 512:1024], start=st_, stop=sp_).then_inc(*s_pv.tick(1))

                for kb in range(64):
                    m = hh * 64 + kb
                    if m >= 2:
                        W(tensor, s_xp, m - 1)
                    if kb == 0 and hh >= 1:
                        W(tensor, s_ev, hh)
                    tensor.matmul(st[:, m % 2, 0:512], Ksb[:, h % 2, kb * 128:(kb + 1) * 128],
                                  Q[:, h, q0:q0 + 512], start=True, stop=True)
                    tensor.matmul(st[:, m % 2, 512:1024], Ksb[:, h % 2, kb * 128:(kb + 1) * 128],
                                  Q[:, h, q0 + 512:q0 + 1024], start=True, stop=True).then_inc(*s_st.tick(1))
                    if kb >= 1:
                        pv(kb - 1)
                pv(63)
                W(tensor, s_rc, hh + 1)
                tensor.matmul(st[:, 0, 0:512], p.t["onesr"][:, :],
                              lrow[:, 0:512], start=True, stop=True)
                tensor.matmul(st[:, 0, 512:1024], p.t["onesr"][:, :],
                              lrow[:, 512:1024], start=True, stop=True).then_inc(*s_bc.tick(1))

        @blk.scalar
        def _(scalar):
            _negm_emit_scalar(p, scalar, sn1, sn2, HEADS)
            W(scalar, sn5, 1)
            for hh in range(NH2):
                h = hh // 2
                for kb in range(64):
                    m = hh * 64 + kb
                    W(scalar, s_st, m + 1)
                    if m >= 2:
                        W(scalar, s_pv, m - 1)
                    scalar.activation(pt[:, m % 2, :], st[:, m % 2, :], AF.Exp,
                                      bias=negM[:, h:h + 1], scale=SM_SCALE).then_inc(*s_xp.tick(1))

        @blk.vector
        def _(vector):
            _negm_emit_vector(p, "a", vector, 0, 2, SM_SCALE, HEADS, p.t["negMa"],
                              "wq_nrm", ops[:, 0:128], sn1, sn2, sn3, sn4, sn5)
            for hh in range(NH2):
                W(vector, s_pv, 64 * (hh + 1))
                vector.reciprocal(lrow[:, :], lps[:, :]).then_inc(*s_rc.tick(1))
                W(vector, s_bc, hh + 1)
                vector.tensor_copy(lbsb[:, :], st[:, 0, :])
                vector.drain()
                if hh >= 2:
                    W(vector, sctx[hh % 2], 16 * (hh // 2))
                vector.tensor_mul(ctile[:, hh % 2, :], ops[:, :], lbsb[:, :]).then_inc(*s_ev.tick(1))




def ph_attn_b(p):
    p.phase_begin()
    nc = p.nc
    Q = p.t["Q"]
    Kg = p.arena("b3k", [128, 2, 4 * TPC], BF16, "XN", 0)
    Vg = p.arena("b3v", [128, 2, 64, 128], BF16, "XN", 32768)
    pt = p.arena("b3p", [128, 4, TC], BF16, "SCR", 0)
    lbsb = p.arena("b3lb", [128, TC], F32, "SCR", 4608)
    ctile = p.arena("b3c", [128, 2, TC], BF16, "SCR", 6656)
    lrow = p.arena("b3lr", [1, TC], F32, "SCR", 8704)
    btab = p.arena("b3bt", [128, HEADS, 64], F32, "SCR", 10752)
    indsb = p.arena("b3in", [128, 64], F32, "SCR", 15104)
    bms = p.arena("b3bm", [128, 2, 16, TC], BF16, "SCR", 16384)
    negM = p.t["negMb"]
    kcc, vcc, ctxd = p.t["kbcc_out"], p.t["vbcc_out"], p.t["ctxd"]

    skv = p.slots("b3kv", 2, "g")
    svv = p.slots("b3vv", 2, "g")
    sbm = p.slots("b3bm", 2)
    sctx = p.slots("b3cx", 2, "g")
    scst = p.slots("b3cs", 2)
    s_st = p.new_sem("b3st")
    s_xp = p.new_sem("b3xp")
    s_tri = p.new_sem("b3tr")
    s_pv = p.new_sem("b3pv")
    s_rc = p.new_sem("b3rc")
    s_bc = p.new_sem("b3bc")
    s_ev = p.new_sem("b3ev")
    s_bt = p.new_sem("b3btv")
    bn1, bn2, bn3, bn4, bn5 = (p.new_sem("b3n1"), p.new_sem("b3n2"),
                               p.new_sem("b3n3"), p.new_sem("b3n4"),
                               p.new_sem("b3n5"))

    blocks = []
    for g in range(GROUPS):
        for j in range(NTC):
            for h4 in range(4):
                blocks.append((g, j, g * 4 + h4))
    nw_of = lambda j: 16 * j + 16
    m_start = []
    m = 0
    for (g, j, h) in blocks:
        m_start.append(m)
        m += nw_of(j)
    ntri_before = []
    nt = 0
    for bi, (g, j, h) in enumerate(blocks):
        for w in range(nw_of(j)):
            ntri_before.append(nt)
            if w >= 16 * j:
                nt += 1

    def phys_win(j, u):
        return (u // 4) * 16 + j * 4 + (u % 4)

    def slab_of(bi):
        g, j, h = blocks[bi]
        return g * NTC + j

    tris_through_slab = {}
    cnt = 0
    for bi, (g, j, h) in enumerate(blocks):
        cnt += 16
        tris_through_slab[g * NTC + j] = cnt

    with (
        nc.psum_tensor("b3st0", [128, TC], F32) as st0,
        nc.psum_tensor("b3st1", [128, TC], F32) as st1,
        nc.psum_tensor("b3st2", [128, TC], F32) as st2,
        nc.psum_tensor("b3st3", [128, TC], F32) as st3,
        nc.psum_tensor("b3op", [128, TC], F32) as ops,
        nc.psum_tensor("b3lp", [1, TC], F32) as lps,
        nc.Block() as blk,
    ):
        stN = (st0, st1)
        @blk.sync
        def _(sync):
            sync.dma_start(indsb[:, :], p.inp("ind")[:, :]).then_inc(*scst[0].tick(16))
            bm = p.inp("bmask").ap().rearrange("(ju pp) q -> pp ju q", pp=128)
            for si in range(GROUPS * NTC):
                g, j = divmod(si, NTC)
                if si >= 2:
                    W(sync, s_tri, tris_through_slab[si - 2])
                sync.dma_start(bms[:, si % 2, :, :],
                               bm[:, j * 16:(j + 1) * 16, :]).then_inc(*sbm[si % 2].tick(16))

        @blk.vector
        def _(vector):
            _negm_emit_vector(p, "b", vector, 3, 5, SM_SCALE, HEADS, p.t["negMb"],
                              "bq_nrm", ops[:, 0:128], bn1, bn2, bn3, bn4, bn5,
                              group_of_head=True)
            W(vector, scst[0], 16)
            for h in range(HEADS):
                vector.tensor_scalar(btab[:, h, :], indsb[:, :], negM[:, h:h + 1],
                                     None, ALU.add)
            vector.drain()
            vector.tensor_copy(btab[:, 0, 0:1], btab[:, 0, 0:1]).then_inc(*s_bt.tick(1))
            m = 0
            for bi, (g, j, h) in enumerate(blocks):
                nw = nw_of(j)
                si = g * NTC + j
                if h % 4 == 0:
                    W(vector, sbm[si % 2], 16 * (si // 2 + 1))
                for w in range(nw):
                    if w >= 16 * j:
                        W(vector, s_xp, m + 1)
                        vector.tensor_mul(pt[:, m % 2, :], pt[:, m % 2, :],
                                          bms[:, si % 2, w - 16 * j, :]
                                          ).then_inc(*s_tri.tick(1))
                    m += 1
                W(vector, s_pv, m_start[bi] + nw)
                vector.reciprocal(lrow[:, :], lps[:, :]).then_inc(*s_rc.tick(1))
                W(vector, s_bc, bi + 1)
                vector.tensor_copy(lbsb[:, :], st0[:, :])
                vector.drain()
                if bi >= 2:
                    W(vector, sctx[bi % 2], 16 * (bi // 2))
                vector.tensor_mul(ctile[:, bi % 2, :], ops[:, :],
                                  lbsb[:, :]).then_inc(*s_ev.tick(1))

        @blk.gpsimd
        def _(g_):
            p.wait_mark(g_, "gath_b")
            for gi in range(GROUPS):
                if gi >= 1:
                    for bi in range((gi - 1) * 16 - 4 if gi >= 2 else 0, gi * 16 - 4):
                        _b3_store(p, g_, bi, blocks, ctile, ctxd, s_ev, sctx)
                sl = gi % 2
                if gi >= 2:
                    W(g_, s_st, m_start[(gi - 1) * 16])
                for r in range(4):
                    g_.dma_start(Kg[:, sl, r * TPC:(r + 1) * TPC],
                                 kcc[r * 512 + gi * 128:r * 512 + (gi + 1) * 128, :]
                                 ).then_inc(*skv[sl].tick(16))
                if gi >= 2:
                    W(g_, s_pv, m_start[(gi - 1) * 16])
                vv = vcc.ap().rearrange("(kb pp) f -> pp kb f", pp=128)
                for kq in range(8):
                    g_.dma_start(Vg[:, sl, kq * 8:(kq + 1) * 8, :],
                                 vv[:, kq * 8:(kq + 1) * 8, gi * 128:(gi + 1) * 128]
                                 ).then_inc(*svv[sl].tick(16))
            for bi in range(3 * 16 - 4, 4 * 16):
                _b3_store(p, g_, bi, blocks, ctile, ctxd, s_ev, sctx)
            for k in range(2):
                W(g_, sctx[k], 16 * (len(blocks) // 2))
            p.mark("ctxb", g_.nop(), 1)

        @blk.tensor
        def _(tensor):
            _negm_emit_tensor(p, tensor, bn3, bn4, ops[:, 0:128], HEADS)
            p.wait_mark(tensor, "bq_ev")
            m = 0
            for bi, (g, j, h) in enumerate(blocks):
                nw = nw_of(j)
                sl = g % 2
                if j == 0 and h % 4 == 0:
                    W(tensor, skv[sl], 64 * (g // 2 + 1))
                    W(tensor, svv[sl], 128 * (g // 2 + 1))

                def pv(mm, w):
                    if w >= 16 * j:
                        W(tensor, s_tri, ntri_before[mm] + 1)
                    else:
                        W(tensor, s_xp, mm + 1)
                    if w == 0:
                        W(tensor, s_ev, bi)
                        W(tensor, s_rc, bi)
                    P = _phys_chunk(w) if w < 16 * j else phys_win(j, w - 16 * j)
                    st_, sp_ = (w == 0), (w == nw - 1)
                    tensor.matmul(ops[:, :], Vg[:, sl, P, :], pt[:, mm % 2, :],
                                  start=st_, stop=sp_)
                    tensor.matmul(lps[:, :], p.t["ones1b"][:, :], pt[:, mm % 2, :],
                                  start=st_, stop=sp_).then_inc(*s_pv.tick(1))

                for w in range(nw):
                    if m >= 2:
                        W(tensor, s_xp, m - 1)
                    if w == 0 and bi >= 1:
                        W(tensor, s_ev, bi)
                    P = _phys_chunk(w) if w < 16 * j else phys_win(j, w - 16 * j)
                    kt = Kg[:, sl, P * 128:(P + 1) * 128]
                    stp = stN[m % 2]
                    tensor.matmul(stp[:, :], kt, Q[:, h, j * TC:(j + 1) * TC],
                                  start=True, stop=True).then_inc(*s_st.tick(1))
                    if w >= 1:
                        pv(m - 1, w - 1)
                    m += 1
                pv(m - 1, nw - 1)
                W(tensor, s_rc, bi + 1)
                tensor.matmul(st0[:, :], p.t["onesr"][:, :], lrow[:, :],
                              start=True, stop=True).then_inc(*s_bc.tick(1))

        @blk.scalar
        def _(scalar):
            _negm_emit_scalar(p, scalar, bn1, bn2, HEADS)
            W(scalar, s_bt, 1)
            m = 0
            for bi, (g, j, h) in enumerate(blocks):
                nw = nw_of(j)
                for w in range(nw):
                    W(scalar, s_st, m + 1)
                    if m >= 2:
                        W(scalar, s_pv, m - 1)
                    stp = stN[m % 2]
                    if w < 16 * j:
                        bias = negM[:, h:h + 1]
                    else:
                        col = j * 16 + (w - 16 * j)
                        bias = btab[:, h, col:col + 1]
                    scalar.activation(pt[:, m % 2, :], stp[:, :], AF.Exp,
                                      bias=bias, scale=SM_SCALE).then_inc(*s_xp.tick(1))
                    m += 1


def _b3_store(p, g_, bi, blocks, ctile, ctxd, s_ev, sctx):
    (g, j, h) = blocks[bi]
    W(g_, s_ev, bi + 1)
    g_.dma_start(ctxd[h * 128:(h + 1) * 128, j * TC:(j + 1) * TC],
                 ctile[:, bi % 2, :]).then_inc(*sctx[bi % 2].tick(16))



def ph_wcast(p):
    p.phase_begin()
    nc = p.nc
    jobs = [(p.inp("wd_t").ap().rearrange("e h d -> (e h) d"), p.t["wdb"], NEXP * HEXP, D),
            (p.inp("shd_t").ap(), p.t["shdb"], HSH, D)]
    sf = p.slots("wcf", 4)
    sb = p.slots("wcb", 2, "g")
    sv = p.new_sem("wcv")
    buf_f = p.arena("wcff", [128, 4, D], F32, "SCR", 16384)
    buf_b = p.arena("wcbb", [128, 2, D], BF16, "SCR", 0)
    with nc.Block() as blk:
        tiles = []
        for src, dst, R, C in jobs:
            for r in range(R // 128):
                tiles.append((src, dst, r, C))

        @blk.sync
        def _(sync):
            for i, (src, dst, r, C) in enumerate(tiles):
                if i >= 4:
                    W(sync, sv, i - 3)
                sync.dma_start(buf_f[:, i % 4, 0:C],
                               src[r * 128:(r + 1) * 128, :]).then_inc(*sf[i % 4].tick(16))

        @blk.vector
        def _(vector):
            for i, (src, dst, r, C) in enumerate(tiles):
                W(vector, sf[i % 4], 16 * (i // 4 + 1))
                if i >= 2:
                    W(vector, sb[i % 2], 16 * (i // 2))
                vector.tensor_copy(buf_b[:, i % 2, 0:C], buf_f[:, i % 4, 0:C]).then_inc(*sv.tick(1))

        @blk.gpsimd
        def _(g):
            for i, (src, dst, r, C) in enumerate(tiles):
                W(g, sv, i + 1)
                g.dma_start(dst[r * 128:(r + 1) * 128, :], buf_b[:, i % 2, 0:C]).then_inc(*sb[i % 2].tick(16))
            n = len(tiles)
            for k in range(2):
                W(g, sb[k], 16 * ((n - k + 1) // 2))
            p.mark("wcast", g.nop(), 1)


def ph_gconst(p):
    p.phase_begin()
    nc = p.nc
    p.t["ggb"] = nc.alloc_sbuf_tensor("ggb", [128, NFC, 8], BF16)
    p.t["seb"] = nc.alloc_sbuf_tensor("seb", [128, NFC, 1], BF16)
    p.t["selb"] = nc.alloc_sbuf_tensor("selb", [8, 8 * 128], BF16)
    p.t["sg"] = p.arena("sg", [1, TPC], F32, "SCR", 49152)
    gf = p.arena("gcf", [128, NFC, 8], F32, "SCR", 16384)
    sef = p.arena("gce", [128, NFC, 1], F32, "SCR", 22528)
    self_sel = p.arena("gcs", [8, 8 * 128], F32, "SCR", 24576)
    sd = p.slots("gcd", 3)
    with nc.Block() as blk:
        @blk.sync
        def _(sync):
            sync.dma_start(gf[:, :, :],
                           p.inp("ggate").ap().rearrange("(dc p) e -> p dc e", p=128)
                           ).then_inc(*sd[0].tick(16))
            sync.dma_start(sef[:, :, :],
                           p.inp("se_t").ap().rearrange("(dc p) e -> p dc e", p=128),
                           allow_slow_non_contiguous=True).then_inc(*sd[1].tick(16))
            sync.dma_start(self_sel[:, :], p.inp("selmat")[:, :]).then_inc(*sd[2].tick(16))

        @blk.vector
        def _(vector):
            W(vector, sd[0], 16)
            vector.tensor_copy(p.t["ggb"][:, :, :], gf[:, :, :])
            W(vector, sd[1], 16)
            vector.tensor_copy(p.t["seb"][:, :, :], sef[:, :, :])
            W(vector, sd[2], 16)
            vector.drain()
            inst = vector.tensor_copy(p.t["selb"][:, :], self_sel[:, :])
            p.mark("gconst", inst, 1)


def ph_ffn_gate(p, tag):
    p.phase_begin()
    nc = p.nc
    xn = p.t["xn"]
    comb_bc = p.arena("cbc" + tag, [128, NEXP, TPC], BF16, "BIG", 0)
    p.t["comb_bc"] = comb_bc
    sg = p.t["sg"]
    ggb = p.t["ggb"]
    seb = p.t["seb"]
    selb = p.t["selb"]
    comb_tm = p.arena("ctm" + tag, [128, NFC, 8], F32, "SCR", 0)
    comb_T = p.arena("cT" + tag, [8, NFC, 128], BF16, "SCR", 8192)
    row8 = p.arena("r8" + tag, [128, 8, 8], F32, "SCR", 12288)
    s_mm = p.new_sem(tag + "mm")
    s_cp = p.new_sem(tag + "cp")
    s_d = p.new_sem(tag + "d")
    s_w1 = p.new_sem(tag + "w1")
    s_cb = p.new_sem(tag + "cb")
    s_tr = p.new_sem(tag + "tr")
    s_tc = p.new_sem(tag + "tc")
    s_bc = p.new_sem(tag + "bc")
    s_bv = p.new_sem(tag + "bv")
    s_sgm = p.new_sem(tag + "sgm")
    s_sgs = p.new_sem(tag + "sgs")
    with (
        nc.psum_tensor(tag + "sc0", [128, 8], F32) as sc0,
        nc.psum_tensor(tag + "sc1", [128, 8], F32) as sc1,
        nc.psum_tensor(tag + "tr0", [8, 128], F32) as tr0,
        nc.psum_tensor(tag + "tr1", [8, 128], F32) as tr1,
        nc.psum_tensor(tag + "bc0", [128, 128], F32) as bc0,
        nc.psum_tensor(tag + "bc1", [128, 128], F32) as bc1,
        nc.psum_tensor(tag + "sgp", [1, TC], F32) as sgp,
        nc.Block() as blk,
    ):
        @blk.tensor
        def _(tensor):
            p.wait_mark(tensor, "xn")
            p.wait_mark(tensor, "gconst")
            for tt in range(NFC):
                if tt >= 2:
                    W(tensor, s_cp, tt - 1)
                sc = sc0 if tt % 2 == 0 else sc1
                inst = None
                for dc in range(NFC):
                    inst = tensor.matmul(sc[:, :], xn[:, dc, tt * 128:(tt + 1) * 128],
                                         ggb[:, dc, :], start=(dc == 0),
                                         stop=(dc == NFC - 1))
                inst.then_inc(*s_mm.tick(1))
            for tc in range(NTC):
                if tc >= 1:
                    W(tensor, s_sgs, tc)
                for dc in range(NFC):
                    inst = tensor.matmul(sgp[:, :], seb[:, dc, :],
                                         xn[:, dc, tc * TC:(tc + 1) * TC],
                                         start=(dc == 0), stop=(dc == NFC - 1))
                inst.then_inc(*s_sgm.tick(1))
            for tt in range(NFC):
                W(tensor, s_cb, tt + 1)
                if tt >= 2:
                    W(tensor, s_tc, tt - 1)
                tr = tr0 if tt % 2 == 0 else tr1
                tensor.transpose(tr[:, :], comb_tm[:, tt, :], p.t["identf"][:, :]).then_inc(*s_tr.tick(1))
            for e in range(NEXP):
                for tt in range(NFC):
                    i = e * NFC + tt
                    W(tensor, s_tc, NFC)
                    if i >= 2:
                        W(tensor, s_bv, i - 1)
                    bc = bc0 if i % 2 == 0 else bc1
                    tensor.matmul(bc[:, :], selb[:, e * 128:(e + 1) * 128],
                                  comb_T[:, tt, :], start=True, stop=True).then_inc(*s_bc.tick(1))

        @blk.vector
        def _(vector):
            for tt in range(NFC):
                W(vector, s_mm, tt + 1)
                sc = sc0 if tt % 2 == 0 else sc1
                s8 = row8[:, 0, :]
                m1 = row8[:, 1, 0:1]
                eq1 = row8[:, 2, :]
                s2 = row8[:, 3, :]
                m2 = row8[:, 4, 0:1]
                eq2 = row8[:, 5, :]
                dd = row8[:, 6, 0:1]
                w2 = row8[:, 7, 0:1]
                vector.tensor_copy(s8, sc[:, :]).then_inc(*s_cp.tick(1))
                vector.drain()
                vector.tensor_reduce(m1, s8, AX.X, ALU.max)
                vector.drain()
                vector.tensor_scalar(eq1, s8, m1, None, ALU.is_ge)
                vector.drain()
                vector.scalar_tensor_tensor(s2, eq1, NEG, s8, ALU.mult, ALU.add)
                vector.drain()
                vector.tensor_reduce(m2, s2, AX.X, ALU.max)
                vector.drain()
                vector.tensor_scalar(eq2, s2, m2, None, ALU.is_ge)
                vector.drain()
                vector.tensor_sub(dd, m1, m2).then_inc(*s_d.tick(1))
                W(vector, s_w1, tt + 1)
                vector.tensor_scalar(w2, row8[:, 1, 1:2], -1.0, 1.0, ALU.mult, ALU.add)
                vector.drain()
                vector.tensor_scalar(eq1, eq1, row8[:, 1, 1:2], None, ALU.mult)
                vector.drain()
                vector.scalar_tensor_tensor(comb_tm[:, tt, :], eq2, w2, eq1,
                                            ALU.mult, ALU.add).then_inc(*s_cb.tick(1))
                vector.drain()
            for tt in range(NFC):
                W(vector, s_tr, tt + 1)
                tr = tr0 if tt % 2 == 0 else tr1
                vector.tensor_copy(comb_T[:, tt, :], tr[:, :]).then_inc(*s_tc.tick(1))
            for e in range(NEXP):
                for tt in range(NFC):
                    i = e * NFC + tt
                    W(vector, s_bc, i + 1)
                    bc = bc0 if i % 2 == 0 else bc1
                    vector.tensor_copy(comb_bc[:, e, tt * 128:(tt + 1) * 128],
                                       bc[:, :]).then_inc(*s_bv.tick(1))
            vector.drain()
            inst = vector.tensor_copy(row8[:, 0, 0:1], row8[:, 0, 0:1])
            p.mark(tag + "_comb", inst, 1)

        @blk.scalar
        def _(scalar):
            for tt in range(NFC):
                W(scalar, s_d, tt + 1)
                scalar.activation(row8[:, 1, 1:2], row8[:, 6, 0:1], AF.Sigmoid).then_inc(*s_w1.tick(1))
            for tc in range(NTC):
                W(scalar, s_sgm, tc + 1)
                scalar.activation(sg[:, tc * TC:(tc + 1) * TC], sgp[:, :],
                                  AF.Sigmoid).then_inc(*s_sgs.tick(1))
            p.mark(tag + "_sg", scalar.nop(), 1)


def ph_ffn_hidden(p, tag, wgsrc, wusrc, n_hc, row0, comb_e, cache, crow0, comb_mark=None):
    p.phase_begin()
    nc = p.nc
    xn = p.t["xn"]
    wrb = p.arena("fhw" + tag, [128, 2, NFC, 128], BF16, "SCR", 0)
    hgs = p.arena("fhg" + tag, [128, TPC], BF16, "SCR", 8192)
    t1 = p.arena("fht" + tag, [128, 2, TPC], BF16, "SCR", 12288)
    wrf = p.arena("fhwf" + tag, [128, 2, NFC, 128], F32, "SCR", 20480)
    sw = p.slots(tag + "w", 2)
    shd = p.slots(tag + "hd", 2, "g")
    swc = p.new_sem(tag + "wc")
    s_g = p.new_sem(tag + "g")
    s_u = p.new_sem(tag + "u")
    s_sl = p.new_sem(tag + "sl")
    s_t = p.new_sem(tag + "t")
    with (
        nc.psum_tensor("fhpg" + tag, [128, TPC], F32) as pg,
        nc.psum_tensor("fhpu" + tag, [128, TPC], F32) as pu,
        nc.Block() as blk,
    ):
        @blk.sync
        def _(sync):
            for hc in range(n_hc):
                for half, src in ((0, wgsrc), (1, wusrc)):
                    i = 2 * hc + half
                    if i >= 2:
                        W(sync, swc, i - 1)
                    sync.dma_start(
                        wrf[:, half, :, :],
                        src[row0:row0 + D, hc * 128:(hc + 1) * 128]
                        .rearrange("(dc p) o -> p dc o", p=128),
                    ).then_inc(*sw[half].tick(16))

        @blk.tensor
        def _(tensor):
            p.wait_mark(tensor, "xn")
            for hc in range(n_hc):
                W(tensor, swc, 2 * hc + 1)
                if hc >= 1:
                    W(tensor, s_sl, hc)
                inst = None
                for dc in range(NFC):
                    for tc in range(NTC):
                        inst = tensor.matmul(pg[:, tc * TC:(tc + 1) * TC],
                                             wrb[:, 0, dc, :],
                                             xn[:, dc, tc * TC:(tc + 1) * TC],
                                             start=(dc == 0), stop=(dc == NFC - 1))
                inst.then_inc(*s_g.tick(1))
                W(tensor, swc, 2 * hc + 2)
                if hc >= 1:
                    W(tensor, s_t, hc)
                for dc in range(NFC):
                    for tc in range(NTC):
                        inst = tensor.matmul(pu[:, tc * TC:(tc + 1) * TC],
                                             wrb[:, 1, dc, :],
                                             xn[:, dc, tc * TC:(tc + 1) * TC],
                                             start=(dc == 0), stop=(dc == NFC - 1))
                inst.then_inc(*s_u.tick(1))

        @blk.scalar
        def _(scalar):
            for hc in range(n_hc):
                W(scalar, s_g, hc + 1)
                if hc >= 1:
                    W(scalar, s_t, hc)
                scalar.activation(hgs[:, :], pg[:, :], AF.Silu).then_inc(*s_sl.tick(1))

        @blk.vector
        def _(vector):
            if comb_mark is not None:
                p.wait_mark(vector, comb_mark)
            for hc in range(n_hc):
                for half in (0, 1):
                    i = 2 * hc + half
                    W(vector, sw[half], 16 * (hc + 1))
                    if i >= 4:
                        W(vector, s_g if half == 0 else s_u, hc - 1)
                    vector.tensor_copy(wrb[:, half, :, :],
                                       wrf[:, half, :, :]).then_inc(*swc.tick(1))
                W(vector, s_u, hc + 1)
                W(vector, s_sl, hc + 1)
                if hc >= 2:
                    W(vector, shd[hc % 2], 16 * (hc // 2))
                vector.tensor_mul(t1[:, hc % 2, :], pu[:, :], hgs[:, :])
                vector.drain()
                if comb_e is not None:
                    vector.tensor_mul(t1[:, hc % 2, :], t1[:, hc % 2, :], comb_e)
                    vector.drain()
                vector.tensor_copy(t1[:, hc % 2, 0:1],
                                   t1[:, hc % 2, 0:1]).then_inc(*s_t.tick(1))

        @blk.gpsimd
        def _(g):
            for hc in range(n_hc):
                W(g, s_t, hc + 1)
                g.dma_start(cache[crow0 + hc * 128:crow0 + (hc + 1) * 128, :],
                            t1[:, hc % 2, :]).then_inc(*shd[hc % 2].tick(16))
            for k in range(2):
                W(g, shd[k], 16 * ((n_hc - k + 1) // 2))
            p.mark("hc_" + tag, g.nop(), 1)


def ph_ffn_down(p, tag, xprev, xnext, ready_mark, hmark, shmark, sgmark,
                with_stats=True):
    p.phase_begin()
    nc = p.nc
    hsl = p.arena("fd_h" + tag, [128, NEXP, 8, TC], BF16, "BIG", 0)
    shsl = p.arena("fd_s" + tag, [128, NFC, TC], BF16, "XN", 0)
    wdr = p.arena("fd_w" + tag, [128, 2, NEXP, 8, 128], BF16, "XN", 16384)
    sdr = p.arena("fd_sw" + tag, [128, 2, NFC, 128], BF16, "XN", 49152)
    sgb = p.arena("fd_sg" + tag, [128, TC], F32, "SCR", 16384)
    xpr = p.arena("fd_xp" + tag, [128, 4, TC], F32, "SCR", 20480)
    xnw = p.arena("fd_xn" + tag, [128, 2, TC], F32, "SCR", 28672)
    tsh = p.arena("fd_t" + tag, [128, 2, TC], F32, "SCR", 32768)
    sg = p.t["sg"]
    wdb, shdb = p.t["wdb"], p.t["shdb"]
    sh_sl = p.slots(tag + "hs", 2, "g")
    ss_sl = p.slots(tag + "ss", 2, "g")
    sw_sl = p.slots(tag + "ws", 2)
    sd_sl = p.slots(tag + "ds", 2)
    sxp = p.slots(tag + "xp", 4)
    sxn = p.slots(tag + "xn", 2, "g")
    s_rm = p.new_sem(tag + "rm")
    s_shm = p.new_sem(tag + "shm")
    s_ev = p.new_sem(tag + "ev")
    s_sgb = p.new_sem(tag + "sgb")
    s_sgc = p.new_sem(tag + "sgc")
    s_sq = p.new_sem(tag + "sq")
    s_sm = p.new_sem(tag + "sm")
    s_fv = p.new_sem(tag + "fv")
    s_fs = p.new_sem(tag + "fs")
    sq4 = p.arena("dsq" + tag, [128, 2, TC], F32, "SCR", 4096)
    sel4f = p.t["sel4fs"]
    with (
        nc.psum_tensor("fdr0" + tag, [128, TC], F32) as pr0,
        nc.psum_tensor("fdr1" + tag, [128, TC], F32) as pr1,
        nc.psum_tensor("fds0" + tag, [128, TC], F32) as ps0,
        nc.psum_tensor("fds1" + tag, [128, TC], F32) as ps1,
        nc.psum_tensor("fdsg" + tag, [128, TC], F32) as psg,
        nc.psum_tensor("fdp4" + tag, [4, 2, TC], F32) as ps4,
        nc.Block() as blk,
    ):
        def stats_mm(tensor, i):
            tci = i // NFC
            W(tensor, s_ev, i + 1)
            tensor.matmul(ps4[:, 0, :], sel4f[:, tci * 4:(tci + 1) * 4],
                          xnw[:, i % 2, :], start=(i == 0), stop=(i == NFC * NTC - 1))
            W(tensor, s_sq, i + 1)
            tensor.matmul(ps4[:, 1, :], sel4f[:, tci * 4:(tci + 1) * 4],
                          sq4[:, i % 2, :], start=(i == 0),
                          stop=(i == NFC * NTC - 1)).then_inc(*s_sm.tick(1))
        @blk.gpsimd
        def _(g):
            p.wait_mark(g, hmark)
            p.wait_mark(g, shmark)
            def store_tile(i):
                W(g, s_ev, i + 1)
                tc_, oc_ = divmod(i, NFC)
                g.dma_start(xnext[oc_ * 128:(oc_ + 1) * 128, tc_ * TC:(tc_ + 1) * TC],
                            xnw[:, i % 2, :]).then_inc(*sxn[i % 2].tick(16))

            for tc in range(NTC):
                if tc >= 1:
                    for i in range((tc - 1) * NFC, tc * NFC):
                        store_tile(i)
                if tc >= 1:
                    W(g, s_rm, NFC * tc)
                    W(g, s_shm, NFC * tc)
                for e in range(NEXP):
                    g.dma_start(
                        hsl[:, e, :, :],
                        p.t["hcache"][e * HEXP:(e + 1) * HEXP, tc * TC:(tc + 1) * TC]
                        .rearrange("(hc pp) t -> pp hc t", pp=128),
                    ).then_inc(*sh_sl[tc % 2].tick(16))
                g.dma_start(
                    shsl[:, :, :],
                    p.t["shcache"][:, tc * TC:(tc + 1) * TC]
                    .rearrange("(hc pp) t -> pp hc t", pp=128),
                ).then_inc(*ss_sl[tc % 2].tick(16))
            for i in range((NTC - 1) * NFC, NTC * NFC):
                store_tile(i)
            for k in range(2):
                W(g, sxn[k], 16 * (NTC * NFC // 2))
            p.mark(ready_mark, g.nop(), 1)

        @blk.sync
        def _(sync):
            p.wait_mark(sync, "wcast")
            for tc in range(NTC):
                for oc in range(NFC):
                    i = tc * NFC + oc
                    if i >= 2:
                        W(sync, s_rm, i - 1)
                    for e in range(NEXP):
                        sync.dma_start(
                            wdr[:, i % 2, e, :, :],
                            wdb[e * HEXP:(e + 1) * HEXP, oc * 128:(oc + 1) * 128]
                            .rearrange("(hc pp) o -> pp hc o", pp=128),
                        ).then_inc(*sw_sl[i % 2].tick(16))
                    if i >= 2:
                        W(sync, s_shm, i - 1)
                    sync.dma_start(
                        sdr[:, i % 2, :, :],
                        shdb[:, oc * 128:(oc + 1) * 128]
                        .rearrange("(hc pp) o -> pp hc o", pp=128),
                    ).then_inc(*sd_sl[i % 2].tick(16))
                    if i >= 4:
                        W(sync, s_ev, i - 3)
                    sync.dma_start(xpr[:, i % 4, :],
                                   xprev[oc * 128:(oc + 1) * 128, tc * TC:(tc + 1) * TC]
                                   ).then_inc(*sxp[i % 4].tick(16))

        @blk.tensor
        def _(tensor):
            p.wait_mark(tensor, sgmark)
            for tc in range(NTC):
                if tc >= 1:
                    W(tensor, s_sgc, tc)
                tensor.matmul(psg[:, :], p.t["onesr"][:, :], sg[:, tc * TC:(tc + 1) * TC],
                              start=True, stop=True).then_inc(*s_sgb.tick(1))
                W(tensor, sh_sl[tc % 2], 128 * (tc // 2 + 1))
                W(tensor, ss_sl[tc % 2], 16 * (tc // 2 + 1))
                for oc in range(NFC):
                    i = tc * NFC + oc
                    if with_stats and i >= 2:
                        stats_mm(tensor, i - 2)
                    W(tensor, sw_sl[i % 2], 128 * (i // 2 + 1))
                    if i >= 2:
                        W(tensor, s_ev, i - 1)
                    pr = pr0 if i % 2 == 0 else pr1
                    ps = ps0 if i % 2 == 0 else ps1
                    inst = None
                    for e in range(NEXP):
                        for hc in range(8):
                            inst = tensor.matmul(pr[:, :], wdr[:, i % 2, e, hc, :],
                                                 hsl[:, e, hc, :],
                                                 start=(e == 0 and hc == 0),
                                                 stop=(e == NEXP - 1 and hc == 7))
                    inst.then_inc(*s_rm.tick(1))
                    W(tensor, sd_sl[i % 2], 16 * (i // 2 + 1))
                    for hc in range(NFC):
                        inst = tensor.matmul(ps[:, :], sdr[:, i % 2, hc, :],
                                             shsl[:, hc, :],
                                             start=(hc == 0), stop=(hc == NFC - 1))
                    inst.then_inc(*s_shm.tick(1))
            if with_stats:
                stats_mm(tensor, NFC * NTC - 2)
                stats_mm(tensor, NFC * NTC - 1)

        @blk.scalar
        def _(scalar):
            if with_stats:
                for i in range(NFC * NTC):
                    W(scalar, s_ev, i + 1)
                    if i >= 2:
                        W(scalar, s_sm, i - 1)
                    scalar.activation(sq4[:, i % 2, :], xnw[:, i % 2, :],
                                      AF.Square).then_inc(*s_sq.tick(1))
                _stats_fin_s(p, scalar, s_fv, s_fs)

        @blk.vector
        def _(vector):
            for tc in range(NTC):
                W(vector, s_sgb, tc + 1)
                vector.tensor_copy(sgb[:, :], psg[:, :]).then_inc(*s_sgc.tick(1))
                vector.drain()
                for oc in range(NFC):
                    i = tc * NFC + oc
                    W(vector, s_shm, i + 1)
                    if "nosgate" in p.dbg:
                        vector.tensor_copy(tsh[:, i % 2, :], (ps0 if i % 2 == 0 else ps1)[:, :])
                    else:
                        vector.tensor_mul(tsh[:, i % 2, :], (ps0 if i % 2 == 0 else ps1)[:, :],
                                          sgb[:, :])
                    if "noshared" in p.dbg:
                        vector.drain()
                        vector.memset(tsh[:, i % 2, :], 0.0)
                    vector.drain()
                    W(vector, s_rm, i + 1)
                    if "norouted" not in p.dbg:
                        vector.tensor_add(tsh[:, i % 2, :], (pr0 if i % 2 == 0 else pr1)[:, :],
                                          tsh[:, i % 2, :])
                        vector.drain()
                    W(vector, sxp[i % 4], 16 * (i // 4 + 1))
                    if i >= 2:
                        W(vector, sxn[i % 2], 16 * (i // 2))
                        if with_stats:
                            W(vector, s_sm, i - 1)
                    vector.tensor_add(xnw[:, i % 2, :], tsh[:, i % 2, :],
                                      xpr[:, i % 4, :]).then_inc(*s_ev.tick(1))
            if with_stats:
                W(vector, s_sm, NFC * NTC)
                _stats_fin_v1(p, vector, ps4, s_fv)
                _stats_fin_v2(p, vector, s_fs)


def ph_attn_out(p, tag, wt, xprev, xnext, ready_mark, with_stats=True):
    p.phase_begin()
    nc = p.nc
    ctxsb = p.arena("cxs" + tag, [128, NFC, TPC], BF16, "BIG")
    wrf = p.arena("owf" + tag, [128, 2, NFC, 128], F32, "SCR", 16384)
    wrb = p.arena("owb" + tag, [128, 2, NFC, 128], BF16, "SCR", 32768)
    xpr = p.arena("oxp" + tag, [128, 4, TC], F32, "SCR", 40960)
    xnw = p.arena("oxn" + tag, [128, 2, TC], F32, "SCR", 49152)
    ctxd = p.t["ctxd"]
    sw = p.slots(tag + "w", 2)
    scx = p.slots(tag + "cx", 2, "g")
    sxp = p.slots(tag + "xp", 4)
    sxn = p.slots(tag + "xn", 2, "g")
    swc = p.new_sem(tag + "wc")
    smm = p.new_sem(tag + "mm")
    sev = p.new_sem(tag + "ev")
    s_sq = p.new_sem(tag + "sq")
    s_sm = p.new_sem(tag + "sm")
    s_fv = p.new_sem(tag + "fv")
    s_fs = p.new_sem(tag + "fs")
    sq4 = p.arena("osq" + tag, [128, 2, TC], F32, "SCR", 53248)
    sel4f = p.t["sel4fs"]

    with (
        nc.psum_tensor("po" + tag, [128, TPC], F32) as pp,
        nc.psum_tensor("ps4" + tag, [4, 2, TC], F32) as ps4,
        nc.Block() as blk,
    ):
        def stats_mm(tensor, i):
            tc = i % NTC
            W(tensor, sev, i + 1)
            tensor.matmul(ps4[:, 0, :], sel4f[:, tc * 4:(tc + 1) * 4],
                          xnw[:, i % 2, :], start=(i == 0), stop=(i == NFC * NTC - 1))
            W(tensor, s_sq, i + 1)
            tensor.matmul(ps4[:, 1, :], sel4f[:, tc * 4:(tc + 1) * 4],
                          sq4[:, i % 2, :], start=(i == 0),
                          stop=(i == NFC * NTC - 1)).then_inc(*s_sm.tick(1))
        @blk.gpsimd
        def _(g):
            for fc in range(NFC):
                if fc >= 2:
                    W(g, scx[fc % 2], 16 * (fc // 2))
                g.dma_start(ctxsb[:, fc, :], ctxd[fc * 128:(fc + 1) * 128, :]
                            ).then_inc(*scx[fc % 2].tick(16))
            for i in range(NFC * NTC):
                W(g, sev, i + 1)
                oc, tc = divmod(i, NTC)
                g.dma_start(xnext[oc * 128:(oc + 1) * 128, tc * TC:(tc + 1) * TC],
                            xnw[:, i % 2, :]).then_inc(*sxn[i % 2].tick(16))
            for k in range(2):
                W(g, sxn[k], 16 * (NFC * NTC // 2))
            inst = g.nop()
            p.mark(ready_mark, inst, 1)

        @blk.sync
        def _(sync):
            for oc in range(NFC):
                if oc >= 2:
                    W(sync, swc, oc - 1)
                sync.dma_start(
                    wrf[:, oc % 2, :, :],
                    wt[:, oc * 128:(oc + 1) * 128].rearrange("(dc p) o -> p dc o", p=128),
                ).then_inc(*sw[oc % 2].tick(16))
                for tc in range(NTC):
                    i = oc * NTC + tc
                    if i >= 4:
                        W(sync, sev, i - 3)
                    sync.dma_start(xpr[:, i % 4, :],
                                   xprev[oc * 128:(oc + 1) * 128, tc * TC:(tc + 1) * TC]
                                   ).then_inc(*sxp[i % 4].tick(16))
            for k in range(2):
                W(sync, sw[k], 16 * (NFC // 2))

        @blk.scalar
        def _(scalar):
            if with_stats:
                for i in range(NFC * NTC):
                    W(scalar, sev, i + 1)
                    if i >= 2:
                        W(scalar, s_sm, i - 1)
                    scalar.activation(sq4[:, i % 2, :], xnw[:, i % 2, :],
                                      AF.Square).then_inc(*s_sq.tick(1))
                _stats_fin_s(p, scalar, s_fv, s_fs)

        @blk.vector
        def _(vector):
            for oc in range(NFC):
                W(vector, sw[oc % 2], 16 * (oc // 2 + 1))
                if oc >= 2:
                    W(vector, smm, oc - 1)
                vector.tensor_copy(wrb[:, oc % 2, :, :], wrf[:, oc % 2, :, :]).then_inc(*swc.tick(1))
                W(vector, smm, oc + 1)
                for tc in range(NTC):
                    i = oc * NTC + tc
                    W(vector, sxp[i % 4], 16 * (i // 4 + 1))
                    if i >= 2:
                        W(vector, sxn[i % 2], 16 * (i // 2))
                        if with_stats:
                            W(vector, s_sm, i - 1)
                    vector.tensor_add(xnw[:, i % 2, :], pp[:, tc * TC:(tc + 1) * TC],
                                      xpr[:, i % 4, :]).then_inc(*sev.tick(1))
            if with_stats:
                W(vector, s_sm, NFC * NTC)
                _stats_fin_v1(p, vector, ps4, s_fv)
                _stats_fin_v2(p, vector, s_fs)

        @blk.tensor
        def _(tensor):
            for oc in range(NFC):
                if with_stats and oc >= 1:
                    lo = 0 if oc == 1 else 4 * oc - 5
                    for i in range(lo, 4 * oc - 1):
                        stats_mm(tensor, i)
                W(tensor, swc, oc + 1)
                if oc == 0:
                    for k in range(2):
                        W(tensor, scx[k], 16 * (NFC // 2))
                if oc >= 1:
                    W(tensor, sev, 4 * oc)
                inst = None
                for dc in range(NFC):
                    for tc in range(NTC):
                        inst = tensor.matmul(pp[:, tc * TC:(tc + 1) * TC],
                                             wrb[:, oc % 2, dc, :],
                                             ctxsb[:, dc, tc * TC:(tc + 1) * TC],
                                             start=(dc == 0), stop=(dc == NFC - 1))
                inst.then_inc(*smm.tick(1))
            if with_stats:
                for i in range(4 * (NFC - 1) - 1, 4 * NFC):
                    stats_mm(tensor, i)



def _sub_ffn(p, tag, xprev, xnext, prev_mark, ready_mark, first,
             last=False):
    u = p.upto
    ph_xn(p, xprev, tag + "xn", wait_marks=(prev_mark,))
    if first:
        ph_gconst(p)
        ph_wcast(p)
    if u < 2.4:
        return False
    ph_ffn_gate(p, tag + "g")
    if "comb" in p.dbg:
        _dbg_dump_fm(p, "comb_bc", "dbg_comb", nfc=NEXP, width=TPC,
                     marks=(tag + "g_comb",))
    if u < 2.6:
        return False
    nexp = 2 if "emini" in p.dbg else NEXP
    wgf = p.inp("wg_t").ap().rearrange("e d h -> (e d) h")
    wuf = p.inp("wu_t").ap().rearrange("e d h -> (e d) h")
    for e in range(nexp):
        ph_ffn_hidden(p, tag + "h%d" % e, wgf, wuf, 8, e * D,
                      p.t["comb_bc"][:, e, :],
                      p.t["hcache"], e * HEXP, comb_mark=tag + "g_comb")
    if "hc" in p.dbg:
        _dbg_dram(p, "hcache", "dbg_hc", [NEXP * HEXP, TPC], BF16,
                  n_rows=nexp * HEXP)
    if u < 2.8:
        return False
    ph_ffn_hidden(p, tag + "sh", p.inp("shg_t").ap(), p.inp("shu_t").ap(),
                  NFC, 0, None, p.t["shcache"], 0)
    if "shc" in p.dbg:
        _dbg_dram(p, "shcache", "dbg_shc", [HSH, TPC], BF16)
    if u < 3:
        return False
    ph_ffn_down(p, tag + "d", xprev, xnext, ready_mark,
                "hc_" + tag + "h%d" % (NEXP - 1), "hc_" + tag + "sh",
                tag + "g_sg", with_stats=not last)
    return True



def build_program(upto=99, dbg=()):
    nc = bass.Bass("TRN2", target_bir_lowering=False)
    p = Prog(nc, upto, dbg)

    x0 = nc.dram_tensor("x0", [D, TPC], F32, kind="ExternalInput")
    xout = nc.dram_tensor("xout", [D, TPC], F32, kind="ExternalOutput")
    p.t["x0"], p.t["xout"] = x0, xout
    p.used_inputs = {"x0"}
    _in_shapes = {"wq_t": [D, D], "wk_t": [D, D], "wv_t": [D, D],
                  "wo_t": [D, D], "bwq": [D, D], "bwkv": [D, 1024],
                  "bwd": [D, D], "ggate": [D, 8],
                  "wg_t": [NEXP, D, HEXP], "wu_t": [NEXP, D, HEXP],
                  "wd_t": [NEXP, HEXP, D], "shg_t": [D, HSH],
                  "shu_t": [D, HSH], "shd_t": [HSH, D], "se_t": [D, 1],
                  "tri01": [512, TC], "ident": [128, 128], "ind": [128, 64],
                  "selmat": [8, 1024], "bmask": [NTC * 16 * 128, TC],
                  "sel4f": [128, 16], "sel4t": [4, 512]}

    def inp(nm):
        if nm not in p.t:
            dt = BF16 if nm == "bmask" else F32
            p.t[nm] = nc.dram_tensor(nm, _in_shapes[nm], dt, kind="ExternalInput")
            p.used_inputs.add(nm)
        return p.t[nm]

    p.inp = inp

    p.t["xd1"] = nc.dram_tensor("xd1", [D, TPC], F32)
    p.t["xd2"] = nc.dram_tensor("xd2", [D, TPC], F32)
    p.t["kcc_in"] = nc.dram_tensor("kcc_in", [D, TPC], BF16)
    p.t["kcc_out"] = nc.dram_tensor("kcc_out", [4 * D, TPC], BF16)
    p.t["vcc_in"] = nc.dram_tensor("vcc_in", [TPC, D], BF16)
    p.t["vcc_out"] = nc.dram_tensor("vcc_out", [4 * TPC, D], BF16)
    p.t["nrma_in"] = nc.dram_tensor("nrma_in", [1, 16], F32)
    p.t["nrma_out"] = nc.dram_tensor("nrma_out", [1, 16], F32)
    p.t["nrmb_in"] = nc.dram_tensor("nrmb_in", [1, 4], F32)
    p.t["nrmb_out"] = nc.dram_tensor("nrmb_out", [1, 4], F32)
    p.t["kbcc_in"] = nc.dram_tensor("kbcc_in", [512, TPC], BF16)
    p.t["kbcc_out"] = nc.dram_tensor("kbcc_out", [2048, TPC], BF16)
    p.t["vbcc_in"] = nc.dram_tensor("vbcc_in", [TPC, 512], BF16)
    p.t["vbcc_out"] = nc.dram_tensor("vbcc_out", [4 * TPC, 512], BF16)
    p.t["ctxd"] = nc.dram_tensor("ctxd", [D, TPC], BF16)
    p.t["hcache"] = nc.dram_tensor("hcache", [NEXP * HEXP, TPC], BF16)
    p.t["shcache"] = nc.dram_tensor("shcache", [HSH, TPC], BF16)
    p.t["wdb"] = nc.dram_tensor("wdb", [NEXP * HEXP, D], BF16)
    p.t["shdb"] = nc.dram_tensor("shdb", [HSH, D], BF16)

    if "xn" in p.dbg:
        p.t["dbg_xn"] = nc.dram_tensor("dbg_xn", [D, TPC], F32, kind="ExternalOutput")

    p.t["xn"] = p.arena("xn", [128, NFC, TPC], BF16, "XN")
    p.t["stat"] = p.arena("stat", [4, 2, TC], F32, "SCR", 0)
    p.t["r4"] = p.arena("r4", [4, 4, TC], F32, "SCR", 8192)
    p.t["xring"] = p.arena("xring", [128, 4, TC], F32, "SCR", 16384)
    p.t["bcb"] = p.arena("bcb", [128, 2, TC], F32, "SCR", 24576)
    p.t["ones1f"] = nc.alloc_sbuf_tensor("ones1f", [128, 1], F32)
    p.t["ones1b"] = nc.alloc_sbuf_tensor("ones1b", [128, 1], BF16)
    p.t["onesr"] = nc.alloc_sbuf_tensor("onesr", [1, 128], F32)
    p.t["rowtmp"] = nc.alloc_sbuf_tensor("rowtmp", [1, 4, TC], F32)
    p.t["nrows"] = nc.alloc_sbuf_tensor("nrows", [1, 8, NFC], F32)
    p.t["negMa"] = nc.alloc_sbuf_tensor("negMa", [128, NFC], F32)
    p.t["negMb"] = nc.alloc_sbuf_tensor("negMb", [128, NFC], F32)
    p.t["identf"] = nc.alloc_sbuf_tensor("identf", [128, 128], F32)
    p.t["identb"] = nc.alloc_sbuf_tensor("identb", [128, 128], BF16)
    p.t["sel4fs"] = nc.alloc_sbuf_tensor("sel4fs", [128, 16], F32)
    p.t["sel4ts"] = nc.alloc_sbuf_tensor("sel4ts", [4, 512], F32)

    sid = p.new_sem("cinit", "s")
    with nc.Block() as blk:
        @blk.sync
        def _(sync):
            sync.dma_start(p.t["identf"][:, :], p.inp("ident")[:, :]).then_inc(*sid.tick(16))
            sync.dma_start(p.t["sel4fs"][:, :], p.inp("sel4f")[:, :]).then_inc(*sid.tick(16))
            sync.dma_start(p.t["sel4ts"][:, :], p.inp("sel4t")[:, :]).then_inc(*sid.tick(16))

        @blk.vector
        def _(vector):
            p.mark("const", vector.memset(p.t["ones1f"][:, :], 1.0), 1)
            p.mark("const", vector.memset(p.t["ones1b"][:, :], 1.0), 1)
            p.mark("const", vector.memset(p.t["onesr"][:, :], 1.0), 1)
            W(vector, sid, 48)
            p.mark("const", vector.tensor_copy(p.t["identb"][:, :], p.t["identf"][:, :]), 1)

    ph_stats(p, x0, "st0")
    ph_xn(p, x0, "xn0")

    if "xn" in p.dbg:
        _dbg_dump_fm(p, "xn", "dbg_xn")

    if upto >= 1:
        Q = p.arena("Q", [128, NFC, TPC], BF16, "BIG")
        p.t["Q"] = Q
    if upto >= 1.1:
        ph_proj(p, "wk", p.inp("wk_t"), "kf", NFC, dst=p.t["kcc_in"], norm_row=1)
        if "kin" in p.dbg:
            _dbg_dram(p, "kcc_in", "dbg_kin", [D, TPC], BF16)
    if upto >= 1.15:
        ph_proj(p, "wv", p.inp("wv_t"), "vt", NFC, dst=p.t["vcc_in"])
        if "vin" in p.dbg:
            _dbg_dram(p, "vcc_in", "dbg_vin", [TPC, D], BF16)
    if upto >= 1.2:
        ph_cc(p, "a", [("kcc_in", "kcc_out"), ("vcc_in", "vcc_out")],
              krow=1, grow=2, nin=p.t["nrma_in"], nout=p.t["nrma_out"])
    if upto >= 1:
        ph_proj(p, "wq", p.inp("wq_t"), "q", NFC, dst=Q, norm_row=0)
    if upto >= 1.5:
        ph_attn_a(p)
        if "ctx" in p.dbg:
            _dbg_dram(p, "ctxd", "dbg_ctx", [D, TPC], BF16,
                      n_rows=(256 if "mini" in p.dbg else None))
    cur_x, cur_mark = x0, None
    if upto >= 2:
        ph_attn_out(p, "wo", p.inp("wo_t"), x0, p.t["xd1"], "x1rdy",
                    with_stats="nostat" not in p.dbg)
        cur_x, cur_mark = p.t["xd1"], "x1rdy"
    if upto >= 2.2:
        done = _sub_ffn(p, "f1", p.t["xd1"], p.t["xd2"], "x1rdy", "x2rdy", first=True)
        if done:
            cur_x, cur_mark = p.t["xd2"], "x2rdy"
    if upto >= 3.2:
        ph_xn(p, p.t["xd2"], "bxn", wait_marks=("x2rdy",))
        ph_proj(p, "bk", p.inp("bwkv"), "kf", 4, dst=p.t["kbcc_in"], norm_row=4)
        ph_proj(p, "bv", p.inp("bwkv"), "vt", 4, dst=p.t["vbcc_in"], wt_col0=512)
        ph_cc(p, "b", [("kbcc_in", "kbcc_out"), ("vbcc_in", "vbcc_out")],
              krow=4, grow=5, nin=p.t["nrmb_in"], nout=p.t["nrmb_out"])
        ph_proj(p, "bq", p.inp("bwq"), "q", NFC, dst=p.t["Q"], norm_row=3)
        ph_attn_b(p)
        if "ctxb" in p.dbg:
            _dbg_dram(p, "ctxd", "dbg_ctxb", [D, TPC], BF16)
    if upto >= 4:
        ph_attn_out(p, "bo", p.inp("bwd"), p.t["xd2"], p.t["xd1"], "x3rdy")
        cur_x, cur_mark = p.t["xd1"], "x3rdy"
    if upto >= 5:
        done = _sub_ffn(p, "f2", p.t["xd1"], xout, "x3rdy", "x4rdy", first=False,
                        last=True)
        if done:
            cur_x, cur_mark = xout, "x4rdy"

    if "q" in p.dbg:
        _dbg_dump_fm(p, "Q", "dbg_q", marks=("wq_ev",))
    if "kv" in p.dbg:
        _dbg_dram(p, "kcc_out", "dbg_k", [4 * D, TPC], BF16)
        _dbg_dram(p, "vcc_out", "dbg_v", [4 * TPC, D], BF16)
    if "negm" in p.dbg:
        _dbg_rows(p, "negMa", "dbg_negm", NFC, marks=("negm_a",))

    if cur_x is not xout:
        sg = p.new_sem("out", "g")
        with nc.Block() as blk:
            @blk.gpsimd
            def _(g):
                if cur_mark is not None:
                    p.wait_mark(g, cur_mark)
                g.dma_start(p.t["xout"][:, :], cur_x[:, :]).then_inc(*sg.tick(16))
                W(g, sg, 16)

    p.ctx.close()
    nc.finalize()
    return nc, set(p.used_inputs)



def _run(inputs, upto=99, dbg=(), _nc_out=None):
    x = np.asarray(inputs["L0_attn_a_x"], dtype=np.float32)
    shards = _shard_x(x)
    shared = _prep_inputs(inputs)
    nc, used = build_program(upto=upto, dbg=dbg)
    if _nc_out is not None:
        _nc_out.append(nc)
    in_maps = []
    for core in range(NCORE):
        m = dict(shared)
        m.update(_host_tables(core))
        m["x0"] = shards[core]
        in_maps.append({k: v for k, v in m.items() if k in used})
    return run_bass_kernel_spmd(nc, in_maps, core_ids=list(range(NCORE)))


def kernel(**inputs):
    res = _run(inputs)
    outs = [res.results[c]["xout"] for c in range(NCORE)]
    return _unshard_x(outs)


# version 04
# speedup vs baseline: 1.0615x; 1.0178x over previous
import os
import sys
import numpy as np

sys.path.insert(0, "/opt/trn_rl_repo")

import concourse.bass as bass
import concourse.mybir as mybir
from concourse.bass_utils import run_bass_kernel_spmd

F32 = mybir.dt.float32
BF16 = mybir.dt.bfloat16
I32 = mybir.dt.int32
AF = mybir.ActivationFunctionType
ALU = mybir.AluOpType
AX = mybir.AxisListType

B = 2
S = 8192
D = 2048
NCORE = 8
TPC = 2048
NTC = 4
TC = 512
NFC = 16
HEADS = 16
HD = 128
GROUPS = 4
NEXP = 8
HEXP = 1024
HSH = 2048
EPS = 1e-5
SM_SCALE = float(HD) ** -0.5
NEG = -1.0e9


def _gchunk(c):
    return [4 * j + c for j in range(NTC)]


def _phys_chunk(w):
    g = w // 4
    return (g % 4) * 16 + (g // 4) * 4 + (w % 4)



def _shard_x(x):
    shards = []
    for core in range(NCORE):
        s, c = divmod(core, 4)
        toks = np.concatenate([x[s, g * TC:(g + 1) * TC, :] for g in _gchunk(c)], axis=0)
        shards.append(np.ascontiguousarray(toks.T))
    return shards


def _unshard_x(shards):
    out = np.empty((B, S, D), dtype=np.float32)
    for core in range(NCORE):
        s, c = divmod(core, 4)
        tok = np.ascontiguousarray(np.asarray(shards[core]).T)
        for j, g in enumerate(_gchunk(c)):
            out[s, g * TC:(g + 1) * TC, :] = tok[j * TC:(j + 1) * TC, :]
    return out


def _host_consts():
    tri = np.zeros((4, 128, TC), dtype=np.float32)
    ql = np.arange(TC)[None, :]
    kl = np.arange(128)[:, None]
    for t in range(4):
        tri[t] = (ql >= 128 * t + kl).astype(np.float32)
    sel4f = np.zeros((128, 16), dtype=np.float32)
    sel4t = np.zeros((4, 512), dtype=np.float32)
    for tc in range(4):
        sel4f[:, tc * 4 + tc] = 1.0
        sel4t[tc, tc * 128:(tc + 1) * 128] = 1.0
    return {"tri01": tri.reshape(512, TC), "ident": np.eye(128, dtype=np.float32),
            "sel4f": sel4f, "sel4t": sel4t}


def _host_tables(core):
    import ml_dtypes
    s, c = divmod(core, 4)
    ind = np.zeros((128, 64), dtype=np.float32)
    tri = np.zeros((4, 128, TC), dtype=np.float32)
    ql = np.arange(TC)[None, :]
    kl = np.arange(128)[:, None]
    for t in range(4):
        tri[t] = (ql >= 128 * t + kl).astype(np.float32)
    bmask = np.zeros((NTC, 16, 128, TC), dtype=np.float32)
    for j in range(NTC):
        for u in range(16):
            col = j * 16 + u
            if u < 4 * c:
                bmask[j, u] = 1.0
            elif u < 4 * c + 4:
                bmask[j, u] = tri[u - 4 * c]
            else:
                ind[:, col] = NEG
    bmask = bmask.reshape(NTC * 16 * 128, TC).astype(ml_dtypes.bfloat16)
    return {"ind": ind, "bmask": bmask}


def _prep_inputs(inputs):
    a = {k: np.asarray(v) for k, v in inputs.items()}
    T = lambda m: np.ascontiguousarray(np.asarray(m).T)
    shared = {
        "wq_t": T(a["L0_attn_a_Wq"]),
        "wk_t": T(a["L0_attn_a_Wk"]),
        "wv_t": T(a["L0_attn_a_Wv"]),
        "wo_t": T(a["L0_attn_a_Wo"]),
        "bwq": np.ascontiguousarray(a["L0_attn_b_w_q"]),
        "bwkv": np.ascontiguousarray(a["L0_attn_b_w_kv"]),
        "bwd": np.ascontiguousarray(a["L0_attn_b_w_dense"]),
        "ggate": T(a["L0_ffn_gate_w"]),
        "wg_t": np.ascontiguousarray(np.transpose(a["L0_ffn_w_gate"], (0, 2, 1))),
        "wu_t": np.ascontiguousarray(np.transpose(a["L0_ffn_w_up"], (0, 2, 1))),
        "wd_t": np.ascontiguousarray(np.transpose(a["L0_ffn_w_down"], (0, 2, 1))),
        "shg_t": T(a["L0_ffn_sh_gate"]),
        "shu_t": T(a["L0_ffn_sh_up"]),
        "shd_t": T(a["L0_ffn_sh_down"]),
        "se_t": T(a["L0_ffn_se_gate_w"]),
    }
    sel = np.zeros((8, 8 * 128), dtype=np.float32)
    for e in range(8):
        sel[e, e * 128:(e + 1) * 128] = 1.0
    shared["selmat"] = sel
    shared.update(_host_consts())
    return shared



class RSem:

    def __init__(self, h, b):
        self.h = h
        self.b = b
        self.t = 0

    def tick(self, n):
        self.t += n
        return (self.h, n)


def W(eng, s, v):
    if isinstance(s, RSem):
        eng.wait_ge(s.h, s.b + v)
    else:
        eng.wait_ge(s, v)


class Prog:
    def __init__(self, nc, upto, dbg):
        import contextlib
        self.nc = nc
        self.upto = upto
        self.dbg = set(dbg)
        self.ctx = contextlib.ExitStack()
        self.t = {}
        self._marks = {}
        self._nsem = 0
        self._pool = {}
        self._phase = 0
        self._rentals = {}

    def phase_begin(self):
        self._phase += 1
        for ph in list(self._rentals):
            if ph <= self._phase - 2:
                for klass, rs in self._rentals.pop(ph):
                    self._pool.setdefault(klass, []).append((rs.h, rs.b + rs.t))

    def _raw_sem(self, name):
        self._nsem += 1
        return self.ctx.enter_context(self.nc.semaphore(f"{name}_{self._nsem}"))

    def new_sem(self, name, klass="c"):
        pool = self._pool.setdefault(klass, [])
        if "nopool" in self.dbg:
            pool = []
        if pool:
            h, b = pool.pop()
            rs = RSem(h, b)
        else:
            rs = RSem(self._raw_sem(klass + name), 0)
        self._rentals.setdefault(self._phase, []).append((klass, rs))
        return rs

    def mark(self, name, inst, n=1):
        if name not in self._marks or self._marks[name][2]:
            self._marks[name] = [self._raw_sem("mk_" + name), 0, False]
        m = self._marks[name]
        inst.then_inc(m[0], n)
        m[1] += n

    def alias_mark(self, name, sem, total):
        if isinstance(sem, RSem):
            self._marks[name] = [sem.h, sem.b + total, True]
        else:
            self._marks[name] = [sem, total, True]

    def wait_mark(self, eng, name):
        m = self._marks[name]
        eng.wait_ge(m[0], m[1])

    def slots(self, name, n, klass="s"):
        return [self.new_sem(f"{name}{k}", klass) for k in range(n)]

    def arena(self, name, shape, dtype, region, off=0):
        base = {"XN": 159744, "BIG": 94208, "SCR": 36864}[region]
        return self.nc.alloc_sbuf_tensor_at(name, shape, dtype, offset=base + off)



def _stats_fin_v1(p, vector, ps4, sv):
    r4 = p.t["r4"]
    mean, ex2, var = r4[:, 0, :], r4[:, 1, :], r4[:, 2, :]
    vector.tensor_scalar(mean, ps4[:, 0, :], 1.0 / D, None, ALU.mult)
    vector.tensor_scalar(ex2, ps4[:, 1, :], 1.0 / D, None, ALU.mult)
    vector.drain()
    vector.tensor_mul(var, mean, mean)
    vector.drain()
    vector.scalar_tensor_tensor(var, ex2, EPS, var, ALU.add,
                                ALU.subtract).then_inc(*sv.tick(1))


def _stats_fin_s(p, scalar, sv, ss):
    r4 = p.t["r4"]
    stat = p.t["stat"]
    W(scalar, sv, 1)
    scalar.drain()
    scalar.activation(r4[:, 3, :], r4[:, 2, :], AF.Ln)
    scalar.drain()
    scalar.activation(stat[:, 1, :], r4[:, 3, :], AF.Exp,
                      scale=-0.5).then_inc(*ss.tick(1))


def _stats_fin_v2(p, vector, ss):
    r4 = p.t["r4"]
    stat = p.t["stat"]
    W(vector, ss, 1)
    inst = vector.tensor_mul(stat[:, 0, :], r4[:, 0, :], stat[:, 1, :])
    p.mark("stat", inst, 1)
    vector.drain()


def ph_stats(p, xsrc, tag, wait_marks=()):
    p.phase_begin()
    nc = p.nc
    xr = p.t["xring"]
    sel4f = p.t["sel4fs"]
    sx = p.slots(tag + "x", 4)
    sq_done = p.new_sem(tag + "q")
    mm = p.new_sem(tag + "m")
    sv = p.new_sem(tag + "v")
    ss = p.new_sem(tag + "s")
    with (
        nc.psum_tensor("ps" + tag, [4, 2, TC], F32) as ps4,
        nc.sbuf_tensor("sq" + tag, [128, 2, TC], F32) as sq,
        nc.Block() as blk,
    ):
        @blk.sync
        def _(sync):
            for mname in wait_marks:
                p.wait_mark(sync, mname)
            for i in range(NTC * NFC):
                if i >= 4:
                    W(sync, mm, 2 * (i - 3))
                sync.dma_start(xr[:, i % 4, :],
                               xsrc[(i % NFC) * 128:(i % NFC + 1) * 128,
                                    (i // NFC) * TC:(i // NFC + 1) * TC]).then_inc(*sx[i % 4].tick(16))
            for k in range(4):
                W(sync, sx[k], 16 * (NTC * NFC // 4))

        @blk.tensor
        def _(tensor):
            p.wait_mark(tensor, "const")
            for i in range(NTC * NFC):
                tc = i // NFC
                W(tensor, sx[i % 4], 16 * (i // 4 + 1))
                tensor.matmul(ps4[:, 0, :], sel4f[:, tc * 4:(tc + 1) * 4],
                              xr[:, i % 4, :], start=(i == 0), stop=(i == NTC * NFC - 1))
                W(tensor, sq_done, i + 1)
                tensor.matmul(ps4[:, 1, :], sel4f[:, tc * 4:(tc + 1) * 4],
                              sq[:, i % 2, :], start=(i == 0),
                              stop=(i == NTC * NFC - 1)).then_inc(*mm.tick(2))

        @blk.scalar
        def _(scalar):
            for i in range(NTC * NFC):
                W(scalar, sx[i % 4], 16 * (i // 4 + 1))
                if i >= 2:
                    W(scalar, mm, 2 * (i - 1))
                scalar.activation(sq[:, i % 2, :], xr[:, i % 4, :],
                                  AF.Square).then_inc(*sq_done.tick(1))
            _stats_fin_s(p, scalar, sv, ss)

        @blk.vector
        def _(vector):
            W(vector, mm, 2 * NTC * NFC)
            _stats_fin_v1(p, vector, ps4, sv)
            _stats_fin_v2(p, vector, ss)


def ph_xn(p, xsrc, tag, wait_marks=()):
    p.phase_begin()
    nc = p.nc
    xbig = p.arena("xnb" + tag, [128, 2, NFC, TC], F32, "BIG")
    xn = p.t["xn"]
    stat = p.t["stat"]
    bb = p.t["bcb"]
    sx = p.slots(tag + "x", 2)
    mm = p.new_sem(tag + "m")
    sv = p.new_sem(tag + "v")
    with (
        nc.psum_tensor("ps" + tag, [128, 2, TC], F32) as ps,
        nc.Block() as blk,
    ):
        @blk.sync
        def _(sync):
            for mname in wait_marks:
                p.wait_mark(sync, mname)
            for tc in range(NTC):
                for fc in range(NFC):
                    if tc >= 2 and fc == 0:
                        W(sync, sv, tc - 1)
                    sync.dma_start(xbig[:, tc % 2, fc, :],
                                   xsrc[fc * 128:(fc + 1) * 128,
                                        tc * TC:(tc + 1) * TC]).then_inc(*sx[tc % 2].tick(16))
            for k in range(2):
                W(sync, sx[k], 16 * NFC * (NTC // 2))

        @blk.tensor
        def _(tensor):
            p.wait_mark(tensor, "stat")
            sel4t = p.t["sel4ts"]
            for tc in range(NTC):
                if tc >= 1:
                    W(tensor, sv, tc)
                tensor.matmul(ps[:, 0, :], sel4t[:, tc * 128:(tc + 1) * 128],
                              stat[:, 1, :], start=True, stop=True)
                tensor.matmul(ps[:, 1, :], sel4t[:, tc * 128:(tc + 1) * 128],
                              stat[:, 0, :], start=True, stop=True).then_inc(*mm.tick(1))

        @blk.vector
        def _(vector):
            for tc in range(NTC):
                W(vector, mm, tc + 1)
                if tc >= 1:
                    vector.drain()
                vector.tensor_copy(bb[:, 0, :], ps[:, 0, :])
                vector.tensor_copy(bb[:, 1, :], ps[:, 1, :])
                vector.drain()
                W(vector, sx[tc % 2], 16 * NFC * (tc // 2 + 1))
                for fc in range(NFC):
                    vector.tensor_mul(xbig[:, tc % 2, fc, :], xbig[:, tc % 2, fc, :],
                                      bb[:, 0, :])
                vector.drain()
                for fc in range(NFC - 1):
                    vector.tensor_sub(xn[:, fc, tc * TC:(tc + 1) * TC],
                                      xbig[:, tc % 2, fc, :], bb[:, 1, :])
                vector.drain()
                vector.tensor_sub(xn[:, NFC - 1, tc * TC:(tc + 1) * TC],
                                  xbig[:, tc % 2, NFC - 1, :], bb[:, 1, :]).then_inc(*sv.tick(1))
            p.alias_mark("xn", sv, NTC)


def _dbg_rows(p, srcname, outname, w, marks=()):
    p.phase_begin()
    nc = p.nc
    out = nc.dram_tensor(outname, [128, w], F32, kind="ExternalOutput")
    sd = p.new_sem("dbgr", "g")
    with nc.Block() as blk:
        @blk.gpsimd
        def _(g):
            for m in marks:
                p.wait_mark(g, m)
            g.dma_start(out[:, :], p.t[srcname][:, :]).then_inc(*sd.tick(16))
            W(g, sd, 16)


def _dbg_dump_fm(p, srcname, outname, nfc=NFC, width=TPC, marks=("xn",)):
    p.phase_begin()
    nc = p.nc
    src = p.t[srcname]
    out = p.t.get(outname)
    if out is None:
        out = nc.dram_tensor(outname, [nfc * 128, width], F32, kind="ExternalOutput")
        p.t[outname] = out
    sv = p.new_sem("dmp")
    sx = p.slots("dmpx", 2)
    with (
        nc.sbuf_tensor("dmp" + outname, [128, 2, width], F32) as tmp,
        nc.Block() as blk,
    ):
        @blk.vector
        def _(vector):
            for m in marks:
                p.wait_mark(vector, m)
            for fc in range(nfc):
                if fc >= 2:
                    W(vector, sx[fc % 2], 16 * ((fc - 2) // 2 + 1))
                vector.tensor_copy(tmp[:, fc % 2, :], src[:, fc, :]).then_inc(*sv.tick(1))

        @blk.sync
        def _(sync):
            for fc in range(nfc):
                W(sync, sv, fc + 1)
                sync.dma_start(out[fc * 128:(fc + 1) * 128, :],
                               tmp[:, fc % 2, :]).then_inc(*sx[fc % 2].tick(16))
            for k in range(2):
                W(sync, sx[k], 16 * (nfc // 2))



def ph_proj(p, tag, wt, kind, n_oc, *, dst, dst_col0=0, dst_row0=0,
            norm_row=None, norm_col0=0, wt_col0=0, n_dc=NFC):
    p.phase_begin()
    nc = p.nc
    xn = p.t["xn"]
    nrows = p.t.get("nrows")
    row = p.t["rowtmp"]
    sw = p.slots(tag + "w", 2)
    swc = p.new_sem(tag + "wc")
    smm = p.new_sem(tag + "mm")
    sev = p.new_sem(tag + "ev")
    ssq = p.new_sem(tag + "sq")
    snm = p.new_sem(tag + "nm")
    snr = p.new_sem(tag + "nr")
    svt = p.new_sem(tag + "vt")
    svc = p.new_sem(tag + "vc")
    skd = p.slots(tag + "kd", 2, "g")
    if "nonorm" in p.dbg:
        norm_row = None
    normed = norm_row is not None

    wrf = p.arena("wrf" + tag, [128, 2, n_dc, 128], F32, "SCR", 16384)
    wrb = p.arena("wrb" + tag, [128, 2, n_dc, 128], BF16, "SCR", 32768)
    stg = p.arena("stg" + tag, [128, 2, TPC], BF16, "SCR", 40960)
    vstg = p.arena("vsg" + tag, [128, 2, NFC, 128], BF16, "SCR", 49152)
    with (
        nc.sbuf_tensor("sqn" + tag, [128, TPC], BF16) as sqn,
        nc.psum_tensor("pp" + tag, [128, TPC], F32) as pp,
        nc.psum_tensor("vt0" + tag, [128, 128], BF16) as vtp0,
        nc.psum_tensor("vt1" + tag, [128, 128], BF16) as vtp1,
        nc.psum_tensor("np" + tag, [1, 2, TC], F32) as nps,
        nc.Block() as blk,
    ):
        @blk.sync
        def _(sync):
            for oc in range(n_oc):
                if oc >= 2:
                    W(sync, swc, oc - 1)
                c0 = wt_col0 + oc * 128
                sync.dma_start(
                    wrf[:, oc % 2, :, :],
                    wt[:, c0:c0 + 128].rearrange("(dc p) o -> p dc o", p=128),
                ).then_inc(*sw[oc % 2].tick(16))
            for k in range(2):
                W(sync, sw[k], 16 * ((n_oc - k + 1) // 2))

        @blk.tensor
        def _(tensor):
            p.wait_mark(tensor, "xn")
            p.wait_mark(tensor, "const")
            for oc in range(n_oc):
                W(tensor, swc, oc + 1)
                if oc >= 1:
                    W(tensor, sev, oc)
                    if normed:
                        W(tensor, ssq, oc)
                inst = None
                for dc in range(n_dc):
                    for tc in range(NTC):
                        inst = tensor.matmul(pp[:, tc * TC:(tc + 1) * TC],
                                             wrb[:, oc % 2, dc, :],
                                             xn[:, dc, tc * TC:(tc + 1) * TC],
                                             start=(dc == 0), stop=(dc == n_dc - 1))
                inst.then_inc(*smm.tick(1))
                if normed and "nnm" not in p.dbg:
                    W(tensor, ssq, oc + 1)
                    for tc in range(NTC):
                        g = 4 * oc + tc
                        if g >= 2:
                            W(tensor, snr, g - 1)
                        tensor.matmul(nps[:, tc % 2, :], p.t["ones1b"][:, :],
                                      sqn[:, tc * TC:(tc + 1) * TC],
                                      start=True, stop=True).then_inc(*snm.tick(1))
                if kind == "vt":
                    W(tensor, sev, oc + 1)
                    for j in range(NFC):
                        m = oc * NFC + j
                        if m >= 2:
                            W(tensor, svc, m - 1)
                        vtp = vtp0 if m % 2 == 0 else vtp1
                        tensor.transpose(vtp[:, :],
                                         stg[:, oc % 2, j * 128:(j + 1) * 128],
                                         p.t["identb"][:, :]).then_inc(*svt.tick(1))

        @blk.scalar
        def _(scalar):
            if normed:
                for oc in range(n_oc):
                    W(scalar, sev, oc + 1)
                    if oc >= 1 and "nnm" not in p.dbg:
                        W(scalar, snm, 4 * oc)
                    scalar.activation(sqn[:, :], pp[:, :], AF.Square).then_inc(*ssq.tick(1))

        @blk.vector
        def _(vector):
            for oc in range(n_oc):
                W(vector, sw[oc % 2], 16 * (oc // 2 + 1))
                if oc >= 2:
                    W(vector, smm, oc - 1)
                vector.tensor_copy(wrb[:, oc % 2, :, :], wrf[:, oc % 2, :, :]).then_inc(*swc.tick(1))
                W(vector, smm, oc + 1)
                if kind == "q":
                    vector.tensor_copy(dst[:, dst_col0 + oc, :], pp[:, :]).then_inc(*sev.tick(1))
                else:
                    if oc >= 2:
                        if kind == "kf":
                            W(vector, skd[oc % 2], 16 * (oc // 2))
                        else:
                            W(vector, svt, 16 * (oc - 1))
                    vector.tensor_copy(stg[:, oc % 2, :], pp[:, :]).then_inc(*sev.tick(1))
                if normed and "nnm" not in p.dbg:
                    for tc in range(NTC):
                        g = 4 * oc + tc
                        W(vector, snm, g + 1)
                        vector.tensor_reduce(row[:, 3, tc:tc + 1], nps[:, tc % 2, :],
                                             AX.X, ALU.max).then_inc(*snr.tick(1))
                    vector.drain()
                    vector.tensor_reduce(row[:, 2, 0:1], row[:, 3, 0:4], AX.X, ALU.max)
                    vector.drain()
                    inst = vector.tensor_copy(nrows[:, norm_row, norm_col0 + oc:norm_col0 + oc + 1],
                                              row[:, 2, 0:1])
                    p.mark(tag + "_nrm", inst, 1)
                    vector.drain()
                if kind == "vt":
                    for j in range(NFC):
                        m = oc * NFC + j
                        W(vector, svt, m + 1)
                        if oc >= 2 and j == 0:
                            W(vector, skd[oc % 2], 16 * (oc // 2))
                        vector.tensor_copy(vstg[:, oc % 2, j, :],
                                           (vtp0 if m % 2 == 0 else vtp1)[:, :]).then_inc(*svc.tick(1))

        @blk.gpsimd
        def _(g):
            for oc in range(n_oc):
                if kind == "kf":
                    W(g, sev, oc + 1)
                    r0 = dst_row0 + oc * 128
                    g.dma_start(dst[r0:r0 + 128, :], stg[:, oc % 2, :]).then_inc(*skd[oc % 2].tick(16))
                elif kind == "vt":
                    W(g, svc, 16 * (oc + 1))
                    c0 = dst_col0 + oc * 128
                    g.dma_start(dst.ap().rearrange("(j p) w -> p j w", p=128)[:, :, c0:c0 + 128],
                                vstg[:, oc % 2, :, :]).then_inc(*skd[oc % 2].tick(16))
            if kind in ("kf", "vt"):
                for k in range(2):
                    W(g, skd[k], 16 * ((n_oc - k + 1) // 2))

    if kind == "q":
        p.alias_mark(tag + "_ev", sev, n_oc)


def ph_cc(p, tag, pairs, krow, grow, nin, nout, nrm_mark=None):
    p.phase_begin()
    nc = p.nc
    s_nrm = p.new_sem(tag + "nr", "g")
    scc = p.new_sem(tag + "cc", "g")
    sd = p.slots(tag + "cd", 2, "g")
    nrows = p.t["nrows"]
    grp = [[0, 1, 2, 3], [4, 5, 6, 7]]
    nk = nin.shape[1]
    with nc.Block() as blk:
        @blk.gpsimd
        def _(g):
            if nrm_mark is not None:
                p.wait_mark(g, nrm_mark)
            g.dma_start(nin[:, :], nrows[:, krow, 0:nk]).then_inc(*sd[0].tick(16))
            W(g, sd[0], 16)
            g.collective_compute("AllReduce", ALU.max, grp,
                                 [nin.ap().opt()], [nout.ap().opt()]).then_inc(*s_nrm.tick(1))
            for (ci, co) in pairs:
                g.collective_compute("AllGather", ALU.bypass, grp,
                                     [p.t[ci].ap().opt()],
                                     [p.t[co].ap().opt()]).then_inc(*scc.tick(1))
            W(g, s_nrm, 1)
            g.dma_start(nrows[:, grow, 0:nk], nout[:, :]).then_inc(*sd[1].tick(16))
            W(g, sd[1], 16)
            p.mark("nrmg_" + tag, g.nop(), 1)
        p.alias_mark("gath_" + tag, scc, len(pairs))


def ph_cc_gather(p, tag, pairs):
    p.phase_begin()
    nc = p.nc
    scc = p.new_sem(tag + "cc", "g")
    grp = [[0, 1, 2, 3], [4, 5, 6, 7]]
    with nc.Block() as blk:
        @blk.gpsimd
        def _(g):
            for (ci, co) in pairs:
                g.collective_compute("AllGather", ALU.bypass, grp,
                                     [p.t[ci].ap().opt()],
                                     [p.t[co].ap().opt()]).then_inc(*scc.tick(1))
        p.alias_mark("gath_" + tag, scc, len(pairs))


def _negm_emit_vector(p, tag, vector, qrow, grow, scale, nh, negM,
                      q_nrm_mark, bps, s1, s2, s3, s4, s5, group_of_head=None):
    nrows = p.t["nrows"]
    row = p.t["rowtmp"]
    p.wait_mark(vector, q_nrm_mark)
    p.wait_mark(vector, "nrmg_" + tag)
    if group_of_head is None:
        vector.tensor_copy(row[:, 3, 0:nh], nrows[:, grow, 0:nh])
    else:
        for h in range(nh):
            vector.tensor_copy(row[:, 3, h:h + 1],
                               nrows[:, grow, h // GROUPS:h // GROUPS + 1])
    vector.drain()
    vector.tensor_mul(row[:, 0, 0:nh], nrows[:, qrow, 0:nh],
                      row[:, 3, 0:nh]).then_inc(*s1.tick(1))
    W(vector, s2, 1)
    vector.tensor_scalar(row[:, 2, 0:nh], row[:, 1, 0:nh],
                         -scale, None, ALU.mult).then_inc(*s3.tick(1))
    W(vector, s4, 1)
    vector.tensor_copy(negM[:, 0:nh], bps[:, 0:nh]).then_inc(*s5.tick(1))
    vector.drain()


def _negm_emit_scalar(p, scalar, s1, s2, nh):
    row = p.t["rowtmp"]
    W(scalar, s1, 1)
    scalar.activation(row[:, 1, 0:nh], row[:, 0, 0:nh], AF.Sqrt).then_inc(*s2.tick(1))
    scalar.drain()


def _negm_emit_tensor(p, tensor, s3, s4, bps, nh):
    row = p.t["rowtmp"]
    W(tensor, s3, 1)
    tensor.matmul(bps[:, 0:nh], p.t["onesr"][:, :], row[:, 2, 0:nh],
                  start=True, stop=True).then_inc(*s4.tick(1))


def _dbg_dram(p, src_name, out_name, shape, dtype=F32, n_rows=None):
    p.phase_begin()
    nc = p.nc
    out = nc.dram_tensor(out_name, shape, dtype, kind="ExternalOutput")
    r = shape[0] if n_rows is None else n_rows
    sd = p.new_sem("dbgd", "g")
    with nc.Block() as blk:
        @blk.gpsimd
        def _(g):
            g.dma_start(out[:r, :], p.t[src_name][:r, :]).then_inc(*sd.tick(16))
            W(g, sd, 16)



def ph_attn_a(p):
    p.phase_begin()
    nc = p.nc
    Q = p.t["Q"]
    Vsb = p.arena("a3v", [128, 2, 64, 256], BF16, "XN", 0)
    pt = p.arena("a3p", [128, 2, 1024], BF16, "SCR", 0)
    lbsb = p.arena("a3lb", [128, 1024], F32, "SCR", 4096)
    ctile = p.arena("a3c", [128, 2, 1024], BF16, "SCR", 8192)
    lrow = p.arena("a3lr", [1, 1024], F32, "SCR", 12288)
    Ksb = p.arena("a3k", [128, 2, 4 * TPC], BF16, "SCR", 16384)
    negM = p.t["negMa"]
    kcc, vcc, ctxd = p.t["kcc_out"], p.t["vcc_out"], p.t["ctxd"]
    skv = p.slots("a3kv", 2, "g")
    svv = p.slots("a3vv", 2, "g")
    sctx = p.slots("a3cx", 2, "g")
    s_st = p.new_sem("a3st")
    s_xp = p.new_sem("a3xp")
    s_pv = p.new_sem("a3pv")
    s_rc = p.new_sem("a3rc")
    s_bc = p.new_sem("a3bc")
    s_ev = p.new_sem("a3ev")
    sn1, sn2, sn3, sn4, sn5 = (p.new_sem("a3n1"), p.new_sem("a3n2"),
                               p.new_sem("a3n3"), p.new_sem("a3n4"),
                               p.new_sem("a3n5"))
    NH = 2 if "mini" in p.dbg else HEADS
    NH2 = NH * 2

    with (
        nc.psum_tensor("a3stp", [128, 2, 1024], F32) as st,
        nc.psum_tensor("a3op", [128, 1024], F32) as ops,
        nc.psum_tensor("a3lp", [1, 1024], F32) as lps,
        nc.Block() as blk,
    ):
        @blk.gpsimd
        def _(g):
            p.wait_mark(g, "gath_a")

            def store_ctx(hh):
                h, half = hh // 2, hh % 2
                W(g, s_ev, hh + 1)
                g.dma_start(ctxd[h * 128:(h + 1) * 128, half * 1024:(half + 1) * 1024],
                            ctile[:, hh % 2, :]).then_inc(*sctx[hh % 2].tick(16))

            for h in range(NH):
                if h >= 2:
                    store_ctx(2 * (h - 2))
                    store_ctx(2 * (h - 2) + 1)
                sl = h % 2
                if h >= 2:
                    W(g, s_st, 128 * (h - 1))
                for r in range(4):
                    g.dma_start(Ksb[:, sl, r * TPC:(r + 1) * TPC],
                                kcc[r * D + h * 128:r * D + (h + 1) * 128, :]
                                ).then_inc(*skv[sl].tick(16))
                if h % 2 == 0:
                    P = h // 2
                    vs = P % 2
                    if P == 0:
                        p.wait_mark(g, "gath_av")
                    if P >= 2:
                        W(g, s_pv, 256 * (P - 1))
                    vv = vcc.ap().rearrange("(kb pp) f -> pp kb f", pp=128)
                    for kq in range(8):
                        g.dma_start(
                            Vsb[:, vs, kq * 8:(kq + 1) * 8, :],
                            vv[:, kq * 8:(kq + 1) * 8, h * 128:(h + 2) * 128],
                        ).then_inc(*svv[vs].tick(16))
            for hh in range(max(0, NH2 - 4), NH2):
                store_ctx(hh)
            for k in range(2):
                W(g, sctx[k], 16 * (NH2 // 2))

        @blk.tensor
        def _(tensor):
            _negm_emit_tensor(p, tensor, sn3, sn4, ops[:, 0:128], HEADS)
            p.wait_mark(tensor, "wq_ev")
            for hh in range(NH2):
                h, half = hh // 2, hh % 2
                hp = h % 2
                P = h // 2
                q0 = half * 1024
                if half == 0:
                    W(tensor, skv[h % 2], 64 * (h // 2 + 1))
                    if hp == 0:
                        W(tensor, svv[P % 2], 128 * (P // 2 + 1))

                def pv(j):
                    m = hh * 64 + j
                    W(tensor, s_xp, m + 1)
                    if j == 0:
                        W(tensor, s_ev, hh)
                        W(tensor, s_rc, hh)
                    st_, sp_ = (j == 0), (j == 63)
                    tensor.matmul(ops[:, 0:512], Vsb[:, P % 2, j, hp * 128:(hp + 1) * 128],
                                  pt[:, m % 2, 0:512], start=st_, stop=sp_)
                    tensor.matmul(ops[:, 512:1024], Vsb[:, P % 2, j, hp * 128:(hp + 1) * 128],
                                  pt[:, m % 2, 512:1024], start=st_, stop=sp_)
                    tensor.matmul(lps[:, 0:512], p.t["ones1b"][:, :],
                                  pt[:, m % 2, 0:512], start=st_, stop=sp_)
                    tensor.matmul(lps[:, 512:1024], p.t["ones1b"][:, :],
                                  pt[:, m % 2, 512:1024], start=st_, stop=sp_).then_inc(*s_pv.tick(1))

                for kb in range(64):
                    m = hh * 64 + kb
                    if m >= 2:
                        W(tensor, s_xp, m - 1)
                    if kb == 0 and hh >= 1:
                        W(tensor, s_ev, hh)
                    tensor.matmul(st[:, m % 2, 0:512], Ksb[:, h % 2, kb * 128:(kb + 1) * 128],
                                  Q[:, h, q0:q0 + 512], start=True, stop=True)
                    tensor.matmul(st[:, m % 2, 512:1024], Ksb[:, h % 2, kb * 128:(kb + 1) * 128],
                                  Q[:, h, q0 + 512:q0 + 1024], start=True, stop=True).then_inc(*s_st.tick(1))
                    if kb >= 1:
                        pv(kb - 1)
                pv(63)
                W(tensor, s_rc, hh + 1)
                tensor.matmul(st[:, 0, 0:512], p.t["onesr"][:, :],
                              lrow[:, 0:512], start=True, stop=True)
                tensor.matmul(st[:, 0, 512:1024], p.t["onesr"][:, :],
                              lrow[:, 512:1024], start=True, stop=True).then_inc(*s_bc.tick(1))

        @blk.scalar
        def _(scalar):
            _negm_emit_scalar(p, scalar, sn1, sn2, HEADS)
            W(scalar, sn5, 1)
            for hh in range(NH2):
                h = hh // 2
                for kb in range(64):
                    m = hh * 64 + kb
                    W(scalar, s_st, m + 1)
                    if m >= 2:
                        W(scalar, s_pv, m - 1)
                    scalar.activation(pt[:, m % 2, :], st[:, m % 2, :], AF.Exp,
                                      bias=negM[:, h:h + 1], scale=SM_SCALE).then_inc(*s_xp.tick(1))

        @blk.vector
        def _(vector):
            _negm_emit_vector(p, "a", vector, 0, 2, SM_SCALE, HEADS, p.t["negMa"],
                              "wq_nrm", ops[:, 0:128], sn1, sn2, sn3, sn4, sn5)
            for hh in range(NH2):
                W(vector, s_pv, 64 * (hh + 1))
                vector.reciprocal(lrow[:, :], lps[:, :]).then_inc(*s_rc.tick(1))
                W(vector, s_bc, hh + 1)
                vector.tensor_copy(lbsb[:, :], st[:, 0, :])
                vector.drain()
                if hh >= 2:
                    W(vector, sctx[hh % 2], 16 * (hh // 2))
                vector.tensor_mul(ctile[:, hh % 2, :], ops[:, :], lbsb[:, :]).then_inc(*s_ev.tick(1))




def ph_attn_b(p):
    p.phase_begin()
    nc = p.nc
    Q = p.t["Q"]
    Kg = p.arena("b3k", [128, 2, 4 * TPC], BF16, "XN", 0)
    Vg = p.arena("b3v", [128, 2, 64, 128], BF16, "XN", 32768)
    pt = p.arena("b3p", [128, 4, TC], BF16, "SCR", 0)
    lbsb = p.arena("b3lb", [128, TC], F32, "SCR", 4608)
    ctile = p.arena("b3c", [128, 2, TC], BF16, "SCR", 6656)
    lrow = p.arena("b3lr", [1, TC], F32, "SCR", 8704)
    btab = p.arena("b3bt", [128, HEADS, 64], F32, "SCR", 10752)
    indsb = p.arena("b3in", [128, 64], F32, "SCR", 15104)
    bms = p.arena("b3bm", [128, 2, 16, TC], BF16, "SCR", 16384)
    negM = p.t["negMb"]
    kcc, vcc, ctxd = p.t["kbcc_out"], p.t["vbcc_out"], p.t["ctxd"]

    skv = p.slots("b3kv", 2, "g")
    svv = p.slots("b3vv", 2, "g")
    sbm = p.slots("b3bm", 2)
    sctx = p.slots("b3cx", 2, "g")
    scst = p.slots("b3cs", 2)
    s_st = p.new_sem("b3st")
    s_xp = p.new_sem("b3xp")
    s_tri = p.new_sem("b3tr")
    s_pv = p.new_sem("b3pv")
    s_rc = p.new_sem("b3rc")
    s_bc = p.new_sem("b3bc")
    s_ev = p.new_sem("b3ev")
    s_bt = p.new_sem("b3btv")
    bn1, bn2, bn3, bn4, bn5 = (p.new_sem("b3n1"), p.new_sem("b3n2"),
                               p.new_sem("b3n3"), p.new_sem("b3n4"),
                               p.new_sem("b3n5"))

    blocks = []
    for g in range(GROUPS):
        for j in range(NTC):
            for h4 in range(4):
                blocks.append((g, j, g * 4 + h4))
    nw_of = lambda j: 16 * j + 16
    m_start = []
    m = 0
    for (g, j, h) in blocks:
        m_start.append(m)
        m += nw_of(j)
    ntri_before = []
    nt = 0
    for bi, (g, j, h) in enumerate(blocks):
        for w in range(nw_of(j)):
            ntri_before.append(nt)
            if w >= 16 * j:
                nt += 1

    def phys_win(j, u):
        return (u // 4) * 16 + j * 4 + (u % 4)

    def slab_of(bi):
        g, j, h = blocks[bi]
        return g * NTC + j

    tris_through_slab = {}
    cnt = 0
    for bi, (g, j, h) in enumerate(blocks):
        cnt += 16
        tris_through_slab[g * NTC + j] = cnt

    with (
        nc.psum_tensor("b3st0", [128, TC], F32) as st0,
        nc.psum_tensor("b3st1", [128, TC], F32) as st1,
        nc.psum_tensor("b3st2", [128, TC], F32) as st2,
        nc.psum_tensor("b3st3", [128, TC], F32) as st3,
        nc.psum_tensor("b3op", [128, TC], F32) as ops,
        nc.psum_tensor("b3lp", [1, TC], F32) as lps,
        nc.Block() as blk,
    ):
        stN = (st0, st1)
        @blk.sync
        def _(sync):
            sync.dma_start(indsb[:, :], p.inp("ind")[:, :]).then_inc(*scst[0].tick(16))
            bm = p.inp("bmask").ap().rearrange("(ju pp) q -> pp ju q", pp=128)
            for si in range(GROUPS * NTC):
                g, j = divmod(si, NTC)
                if si >= 2:
                    W(sync, s_tri, tris_through_slab[si - 2])
                sync.dma_start(bms[:, si % 2, :, :],
                               bm[:, j * 16:(j + 1) * 16, :]).then_inc(*sbm[si % 2].tick(16))

        @blk.vector
        def _(vector):
            _negm_emit_vector(p, "b", vector, 3, 5, SM_SCALE, HEADS, p.t["negMb"],
                              "bq_nrm", ops[:, 0:128], bn1, bn2, bn3, bn4, bn5,
                              group_of_head=True)
            W(vector, scst[0], 16)
            for h in range(HEADS):
                vector.tensor_scalar(btab[:, h, :], indsb[:, :], negM[:, h:h + 1],
                                     None, ALU.add)
            vector.drain()
            vector.tensor_copy(btab[:, 0, 0:1], btab[:, 0, 0:1]).then_inc(*s_bt.tick(1))
            m = 0
            for bi, (g, j, h) in enumerate(blocks):
                nw = nw_of(j)
                si = g * NTC + j
                if h % 4 == 0:
                    W(vector, sbm[si % 2], 16 * (si // 2 + 1))
                for w in range(nw):
                    if w >= 16 * j:
                        W(vector, s_xp, m + 1)
                        vector.tensor_mul(pt[:, m % 2, :], pt[:, m % 2, :],
                                          bms[:, si % 2, w - 16 * j, :]
                                          ).then_inc(*s_tri.tick(1))
                    m += 1
                W(vector, s_pv, m_start[bi] + nw)
                vector.reciprocal(lrow[:, :], lps[:, :]).then_inc(*s_rc.tick(1))
                W(vector, s_bc, bi + 1)
                vector.tensor_copy(lbsb[:, :], st0[:, :])
                vector.drain()
                if bi >= 2:
                    W(vector, sctx[bi % 2], 16 * (bi // 2))
                vector.tensor_mul(ctile[:, bi % 2, :], ops[:, :],
                                  lbsb[:, :]).then_inc(*s_ev.tick(1))

        @blk.gpsimd
        def _(g_):
            p.wait_mark(g_, "gath_b")
            for gi in range(GROUPS):
                if gi >= 1:
                    for bi in range((gi - 1) * 16 - 4 if gi >= 2 else 0, gi * 16 - 4):
                        _b3_store(p, g_, bi, blocks, ctile, ctxd, s_ev, sctx)
                sl = gi % 2
                if gi >= 2:
                    W(g_, s_st, m_start[(gi - 1) * 16])
                for r in range(4):
                    g_.dma_start(Kg[:, sl, r * TPC:(r + 1) * TPC],
                                 kcc[r * 512 + gi * 128:r * 512 + (gi + 1) * 128, :]
                                 ).then_inc(*skv[sl].tick(16))
                if gi == 0:
                    p.wait_mark(g_, "gath_bv")
                if gi >= 2:
                    W(g_, s_pv, m_start[(gi - 1) * 16])
                vv = vcc.ap().rearrange("(kb pp) f -> pp kb f", pp=128)
                for kq in range(8):
                    g_.dma_start(Vg[:, sl, kq * 8:(kq + 1) * 8, :],
                                 vv[:, kq * 8:(kq + 1) * 8, gi * 128:(gi + 1) * 128]
                                 ).then_inc(*svv[sl].tick(16))
            for bi in range(3 * 16 - 4, 4 * 16):
                _b3_store(p, g_, bi, blocks, ctile, ctxd, s_ev, sctx)
            for k in range(2):
                W(g_, sctx[k], 16 * (len(blocks) // 2))
            p.mark("ctxb", g_.nop(), 1)

        @blk.tensor
        def _(tensor):
            _negm_emit_tensor(p, tensor, bn3, bn4, ops[:, 0:128], HEADS)
            p.wait_mark(tensor, "bq_ev")
            m = 0
            for bi, (g, j, h) in enumerate(blocks):
                nw = nw_of(j)
                sl = g % 2
                if j == 0 and h % 4 == 0:
                    W(tensor, skv[sl], 64 * (g // 2 + 1))
                    W(tensor, svv[sl], 128 * (g // 2 + 1))

                def pv(mm, w):
                    if w >= 16 * j:
                        W(tensor, s_tri, ntri_before[mm] + 1)
                    else:
                        W(tensor, s_xp, mm + 1)
                    if w == 0:
                        W(tensor, s_ev, bi)
                        W(tensor, s_rc, bi)
                    P = _phys_chunk(w) if w < 16 * j else phys_win(j, w - 16 * j)
                    st_, sp_ = (w == 0), (w == nw - 1)
                    tensor.matmul(ops[:, :], Vg[:, sl, P, :], pt[:, mm % 2, :],
                                  start=st_, stop=sp_)
                    tensor.matmul(lps[:, :], p.t["ones1b"][:, :], pt[:, mm % 2, :],
                                  start=st_, stop=sp_).then_inc(*s_pv.tick(1))

                for w in range(nw):
                    if m >= 2:
                        W(tensor, s_xp, m - 1)
                    if w == 0 and bi >= 1:
                        W(tensor, s_ev, bi)
                    P = _phys_chunk(w) if w < 16 * j else phys_win(j, w - 16 * j)
                    kt = Kg[:, sl, P * 128:(P + 1) * 128]
                    stp = stN[m % 2]
                    tensor.matmul(stp[:, :], kt, Q[:, h, j * TC:(j + 1) * TC],
                                  start=True, stop=True).then_inc(*s_st.tick(1))
                    if w >= 1:
                        pv(m - 1, w - 1)
                    m += 1
                pv(m - 1, nw - 1)
                W(tensor, s_rc, bi + 1)
                tensor.matmul(st0[:, :], p.t["onesr"][:, :], lrow[:, :],
                              start=True, stop=True).then_inc(*s_bc.tick(1))

        @blk.scalar
        def _(scalar):
            _negm_emit_scalar(p, scalar, bn1, bn2, HEADS)
            W(scalar, s_bt, 1)
            m = 0
            for bi, (g, j, h) in enumerate(blocks):
                nw = nw_of(j)
                for w in range(nw):
                    W(scalar, s_st, m + 1)
                    if m >= 2:
                        W(scalar, s_pv, m - 1)
                    stp = stN[m % 2]
                    if w < 16 * j:
                        bias = negM[:, h:h + 1]
                    else:
                        col = j * 16 + (w - 16 * j)
                        bias = btab[:, h, col:col + 1]
                    scalar.activation(pt[:, m % 2, :], stp[:, :], AF.Exp,
                                      bias=bias, scale=SM_SCALE).then_inc(*s_xp.tick(1))
                    m += 1


def _b3_store(p, g_, bi, blocks, ctile, ctxd, s_ev, sctx):
    (g, j, h) = blocks[bi]
    W(g_, s_ev, bi + 1)
    g_.dma_start(ctxd[h * 128:(h + 1) * 128, j * TC:(j + 1) * TC],
                 ctile[:, bi % 2, :]).then_inc(*sctx[bi % 2].tick(16))



def ph_wcast(p, e0, e1, with_shd):
    p.phase_begin()
    nc = p.nc
    wdf = p.inp("wd_t").ap().rearrange("e h d -> (e h) d")
    tag = "wc%d" % e0
    sf = p.slots(tag + "f", 4)
    sb = p.slots(tag + "b", 2, "g")
    sv = p.new_sem(tag + "v")
    buf_f = p.arena("wcff" + tag, [128, 4, D], F32, "BIG", 32768)
    buf_b = p.arena("wcbb" + tag, [128, 2, D], BF16, "SCR", 36864)
    with nc.Block() as blk:
        tiles = []
        for r in range(e0 * HEXP // 128, e1 * HEXP // 128):
            tiles.append((wdf, p.t["wdb"], r, D))
        if with_shd:
            for r in range(HSH // 128):
                tiles.append((p.inp("shd_t").ap(), p.t["shdb"], r, D))

        @blk.sync
        def _(sync):
            for i, (src, dst, r, C) in enumerate(tiles):
                if i >= 4:
                    W(sync, sv, i - 3)
                sync.dma_start(buf_f[:, i % 4, 0:C],
                               src[r * 128:(r + 1) * 128, :]).then_inc(*sf[i % 4].tick(16))

        @blk.vector
        def _(vector):
            for i, (src, dst, r, C) in enumerate(tiles):
                W(vector, sf[i % 4], 16 * (i // 4 + 1))
                if i >= 2:
                    W(vector, sb[i % 2], 16 * (i // 2))
                vector.tensor_copy(buf_b[:, i % 2, 0:C], buf_f[:, i % 4, 0:C]).then_inc(*sv.tick(1))

        @blk.gpsimd
        def _(g):
            for i, (src, dst, r, C) in enumerate(tiles):
                W(g, sv, i + 1)
                g.dma_start(dst[r * 128:(r + 1) * 128, :], buf_b[:, i % 2, 0:C]).then_inc(*sb[i % 2].tick(16))
            n = len(tiles)
            for k in range(2):
                W(g, sb[k], 16 * ((n - k + 1) // 2))
            p.mark("wcast", g.nop(), 1)


def ph_gconst(p):
    p.phase_begin()
    nc = p.nc
    p.t["ggb"] = nc.alloc_sbuf_tensor("ggb", [128, NFC, 8], BF16)
    p.t["seb"] = nc.alloc_sbuf_tensor("seb", [128, NFC, 1], BF16)
    p.t["selb"] = nc.alloc_sbuf_tensor("selb", [8, 8 * 128], BF16)
    p.t["sg"] = p.arena("sg", [1, TPC], F32, "SCR", 49152)
    gf = p.arena("gcf", [128, NFC, 8], F32, "SCR", 16384)
    sef = p.arena("gce", [128, NFC, 1], F32, "SCR", 22528)
    self_sel = p.arena("gcs", [8, 8 * 128], F32, "SCR", 24576)
    sd = p.slots("gcd", 3)
    with nc.Block() as blk:
        @blk.sync
        def _(sync):
            sync.dma_start(gf[:, :, :],
                           p.inp("ggate").ap().rearrange("(dc p) e -> p dc e", p=128)
                           ).then_inc(*sd[0].tick(16))
            sync.dma_start(sef[:, :, :],
                           p.inp("se_t").ap().rearrange("(dc p) e -> p dc e", p=128),
                           allow_slow_non_contiguous=True).then_inc(*sd[1].tick(16))
            sync.dma_start(self_sel[:, :], p.inp("selmat")[:, :]).then_inc(*sd[2].tick(16))

        @blk.vector
        def _(vector):
            W(vector, sd[0], 16)
            vector.tensor_copy(p.t["ggb"][:, :, :], gf[:, :, :])
            W(vector, sd[1], 16)
            vector.tensor_copy(p.t["seb"][:, :, :], sef[:, :, :])
            W(vector, sd[2], 16)
            vector.drain()
            inst = vector.tensor_copy(p.t["selb"][:, :], self_sel[:, :])
            p.mark("gconst", inst, 1)


def ph_ffn_gate(p, tag):
    p.phase_begin()
    nc = p.nc
    xn = p.t["xn"]
    comb_bc = p.arena("cbc" + tag, [128, NEXP, TPC], BF16, "BIG", 0)
    p.t["comb_bc"] = comb_bc
    sg = p.t["sg"]
    ggb = p.t["ggb"]
    seb = p.t["seb"]
    selb = p.t["selb"]
    comb_tm = p.arena("ctm" + tag, [128, NFC, 8], F32, "SCR", 0)
    comb_T = p.arena("cT" + tag, [8, NFC, 128], BF16, "SCR", 8192)
    row8 = p.arena("r8" + tag, [128, 8, 8], F32, "SCR", 12288)
    s_mm = p.new_sem(tag + "mm")
    s_cp = p.new_sem(tag + "cp")
    s_d = p.new_sem(tag + "d")
    s_w1 = p.new_sem(tag + "w1")
    s_cb = p.new_sem(tag + "cb")
    s_tr = p.new_sem(tag + "tr")
    s_tc = p.new_sem(tag + "tc")
    s_bc = p.new_sem(tag + "bc")
    s_bv = p.new_sem(tag + "bv")
    s_sgm = p.new_sem(tag + "sgm")
    s_sgs = p.new_sem(tag + "sgs")
    with (
        nc.psum_tensor(tag + "sc0", [128, 8], F32) as sc0,
        nc.psum_tensor(tag + "sc1", [128, 8], F32) as sc1,
        nc.psum_tensor(tag + "tr0", [8, 128], F32) as tr0,
        nc.psum_tensor(tag + "tr1", [8, 128], F32) as tr1,
        nc.psum_tensor(tag + "bc0", [128, 128], F32) as bc0,
        nc.psum_tensor(tag + "bc1", [128, 128], F32) as bc1,
        nc.psum_tensor(tag + "sgp", [1, TC], F32) as sgp,
        nc.Block() as blk,
    ):
        @blk.tensor
        def _(tensor):
            p.wait_mark(tensor, "xn")
            p.wait_mark(tensor, "gconst")
            for tt in range(NFC):
                if tt >= 2:
                    W(tensor, s_cp, tt - 1)
                sc = sc0 if tt % 2 == 0 else sc1
                inst = None
                for dc in range(NFC):
                    inst = tensor.matmul(sc[:, :], xn[:, dc, tt * 128:(tt + 1) * 128],
                                         ggb[:, dc, :], start=(dc == 0),
                                         stop=(dc == NFC - 1))
                inst.then_inc(*s_mm.tick(1))
            for tc in range(NTC):
                if tc >= 1:
                    W(tensor, s_sgs, tc)
                for dc in range(NFC):
                    inst = tensor.matmul(sgp[:, :], seb[:, dc, :],
                                         xn[:, dc, tc * TC:(tc + 1) * TC],
                                         start=(dc == 0), stop=(dc == NFC - 1))
                inst.then_inc(*s_sgm.tick(1))
            for tt in range(NFC):
                W(tensor, s_cb, tt + 1)
                if tt >= 2:
                    W(tensor, s_tc, tt - 1)
                tr = tr0 if tt % 2 == 0 else tr1
                tensor.transpose(tr[:, :], comb_tm[:, tt, :], p.t["identf"][:, :]).then_inc(*s_tr.tick(1))
            for e in range(NEXP):
                for tt in range(NFC):
                    i = e * NFC + tt
                    W(tensor, s_tc, NFC)
                    if i >= 2:
                        W(tensor, s_bv, i - 1)
                    bc = bc0 if i % 2 == 0 else bc1
                    tensor.matmul(bc[:, :], selb[:, e * 128:(e + 1) * 128],
                                  comb_T[:, tt, :], start=True, stop=True).then_inc(*s_bc.tick(1))

        @blk.vector
        def _(vector):
            for tt in range(NFC):
                W(vector, s_mm, tt + 1)
                sc = sc0 if tt % 2 == 0 else sc1
                s8 = row8[:, 0, :]
                m1 = row8[:, 1, 0:1]
                eq1 = row8[:, 2, :]
                s2 = row8[:, 3, :]
                m2 = row8[:, 4, 0:1]
                eq2 = row8[:, 5, :]
                dd = row8[:, 6, 0:1]
                w2 = row8[:, 7, 0:1]
                vector.tensor_copy(s8, sc[:, :]).then_inc(*s_cp.tick(1))
                vector.drain()
                vector.tensor_reduce(m1, s8, AX.X, ALU.max)
                vector.drain()
                vector.tensor_scalar(eq1, s8, m1, None, ALU.is_ge)
                vector.drain()
                vector.scalar_tensor_tensor(s2, eq1, NEG, s8, ALU.mult, ALU.add)
                vector.drain()
                vector.tensor_reduce(m2, s2, AX.X, ALU.max)
                vector.drain()
                vector.tensor_scalar(eq2, s2, m2, None, ALU.is_ge)
                vector.drain()
                vector.tensor_sub(dd, m1, m2).then_inc(*s_d.tick(1))
                W(vector, s_w1, tt + 1)
                vector.tensor_scalar(w2, row8[:, 1, 1:2], -1.0, 1.0, ALU.mult, ALU.add)
                vector.drain()
                vector.tensor_scalar(eq1, eq1, row8[:, 1, 1:2], None, ALU.mult)
                vector.drain()
                vector.scalar_tensor_tensor(comb_tm[:, tt, :], eq2, w2, eq1,
                                            ALU.mult, ALU.add).then_inc(*s_cb.tick(1))
                vector.drain()
            for tt in range(NFC):
                W(vector, s_tr, tt + 1)
                tr = tr0 if tt % 2 == 0 else tr1
                vector.tensor_copy(comb_T[:, tt, :], tr[:, :]).then_inc(*s_tc.tick(1))
            for e in range(NEXP):
                for tt in range(NFC):
                    i = e * NFC + tt
                    W(vector, s_bc, i + 1)
                    bc = bc0 if i % 2 == 0 else bc1
                    vector.tensor_copy(comb_bc[:, e, tt * 128:(tt + 1) * 128],
                                       bc[:, :]).then_inc(*s_bv.tick(1))
            vector.drain()
            inst = vector.tensor_copy(row8[:, 0, 0:1], row8[:, 0, 0:1])
            p.mark(tag + "_comb", inst, 1)

        @blk.scalar
        def _(scalar):
            for tt in range(NFC):
                W(scalar, s_d, tt + 1)
                scalar.activation(row8[:, 1, 1:2], row8[:, 6, 0:1], AF.Sigmoid).then_inc(*s_w1.tick(1))
            for tc in range(NTC):
                W(scalar, s_sgm, tc + 1)
                scalar.activation(sg[:, tc * TC:(tc + 1) * TC], sgp[:, :],
                                  AF.Sigmoid).then_inc(*s_sgs.tick(1))
            p.mark(tag + "_sg", scalar.nop(), 1)


def ph_ffn_hidden(p, tag, wgsrc, wusrc, n_hc, row0, comb_e, cache, crow0, comb_mark=None):
    p.phase_begin()
    nc = p.nc
    xn = p.t["xn"]
    wrb = p.arena("fhw" + tag, [128, 2, NFC, 128], BF16, "SCR", 0)
    hgs = p.arena("fhg" + tag, [128, TPC], BF16, "SCR", 8192)
    t1 = p.arena("fht" + tag, [128, 2, TPC], BF16, "SCR", 12288)
    wrf = p.arena("fhwf" + tag, [128, 2, NFC, 128], F32, "SCR", 20480)
    sw = p.slots(tag + "w", 2)
    shd = p.slots(tag + "hd", 2, "g")
    swc = p.new_sem(tag + "wc")
    s_g = p.new_sem(tag + "g")
    s_u = p.new_sem(tag + "u")
    s_sl = p.new_sem(tag + "sl")
    s_t = p.new_sem(tag + "t")
    with (
        nc.psum_tensor("fhpg" + tag, [128, TPC], F32) as pg,
        nc.psum_tensor("fhpu" + tag, [128, TPC], F32) as pu,
        nc.Block() as blk,
    ):
        @blk.sync
        def _(sync):
            for hc in range(n_hc):
                for half, src in ((0, wgsrc), (1, wusrc)):
                    i = 2 * hc + half
                    if i >= 2:
                        W(sync, swc, i - 1)
                    sync.dma_start(
                        wrf[:, half, :, :],
                        src[row0:row0 + D, hc * 128:(hc + 1) * 128]
                        .rearrange("(dc p) o -> p dc o", p=128),
                    ).then_inc(*sw[half].tick(16))

        @blk.tensor
        def _(tensor):
            p.wait_mark(tensor, "xn")
            for hc in range(n_hc):
                W(tensor, swc, 2 * hc + 1)
                if hc >= 1:
                    W(tensor, s_sl, hc)
                inst = None
                for dc in range(NFC):
                    for tc in range(NTC):
                        inst = tensor.matmul(pg[:, tc * TC:(tc + 1) * TC],
                                             wrb[:, 0, dc, :],
                                             xn[:, dc, tc * TC:(tc + 1) * TC],
                                             start=(dc == 0), stop=(dc == NFC - 1))
                inst.then_inc(*s_g.tick(1))
                W(tensor, swc, 2 * hc + 2)
                if hc >= 1:
                    W(tensor, s_t, hc)
                for dc in range(NFC):
                    for tc in range(NTC):
                        inst = tensor.matmul(pu[:, tc * TC:(tc + 1) * TC],
                                             wrb[:, 1, dc, :],
                                             xn[:, dc, tc * TC:(tc + 1) * TC],
                                             start=(dc == 0), stop=(dc == NFC - 1))
                inst.then_inc(*s_u.tick(1))

        @blk.scalar
        def _(scalar):
            for hc in range(n_hc):
                W(scalar, s_g, hc + 1)
                if hc >= 1:
                    W(scalar, s_t, hc)
                scalar.activation(hgs[:, :], pg[:, :], AF.Silu).then_inc(*s_sl.tick(1))

        @blk.vector
        def _(vector):
            if comb_mark is not None:
                p.wait_mark(vector, comb_mark)
            for hc in range(n_hc):
                for half in (0, 1):
                    i = 2 * hc + half
                    W(vector, sw[half], 16 * (hc + 1))
                    if i >= 4:
                        W(vector, s_g if half == 0 else s_u, hc - 1)
                    vector.tensor_copy(wrb[:, half, :, :],
                                       wrf[:, half, :, :]).then_inc(*swc.tick(1))
                W(vector, s_u, hc + 1)
                W(vector, s_sl, hc + 1)
                if hc >= 2:
                    W(vector, shd[hc % 2], 16 * (hc // 2))
                vector.tensor_mul(t1[:, hc % 2, :], pu[:, :], hgs[:, :])
                vector.drain()
                if comb_e is not None:
                    vector.tensor_mul(t1[:, hc % 2, :], t1[:, hc % 2, :], comb_e)
                    vector.drain()
                vector.tensor_copy(t1[:, hc % 2, 0:1],
                                   t1[:, hc % 2, 0:1]).then_inc(*s_t.tick(1))

        @blk.gpsimd
        def _(g):
            for hc in range(n_hc):
                W(g, s_t, hc + 1)
                g.dma_start(cache[crow0 + hc * 128:crow0 + (hc + 1) * 128, :],
                            t1[:, hc % 2, :]).then_inc(*shd[hc % 2].tick(16))
            for k in range(2):
                W(g, shd[k], 16 * ((n_hc - k + 1) // 2))
            p.mark("hc_" + tag, g.nop(), 1)


def ph_ffn_down(p, tag, xprev, xnext, ready_mark, hmark, shmark, sgmark,
                with_stats=True):
    p.phase_begin()
    nc = p.nc
    hsl = p.arena("fd_h" + tag, [128, NEXP, 8, TC], BF16, "BIG", 0)
    shsl = p.arena("fd_s" + tag, [128, NFC, TC], BF16, "XN", 0)
    wdr = p.arena("fd_w" + tag, [128, 2, NEXP, 8, 128], BF16, "XN", 16384)
    sdr = p.arena("fd_sw" + tag, [128, 2, NFC, 128], BF16, "XN", 49152)
    sgb = p.arena("fd_sg" + tag, [128, TC], F32, "SCR", 16384)
    xpr = p.arena("fd_xp" + tag, [128, 4, TC], F32, "SCR", 20480)
    xnw = p.arena("fd_xn" + tag, [128, 2, TC], F32, "SCR", 28672)
    tsh = p.arena("fd_t" + tag, [128, 2, TC], F32, "SCR", 32768)
    sg = p.t["sg"]
    wdb, shdb = p.t["wdb"], p.t["shdb"]
    sh_sl = p.slots(tag + "hs", 2, "g")
    ss_sl = p.slots(tag + "ss", 2, "g")
    sw_sl = p.slots(tag + "ws", 2)
    sd_sl = p.slots(tag + "ds", 2)
    sxp = p.slots(tag + "xp", 4)
    sxn = p.slots(tag + "xn", 2, "g")
    s_rm = p.new_sem(tag + "rm")
    s_shm = p.new_sem(tag + "shm")
    s_ev = p.new_sem(tag + "ev")
    s_sgb = p.new_sem(tag + "sgb")
    s_sgc = p.new_sem(tag + "sgc")
    s_sq = p.new_sem(tag + "sq")
    s_sm = p.new_sem(tag + "sm")
    s_fv = p.new_sem(tag + "fv")
    s_fs = p.new_sem(tag + "fs")
    sq4 = p.arena("dsq" + tag, [128, 2, TC], F32, "SCR", 4096)
    sel4f = p.t["sel4fs"]
    with (
        nc.psum_tensor("fdr0" + tag, [128, TC], F32) as pr0,
        nc.psum_tensor("fdr1" + tag, [128, TC], F32) as pr1,
        nc.psum_tensor("fds0" + tag, [128, TC], F32) as ps0,
        nc.psum_tensor("fds1" + tag, [128, TC], F32) as ps1,
        nc.psum_tensor("fdsg" + tag, [128, TC], F32) as psg,
        nc.psum_tensor("fdp4" + tag, [4, 2, TC], F32) as ps4,
        nc.Block() as blk,
    ):
        def stats_mm(tensor, i):
            tci = i // NFC
            W(tensor, s_ev, i + 1)
            tensor.matmul(ps4[:, 0, :], sel4f[:, tci * 4:(tci + 1) * 4],
                          xnw[:, i % 2, :], start=(i == 0), stop=(i == NFC * NTC - 1))
            W(tensor, s_sq, i + 1)
            tensor.matmul(ps4[:, 1, :], sel4f[:, tci * 4:(tci + 1) * 4],
                          sq4[:, i % 2, :], start=(i == 0),
                          stop=(i == NFC * NTC - 1)).then_inc(*s_sm.tick(1))
        @blk.gpsimd
        def _(g):
            p.wait_mark(g, hmark)
            p.wait_mark(g, shmark)
            def store_tile(i):
                W(g, s_ev, i + 1)
                tc_, oc_ = divmod(i, NFC)
                g.dma_start(xnext[oc_ * 128:(oc_ + 1) * 128, tc_ * TC:(tc_ + 1) * TC],
                            xnw[:, i % 2, :]).then_inc(*sxn[i % 2].tick(16))

            for tc in range(NTC):
                if tc >= 1:
                    for i in range((tc - 1) * NFC, tc * NFC):
                        store_tile(i)
                if tc >= 1:
                    W(g, s_rm, NFC * tc)
                    W(g, s_shm, NFC * tc)
                for e in range(NEXP):
                    g.dma_start(
                        hsl[:, e, :, :],
                        p.t["hcache"][e * HEXP:(e + 1) * HEXP, tc * TC:(tc + 1) * TC]
                        .rearrange("(hc pp) t -> pp hc t", pp=128),
                    ).then_inc(*sh_sl[tc % 2].tick(16))
                g.dma_start(
                    shsl[:, :, :],
                    p.t["shcache"][:, tc * TC:(tc + 1) * TC]
                    .rearrange("(hc pp) t -> pp hc t", pp=128),
                ).then_inc(*ss_sl[tc % 2].tick(16))
            for i in range((NTC - 1) * NFC, NTC * NFC):
                store_tile(i)
            for k in range(2):
                W(g, sxn[k], 16 * (NTC * NFC // 2))
            p.mark(ready_mark, g.nop(), 1)

        @blk.sync
        def _(sync):
            p.wait_mark(sync, "wcast")
            for tc in range(NTC):
                for oc in range(NFC):
                    i = tc * NFC + oc
                    if i >= 2:
                        W(sync, s_rm, i - 1)
                    for e in range(NEXP):
                        sync.dma_start(
                            wdr[:, i % 2, e, :, :],
                            wdb[e * HEXP:(e + 1) * HEXP, oc * 128:(oc + 1) * 128]
                            .rearrange("(hc pp) o -> pp hc o", pp=128),
                        ).then_inc(*sw_sl[i % 2].tick(16))
                    if i >= 2:
                        W(sync, s_shm, i - 1)
                    sync.dma_start(
                        sdr[:, i % 2, :, :],
                        shdb[:, oc * 128:(oc + 1) * 128]
                        .rearrange("(hc pp) o -> pp hc o", pp=128),
                    ).then_inc(*sd_sl[i % 2].tick(16))
                    if i >= 4:
                        W(sync, s_ev, i - 3)
                    sync.dma_start(xpr[:, i % 4, :],
                                   xprev[oc * 128:(oc + 1) * 128, tc * TC:(tc + 1) * TC]
                                   ).then_inc(*sxp[i % 4].tick(16))

        @blk.tensor
        def _(tensor):
            p.wait_mark(tensor, sgmark)
            for tc in range(NTC):
                if tc >= 1:
                    W(tensor, s_sgc, tc)
                tensor.matmul(psg[:, :], p.t["onesr"][:, :], sg[:, tc * TC:(tc + 1) * TC],
                              start=True, stop=True).then_inc(*s_sgb.tick(1))
                W(tensor, sh_sl[tc % 2], 128 * (tc // 2 + 1))
                W(tensor, ss_sl[tc % 2], 16 * (tc // 2 + 1))
                for oc in range(NFC):
                    i = tc * NFC + oc
                    if with_stats and i >= 2:
                        stats_mm(tensor, i - 2)
                    W(tensor, sw_sl[i % 2], 128 * (i // 2 + 1))
                    if i >= 2:
                        W(tensor, s_ev, i - 1)
                    pr = pr0 if i % 2 == 0 else pr1
                    ps = ps0 if i % 2 == 0 else ps1
                    inst = None
                    for e in range(NEXP):
                        for hc in range(8):
                            inst = tensor.matmul(pr[:, :], wdr[:, i % 2, e, hc, :],
                                                 hsl[:, e, hc, :],
                                                 start=(e == 0 and hc == 0),
                                                 stop=(e == NEXP - 1 and hc == 7))
                    inst.then_inc(*s_rm.tick(1))
                    W(tensor, sd_sl[i % 2], 16 * (i // 2 + 1))
                    for hc in range(NFC):
                        inst = tensor.matmul(ps[:, :], sdr[:, i % 2, hc, :],
                                             shsl[:, hc, :],
                                             start=(hc == 0), stop=(hc == NFC - 1))
                    inst.then_inc(*s_shm.tick(1))
            if with_stats:
                stats_mm(tensor, NFC * NTC - 2)
                stats_mm(tensor, NFC * NTC - 1)

        @blk.scalar
        def _(scalar):
            if with_stats:
                for i in range(NFC * NTC):
                    W(scalar, s_ev, i + 1)
                    if i >= 2:
                        W(scalar, s_sm, i - 1)
                    scalar.activation(sq4[:, i % 2, :], xnw[:, i % 2, :],
                                      AF.Square).then_inc(*s_sq.tick(1))
                _stats_fin_s(p, scalar, s_fv, s_fs)

        @blk.vector
        def _(vector):
            for tc in range(NTC):
                W(vector, s_sgb, tc + 1)
                vector.tensor_copy(sgb[:, :], psg[:, :]).then_inc(*s_sgc.tick(1))
                vector.drain()
                for oc in range(NFC):
                    i = tc * NFC + oc
                    W(vector, s_shm, i + 1)
                    if "nosgate" in p.dbg:
                        vector.tensor_copy(tsh[:, i % 2, :], (ps0 if i % 2 == 0 else ps1)[:, :])
                    else:
                        vector.tensor_mul(tsh[:, i % 2, :], (ps0 if i % 2 == 0 else ps1)[:, :],
                                          sgb[:, :])
                    if "noshared" in p.dbg:
                        vector.drain()
                        vector.memset(tsh[:, i % 2, :], 0.0)
                    vector.drain()
                    W(vector, s_rm, i + 1)
                    if "norouted" not in p.dbg:
                        vector.tensor_add(tsh[:, i % 2, :], (pr0 if i % 2 == 0 else pr1)[:, :],
                                          tsh[:, i % 2, :])
                        vector.drain()
                    W(vector, sxp[i % 4], 16 * (i // 4 + 1))
                    if i >= 2:
                        W(vector, sxn[i % 2], 16 * (i // 2))
                        if with_stats:
                            W(vector, s_sm, i - 1)
                    vector.tensor_add(xnw[:, i % 2, :], tsh[:, i % 2, :],
                                      xpr[:, i % 4, :]).then_inc(*s_ev.tick(1))
            if with_stats:
                W(vector, s_sm, NFC * NTC)
                _stats_fin_v1(p, vector, ps4, s_fv)
                _stats_fin_v2(p, vector, s_fs)


def ph_attn_out(p, tag, wt, xprev, xnext, ready_mark, with_stats=True):
    p.phase_begin()
    nc = p.nc
    ctxsb = p.arena("cxs" + tag, [128, NFC, TPC], BF16, "BIG")
    wrf = p.arena("owf" + tag, [128, 2, NFC, 128], F32, "SCR", 16384)
    wrb = p.arena("owb" + tag, [128, 2, NFC, 128], BF16, "SCR", 32768)
    xpr = p.arena("oxp" + tag, [128, 4, TC], F32, "SCR", 40960)
    xnw = p.arena("oxn" + tag, [128, 2, TC], F32, "SCR", 49152)
    ctxd = p.t["ctxd"]
    sw = p.slots(tag + "w", 2)
    scx = p.slots(tag + "cx", 2, "g")
    sxp = p.slots(tag + "xp", 4)
    sxn = p.slots(tag + "xn", 2, "g")
    swc = p.new_sem(tag + "wc")
    smm = p.new_sem(tag + "mm")
    sev = p.new_sem(tag + "ev")
    s_sq = p.new_sem(tag + "sq")
    s_sm = p.new_sem(tag + "sm")
    s_fv = p.new_sem(tag + "fv")
    s_fs = p.new_sem(tag + "fs")
    sq4 = p.arena("osq" + tag, [128, 2, TC], F32, "SCR", 53248)
    sel4f = p.t["sel4fs"]

    with (
        nc.psum_tensor("po" + tag, [128, TPC], F32) as pp,
        nc.psum_tensor("ps4" + tag, [4, 2, TC], F32) as ps4,
        nc.Block() as blk,
    ):
        def stats_mm(tensor, i):
            tc = i % NTC
            W(tensor, sev, i + 1)
            tensor.matmul(ps4[:, 0, :], sel4f[:, tc * 4:(tc + 1) * 4],
                          xnw[:, i % 2, :], start=(i == 0), stop=(i == NFC * NTC - 1))
            W(tensor, s_sq, i + 1)
            tensor.matmul(ps4[:, 1, :], sel4f[:, tc * 4:(tc + 1) * 4],
                          sq4[:, i % 2, :], start=(i == 0),
                          stop=(i == NFC * NTC - 1)).then_inc(*s_sm.tick(1))
        @blk.gpsimd
        def _(g):
            for fc in range(NFC):
                if fc >= 2:
                    W(g, scx[fc % 2], 16 * (fc // 2))
                g.dma_start(ctxsb[:, fc, :], ctxd[fc * 128:(fc + 1) * 128, :]
                            ).then_inc(*scx[fc % 2].tick(16))
            for i in range(NFC * NTC):
                W(g, sev, i + 1)
                oc, tc = divmod(i, NTC)
                g.dma_start(xnext[oc * 128:(oc + 1) * 128, tc * TC:(tc + 1) * TC],
                            xnw[:, i % 2, :]).then_inc(*sxn[i % 2].tick(16))
            for k in range(2):
                W(g, sxn[k], 16 * (NFC * NTC // 2))
            inst = g.nop()
            p.mark(ready_mark, inst, 1)

        @blk.sync
        def _(sync):
            for oc in range(NFC):
                if oc >= 2:
                    W(sync, swc, oc - 1)
                sync.dma_start(
                    wrf[:, oc % 2, :, :],
                    wt[:, oc * 128:(oc + 1) * 128].rearrange("(dc p) o -> p dc o", p=128),
                ).then_inc(*sw[oc % 2].tick(16))
                for tc in range(NTC):
                    i = oc * NTC + tc
                    if i >= 4:
                        W(sync, sev, i - 3)
                    sync.dma_start(xpr[:, i % 4, :],
                                   xprev[oc * 128:(oc + 1) * 128, tc * TC:(tc + 1) * TC]
                                   ).then_inc(*sxp[i % 4].tick(16))
            for k in range(2):
                W(sync, sw[k], 16 * (NFC // 2))

        @blk.scalar
        def _(scalar):
            if with_stats:
                for i in range(NFC * NTC):
                    W(scalar, sev, i + 1)
                    if i >= 2:
                        W(scalar, s_sm, i - 1)
                    scalar.activation(sq4[:, i % 2, :], xnw[:, i % 2, :],
                                      AF.Square).then_inc(*s_sq.tick(1))
                _stats_fin_s(p, scalar, s_fv, s_fs)

        @blk.vector
        def _(vector):
            for oc in range(NFC):
                W(vector, sw[oc % 2], 16 * (oc // 2 + 1))
                if oc >= 2:
                    W(vector, smm, oc - 1)
                vector.tensor_copy(wrb[:, oc % 2, :, :], wrf[:, oc % 2, :, :]).then_inc(*swc.tick(1))
                W(vector, smm, oc + 1)
                for tc in range(NTC):
                    i = oc * NTC + tc
                    W(vector, sxp[i % 4], 16 * (i // 4 + 1))
                    if i >= 2:
                        W(vector, sxn[i % 2], 16 * (i // 2))
                        if with_stats:
                            W(vector, s_sm, i - 1)
                    vector.tensor_add(xnw[:, i % 2, :], pp[:, tc * TC:(tc + 1) * TC],
                                      xpr[:, i % 4, :]).then_inc(*sev.tick(1))
            if with_stats:
                W(vector, s_sm, NFC * NTC)
                _stats_fin_v1(p, vector, ps4, s_fv)
                _stats_fin_v2(p, vector, s_fs)

        @blk.tensor
        def _(tensor):
            for oc in range(NFC):
                if with_stats and oc >= 1:
                    lo = 0 if oc == 1 else 4 * oc - 5
                    for i in range(lo, 4 * oc - 1):
                        stats_mm(tensor, i)
                W(tensor, swc, oc + 1)
                if oc == 0:
                    for k in range(2):
                        W(tensor, scx[k], 16 * (NFC // 2))
                if oc >= 1:
                    W(tensor, sev, 4 * oc)
                inst = None
                for dc in range(NFC):
                    for tc in range(NTC):
                        inst = tensor.matmul(pp[:, tc * TC:(tc + 1) * TC],
                                             wrb[:, oc % 2, dc, :],
                                             ctxsb[:, dc, tc * TC:(tc + 1) * TC],
                                             start=(dc == 0), stop=(dc == NFC - 1))
                inst.then_inc(*smm.tick(1))
            if with_stats:
                for i in range(4 * (NFC - 1) - 1, 4 * NFC):
                    stats_mm(tensor, i)



def _sub_ffn(p, tag, xprev, xnext, prev_mark, ready_mark, first,
             last=False):
    u = p.upto
    ph_xn(p, xprev, tag + "xn", wait_marks=(prev_mark,))
    if first:
        ph_gconst(p)
    if u < 2.4:
        return False
    ph_ffn_gate(p, tag + "g")
    if "comb" in p.dbg:
        _dbg_dump_fm(p, "comb_bc", "dbg_comb", nfc=NEXP, width=TPC,
                     marks=(tag + "g_comb",))
    if u < 2.6:
        return False
    nexp = 2 if "emini" in p.dbg else NEXP
    wgf = p.inp("wg_t").ap().rearrange("e d h -> (e d) h")
    wuf = p.inp("wu_t").ap().rearrange("e d h -> (e d) h")
    for e in range(nexp):
        ph_ffn_hidden(p, tag + "h%d" % e, wgf, wuf, 8, e * D,
                      p.t["comb_bc"][:, e, :],
                      p.t["hcache"], e * HEXP, comb_mark=tag + "g_comb")
        if first:
            ph_wcast(p, e, e + 1, with_shd=(e == nexp - 1))
    if "hc" in p.dbg:
        _dbg_dram(p, "hcache", "dbg_hc", [NEXP * HEXP, TPC], BF16,
                  n_rows=nexp * HEXP)
    if u < 2.8:
        return False
    ph_ffn_hidden(p, tag + "sh", p.inp("shg_t").ap(), p.inp("shu_t").ap(),
                  NFC, 0, None, p.t["shcache"], 0)
    if "shc" in p.dbg:
        _dbg_dram(p, "shcache", "dbg_shc", [HSH, TPC], BF16)
    if u < 3:
        return False
    ph_ffn_down(p, tag + "d", xprev, xnext, ready_mark,
                "hc_" + tag + "h%d" % (NEXP - 1), "hc_" + tag + "sh",
                tag + "g_sg", with_stats=not last)
    return True



def build_program(upto=99, dbg=()):
    nc = bass.Bass("TRN2", target_bir_lowering=False)
    p = Prog(nc, upto, dbg)

    x0 = nc.dram_tensor("x0", [D, TPC], F32, kind="ExternalInput")
    xout = nc.dram_tensor("xout", [D, TPC], F32, kind="ExternalOutput")
    p.t["x0"], p.t["xout"] = x0, xout
    p.used_inputs = {"x0"}
    _in_shapes = {"wq_t": [D, D], "wk_t": [D, D], "wv_t": [D, D],
                  "wo_t": [D, D], "bwq": [D, D], "bwkv": [D, 1024],
                  "bwd": [D, D], "ggate": [D, 8],
                  "wg_t": [NEXP, D, HEXP], "wu_t": [NEXP, D, HEXP],
                  "wd_t": [NEXP, HEXP, D], "shg_t": [D, HSH],
                  "shu_t": [D, HSH], "shd_t": [HSH, D], "se_t": [D, 1],
                  "tri01": [512, TC], "ident": [128, 128], "ind": [128, 64],
                  "selmat": [8, 1024], "bmask": [NTC * 16 * 128, TC],
                  "sel4f": [128, 16], "sel4t": [4, 512]}

    def inp(nm):
        if nm not in p.t:
            dt = BF16 if nm == "bmask" else F32
            p.t[nm] = nc.dram_tensor(nm, _in_shapes[nm], dt, kind="ExternalInput")
            p.used_inputs.add(nm)
        return p.t[nm]

    p.inp = inp

    p.t["xd1"] = nc.dram_tensor("xd1", [D, TPC], F32)
    p.t["xd2"] = nc.dram_tensor("xd2", [D, TPC], F32)
    p.t["kcc_in"] = nc.dram_tensor("kcc_in", [D, TPC], BF16)
    p.t["kcc_out"] = nc.dram_tensor("kcc_out", [4 * D, TPC], BF16)
    p.t["vcc_in"] = nc.dram_tensor("vcc_in", [TPC, D], BF16)
    p.t["vcc_out"] = nc.dram_tensor("vcc_out", [4 * TPC, D], BF16)
    p.t["nrma_in"] = nc.dram_tensor("nrma_in", [1, 16], F32)
    p.t["nrma_out"] = nc.dram_tensor("nrma_out", [1, 16], F32)
    p.t["nrmb_in"] = nc.dram_tensor("nrmb_in", [1, 4], F32)
    p.t["nrmb_out"] = nc.dram_tensor("nrmb_out", [1, 4], F32)
    p.t["kbcc_in"] = nc.dram_tensor("kbcc_in", [512, TPC], BF16)
    p.t["kbcc_out"] = nc.dram_tensor("kbcc_out", [2048, TPC], BF16)
    p.t["vbcc_in"] = nc.dram_tensor("vbcc_in", [TPC, 512], BF16)
    p.t["vbcc_out"] = nc.dram_tensor("vbcc_out", [4 * TPC, 512], BF16)
    p.t["ctxd"] = nc.dram_tensor("ctxd", [D, TPC], BF16)
    p.t["hcache"] = nc.dram_tensor("hcache", [NEXP * HEXP, TPC], BF16)
    p.t["shcache"] = nc.dram_tensor("shcache", [HSH, TPC], BF16)
    p.t["wdb"] = nc.dram_tensor("wdb", [NEXP * HEXP, D], BF16)
    p.t["shdb"] = nc.dram_tensor("shdb", [HSH, D], BF16)

    if "xn" in p.dbg:
        p.t["dbg_xn"] = nc.dram_tensor("dbg_xn", [D, TPC], F32, kind="ExternalOutput")

    p.t["xn"] = p.arena("xn", [128, NFC, TPC], BF16, "XN")
    p.t["stat"] = p.arena("stat", [4, 2, TC], F32, "SCR", 0)
    p.t["r4"] = p.arena("r4", [4, 4, TC], F32, "SCR", 8192)
    p.t["xring"] = p.arena("xring", [128, 4, TC], F32, "SCR", 16384)
    p.t["bcb"] = p.arena("bcb", [128, 2, TC], F32, "SCR", 24576)
    p.t["ones1f"] = nc.alloc_sbuf_tensor("ones1f", [128, 1], F32)
    p.t["ones1b"] = nc.alloc_sbuf_tensor("ones1b", [128, 1], BF16)
    p.t["onesr"] = nc.alloc_sbuf_tensor("onesr", [1, 128], F32)
    p.t["rowtmp"] = nc.alloc_sbuf_tensor("rowtmp", [1, 4, TC], F32)
    p.t["nrows"] = nc.alloc_sbuf_tensor("nrows", [1, 8, NFC], F32)
    p.t["negMa"] = nc.alloc_sbuf_tensor("negMa", [128, NFC], F32)
    p.t["negMb"] = nc.alloc_sbuf_tensor("negMb", [128, NFC], F32)
    p.t["identf"] = nc.alloc_sbuf_tensor("identf", [128, 128], F32)
    p.t["identb"] = nc.alloc_sbuf_tensor("identb", [128, 128], BF16)
    p.t["sel4fs"] = nc.alloc_sbuf_tensor("sel4fs", [128, 16], F32)
    p.t["sel4ts"] = nc.alloc_sbuf_tensor("sel4ts", [4, 512], F32)

    sid = p.new_sem("cinit", "s")
    with nc.Block() as blk:
        @blk.sync
        def _(sync):
            sync.dma_start(p.t["identf"][:, :], p.inp("ident")[:, :]).then_inc(*sid.tick(16))
            sync.dma_start(p.t["sel4fs"][:, :], p.inp("sel4f")[:, :]).then_inc(*sid.tick(16))
            sync.dma_start(p.t["sel4ts"][:, :], p.inp("sel4t")[:, :]).then_inc(*sid.tick(16))

        @blk.vector
        def _(vector):
            p.mark("const", vector.memset(p.t["ones1f"][:, :], 1.0), 1)
            p.mark("const", vector.memset(p.t["ones1b"][:, :], 1.0), 1)
            p.mark("const", vector.memset(p.t["onesr"][:, :], 1.0), 1)
            W(vector, sid, 48)
            p.mark("const", vector.tensor_copy(p.t["identb"][:, :], p.t["identf"][:, :]), 1)

    ph_stats(p, x0, "st0")
    ph_xn(p, x0, "xn0")

    if "xn" in p.dbg:
        _dbg_dump_fm(p, "xn", "dbg_xn")

    if upto >= 1:
        Q = p.arena("Q", [128, NFC, TPC], BF16, "BIG")
        p.t["Q"] = Q
    if upto >= 1.1:
        ph_proj(p, "wk", p.inp("wk_t"), "kf", NFC, dst=p.t["kcc_in"], norm_row=1)
        if "kin" in p.dbg:
            _dbg_dram(p, "kcc_in", "dbg_kin", [D, TPC], BF16)
    if upto >= 1.2:
        ph_cc(p, "a", [("kcc_in", "kcc_out")], krow=1, grow=2,
              nin=p.t["nrma_in"], nout=p.t["nrma_out"], nrm_mark="wk_nrm")
    if upto >= 1.15:
        ph_proj(p, "wv", p.inp("wv_t"), "vt", NFC, dst=p.t["vcc_in"])
        if "vin" in p.dbg:
            _dbg_dram(p, "vcc_in", "dbg_vin", [TPC, D], BF16)
    if upto >= 1.2:
        ph_cc_gather(p, "av", [("vcc_in", "vcc_out")])
    if upto >= 1:
        ph_proj(p, "wq", p.inp("wq_t"), "q", NFC, dst=Q, norm_row=0)
    if upto >= 1.5:
        ph_attn_a(p)
        if "ctx" in p.dbg:
            _dbg_dram(p, "ctxd", "dbg_ctx", [D, TPC], BF16,
                      n_rows=(256 if "mini" in p.dbg else None))
    cur_x, cur_mark = x0, None
    if upto >= 2:
        ph_attn_out(p, "wo", p.inp("wo_t"), x0, p.t["xd1"], "x1rdy",
                    with_stats="nostat" not in p.dbg)
        cur_x, cur_mark = p.t["xd1"], "x1rdy"
    if upto >= 2.2:
        done = _sub_ffn(p, "f1", p.t["xd1"], p.t["xd2"], "x1rdy", "x2rdy", first=True)
        if done:
            cur_x, cur_mark = p.t["xd2"], "x2rdy"
    if upto >= 3.2:
        ph_xn(p, p.t["xd2"], "bxn", wait_marks=("x2rdy",))
        ph_proj(p, "bk", p.inp("bwkv"), "kf", 4, dst=p.t["kbcc_in"], norm_row=4)
        ph_cc(p, "b", [("kbcc_in", "kbcc_out")], krow=4, grow=5,
              nin=p.t["nrmb_in"], nout=p.t["nrmb_out"], nrm_mark="bk_nrm")
        ph_proj(p, "bv", p.inp("bwkv"), "vt", 4, dst=p.t["vbcc_in"], wt_col0=512)
        ph_cc_gather(p, "bv", [("vbcc_in", "vbcc_out")])
        ph_proj(p, "bq", p.inp("bwq"), "q", NFC, dst=p.t["Q"], norm_row=3)
        ph_attn_b(p)
        if "ctxb" in p.dbg:
            _dbg_dram(p, "ctxd", "dbg_ctxb", [D, TPC], BF16)
    if upto >= 4:
        ph_attn_out(p, "bo", p.inp("bwd"), p.t["xd2"], p.t["xd1"], "x3rdy")
        cur_x, cur_mark = p.t["xd1"], "x3rdy"
    if upto >= 5:
        done = _sub_ffn(p, "f2", p.t["xd1"], xout, "x3rdy", "x4rdy", first=False,
                        last=True)
        if done:
            cur_x, cur_mark = xout, "x4rdy"

    if "q" in p.dbg:
        _dbg_dump_fm(p, "Q", "dbg_q", marks=("wq_ev",))
    if "kv" in p.dbg:
        _dbg_dram(p, "kcc_out", "dbg_k", [4 * D, TPC], BF16)
        _dbg_dram(p, "vcc_out", "dbg_v", [4 * TPC, D], BF16)
    if "negm" in p.dbg:
        _dbg_rows(p, "negMa", "dbg_negm", NFC, marks=("negm_a",))

    if cur_x is not xout:
        sg = p.new_sem("out", "g")
        with nc.Block() as blk:
            @blk.gpsimd
            def _(g):
                if cur_mark is not None:
                    p.wait_mark(g, cur_mark)
                g.dma_start(p.t["xout"][:, :], cur_x[:, :]).then_inc(*sg.tick(16))
                W(g, sg, 16)

    p.ctx.close()
    nc.finalize()
    return nc, set(p.used_inputs)



def _run(inputs, upto=99, dbg=(), _nc_out=None):
    x = np.asarray(inputs["L0_attn_a_x"], dtype=np.float32)
    shards = _shard_x(x)
    shared = _prep_inputs(inputs)
    nc, used = build_program(upto=upto, dbg=dbg)
    if _nc_out is not None:
        _nc_out.append(nc)
    in_maps = []
    for core in range(NCORE):
        m = dict(shared)
        m.update(_host_tables(core))
        m["x0"] = shards[core]
        in_maps.append({k: v for k, v in m.items() if k in used})
    return run_bass_kernel_spmd(nc, in_maps, core_ids=list(range(NCORE)))


def kernel(**inputs):
    res = _run(inputs)
    outs = [res.results[c]["xout"] for c in range(NCORE)]
    return _unshard_x(outs)


# version 05
# speedup vs baseline: 1.0827x; 1.0199x over previous
import os
import sys
import numpy as np

sys.path.insert(0, "/opt/trn_rl_repo")

import concourse.bass as bass
import concourse.mybir as mybir
from concourse.bass_utils import run_bass_kernel_spmd

F32 = mybir.dt.float32
BF16 = mybir.dt.bfloat16
I32 = mybir.dt.int32
AF = mybir.ActivationFunctionType
ALU = mybir.AluOpType
AX = mybir.AxisListType

B = 2
S = 8192
D = 2048
NCORE = 8
TPC = 2048
NTC = 4
TC = 512
NFC = 16
HEADS = 16
HD = 128
GROUPS = 4
NEXP = 8
HEXP = 1024
HSH = 2048
EPS = 1e-5
SM_SCALE = float(HD) ** -0.5
NEG = -1.0e9


def _gchunk(c):
    return [4 * j + c for j in range(NTC)]


def _phys_chunk(w):
    g = w // 4
    return (g % 4) * 16 + (g // 4) * 4 + (w % 4)



def _shard_x(x):
    shards = []
    for core in range(NCORE):
        s, c = divmod(core, 4)
        toks = np.concatenate([x[s, g * TC:(g + 1) * TC, :] for g in _gchunk(c)], axis=0)
        shards.append(np.ascontiguousarray(toks.T))
    return shards


def _unshard_x(shards):
    out = np.empty((B, S, D), dtype=np.float32)
    for core in range(NCORE):
        s, c = divmod(core, 4)
        tok = np.ascontiguousarray(np.asarray(shards[core]).T)
        for j, g in enumerate(_gchunk(c)):
            out[s, g * TC:(g + 1) * TC, :] = tok[j * TC:(j + 1) * TC, :]
    return out


def _host_consts():
    tri = np.zeros((4, 128, TC), dtype=np.float32)
    ql = np.arange(TC)[None, :]
    kl = np.arange(128)[:, None]
    for t in range(4):
        tri[t] = (ql >= 128 * t + kl).astype(np.float32)
    sel4f = np.zeros((128, 16), dtype=np.float32)
    sel4t = np.zeros((4, 512), dtype=np.float32)
    for tc in range(4):
        sel4f[:, tc * 4 + tc] = 1.0
        sel4t[tc, tc * 128:(tc + 1) * 128] = 1.0
    return {"tri01": tri.reshape(512, TC), "ident": np.eye(128, dtype=np.float32),
            "sel4f": sel4f, "sel4t": sel4t}


def _host_tables(core):
    import ml_dtypes
    s, c = divmod(core, 4)
    ind = np.zeros((128, 64), dtype=np.float32)
    tri = np.zeros((4, 128, TC), dtype=np.float32)
    ql = np.arange(TC)[None, :]
    kl = np.arange(128)[:, None]
    for t in range(4):
        tri[t] = (ql >= 128 * t + kl).astype(np.float32)
    bmask = np.zeros((NTC, 16, 128, TC), dtype=np.float32)
    for j in range(NTC):
        for u in range(16):
            col = j * 16 + u
            if u < 4 * c:
                bmask[j, u] = 1.0
            elif u < 4 * c + 4:
                bmask[j, u] = tri[u - 4 * c]
            else:
                ind[:, col] = NEG
    bmask = bmask.reshape(NTC * 16 * 128, TC).astype(ml_dtypes.bfloat16)
    return {"ind": ind, "bmask": bmask}


def _prep_inputs(inputs):
    a = {k: np.asarray(v) for k, v in inputs.items()}
    T = lambda m: np.ascontiguousarray(np.asarray(m).T)
    shared = {
        "wq_t": T(a["L0_attn_a_Wq"]),
        "wk_t": T(a["L0_attn_a_Wk"]),
        "wv_t": T(a["L0_attn_a_Wv"]),
        "wo_t": T(a["L0_attn_a_Wo"]),
        "bwq": np.ascontiguousarray(a["L0_attn_b_w_q"]),
        "bwkv": np.ascontiguousarray(a["L0_attn_b_w_kv"]),
        "bwd": np.ascontiguousarray(a["L0_attn_b_w_dense"]),
        "ggate": T(a["L0_ffn_gate_w"]),
        "wg_t": np.ascontiguousarray(np.transpose(a["L0_ffn_w_gate"], (0, 2, 1))),
        "wu_t": np.ascontiguousarray(np.transpose(a["L0_ffn_w_up"], (0, 2, 1))),
        "wd_t": np.ascontiguousarray(np.transpose(a["L0_ffn_w_down"], (0, 2, 1))),
        "shg_t": T(a["L0_ffn_sh_gate"]),
        "shu_t": T(a["L0_ffn_sh_up"]),
        "shd_t": T(a["L0_ffn_sh_down"]),
        "se_t": T(a["L0_ffn_se_gate_w"]),
    }
    sel = np.zeros((8, 8 * 128), dtype=np.float32)
    for e in range(8):
        sel[e, e * 128:(e + 1) * 128] = 1.0
    shared["selmat"] = sel
    shared.update(_host_consts())
    return shared



class RSem:

    def __init__(self, h, b):
        self.h = h
        self.b = b
        self.t = 0

    def tick(self, n):
        self.t += n
        return (self.h, n)


def W(eng, s, v):
    if isinstance(s, RSem):
        eng.wait_ge(s.h, s.b + v)
    else:
        eng.wait_ge(s, v)


class Prog:
    def __init__(self, nc, upto, dbg):
        import contextlib
        self.nc = nc
        self.upto = upto
        self.dbg = set(dbg)
        self.ctx = contextlib.ExitStack()
        self.t = {}
        self._marks = {}
        self._nsem = 0
        self._pool = {}
        self._phase = 0
        self._rentals = {}

    def phase_begin(self):
        self._phase += 1
        for ph in list(self._rentals):
            if ph <= self._phase - 2:
                for klass, rs in self._rentals.pop(ph):
                    self._pool.setdefault(klass, []).append((rs.h, rs.b + rs.t))

    def _raw_sem(self, name):
        self._nsem += 1
        return self.ctx.enter_context(self.nc.semaphore(f"{name}_{self._nsem}"))

    def new_sem(self, name, klass="c"):
        pool = self._pool.setdefault(klass, [])
        if "nopool" in self.dbg:
            pool = []
        if pool:
            h, b = pool.pop()
            rs = RSem(h, b)
        else:
            rs = RSem(self._raw_sem(klass + name), 0)
        self._rentals.setdefault(self._phase, []).append((klass, rs))
        return rs

    def mark(self, name, inst, n=1):
        if name not in self._marks or self._marks[name][2]:
            self._marks[name] = [self._raw_sem("mk_" + name), 0, False]
        m = self._marks[name]
        inst.then_inc(m[0], n)
        m[1] += n

    def alias_mark(self, name, sem, total):
        if isinstance(sem, RSem):
            self._marks[name] = [sem.h, sem.b + total, True]
        else:
            self._marks[name] = [sem, total, True]

    def wait_mark(self, eng, name):
        m = self._marks[name]
        eng.wait_ge(m[0], m[1])

    def slots(self, name, n, klass="s"):
        return [self.new_sem(f"{name}{k}", klass) for k in range(n)]

    def arena(self, name, shape, dtype, region, off=0):
        base = {"XN": 159744, "BIG": 94208, "SCR": 36864}[region]
        return self.nc.alloc_sbuf_tensor_at(name, shape, dtype, offset=base + off)



def _stats_fin_v1(p, vector, ps4, sv):
    r4 = p.t["r4"]
    mean, ex2, var = r4[:, 0, :], r4[:, 1, :], r4[:, 2, :]
    vector.tensor_scalar(mean, ps4[:, 0, :], 1.0 / D, None, ALU.mult)
    vector.tensor_scalar(ex2, ps4[:, 1, :], 1.0 / D, None, ALU.mult)
    vector.drain()
    vector.tensor_mul(var, mean, mean)
    vector.drain()
    vector.scalar_tensor_tensor(var, ex2, EPS, var, ALU.add,
                                ALU.subtract).then_inc(*sv.tick(1))


def _stats_fin_s(p, scalar, sv, ss):
    r4 = p.t["r4"]
    stat = p.t["stat"]
    W(scalar, sv, 1)
    scalar.drain()
    scalar.activation(r4[:, 3, :], r4[:, 2, :], AF.Ln)
    scalar.drain()
    scalar.activation(stat[:, 1, :], r4[:, 3, :], AF.Exp,
                      scale=-0.5).then_inc(*ss.tick(1))


def _stats_fin_v2(p, vector, ss):
    r4 = p.t["r4"]
    stat = p.t["stat"]
    W(vector, ss, 1)
    inst = vector.tensor_mul(stat[:, 0, :], r4[:, 0, :], stat[:, 1, :])
    p.mark("stat", inst, 1)
    vector.drain()


def ph_stats(p, xsrc, tag, wait_marks=()):
    p.phase_begin()
    nc = p.nc
    xr = p.t["xring"]
    sel4f = p.t["sel4fs"]
    sx = p.slots(tag + "x", 4)
    sq_done = p.new_sem(tag + "q")
    mm = p.new_sem(tag + "m")
    sv = p.new_sem(tag + "v")
    ss = p.new_sem(tag + "s")
    with (
        nc.psum_tensor("ps" + tag, [4, 2, TC], F32) as ps4,
        nc.sbuf_tensor("sq" + tag, [128, 2, TC], F32) as sq,
        nc.Block() as blk,
    ):
        @blk.sync
        def _(sync):
            for mname in wait_marks:
                p.wait_mark(sync, mname)
            for i in range(NTC * NFC):
                if i >= 4:
                    W(sync, mm, 2 * (i - 3))
                sync.dma_start(xr[:, i % 4, :],
                               xsrc[(i % NFC) * 128:(i % NFC + 1) * 128,
                                    (i // NFC) * TC:(i // NFC + 1) * TC]).then_inc(*sx[i % 4].tick(16))
            for k in range(4):
                W(sync, sx[k], 16 * (NTC * NFC // 4))

        @blk.tensor
        def _(tensor):
            p.wait_mark(tensor, "const")
            for i in range(NTC * NFC):
                tc = i // NFC
                W(tensor, sx[i % 4], 16 * (i // 4 + 1))
                tensor.matmul(ps4[:, 0, :], sel4f[:, tc * 4:(tc + 1) * 4],
                              xr[:, i % 4, :], start=(i == 0), stop=(i == NTC * NFC - 1))
                W(tensor, sq_done, i + 1)
                tensor.matmul(ps4[:, 1, :], sel4f[:, tc * 4:(tc + 1) * 4],
                              sq[:, i % 2, :], start=(i == 0),
                              stop=(i == NTC * NFC - 1)).then_inc(*mm.tick(2))

        @blk.scalar
        def _(scalar):
            for i in range(NTC * NFC):
                W(scalar, sx[i % 4], 16 * (i // 4 + 1))
                if i >= 2:
                    W(scalar, mm, 2 * (i - 1))
                scalar.activation(sq[:, i % 2, :], xr[:, i % 4, :],
                                  AF.Square).then_inc(*sq_done.tick(1))
            _stats_fin_s(p, scalar, sv, ss)

        @blk.vector
        def _(vector):
            W(vector, mm, 2 * NTC * NFC)
            _stats_fin_v1(p, vector, ps4, sv)
            _stats_fin_v2(p, vector, ss)


def ph_xn(p, xsrc, tag, wait_marks=()):
    p.phase_begin()
    nc = p.nc
    xbig = p.arena("xnb" + tag, [128, 2, NFC, TC], F32, "BIG")
    xn = p.t["xn"]
    stat = p.t["stat"]
    bb = p.t["bcb"]
    sx = p.slots(tag + "x", 2)
    mm = p.new_sem(tag + "m")
    sv = p.new_sem(tag + "v")
    with (
        nc.psum_tensor("ps" + tag, [128, 2, TC], F32) as ps,
        nc.Block() as blk,
    ):
        @blk.sync
        def _(sync):
            for mname in wait_marks:
                p.wait_mark(sync, mname)
            for tc in range(NTC):
                for fc in range(NFC):
                    if tc >= 2 and fc == 0:
                        W(sync, sv, tc - 1)
                    sync.dma_start(xbig[:, tc % 2, fc, :],
                                   xsrc[fc * 128:(fc + 1) * 128,
                                        tc * TC:(tc + 1) * TC]).then_inc(*sx[tc % 2].tick(16))
            for k in range(2):
                W(sync, sx[k], 16 * NFC * (NTC // 2))

        @blk.tensor
        def _(tensor):
            p.wait_mark(tensor, "stat")
            sel4t = p.t["sel4ts"]
            for tc in range(NTC):
                if tc >= 1:
                    W(tensor, sv, tc)
                tensor.matmul(ps[:, 0, :], sel4t[:, tc * 128:(tc + 1) * 128],
                              stat[:, 1, :], start=True, stop=True)
                tensor.matmul(ps[:, 1, :], sel4t[:, tc * 128:(tc + 1) * 128],
                              stat[:, 0, :], start=True, stop=True).then_inc(*mm.tick(1))

        @blk.vector
        def _(vector):
            for tc in range(NTC):
                W(vector, mm, tc + 1)
                if tc >= 1:
                    vector.drain()
                vector.tensor_copy(bb[:, 0, :], ps[:, 0, :])
                vector.tensor_copy(bb[:, 1, :], ps[:, 1, :])
                vector.drain()
                W(vector, sx[tc % 2], 16 * NFC * (tc // 2 + 1))
                for fc in range(NFC):
                    vector.tensor_mul(xbig[:, tc % 2, fc, :], xbig[:, tc % 2, fc, :],
                                      bb[:, 0, :])
                vector.drain()
                for fc in range(NFC - 1):
                    vector.tensor_sub(xn[:, fc, tc * TC:(tc + 1) * TC],
                                      xbig[:, tc % 2, fc, :], bb[:, 1, :])
                vector.drain()
                vector.tensor_sub(xn[:, NFC - 1, tc * TC:(tc + 1) * TC],
                                  xbig[:, tc % 2, NFC - 1, :], bb[:, 1, :]).then_inc(*sv.tick(1))
            p.alias_mark("xn", sv, NTC)


def _dbg_rows(p, srcname, outname, w, marks=()):
    p.phase_begin()
    nc = p.nc
    out = nc.dram_tensor(outname, [128, w], F32, kind="ExternalOutput")
    sd = p.new_sem("dbgr", "g")
    with nc.Block() as blk:
        @blk.gpsimd
        def _(g):
            for m in marks:
                p.wait_mark(g, m)
            g.dma_start(out[:, :], p.t[srcname][:, :]).then_inc(*sd.tick(16))
            W(g, sd, 16)


def _dbg_dump_fm(p, srcname, outname, nfc=NFC, width=TPC, marks=("xn",)):
    p.phase_begin()
    nc = p.nc
    src = p.t[srcname]
    out = p.t.get(outname)
    if out is None:
        out = nc.dram_tensor(outname, [nfc * 128, width], F32, kind="ExternalOutput")
        p.t[outname] = out
    sv = p.new_sem("dmp")
    sx = p.slots("dmpx", 2)
    with (
        nc.sbuf_tensor("dmp" + outname, [128, 2, width], F32) as tmp,
        nc.Block() as blk,
    ):
        @blk.vector
        def _(vector):
            for m in marks:
                p.wait_mark(vector, m)
            for fc in range(nfc):
                if fc >= 2:
                    W(vector, sx[fc % 2], 16 * ((fc - 2) // 2 + 1))
                vector.tensor_copy(tmp[:, fc % 2, :], src[:, fc, :]).then_inc(*sv.tick(1))

        @blk.sync
        def _(sync):
            for fc in range(nfc):
                W(sync, sv, fc + 1)
                sync.dma_start(out[fc * 128:(fc + 1) * 128, :],
                               tmp[:, fc % 2, :]).then_inc(*sx[fc % 2].tick(16))
            for k in range(2):
                W(sync, sx[k], 16 * (nfc // 2))



def ph_proj(p, tag, wt, kind, n_oc, *, dst, dst_col0=0, dst_row0=0,
            norm_row=None, norm_col0=0, wt_col0=0, n_dc=NFC):
    p.phase_begin()
    nc = p.nc
    xn = p.t["xn"]
    nrows = p.t.get("nrows")
    row = p.t["rowtmp"]
    sw = p.slots(tag + "w", 2)
    swc = p.new_sem(tag + "wc")
    smm = p.new_sem(tag + "mm")
    sev = p.new_sem(tag + "ev")
    ssq = p.new_sem(tag + "sq")
    snm = p.new_sem(tag + "nm")
    snr = p.new_sem(tag + "nr")
    svt = p.new_sem(tag + "vt")
    svc = p.new_sem(tag + "vc")
    skd = p.slots(tag + "kd", 2, "g")
    if "nonorm" in p.dbg:
        norm_row = None
    normed = norm_row is not None

    wrf = p.arena("wrf" + tag, [128, 2, n_dc, 128], F32, "SCR", 16384)
    wrb = p.arena("wrb" + tag, [128, 2, n_dc, 128], BF16, "SCR", 32768)
    stg = p.arena("stg" + tag, [128, 2, TPC], BF16, "SCR", 40960)
    vstg = p.arena("vsg" + tag, [128, 2, NFC, 128], BF16, "SCR", 49152)
    with (
        nc.sbuf_tensor("sqn" + tag, [128, TPC], BF16) as sqn,
        nc.psum_tensor("pp" + tag, [128, TPC], F32) as pp,
        nc.psum_tensor("vt0" + tag, [128, 128], BF16) as vtp0,
        nc.psum_tensor("vt1" + tag, [128, 128], BF16) as vtp1,
        nc.psum_tensor("np" + tag, [1, 2, TC], F32) as nps,
        nc.Block() as blk,
    ):
        @blk.sync
        def _(sync):
            for oc in range(n_oc):
                if oc >= 2:
                    W(sync, swc, oc - 1)
                c0 = wt_col0 + oc * 128
                sync.dma_start(
                    wrf[:, oc % 2, :, :],
                    wt[:, c0:c0 + 128].rearrange("(dc p) o -> p dc o", p=128),
                ).then_inc(*sw[oc % 2].tick(16))
            for k in range(2):
                W(sync, sw[k], 16 * ((n_oc - k + 1) // 2))

        @blk.tensor
        def _(tensor):
            p.wait_mark(tensor, "xn")
            p.wait_mark(tensor, "const")
            for oc in range(n_oc):
                W(tensor, swc, oc + 1)
                if oc >= 1:
                    W(tensor, sev, oc)
                    if normed:
                        W(tensor, ssq, oc)
                inst = None
                for dc in range(n_dc):
                    for tc in range(NTC):
                        inst = tensor.matmul(pp[:, tc * TC:(tc + 1) * TC],
                                             wrb[:, oc % 2, dc, :],
                                             xn[:, dc, tc * TC:(tc + 1) * TC],
                                             start=(dc == 0), stop=(dc == n_dc - 1))
                inst.then_inc(*smm.tick(1))
                if normed and "nnm" not in p.dbg:
                    W(tensor, ssq, oc + 1)
                    for tc in range(NTC):
                        g = 4 * oc + tc
                        if g >= 2:
                            W(tensor, snr, g - 1)
                        tensor.matmul(nps[:, tc % 2, :], p.t["ones1b"][:, :],
                                      sqn[:, tc * TC:(tc + 1) * TC],
                                      start=True, stop=True).then_inc(*snm.tick(1))
                if kind == "vt":
                    W(tensor, sev, oc + 1)
                    for j in range(NFC):
                        m = oc * NFC + j
                        if m >= 2:
                            W(tensor, svc, m - 1)
                        vtp = vtp0 if m % 2 == 0 else vtp1
                        tensor.transpose(vtp[:, :],
                                         stg[:, oc % 2, j * 128:(j + 1) * 128],
                                         p.t["identb"][:, :]).then_inc(*svt.tick(1))

        @blk.scalar
        def _(scalar):
            if normed:
                for oc in range(n_oc):
                    W(scalar, sev, oc + 1)
                    if oc >= 1 and "nnm" not in p.dbg:
                        W(scalar, snm, 4 * oc)
                    scalar.activation(sqn[:, :], pp[:, :], AF.Square).then_inc(*ssq.tick(1))

        @blk.vector
        def _(vector):
            for oc in range(n_oc):
                W(vector, sw[oc % 2], 16 * (oc // 2 + 1))
                if oc >= 2:
                    W(vector, smm, oc - 1)
                vector.tensor_copy(wrb[:, oc % 2, :, :], wrf[:, oc % 2, :, :]).then_inc(*swc.tick(1))
                W(vector, smm, oc + 1)
                if kind == "q":
                    vector.tensor_copy(dst[:, dst_col0 + oc, :], pp[:, :]).then_inc(*sev.tick(1))
                else:
                    if oc >= 2:
                        if kind == "kf":
                            W(vector, skd[oc % 2], 16 * (oc // 2))
                        else:
                            W(vector, svt, 16 * (oc - 1))
                    vector.tensor_copy(stg[:, oc % 2, :], pp[:, :]).then_inc(*sev.tick(1))
                if normed and "nnm" not in p.dbg:
                    for tc in range(NTC):
                        g = 4 * oc + tc
                        W(vector, snm, g + 1)
                        vector.tensor_reduce(row[:, 3, tc:tc + 1], nps[:, tc % 2, :],
                                             AX.X, ALU.max).then_inc(*snr.tick(1))
                    vector.drain()
                    vector.tensor_reduce(row[:, 2, 0:1], row[:, 3, 0:4], AX.X, ALU.max)
                    vector.drain()
                    inst = vector.tensor_copy(nrows[:, norm_row, norm_col0 + oc:norm_col0 + oc + 1],
                                              row[:, 2, 0:1])
                    p.mark(tag + "_nrm", inst, 1)
                    vector.drain()
                if kind == "vt":
                    for j in range(NFC):
                        m = oc * NFC + j
                        W(vector, svt, m + 1)
                        if oc >= 2 and j == 0:
                            W(vector, skd[oc % 2], 16 * (oc // 2))
                        vector.tensor_copy(vstg[:, oc % 2, j, :],
                                           (vtp0 if m % 2 == 0 else vtp1)[:, :]).then_inc(*svc.tick(1))

        @blk.gpsimd
        def _(g):
            for oc in range(n_oc):
                if kind == "kf":
                    W(g, sev, oc + 1)
                    r0 = dst_row0 + oc * 128
                    g.dma_start(dst[r0:r0 + 128, :], stg[:, oc % 2, :]).then_inc(*skd[oc % 2].tick(16))
                elif kind == "vt":
                    W(g, svc, 16 * (oc + 1))
                    c0 = dst_col0 + oc * 128
                    g.dma_start(dst.ap().rearrange("(j p) w -> p j w", p=128)[:, :, c0:c0 + 128],
                                vstg[:, oc % 2, :, :]).then_inc(*skd[oc % 2].tick(16))
            if kind in ("kf", "vt"):
                for k in range(2):
                    W(g, skd[k], 16 * ((n_oc - k + 1) // 2))

    if kind == "q":
        p.alias_mark(tag + "_ev", sev, n_oc)


def ph_cc(p, tag, pairs, krow, grow, nin, nout, nrm_mark=None):
    p.phase_begin()
    nc = p.nc
    s_nrm = p.new_sem(tag + "nr", "g")
    scc = p.new_sem(tag + "cc", "g")
    sd = p.slots(tag + "cd", 2, "g")
    nrows = p.t["nrows"]
    grp = [[0, 1, 2, 3], [4, 5, 6, 7]]
    nk = nin.shape[1]
    with nc.Block() as blk:
        @blk.gpsimd
        def _(g):
            if nrm_mark is not None:
                p.wait_mark(g, nrm_mark)
            g.dma_start(nin[:, :], nrows[:, krow, 0:nk]).then_inc(*sd[0].tick(16))
            W(g, sd[0], 16)
            g.collective_compute("AllReduce", ALU.max, grp,
                                 [nin.ap().opt()], [nout.ap().opt()]).then_inc(*s_nrm.tick(1))
            for (ci, co) in pairs:
                g.collective_compute("AllGather", ALU.bypass, grp,
                                     [p.t[ci].ap().opt()],
                                     [p.t[co].ap().opt()]).then_inc(*scc.tick(1))
            W(g, s_nrm, 1)
            g.dma_start(nrows[:, grow, 0:nk], nout[:, :]).then_inc(*sd[1].tick(16))
            W(g, sd[1], 16)
            p.mark("nrmg_" + tag, g.nop(), 1)
        p.alias_mark("gath_" + tag, scc, len(pairs))


def ph_cc_gather(p, tag, pairs):
    p.phase_begin()
    nc = p.nc
    scc = p.new_sem(tag + "cc", "g")
    grp = [[0, 1, 2, 3], [4, 5, 6, 7]]
    with nc.Block() as blk:
        @blk.gpsimd
        def _(g):
            for (ci, co) in pairs:
                g.collective_compute("AllGather", ALU.bypass, grp,
                                     [p.t[ci].ap().opt()],
                                     [p.t[co].ap().opt()]).then_inc(*scc.tick(1))
        p.alias_mark("gath_" + tag, scc, len(pairs))


def _negm_emit_vector(p, tag, vector, qrow, grow, scale, nh, negM,
                      q_nrm_mark, bps, s1, s2, s3, s4, s5, group_of_head=None):
    nrows = p.t["nrows"]
    row = p.t["rowtmp"]
    p.wait_mark(vector, q_nrm_mark)
    p.wait_mark(vector, "nrmg_" + tag)
    if group_of_head is None:
        vector.tensor_copy(row[:, 3, 0:nh], nrows[:, grow, 0:nh])
    else:
        for h in range(nh):
            vector.tensor_copy(row[:, 3, h:h + 1],
                               nrows[:, grow, h // GROUPS:h // GROUPS + 1])
    vector.drain()
    vector.tensor_mul(row[:, 0, 0:nh], nrows[:, qrow, 0:nh],
                      row[:, 3, 0:nh]).then_inc(*s1.tick(1))
    W(vector, s2, 1)
    vector.tensor_scalar(row[:, 2, 0:nh], row[:, 1, 0:nh],
                         -scale, None, ALU.mult).then_inc(*s3.tick(1))
    W(vector, s4, 1)
    vector.tensor_copy(negM[:, 0:nh], bps[:, 0:nh]).then_inc(*s5.tick(1))
    vector.drain()


def _negm_emit_scalar(p, scalar, s1, s2, nh):
    row = p.t["rowtmp"]
    W(scalar, s1, 1)
    scalar.activation(row[:, 1, 0:nh], row[:, 0, 0:nh], AF.Sqrt).then_inc(*s2.tick(1))
    scalar.drain()


def _negm_emit_tensor(p, tensor, s3, s4, bps, nh):
    row = p.t["rowtmp"]
    W(tensor, s3, 1)
    tensor.matmul(bps[:, 0:nh], p.t["onesr"][:, :], row[:, 2, 0:nh],
                  start=True, stop=True).then_inc(*s4.tick(1))


def _dbg_dram(p, src_name, out_name, shape, dtype=F32, n_rows=None):
    p.phase_begin()
    nc = p.nc
    out = nc.dram_tensor(out_name, shape, dtype, kind="ExternalOutput")
    r = shape[0] if n_rows is None else n_rows
    sd = p.new_sem("dbgd", "g")
    with nc.Block() as blk:
        @blk.gpsimd
        def _(g):
            g.dma_start(out[:r, :], p.t[src_name][:r, :]).then_inc(*sd.tick(16))
            W(g, sd, 16)



def ph_attn_a(p):
    p.phase_begin()
    nc = p.nc
    Q = p.t["Q"]
    Vsb = p.arena("a3v", [128, 2, 64, 256], BF16, "XN", 0)
    pt = p.arena("a3p", [128, 2, 1024], BF16, "SCR", 0)
    lbsb = p.arena("a3lb", [128, 1024], F32, "SCR", 4096)
    ctile = p.arena("a3c", [128, 2, 1024], BF16, "SCR", 8192)
    lrow = p.arena("a3lr", [1, 1024], F32, "SCR", 12288)
    Ksb = p.arena("a3k", [128, 2, 4 * TPC], BF16, "SCR", 16384)
    acc = p.arena("a3ac", [128, 2, 1024], F32, "SCR", 49152)
    negM = p.t["negMa"]
    kcc, vcc, ctxd = p.t["kcc_out"], p.t["vcc_out"], p.t["ctxd"]
    skv = p.slots("a3kv", 2, "g")
    svv = p.slots("a3vv", 2, "g")
    sctx = p.slots("a3cx", 2, "g")
    s_st = p.new_sem("a3st")
    s_xp = p.new_sem("a3xp")
    s_pv = p.new_sem("a3pv")
    s_la = p.new_sem("a3la")
    s_lf = p.new_sem("a3lf")
    s_rc = p.new_sem("a3rc")
    s_bc = p.new_sem("a3bc")
    s_ev = p.new_sem("a3ev")
    sn1, sn2, sn3, sn4, sn5 = (p.new_sem("a3n1"), p.new_sem("a3n2"),
                               p.new_sem("a3n3"), p.new_sem("a3n4"),
                               p.new_sem("a3n5"))
    NH = 2 if "mini" in p.dbg else HEADS
    NH2 = NH * 2

    with (
        nc.psum_tensor("a3stp", [128, 2, 1024], F32) as st,
        nc.psum_tensor("a3op", [128, 1024], F32) as ops,
        nc.psum_tensor("a3lp", [1, 1024], F32) as lps,
        nc.Block() as blk,
    ):
        @blk.gpsimd
        def _(g):
            p.wait_mark(g, "gath_a")

            def store_ctx(hh):
                h, half = hh // 2, hh % 2
                W(g, s_ev, hh + 1)
                g.dma_start(ctxd[h * 128:(h + 1) * 128, half * 1024:(half + 1) * 1024],
                            ctile[:, hh % 2, :]).then_inc(*sctx[hh % 2].tick(16))

            for h in range(NH):
                if h >= 2:
                    store_ctx(2 * (h - 2))
                    store_ctx(2 * (h - 2) + 1)
                sl = h % 2
                if h >= 2:
                    W(g, s_st, 128 * (h - 1))
                for r in range(4):
                    g.dma_start(Ksb[:, sl, r * TPC:(r + 1) * TPC],
                                kcc[r * D + h * 128:r * D + (h + 1) * 128, :]
                                ).then_inc(*skv[sl].tick(16))
                if h % 2 == 0:
                    P = h // 2
                    vs = P % 2
                    if P == 0:
                        p.wait_mark(g, "gath_av")
                    if P >= 2:
                        W(g, s_pv, 256 * (P - 1))
                    vv = vcc.ap().rearrange("(kb pp) f -> pp kb f", pp=128)
                    for kq in range(8):
                        g.dma_start(
                            Vsb[:, vs, kq * 8:(kq + 1) * 8, :],
                            vv[:, kq * 8:(kq + 1) * 8, h * 128:(h + 2) * 128],
                        ).then_inc(*svv[vs].tick(16))
            for hh in range(max(0, NH2 - 4), NH2):
                store_ctx(hh)
            for k in range(2):
                W(g, sctx[k], 16 * (NH2 // 2))

        @blk.tensor
        def _(tensor):
            _negm_emit_tensor(p, tensor, sn3, sn4, ops[:, 0:128], HEADS)
            p.wait_mark(tensor, "wq_ev")
            for hh in range(NH2):
                h, half = hh // 2, hh % 2
                hp = h % 2
                P = h // 2
                q0 = half * 1024
                if half == 0:
                    W(tensor, skv[h % 2], 64 * (h // 2 + 1))
                    if hp == 0:
                        W(tensor, svv[P % 2], 128 * (P // 2 + 1))

                def pv(j):
                    m = hh * 64 + j
                    W(tensor, s_xp, m + 1)
                    if j == 0:
                        W(tensor, s_ev, hh)
                    st_, sp_ = (j == 0), (j == 63)
                    tensor.matmul(ops[:, 0:512], Vsb[:, P % 2, j, hp * 128:(hp + 1) * 128],
                                  pt[:, m % 2, 0:512], start=st_, stop=sp_)
                    tensor.matmul(ops[:, 512:1024], Vsb[:, P % 2, j, hp * 128:(hp + 1) * 128],
                                  pt[:, m % 2, 512:1024], start=st_, stop=sp_).then_inc(*s_pv.tick(1))

                for kb in range(64):
                    m = hh * 64 + kb
                    if m >= 2:
                        W(tensor, s_xp, m - 1)
                    if kb == 0 and hh >= 1:
                        W(tensor, s_ev, hh)
                    tensor.matmul(st[:, m % 2, 0:512], Ksb[:, h % 2, kb * 128:(kb + 1) * 128],
                                  Q[:, h, q0:q0 + 512], start=True, stop=True)
                    tensor.matmul(st[:, m % 2, 512:1024], Ksb[:, h % 2, kb * 128:(kb + 1) * 128],
                                  Q[:, h, q0 + 512:q0 + 1024], start=True, stop=True).then_inc(*s_st.tick(1))
                    if kb >= 1:
                        pv(kb - 1)
                pv(63)
                W(tensor, s_la, 64 * (hh + 1))
                W(tensor, s_rc, hh)
                tensor.matmul(lps[:, 0:512], p.t["ones1f"][:, :],
                              acc[:, hh % 2, 0:512], start=True, stop=True)
                tensor.matmul(lps[:, 512:1024], p.t["ones1f"][:, :],
                              acc[:, hh % 2, 512:1024], start=True,
                              stop=True).then_inc(*s_lf.tick(1))
                W(tensor, s_rc, hh + 1)
                tensor.matmul(st[:, 0, 0:512], p.t["onesr"][:, :],
                              lrow[:, 0:512], start=True, stop=True)
                tensor.matmul(st[:, 0, 512:1024], p.t["onesr"][:, :],
                              lrow[:, 512:1024], start=True, stop=True).then_inc(*s_bc.tick(1))

        @blk.scalar
        def _(scalar):
            _negm_emit_scalar(p, scalar, sn1, sn2, HEADS)
            W(scalar, sn5, 1)
            for hh in range(NH2):
                h = hh // 2
                for kb in range(64):
                    m = hh * 64 + kb
                    W(scalar, s_st, m + 1)
                    if m >= 2:
                        W(scalar, s_pv, m - 1)
                        W(scalar, s_la, m - 1)
                    scalar.activation(pt[:, m % 2, :], st[:, m % 2, :], AF.Exp,
                                      bias=negM[:, h:h + 1], scale=SM_SCALE).then_inc(*s_xp.tick(1))

        @blk.vector
        def _(vector):
            _negm_emit_vector(p, "a", vector, 0, 2, SM_SCALE, HEADS, p.t["negMa"],
                              "wq_nrm", ops[:, 0:128], sn1, sn2, sn3, sn4, sn5)
            for hh in range(NH2):
                for kb in range(64):
                    m = hh * 64 + kb
                    W(vector, s_xp, m + 1)
                    if kb == 0:
                        if hh >= 2:
                            W(vector, s_lf, hh - 1)
                        vector.tensor_copy(acc[:, hh % 2, :],
                                           pt[:, m % 2, :]).then_inc(*s_la.tick(1))
                    else:
                        vector.tensor_add(acc[:, hh % 2, :], pt[:, m % 2, :],
                                          acc[:, hh % 2, :]).then_inc(*s_la.tick(1))
                W(vector, s_pv, 64 * (hh + 1))
                W(vector, s_lf, hh + 1)
                vector.reciprocal(lrow[:, :], lps[:, :]).then_inc(*s_rc.tick(1))
                W(vector, s_bc, hh + 1)
                vector.tensor_copy(lbsb[:, :], st[:, 0, :])
                vector.drain()
                if hh >= 2:
                    W(vector, sctx[hh % 2], 16 * (hh // 2))
                vector.tensor_mul(ctile[:, hh % 2, :], ops[:, :], lbsb[:, :]).then_inc(*s_ev.tick(1))




def ph_attn_b(p):
    p.phase_begin()
    nc = p.nc
    Q = p.t["Q"]
    Kg = p.arena("b3k", [128, 2, 4 * TPC], BF16, "XN", 0)
    Vg = p.arena("b3v", [128, 2, 64, 128], BF16, "XN", 32768)
    pt = p.arena("b3p", [128, 4, TC], BF16, "SCR", 0)
    lbsb = p.arena("b3lb", [128, TC], F32, "SCR", 4608)
    ctile = p.arena("b3c", [128, 2, TC], BF16, "SCR", 6656)
    lrow = p.arena("b3lr", [1, TC], F32, "SCR", 8704)
    btab = p.arena("b3bt", [128, HEADS, 64], F32, "SCR", 10752)
    indsb = p.arena("b3in", [128, 64], F32, "SCR", 15104)
    bms = p.arena("b3bm", [128, 2, 16, TC], BF16, "SCR", 16384)
    negM = p.t["negMb"]
    kcc, vcc, ctxd = p.t["kbcc_out"], p.t["vbcc_out"], p.t["ctxd"]

    skv = p.slots("b3kv", 2, "g")
    svv = p.slots("b3vv", 2, "g")
    sbm = p.slots("b3bm", 2)
    sctx = p.slots("b3cx", 2, "g")
    scst = p.slots("b3cs", 2)
    s_st = p.new_sem("b3st")
    s_xp = p.new_sem("b3xp")
    s_tri = p.new_sem("b3tr")
    s_pv = p.new_sem("b3pv")
    s_rc = p.new_sem("b3rc")
    s_bc = p.new_sem("b3bc")
    s_ev = p.new_sem("b3ev")
    s_bt = p.new_sem("b3btv")
    bn1, bn2, bn3, bn4, bn5 = (p.new_sem("b3n1"), p.new_sem("b3n2"),
                               p.new_sem("b3n3"), p.new_sem("b3n4"),
                               p.new_sem("b3n5"))

    blocks = []
    for g in range(GROUPS):
        for j in range(NTC):
            for h4 in range(4):
                blocks.append((g, j, g * 4 + h4))
    nw_of = lambda j: 16 * j + 16
    m_start = []
    m = 0
    for (g, j, h) in blocks:
        m_start.append(m)
        m += nw_of(j)
    ntri_before = []
    nt = 0
    for bi, (g, j, h) in enumerate(blocks):
        for w in range(nw_of(j)):
            ntri_before.append(nt)
            if w >= 16 * j:
                nt += 1

    def phys_win(j, u):
        return (u // 4) * 16 + j * 4 + (u % 4)

    def slab_of(bi):
        g, j, h = blocks[bi]
        return g * NTC + j

    tris_through_slab = {}
    cnt = 0
    for bi, (g, j, h) in enumerate(blocks):
        cnt += 16
        tris_through_slab[g * NTC + j] = cnt

    with (
        nc.psum_tensor("b3st0", [128, TC], F32) as st0,
        nc.psum_tensor("b3st1", [128, TC], F32) as st1,
        nc.psum_tensor("b3st2", [128, TC], F32) as st2,
        nc.psum_tensor("b3st3", [128, TC], F32) as st3,
        nc.psum_tensor("b3op", [128, TC], F32) as ops,
        nc.psum_tensor("b3lp", [1, TC], F32) as lps,
        nc.Block() as blk,
    ):
        stN = (st0, st1)
        @blk.sync
        def _(sync):
            sync.dma_start(indsb[:, :], p.inp("ind")[:, :]).then_inc(*scst[0].tick(16))
            bm = p.inp("bmask").ap().rearrange("(ju pp) q -> pp ju q", pp=128)
            for si in range(GROUPS * NTC):
                g, j = divmod(si, NTC)
                if si >= 2:
                    W(sync, s_tri, tris_through_slab[si - 2])
                sync.dma_start(bms[:, si % 2, :, :],
                               bm[:, j * 16:(j + 1) * 16, :]).then_inc(*sbm[si % 2].tick(16))

        @blk.vector
        def _(vector):
            _negm_emit_vector(p, "b", vector, 3, 5, SM_SCALE, HEADS, p.t["negMb"],
                              "bq_nrm", ops[:, 0:128], bn1, bn2, bn3, bn4, bn5,
                              group_of_head=True)
            W(vector, scst[0], 16)
            for h in range(HEADS):
                vector.tensor_scalar(btab[:, h, :], indsb[:, :], negM[:, h:h + 1],
                                     None, ALU.add)
            vector.drain()
            vector.tensor_copy(btab[:, 0, 0:1], btab[:, 0, 0:1]).then_inc(*s_bt.tick(1))
            m = 0
            for bi, (g, j, h) in enumerate(blocks):
                nw = nw_of(j)
                si = g * NTC + j
                if h % 4 == 0:
                    W(vector, sbm[si % 2], 16 * (si // 2 + 1))
                for w in range(nw):
                    if w >= 16 * j:
                        W(vector, s_xp, m + 1)
                        vector.tensor_mul(pt[:, m % 2, :], pt[:, m % 2, :],
                                          bms[:, si % 2, w - 16 * j, :]
                                          ).then_inc(*s_tri.tick(1))
                    m += 1
                W(vector, s_pv, m_start[bi] + nw)
                vector.reciprocal(lrow[:, :], lps[:, :]).then_inc(*s_rc.tick(1))
                W(vector, s_bc, bi + 1)
                vector.tensor_copy(lbsb[:, :], st0[:, :])
                vector.drain()
                if bi >= 2:
                    W(vector, sctx[bi % 2], 16 * (bi // 2))
                vector.tensor_mul(ctile[:, bi % 2, :], ops[:, :],
                                  lbsb[:, :]).then_inc(*s_ev.tick(1))

        @blk.gpsimd
        def _(g_):
            p.wait_mark(g_, "gath_b")
            for gi in range(GROUPS):
                if gi >= 1:
                    for bi in range((gi - 1) * 16 - 4 if gi >= 2 else 0, gi * 16 - 4):
                        _b3_store(p, g_, bi, blocks, ctile, ctxd, s_ev, sctx)
                sl = gi % 2
                if gi >= 2:
                    W(g_, s_st, m_start[(gi - 1) * 16])
                for r in range(4):
                    g_.dma_start(Kg[:, sl, r * TPC:(r + 1) * TPC],
                                 kcc[r * 512 + gi * 128:r * 512 + (gi + 1) * 128, :]
                                 ).then_inc(*skv[sl].tick(16))
                if gi == 0:
                    p.wait_mark(g_, "gath_bv")
                if gi >= 2:
                    W(g_, s_pv, m_start[(gi - 1) * 16])
                vv = vcc.ap().rearrange("(kb pp) f -> pp kb f", pp=128)
                for kq in range(8):
                    g_.dma_start(Vg[:, sl, kq * 8:(kq + 1) * 8, :],
                                 vv[:, kq * 8:(kq + 1) * 8, gi * 128:(gi + 1) * 128]
                                 ).then_inc(*svv[sl].tick(16))
            for bi in range(3 * 16 - 4, 4 * 16):
                _b3_store(p, g_, bi, blocks, ctile, ctxd, s_ev, sctx)
            for k in range(2):
                W(g_, sctx[k], 16 * (len(blocks) // 2))
            p.mark("ctxb", g_.nop(), 1)

        @blk.tensor
        def _(tensor):
            _negm_emit_tensor(p, tensor, bn3, bn4, ops[:, 0:128], HEADS)
            p.wait_mark(tensor, "bq_ev")
            m = 0
            for bi, (g, j, h) in enumerate(blocks):
                nw = nw_of(j)
                sl = g % 2
                if j == 0 and h % 4 == 0:
                    W(tensor, skv[sl], 64 * (g // 2 + 1))
                    W(tensor, svv[sl], 128 * (g // 2 + 1))

                def pv(mm, w):
                    if w >= 16 * j:
                        W(tensor, s_tri, ntri_before[mm] + 1)
                    else:
                        W(tensor, s_xp, mm + 1)
                    if w == 0:
                        W(tensor, s_ev, bi)
                        W(tensor, s_rc, bi)
                    P = _phys_chunk(w) if w < 16 * j else phys_win(j, w - 16 * j)
                    st_, sp_ = (w == 0), (w == nw - 1)
                    tensor.matmul(ops[:, :], Vg[:, sl, P, :], pt[:, mm % 2, :],
                                  start=st_, stop=sp_)
                    tensor.matmul(lps[:, :], p.t["ones1b"][:, :], pt[:, mm % 2, :],
                                  start=st_, stop=sp_).then_inc(*s_pv.tick(1))

                for w in range(nw):
                    if m >= 2:
                        W(tensor, s_xp, m - 1)
                    if w == 0 and bi >= 1:
                        W(tensor, s_ev, bi)
                    P = _phys_chunk(w) if w < 16 * j else phys_win(j, w - 16 * j)
                    kt = Kg[:, sl, P * 128:(P + 1) * 128]
                    stp = stN[m % 2]
                    tensor.matmul(stp[:, :], kt, Q[:, h, j * TC:(j + 1) * TC],
                                  start=True, stop=True).then_inc(*s_st.tick(1))
                    if w >= 1:
                        pv(m - 1, w - 1)
                    m += 1
                pv(m - 1, nw - 1)
                W(tensor, s_rc, bi + 1)
                tensor.matmul(st0[:, :], p.t["onesr"][:, :], lrow[:, :],
                              start=True, stop=True).then_inc(*s_bc.tick(1))

        @blk.scalar
        def _(scalar):
            _negm_emit_scalar(p, scalar, bn1, bn2, HEADS)
            W(scalar, s_bt, 1)
            m = 0
            for bi, (g, j, h) in enumerate(blocks):
                nw = nw_of(j)
                for w in range(nw):
                    W(scalar, s_st, m + 1)
                    if m >= 2:
                        W(scalar, s_pv, m - 1)
                    stp = stN[m % 2]
                    if w < 16 * j:
                        bias = negM[:, h:h + 1]
                    else:
                        col = j * 16 + (w - 16 * j)
                        bias = btab[:, h, col:col + 1]
                    scalar.activation(pt[:, m % 2, :], stp[:, :], AF.Exp,
                                      bias=bias, scale=SM_SCALE).then_inc(*s_xp.tick(1))
                    m += 1


def _b3_store(p, g_, bi, blocks, ctile, ctxd, s_ev, sctx):
    (g, j, h) = blocks[bi]
    W(g_, s_ev, bi + 1)
    g_.dma_start(ctxd[h * 128:(h + 1) * 128, j * TC:(j + 1) * TC],
                 ctile[:, bi % 2, :]).then_inc(*sctx[bi % 2].tick(16))



def ph_wcast(p, e0, e1, with_shd):
    p.phase_begin()
    nc = p.nc
    wdf = p.inp("wd_t").ap().rearrange("e h d -> (e h) d")
    tag = "wc%d" % e0
    sf = p.slots(tag + "f", 4)
    sb = p.slots(tag + "b", 2, "g")
    sv = p.new_sem(tag + "v")
    buf_f = p.arena("wcff" + tag, [128, 4, D], F32, "BIG", 32768)
    buf_b = p.arena("wcbb" + tag, [128, 2, D], BF16, "SCR", 36864)
    with nc.Block() as blk:
        tiles = []
        for r in range(e0 * HEXP // 128, e1 * HEXP // 128):
            tiles.append((wdf, p.t["wdb"], r, D))
        if with_shd:
            for r in range(HSH // 128):
                tiles.append((p.inp("shd_t").ap(), p.t["shdb"], r, D))

        @blk.sync
        def _(sync):
            for i, (src, dst, r, C) in enumerate(tiles):
                if i >= 4:
                    W(sync, sv, i - 3)
                sync.dma_start(buf_f[:, i % 4, 0:C],
                               src[r * 128:(r + 1) * 128, :]).then_inc(*sf[i % 4].tick(16))

        @blk.vector
        def _(vector):
            for i, (src, dst, r, C) in enumerate(tiles):
                W(vector, sf[i % 4], 16 * (i // 4 + 1))
                if i >= 2:
                    W(vector, sb[i % 2], 16 * (i // 2))
                vector.tensor_copy(buf_b[:, i % 2, 0:C], buf_f[:, i % 4, 0:C]).then_inc(*sv.tick(1))

        @blk.gpsimd
        def _(g):
            for i, (src, dst, r, C) in enumerate(tiles):
                W(g, sv, i + 1)
                g.dma_start(dst[r * 128:(r + 1) * 128, :], buf_b[:, i % 2, 0:C]).then_inc(*sb[i % 2].tick(16))
            n = len(tiles)
            for k in range(2):
                W(g, sb[k], 16 * ((n - k + 1) // 2))
            p.mark("wcast", g.nop(), 1)


def ph_gconst(p):
    p.phase_begin()
    nc = p.nc
    p.t["ggb"] = nc.alloc_sbuf_tensor("ggb", [128, NFC, 8], BF16)
    p.t["seb"] = nc.alloc_sbuf_tensor("seb", [128, NFC, 1], BF16)
    p.t["selb"] = nc.alloc_sbuf_tensor("selb", [8, 8 * 128], BF16)
    p.t["sg"] = p.arena("sg", [1, TPC], F32, "SCR", 49152)
    gf = p.arena("gcf", [128, NFC, 8], F32, "SCR", 16384)
    sef = p.arena("gce", [128, NFC, 1], F32, "SCR", 22528)
    self_sel = p.arena("gcs", [8, 8 * 128], F32, "SCR", 24576)
    sd = p.slots("gcd", 3)
    with nc.Block() as blk:
        @blk.sync
        def _(sync):
            sync.dma_start(gf[:, :, :],
                           p.inp("ggate").ap().rearrange("(dc p) e -> p dc e", p=128)
                           ).then_inc(*sd[0].tick(16))
            sync.dma_start(sef[:, :, :],
                           p.inp("se_t").ap().rearrange("(dc p) e -> p dc e", p=128),
                           allow_slow_non_contiguous=True).then_inc(*sd[1].tick(16))
            sync.dma_start(self_sel[:, :], p.inp("selmat")[:, :]).then_inc(*sd[2].tick(16))

        @blk.vector
        def _(vector):
            W(vector, sd[0], 16)
            vector.tensor_copy(p.t["ggb"][:, :, :], gf[:, :, :])
            W(vector, sd[1], 16)
            vector.tensor_copy(p.t["seb"][:, :, :], sef[:, :, :])
            W(vector, sd[2], 16)
            vector.drain()
            inst = vector.tensor_copy(p.t["selb"][:, :], self_sel[:, :])
            p.mark("gconst", inst, 1)


def ph_ffn_gate(p, tag):
    p.phase_begin()
    nc = p.nc
    xn = p.t["xn"]
    comb_bc = p.arena("cbc" + tag, [128, NEXP, TPC], BF16, "BIG", 0)
    p.t["comb_bc"] = comb_bc
    sg = p.t["sg"]
    ggb = p.t["ggb"]
    seb = p.t["seb"]
    selb = p.t["selb"]
    comb_tm = p.arena("ctm" + tag, [128, NFC, 8], F32, "SCR", 0)
    comb_T = p.arena("cT" + tag, [8, NFC, 128], BF16, "SCR", 8192)
    row8 = p.arena("r8" + tag, [128, 8, 8], F32, "SCR", 12288)
    s_mm = p.new_sem(tag + "mm")
    s_cp = p.new_sem(tag + "cp")
    s_d = p.new_sem(tag + "d")
    s_w1 = p.new_sem(tag + "w1")
    s_cb = p.new_sem(tag + "cb")
    s_tr = p.new_sem(tag + "tr")
    s_tc = p.new_sem(tag + "tc")
    s_bc = p.new_sem(tag + "bc")
    s_bv = p.new_sem(tag + "bv")
    s_sgm = p.new_sem(tag + "sgm")
    s_sgs = p.new_sem(tag + "sgs")
    with (
        nc.psum_tensor(tag + "sc0", [128, 8], F32) as sc0,
        nc.psum_tensor(tag + "sc1", [128, 8], F32) as sc1,
        nc.psum_tensor(tag + "tr0", [8, 128], F32) as tr0,
        nc.psum_tensor(tag + "tr1", [8, 128], F32) as tr1,
        nc.psum_tensor(tag + "bc0", [128, 128], F32) as bc0,
        nc.psum_tensor(tag + "bc1", [128, 128], F32) as bc1,
        nc.psum_tensor(tag + "sgp", [1, TC], F32) as sgp,
        nc.Block() as blk,
    ):
        @blk.tensor
        def _(tensor):
            p.wait_mark(tensor, "xn")
            p.wait_mark(tensor, "gconst")
            for tt in range(NFC):
                if tt >= 2:
                    W(tensor, s_cp, tt - 1)
                sc = sc0 if tt % 2 == 0 else sc1
                inst = None
                for dc in range(NFC):
                    inst = tensor.matmul(sc[:, :], xn[:, dc, tt * 128:(tt + 1) * 128],
                                         ggb[:, dc, :], start=(dc == 0),
                                         stop=(dc == NFC - 1))
                inst.then_inc(*s_mm.tick(1))
            for tc in range(NTC):
                if tc >= 1:
                    W(tensor, s_sgs, tc)
                for dc in range(NFC):
                    inst = tensor.matmul(sgp[:, :], seb[:, dc, :],
                                         xn[:, dc, tc * TC:(tc + 1) * TC],
                                         start=(dc == 0), stop=(dc == NFC - 1))
                inst.then_inc(*s_sgm.tick(1))
            for tt in range(NFC):
                W(tensor, s_cb, tt + 1)
                if tt >= 2:
                    W(tensor, s_tc, tt - 1)
                tr = tr0 if tt % 2 == 0 else tr1
                tensor.transpose(tr[:, :], comb_tm[:, tt, :], p.t["identf"][:, :]).then_inc(*s_tr.tick(1))
            for e in range(NEXP):
                for tt in range(NFC):
                    i = e * NFC + tt
                    W(tensor, s_tc, NFC)
                    if i >= 2:
                        W(tensor, s_bv, i - 1)
                    bc = bc0 if i % 2 == 0 else bc1
                    tensor.matmul(bc[:, :], selb[:, e * 128:(e + 1) * 128],
                                  comb_T[:, tt, :], start=True, stop=True).then_inc(*s_bc.tick(1))

        @blk.vector
        def _(vector):
            for tt in range(NFC):
                W(vector, s_mm, tt + 1)
                sc = sc0 if tt % 2 == 0 else sc1
                s8 = row8[:, 0, :]
                m1 = row8[:, 1, 0:1]
                eq1 = row8[:, 2, :]
                s2 = row8[:, 3, :]
                m2 = row8[:, 4, 0:1]
                eq2 = row8[:, 5, :]
                dd = row8[:, 6, 0:1]
                w2 = row8[:, 7, 0:1]
                vector.tensor_copy(s8, sc[:, :]).then_inc(*s_cp.tick(1))
                vector.drain()
                vector.tensor_reduce(m1, s8, AX.X, ALU.max)
                vector.drain()
                vector.tensor_scalar(eq1, s8, m1, None, ALU.is_ge)
                vector.drain()
                vector.scalar_tensor_tensor(s2, eq1, NEG, s8, ALU.mult, ALU.add)
                vector.drain()
                vector.tensor_reduce(m2, s2, AX.X, ALU.max)
                vector.drain()
                vector.tensor_scalar(eq2, s2, m2, None, ALU.is_ge)
                vector.drain()
                vector.tensor_sub(dd, m1, m2).then_inc(*s_d.tick(1))
                W(vector, s_w1, tt + 1)
                vector.tensor_scalar(w2, row8[:, 1, 1:2], -1.0, 1.0, ALU.mult, ALU.add)
                vector.drain()
                vector.tensor_scalar(eq1, eq1, row8[:, 1, 1:2], None, ALU.mult)
                vector.drain()
                vector.scalar_tensor_tensor(comb_tm[:, tt, :], eq2, w2, eq1,
                                            ALU.mult, ALU.add).then_inc(*s_cb.tick(1))
                vector.drain()
            for tt in range(NFC):
                W(vector, s_tr, tt + 1)
                tr = tr0 if tt % 2 == 0 else tr1
                vector.tensor_copy(comb_T[:, tt, :], tr[:, :]).then_inc(*s_tc.tick(1))
            for e in range(NEXP):
                for tt in range(NFC):
                    i = e * NFC + tt
                    W(vector, s_bc, i + 1)
                    bc = bc0 if i % 2 == 0 else bc1
                    vector.tensor_copy(comb_bc[:, e, tt * 128:(tt + 1) * 128],
                                       bc[:, :]).then_inc(*s_bv.tick(1))
            vector.drain()
            inst = vector.tensor_copy(row8[:, 0, 0:1], row8[:, 0, 0:1])
            p.mark(tag + "_comb", inst, 1)

        @blk.scalar
        def _(scalar):
            for tt in range(NFC):
                W(scalar, s_d, tt + 1)
                scalar.activation(row8[:, 1, 1:2], row8[:, 6, 0:1], AF.Sigmoid).then_inc(*s_w1.tick(1))
            for tc in range(NTC):
                W(scalar, s_sgm, tc + 1)
                scalar.activation(sg[:, tc * TC:(tc + 1) * TC], sgp[:, :],
                                  AF.Sigmoid).then_inc(*s_sgs.tick(1))
            p.mark(tag + "_sg", scalar.nop(), 1)


def ph_ffn_hidden(p, tag, wgsrc, wusrc, n_hc, row0, comb_e, cache, crow0, comb_mark=None):
    p.phase_begin()
    nc = p.nc
    xn = p.t["xn"]
    wrb = p.arena("fhw" + tag, [128, 2, NFC, 128], BF16, "SCR", 0)
    hgs = p.arena("fhg" + tag, [128, TPC], BF16, "SCR", 8192)
    t1 = p.arena("fht" + tag, [128, 2, TPC], BF16, "SCR", 12288)
    wrf = p.arena("fhwf" + tag, [128, 2, NFC, 128], F32, "SCR", 20480)
    sw = p.slots(tag + "w", 2)
    shd = p.slots(tag + "hd", 2, "g")
    swc = p.new_sem(tag + "wc")
    s_g = p.new_sem(tag + "g")
    s_u = p.new_sem(tag + "u")
    s_sl = p.new_sem(tag + "sl")
    s_t = p.new_sem(tag + "t")
    with (
        nc.psum_tensor("fhpg" + tag, [128, TPC], F32) as pg,
        nc.psum_tensor("fhpu" + tag, [128, TPC], F32) as pu,
        nc.Block() as blk,
    ):
        @blk.sync
        def _(sync):
            for hc in range(n_hc):
                for half, src in ((0, wgsrc), (1, wusrc)):
                    i = 2 * hc + half
                    if i >= 2:
                        W(sync, swc, i - 1)
                    sync.dma_start(
                        wrf[:, half, :, :],
                        src[row0:row0 + D, hc * 128:(hc + 1) * 128]
                        .rearrange("(dc p) o -> p dc o", p=128),
                    ).then_inc(*sw[half].tick(16))

        @blk.tensor
        def _(tensor):
            p.wait_mark(tensor, "xn")
            for hc in range(n_hc):
                W(tensor, swc, 2 * hc + 1)
                if hc >= 1:
                    W(tensor, s_sl, hc)
                inst = None
                for dc in range(NFC):
                    for tc in range(NTC):
                        inst = tensor.matmul(pg[:, tc * TC:(tc + 1) * TC],
                                             wrb[:, 0, dc, :],
                                             xn[:, dc, tc * TC:(tc + 1) * TC],
                                             start=(dc == 0), stop=(dc == NFC - 1))
                inst.then_inc(*s_g.tick(1))
                W(tensor, swc, 2 * hc + 2)
                if hc >= 1:
                    W(tensor, s_t, hc)
                for dc in range(NFC):
                    for tc in range(NTC):
                        inst = tensor.matmul(pu[:, tc * TC:(tc + 1) * TC],
                                             wrb[:, 1, dc, :],
                                             xn[:, dc, tc * TC:(tc + 1) * TC],
                                             start=(dc == 0), stop=(dc == NFC - 1))
                inst.then_inc(*s_u.tick(1))

        @blk.scalar
        def _(scalar):
            for hc in range(n_hc):
                W(scalar, s_g, hc + 1)
                if hc >= 1:
                    W(scalar, s_t, hc)
                scalar.activation(hgs[:, :], pg[:, :], AF.Silu).then_inc(*s_sl.tick(1))

        @blk.vector
        def _(vector):
            if comb_mark is not None:
                p.wait_mark(vector, comb_mark)
            for hc in range(n_hc):
                for half in (0, 1):
                    i = 2 * hc + half
                    W(vector, sw[half], 16 * (hc + 1))
                    if i >= 4:
                        W(vector, s_g if half == 0 else s_u, hc - 1)
                    vector.tensor_copy(wrb[:, half, :, :],
                                       wrf[:, half, :, :]).then_inc(*swc.tick(1))
                W(vector, s_u, hc + 1)
                W(vector, s_sl, hc + 1)
                if hc >= 2:
                    W(vector, shd[hc % 2], 16 * (hc // 2))
                vector.tensor_mul(t1[:, hc % 2, :], pu[:, :], hgs[:, :])
                vector.drain()
                if comb_e is not None:
                    vector.tensor_mul(t1[:, hc % 2, :], t1[:, hc % 2, :], comb_e)
                    vector.drain()
                vector.tensor_copy(t1[:, hc % 2, 0:1],
                                   t1[:, hc % 2, 0:1]).then_inc(*s_t.tick(1))

        @blk.gpsimd
        def _(g):
            for hc in range(n_hc):
                W(g, s_t, hc + 1)
                g.dma_start(cache[crow0 + hc * 128:crow0 + (hc + 1) * 128, :],
                            t1[:, hc % 2, :]).then_inc(*shd[hc % 2].tick(16))
            for k in range(2):
                W(g, shd[k], 16 * ((n_hc - k + 1) // 2))
            p.mark("hc_" + tag, g.nop(), 1)


def ph_ffn_down(p, tag, xprev, xnext, ready_mark, hmark, shmark, sgmark,
                with_stats=True):
    p.phase_begin()
    nc = p.nc
    hsl = p.arena("fd_h" + tag, [128, NEXP, 8, TC], BF16, "BIG", 0)
    shsl = p.arena("fd_s" + tag, [128, NFC, TC], BF16, "XN", 0)
    wdr = p.arena("fd_w" + tag, [128, 2, NEXP, 8, 128], BF16, "XN", 16384)
    sdr = p.arena("fd_sw" + tag, [128, 2, NFC, 128], BF16, "XN", 49152)
    sgb = p.arena("fd_sg" + tag, [128, TC], F32, "SCR", 16384)
    xpr = p.arena("fd_xp" + tag, [128, 4, TC], F32, "SCR", 20480)
    xnw = p.arena("fd_xn" + tag, [128, 2, TC], F32, "SCR", 28672)
    tsh = p.arena("fd_t" + tag, [128, 2, TC], F32, "SCR", 32768)
    sg = p.t["sg"]
    wdb, shdb = p.t["wdb"], p.t["shdb"]
    sh_sl = p.slots(tag + "hs", 2, "g")
    ss_sl = p.slots(tag + "ss", 2, "g")
    sw_sl = p.slots(tag + "ws", 2)
    sd_sl = p.slots(tag + "ds", 2)
    sxp = p.slots(tag + "xp", 4)
    sxn = p.slots(tag + "xn", 2, "g")
    s_rm = p.new_sem(tag + "rm")
    s_shm = p.new_sem(tag + "shm")
    s_ev = p.new_sem(tag + "ev")
    s_sgb = p.new_sem(tag + "sgb")
    s_sgc = p.new_sem(tag + "sgc")
    s_sq = p.new_sem(tag + "sq")
    s_sm = p.new_sem(tag + "sm")
    s_fv = p.new_sem(tag + "fv")
    s_fs = p.new_sem(tag + "fs")
    sq4 = p.arena("dsq" + tag, [128, 2, TC], F32, "SCR", 4096)
    sel4f = p.t["sel4fs"]
    with (
        nc.psum_tensor("fdr0" + tag, [128, TC], F32) as pr0,
        nc.psum_tensor("fdr1" + tag, [128, TC], F32) as pr1,
        nc.psum_tensor("fds0" + tag, [128, TC], F32) as ps0,
        nc.psum_tensor("fds1" + tag, [128, TC], F32) as ps1,
        nc.psum_tensor("fdsg" + tag, [128, TC], F32) as psg,
        nc.psum_tensor("fdp4" + tag, [4, 2, TC], F32) as ps4,
        nc.Block() as blk,
    ):
        def stats_mm(tensor, i):
            tci = i // NFC
            W(tensor, s_ev, i + 1)
            tensor.matmul(ps4[:, 0, :], sel4f[:, tci * 4:(tci + 1) * 4],
                          xnw[:, i % 2, :], start=(i == 0), stop=(i == NFC * NTC - 1))
            W(tensor, s_sq, i + 1)
            tensor.matmul(ps4[:, 1, :], sel4f[:, tci * 4:(tci + 1) * 4],
                          sq4[:, i % 2, :], start=(i == 0),
                          stop=(i == NFC * NTC - 1)).then_inc(*s_sm.tick(1))
        @blk.gpsimd
        def _(g):
            p.wait_mark(g, hmark)
            p.wait_mark(g, shmark)
            def store_tile(i):
                W(g, s_ev, i + 1)
                tc_, oc_ = divmod(i, NFC)
                g.dma_start(xnext[oc_ * 128:(oc_ + 1) * 128, tc_ * TC:(tc_ + 1) * TC],
                            xnw[:, i % 2, :]).then_inc(*sxn[i % 2].tick(16))

            for tc in range(NTC):
                if tc >= 1:
                    for i in range((tc - 1) * NFC, tc * NFC):
                        store_tile(i)
                if tc >= 1:
                    W(g, s_rm, NFC * tc)
                    W(g, s_shm, NFC * tc)
                for e in range(NEXP):
                    g.dma_start(
                        hsl[:, e, :, :],
                        p.t["hcache"][e * HEXP:(e + 1) * HEXP, tc * TC:(tc + 1) * TC]
                        .rearrange("(hc pp) t -> pp hc t", pp=128),
                    ).then_inc(*sh_sl[tc % 2].tick(16))
                g.dma_start(
                    shsl[:, :, :],
                    p.t["shcache"][:, tc * TC:(tc + 1) * TC]
                    .rearrange("(hc pp) t -> pp hc t", pp=128),
                ).then_inc(*ss_sl[tc % 2].tick(16))
            for i in range((NTC - 1) * NFC, NTC * NFC):
                store_tile(i)
            for k in range(2):
                W(g, sxn[k], 16 * (NTC * NFC // 2))
            p.mark(ready_mark, g.nop(), 1)

        @blk.sync
        def _(sync):
            p.wait_mark(sync, "wcast")
            for tc in range(NTC):
                for oc in range(NFC):
                    i = tc * NFC + oc
                    if i >= 2:
                        W(sync, s_rm, i - 1)
                    for e in range(NEXP):
                        sync.dma_start(
                            wdr[:, i % 2, e, :, :],
                            wdb[e * HEXP:(e + 1) * HEXP, oc * 128:(oc + 1) * 128]
                            .rearrange("(hc pp) o -> pp hc o", pp=128),
                        ).then_inc(*sw_sl[i % 2].tick(16))
                    if i >= 2:
                        W(sync, s_shm, i - 1)
                    sync.dma_start(
                        sdr[:, i % 2, :, :],
                        shdb[:, oc * 128:(oc + 1) * 128]
                        .rearrange("(hc pp) o -> pp hc o", pp=128),
                    ).then_inc(*sd_sl[i % 2].tick(16))
                    if i >= 4:
                        W(sync, s_ev, i - 3)
                    sync.dma_start(xpr[:, i % 4, :],
                                   xprev[oc * 128:(oc + 1) * 128, tc * TC:(tc + 1) * TC]
                                   ).then_inc(*sxp[i % 4].tick(16))

        @blk.tensor
        def _(tensor):
            p.wait_mark(tensor, sgmark)
            for tc in range(NTC):
                if tc >= 1:
                    W(tensor, s_sgc, tc)
                tensor.matmul(psg[:, :], p.t["onesr"][:, :], sg[:, tc * TC:(tc + 1) * TC],
                              start=True, stop=True).then_inc(*s_sgb.tick(1))
                W(tensor, sh_sl[tc % 2], 128 * (tc // 2 + 1))
                W(tensor, ss_sl[tc % 2], 16 * (tc // 2 + 1))
                for oc in range(NFC):
                    i = tc * NFC + oc
                    if with_stats and i >= 2:
                        stats_mm(tensor, i - 2)
                    W(tensor, sw_sl[i % 2], 128 * (i // 2 + 1))
                    if i >= 2:
                        W(tensor, s_ev, i - 1)
                    pr = pr0 if i % 2 == 0 else pr1
                    ps = ps0 if i % 2 == 0 else ps1
                    inst = None
                    for e in range(NEXP):
                        for hc in range(8):
                            inst = tensor.matmul(pr[:, :], wdr[:, i % 2, e, hc, :],
                                                 hsl[:, e, hc, :],
                                                 start=(e == 0 and hc == 0),
                                                 stop=(e == NEXP - 1 and hc == 7))
                    inst.then_inc(*s_rm.tick(1))
                    W(tensor, sd_sl[i % 2], 16 * (i // 2 + 1))
                    for hc in range(NFC):
                        inst = tensor.matmul(ps[:, :], sdr[:, i % 2, hc, :],
                                             shsl[:, hc, :],
                                             start=(hc == 0), stop=(hc == NFC - 1))
                    inst.then_inc(*s_shm.tick(1))
            if with_stats:
                stats_mm(tensor, NFC * NTC - 2)
                stats_mm(tensor, NFC * NTC - 1)

        @blk.scalar
        def _(scalar):
            if with_stats:
                for i in range(NFC * NTC):
                    W(scalar, s_ev, i + 1)
                    if i >= 2:
                        W(scalar, s_sm, i - 1)
                    scalar.activation(sq4[:, i % 2, :], xnw[:, i % 2, :],
                                      AF.Square).then_inc(*s_sq.tick(1))
                _stats_fin_s(p, scalar, s_fv, s_fs)

        @blk.vector
        def _(vector):
            for tc in range(NTC):
                W(vector, s_sgb, tc + 1)
                vector.tensor_copy(sgb[:, :], psg[:, :]).then_inc(*s_sgc.tick(1))
                vector.drain()
                for oc in range(NFC):
                    i = tc * NFC + oc
                    W(vector, s_shm, i + 1)
                    if "nosgate" in p.dbg:
                        vector.tensor_copy(tsh[:, i % 2, :], (ps0 if i % 2 == 0 else ps1)[:, :])
                    else:
                        vector.tensor_mul(tsh[:, i % 2, :], (ps0 if i % 2 == 0 else ps1)[:, :],
                                          sgb[:, :])
                    if "noshared" in p.dbg:
                        vector.drain()
                        vector.memset(tsh[:, i % 2, :], 0.0)
                    vector.drain()
                    W(vector, s_rm, i + 1)
                    if "norouted" not in p.dbg:
                        vector.tensor_add(tsh[:, i % 2, :], (pr0 if i % 2 == 0 else pr1)[:, :],
                                          tsh[:, i % 2, :])
                        vector.drain()
                    W(vector, sxp[i % 4], 16 * (i // 4 + 1))
                    if i >= 2:
                        W(vector, sxn[i % 2], 16 * (i // 2))
                        if with_stats:
                            W(vector, s_sm, i - 1)
                    vector.tensor_add(xnw[:, i % 2, :], tsh[:, i % 2, :],
                                      xpr[:, i % 4, :]).then_inc(*s_ev.tick(1))
            if with_stats:
                W(vector, s_sm, NFC * NTC)
                _stats_fin_v1(p, vector, ps4, s_fv)
                _stats_fin_v2(p, vector, s_fs)


def ph_attn_out(p, tag, wt, xprev, xnext, ready_mark, with_stats=True):
    p.phase_begin()
    nc = p.nc
    ctxsb = p.arena("cxs" + tag, [128, NFC, TPC], BF16, "BIG")
    wrf = p.arena("owf" + tag, [128, 2, NFC, 128], F32, "SCR", 16384)
    wrb = p.arena("owb" + tag, [128, 2, NFC, 128], BF16, "SCR", 32768)
    xpr = p.arena("oxp" + tag, [128, 4, TC], F32, "SCR", 40960)
    xnw = p.arena("oxn" + tag, [128, 2, TC], F32, "SCR", 49152)
    ctxd = p.t["ctxd"]
    sw = p.slots(tag + "w", 2)
    scx = p.slots(tag + "cx", 2, "g")
    sxp = p.slots(tag + "xp", 4)
    sxn = p.slots(tag + "xn", 2, "g")
    swc = p.new_sem(tag + "wc")
    smm = p.new_sem(tag + "mm")
    sev = p.new_sem(tag + "ev")
    s_sq = p.new_sem(tag + "sq")
    s_sm = p.new_sem(tag + "sm")
    s_fv = p.new_sem(tag + "fv")
    s_fs = p.new_sem(tag + "fs")
    sq4 = p.arena("osq" + tag, [128, 2, TC], F32, "SCR", 53248)
    sel4f = p.t["sel4fs"]

    with (
        nc.psum_tensor("po" + tag, [128, TPC], F32) as pp,
        nc.psum_tensor("ps4" + tag, [4, 2, TC], F32) as ps4,
        nc.Block() as blk,
    ):
        def stats_mm(tensor, i):
            tc = i % NTC
            W(tensor, sev, i + 1)
            tensor.matmul(ps4[:, 0, :], sel4f[:, tc * 4:(tc + 1) * 4],
                          xnw[:, i % 2, :], start=(i == 0), stop=(i == NFC * NTC - 1))
            W(tensor, s_sq, i + 1)
            tensor.matmul(ps4[:, 1, :], sel4f[:, tc * 4:(tc + 1) * 4],
                          sq4[:, i % 2, :], start=(i == 0),
                          stop=(i == NFC * NTC - 1)).then_inc(*s_sm.tick(1))
        @blk.gpsimd
        def _(g):
            for fc in range(NFC):
                if fc >= 2:
                    W(g, scx[fc % 2], 16 * (fc // 2))
                g.dma_start(ctxsb[:, fc, :], ctxd[fc * 128:(fc + 1) * 128, :]
                            ).then_inc(*scx[fc % 2].tick(16))
            for i in range(NFC * NTC):
                W(g, sev, i + 1)
                oc, tc = divmod(i, NTC)
                g.dma_start(xnext[oc * 128:(oc + 1) * 128, tc * TC:(tc + 1) * TC],
                            xnw[:, i % 2, :]).then_inc(*sxn[i % 2].tick(16))
            for k in range(2):
                W(g, sxn[k], 16 * (NFC * NTC // 2))
            inst = g.nop()
            p.mark(ready_mark, inst, 1)

        @blk.sync
        def _(sync):
            for oc in range(NFC):
                if oc >= 2:
                    W(sync, swc, oc - 1)
                sync.dma_start(
                    wrf[:, oc % 2, :, :],
                    wt[:, oc * 128:(oc + 1) * 128].rearrange("(dc p) o -> p dc o", p=128),
                ).then_inc(*sw[oc % 2].tick(16))
                for tc in range(NTC):
                    i = oc * NTC + tc
                    if i >= 4:
                        W(sync, sev, i - 3)
                    sync.dma_start(xpr[:, i % 4, :],
                                   xprev[oc * 128:(oc + 1) * 128, tc * TC:(tc + 1) * TC]
                                   ).then_inc(*sxp[i % 4].tick(16))
            for k in range(2):
                W(sync, sw[k], 16 * (NFC // 2))

        @blk.scalar
        def _(scalar):
            if with_stats:
                for i in range(NFC * NTC):
                    W(scalar, sev, i + 1)
                    if i >= 2:
                        W(scalar, s_sm, i - 1)
                    scalar.activation(sq4[:, i % 2, :], xnw[:, i % 2, :],
                                      AF.Square).then_inc(*s_sq.tick(1))
                _stats_fin_s(p, scalar, s_fv, s_fs)

        @blk.vector
        def _(vector):
            for oc in range(NFC):
                W(vector, sw[oc % 2], 16 * (oc // 2 + 1))
                if oc >= 2:
                    W(vector, smm, oc - 1)
                vector.tensor_copy(wrb[:, oc % 2, :, :], wrf[:, oc % 2, :, :]).then_inc(*swc.tick(1))
                W(vector, smm, oc + 1)
                for tc in range(NTC):
                    i = oc * NTC + tc
                    W(vector, sxp[i % 4], 16 * (i // 4 + 1))
                    if i >= 2:
                        W(vector, sxn[i % 2], 16 * (i // 2))
                        if with_stats:
                            W(vector, s_sm, i - 1)
                    vector.tensor_add(xnw[:, i % 2, :], pp[:, tc * TC:(tc + 1) * TC],
                                      xpr[:, i % 4, :]).then_inc(*sev.tick(1))
            if with_stats:
                W(vector, s_sm, NFC * NTC)
                _stats_fin_v1(p, vector, ps4, s_fv)
                _stats_fin_v2(p, vector, s_fs)

        @blk.tensor
        def _(tensor):
            for oc in range(NFC):
                if with_stats and oc >= 1:
                    lo = 0 if oc == 1 else 4 * oc - 5
                    for i in range(lo, 4 * oc - 1):
                        stats_mm(tensor, i)
                W(tensor, swc, oc + 1)
                if oc == 0:
                    for k in range(2):
                        W(tensor, scx[k], 16 * (NFC // 2))
                if oc >= 1:
                    W(tensor, sev, 4 * oc)
                inst = None
                for dc in range(NFC):
                    for tc in range(NTC):
                        inst = tensor.matmul(pp[:, tc * TC:(tc + 1) * TC],
                                             wrb[:, oc % 2, dc, :],
                                             ctxsb[:, dc, tc * TC:(tc + 1) * TC],
                                             start=(dc == 0), stop=(dc == NFC - 1))
                inst.then_inc(*smm.tick(1))
            if with_stats:
                for i in range(4 * (NFC - 1) - 1, 4 * NFC):
                    stats_mm(tensor, i)



def _sub_ffn(p, tag, xprev, xnext, prev_mark, ready_mark, first,
             last=False):
    u = p.upto
    ph_xn(p, xprev, tag + "xn", wait_marks=(prev_mark,))
    if first:
        ph_gconst(p)
    if u < 2.4:
        return False
    ph_ffn_gate(p, tag + "g")
    if "comb" in p.dbg:
        _dbg_dump_fm(p, "comb_bc", "dbg_comb", nfc=NEXP, width=TPC,
                     marks=(tag + "g_comb",))
    if u < 2.6:
        return False
    nexp = 2 if "emini" in p.dbg else NEXP
    wgf = p.inp("wg_t").ap().rearrange("e d h -> (e d) h")
    wuf = p.inp("wu_t").ap().rearrange("e d h -> (e d) h")
    for e in range(nexp):
        ph_ffn_hidden(p, tag + "h%d" % e, wgf, wuf, 8, e * D,
                      p.t["comb_bc"][:, e, :],
                      p.t["hcache"], e * HEXP, comb_mark=tag + "g_comb")
        if first:
            ph_wcast(p, e, e + 1, with_shd=(e == nexp - 1))
    if "hc" in p.dbg:
        _dbg_dram(p, "hcache", "dbg_hc", [NEXP * HEXP, TPC], BF16,
                  n_rows=nexp * HEXP)
    if u < 2.8:
        return False
    ph_ffn_hidden(p, tag + "sh", p.inp("shg_t").ap(), p.inp("shu_t").ap(),
                  NFC, 0, None, p.t["shcache"], 0)
    if "shc" in p.dbg:
        _dbg_dram(p, "shcache", "dbg_shc", [HSH, TPC], BF16)
    if u < 3:
        return False
    ph_ffn_down(p, tag + "d", xprev, xnext, ready_mark,
                "hc_" + tag + "h%d" % (NEXP - 1), "hc_" + tag + "sh",
                tag + "g_sg", with_stats=not last)
    return True



def build_program(upto=99, dbg=()):
    nc = bass.Bass("TRN2", target_bir_lowering=False)
    p = Prog(nc, upto, dbg)

    x0 = nc.dram_tensor("x0", [D, TPC], F32, kind="ExternalInput")
    xout = nc.dram_tensor("xout", [D, TPC], F32, kind="ExternalOutput")
    p.t["x0"], p.t["xout"] = x0, xout
    p.used_inputs = {"x0"}
    _in_shapes = {"wq_t": [D, D], "wk_t": [D, D], "wv_t": [D, D],
                  "wo_t": [D, D], "bwq": [D, D], "bwkv": [D, 1024],
                  "bwd": [D, D], "ggate": [D, 8],
                  "wg_t": [NEXP, D, HEXP], "wu_t": [NEXP, D, HEXP],
                  "wd_t": [NEXP, HEXP, D], "shg_t": [D, HSH],
                  "shu_t": [D, HSH], "shd_t": [HSH, D], "se_t": [D, 1],
                  "tri01": [512, TC], "ident": [128, 128], "ind": [128, 64],
                  "selmat": [8, 1024], "bmask": [NTC * 16 * 128, TC],
                  "sel4f": [128, 16], "sel4t": [4, 512]}

    def inp(nm):
        if nm not in p.t:
            dt = BF16 if nm == "bmask" else F32
            p.t[nm] = nc.dram_tensor(nm, _in_shapes[nm], dt, kind="ExternalInput")
            p.used_inputs.add(nm)
        return p.t[nm]

    p.inp = inp

    p.t["xd1"] = nc.dram_tensor("xd1", [D, TPC], F32)
    p.t["xd2"] = nc.dram_tensor("xd2", [D, TPC], F32)
    p.t["kcc_in"] = nc.dram_tensor("kcc_in", [D, TPC], BF16)
    p.t["kcc_out"] = nc.dram_tensor("kcc_out", [4 * D, TPC], BF16)
    p.t["vcc_in"] = nc.dram_tensor("vcc_in", [TPC, D], BF16)
    p.t["vcc_out"] = nc.dram_tensor("vcc_out", [4 * TPC, D], BF16)
    p.t["nrma_in"] = nc.dram_tensor("nrma_in", [1, 16], F32)
    p.t["nrma_out"] = nc.dram_tensor("nrma_out", [1, 16], F32)
    p.t["nrmb_in"] = nc.dram_tensor("nrmb_in", [1, 4], F32)
    p.t["nrmb_out"] = nc.dram_tensor("nrmb_out", [1, 4], F32)
    p.t["kbcc_in"] = nc.dram_tensor("kbcc_in", [512, TPC], BF16)
    p.t["kbcc_out"] = nc.dram_tensor("kbcc_out", [2048, TPC], BF16)
    p.t["vbcc_in"] = nc.dram_tensor("vbcc_in", [TPC, 512], BF16)
    p.t["vbcc_out"] = nc.dram_tensor("vbcc_out", [4 * TPC, 512], BF16)
    p.t["ctxd"] = nc.dram_tensor("ctxd", [D, TPC], BF16)
    p.t["hcache"] = nc.dram_tensor("hcache", [NEXP * HEXP, TPC], BF16)
    p.t["shcache"] = nc.dram_tensor("shcache", [HSH, TPC], BF16)
    p.t["wdb"] = nc.dram_tensor("wdb", [NEXP * HEXP, D], BF16)
    p.t["shdb"] = nc.dram_tensor("shdb", [HSH, D], BF16)

    if "xn" in p.dbg:
        p.t["dbg_xn"] = nc.dram_tensor("dbg_xn", [D, TPC], F32, kind="ExternalOutput")

    p.t["xn"] = p.arena("xn", [128, NFC, TPC], BF16, "XN")
    p.t["stat"] = p.arena("stat", [4, 2, TC], F32, "SCR", 0)
    p.t["r4"] = p.arena("r4", [4, 4, TC], F32, "SCR", 8192)
    p.t["xring"] = p.arena("xring", [128, 4, TC], F32, "SCR", 16384)
    p.t["bcb"] = p.arena("bcb", [128, 2, TC], F32, "SCR", 24576)
    p.t["ones1f"] = nc.alloc_sbuf_tensor("ones1f", [128, 1], F32)
    p.t["ones1b"] = nc.alloc_sbuf_tensor("ones1b", [128, 1], BF16)
    p.t["onesr"] = nc.alloc_sbuf_tensor("onesr", [1, 128], F32)
    p.t["rowtmp"] = nc.alloc_sbuf_tensor("rowtmp", [1, 4, TC], F32)
    p.t["nrows"] = nc.alloc_sbuf_tensor("nrows", [1, 8, NFC], F32)
    p.t["negMa"] = nc.alloc_sbuf_tensor("negMa", [128, NFC], F32)
    p.t["negMb"] = nc.alloc_sbuf_tensor("negMb", [128, NFC], F32)
    p.t["identf"] = nc.alloc_sbuf_tensor("identf", [128, 128], F32)
    p.t["identb"] = nc.alloc_sbuf_tensor("identb", [128, 128], BF16)
    p.t["sel4fs"] = nc.alloc_sbuf_tensor("sel4fs", [128, 16], F32)
    p.t["sel4ts"] = nc.alloc_sbuf_tensor("sel4ts", [4, 512], F32)

    sid = p.new_sem("cinit", "s")
    with nc.Block() as blk:
        @blk.sync
        def _(sync):
            sync.dma_start(p.t["identf"][:, :], p.inp("ident")[:, :]).then_inc(*sid.tick(16))
            sync.dma_start(p.t["sel4fs"][:, :], p.inp("sel4f")[:, :]).then_inc(*sid.tick(16))
            sync.dma_start(p.t["sel4ts"][:, :], p.inp("sel4t")[:, :]).then_inc(*sid.tick(16))

        @blk.vector
        def _(vector):
            p.mark("const", vector.memset(p.t["ones1f"][:, :], 1.0), 1)
            p.mark("const", vector.memset(p.t["ones1b"][:, :], 1.0), 1)
            p.mark("const", vector.memset(p.t["onesr"][:, :], 1.0), 1)
            W(vector, sid, 48)
            p.mark("const", vector.tensor_copy(p.t["identb"][:, :], p.t["identf"][:, :]), 1)

    ph_stats(p, x0, "st0")
    ph_xn(p, x0, "xn0")

    if "xn" in p.dbg:
        _dbg_dump_fm(p, "xn", "dbg_xn")

    if upto >= 1:
        Q = p.arena("Q", [128, NFC, TPC], BF16, "BIG")
        p.t["Q"] = Q
    if upto >= 1.1:
        ph_proj(p, "wk", p.inp("wk_t"), "kf", NFC, dst=p.t["kcc_in"], norm_row=1)
        if "kin" in p.dbg:
            _dbg_dram(p, "kcc_in", "dbg_kin", [D, TPC], BF16)
    if upto >= 1.2:
        ph_cc(p, "a", [("kcc_in", "kcc_out")], krow=1, grow=2,
              nin=p.t["nrma_in"], nout=p.t["nrma_out"], nrm_mark="wk_nrm")
    if upto >= 1.15:
        ph_proj(p, "wv", p.inp("wv_t"), "vt", NFC, dst=p.t["vcc_in"])
        if "vin" in p.dbg:
            _dbg_dram(p, "vcc_in", "dbg_vin", [TPC, D], BF16)
    if upto >= 1.2:
        ph_cc_gather(p, "av", [("vcc_in", "vcc_out")])
    if upto >= 1:
        ph_proj(p, "wq", p.inp("wq_t"), "q", NFC, dst=Q, norm_row=0)
    if upto >= 1.5:
        ph_attn_a(p)
        if "ctx" in p.dbg:
            _dbg_dram(p, "ctxd", "dbg_ctx", [D, TPC], BF16,
                      n_rows=(256 if "mini" in p.dbg else None))
    cur_x, cur_mark = x0, None
    if upto >= 2:
        ph_attn_out(p, "wo", p.inp("wo_t"), x0, p.t["xd1"], "x1rdy",
                    with_stats="nostat" not in p.dbg)
        cur_x, cur_mark = p.t["xd1"], "x1rdy"
    if upto >= 2.2:
        done = _sub_ffn(p, "f1", p.t["xd1"], p.t["xd2"], "x1rdy", "x2rdy", first=True)
        if done:
            cur_x, cur_mark = p.t["xd2"], "x2rdy"
    if upto >= 3.2:
        ph_xn(p, p.t["xd2"], "bxn", wait_marks=("x2rdy",))
        ph_proj(p, "bk", p.inp("bwkv"), "kf", 4, dst=p.t["kbcc_in"], norm_row=4)
        ph_cc(p, "b", [("kbcc_in", "kbcc_out")], krow=4, grow=5,
              nin=p.t["nrmb_in"], nout=p.t["nrmb_out"], nrm_mark="bk_nrm")
        ph_proj(p, "bv", p.inp("bwkv"), "vt", 4, dst=p.t["vbcc_in"], wt_col0=512)
        ph_cc_gather(p, "bv", [("vbcc_in", "vbcc_out")])
        ph_proj(p, "bq", p.inp("bwq"), "q", NFC, dst=p.t["Q"], norm_row=3)
        ph_attn_b(p)
        if "ctxb" in p.dbg:
            _dbg_dram(p, "ctxd", "dbg_ctxb", [D, TPC], BF16)
    if upto >= 4:
        ph_attn_out(p, "bo", p.inp("bwd"), p.t["xd2"], p.t["xd1"], "x3rdy")
        cur_x, cur_mark = p.t["xd1"], "x3rdy"
    if upto >= 5:
        done = _sub_ffn(p, "f2", p.t["xd1"], xout, "x3rdy", "x4rdy", first=False,
                        last=True)
        if done:
            cur_x, cur_mark = xout, "x4rdy"

    if "q" in p.dbg:
        _dbg_dump_fm(p, "Q", "dbg_q", marks=("wq_ev",))
    if "kv" in p.dbg:
        _dbg_dram(p, "kcc_out", "dbg_k", [4 * D, TPC], BF16)
        _dbg_dram(p, "vcc_out", "dbg_v", [4 * TPC, D], BF16)
    if "negm" in p.dbg:
        _dbg_rows(p, "negMa", "dbg_negm", NFC, marks=("negm_a",))

    if cur_x is not xout:
        sg = p.new_sem("out", "g")
        with nc.Block() as blk:
            @blk.gpsimd
            def _(g):
                if cur_mark is not None:
                    p.wait_mark(g, cur_mark)
                g.dma_start(p.t["xout"][:, :], cur_x[:, :]).then_inc(*sg.tick(16))
                W(g, sg, 16)

    p.ctx.close()
    nc.finalize()
    return nc, set(p.used_inputs)



def _run(inputs, upto=99, dbg=(), _nc_out=None):
    x = np.asarray(inputs["L0_attn_a_x"], dtype=np.float32)
    shards = _shard_x(x)
    shared = _prep_inputs(inputs)
    nc, used = build_program(upto=upto, dbg=dbg)
    if _nc_out is not None:
        _nc_out.append(nc)
    in_maps = []
    for core in range(NCORE):
        m = dict(shared)
        m.update(_host_tables(core))
        m["x0"] = shards[core]
        in_maps.append({k: v for k, v in m.items() if k in used})
    return run_bass_kernel_spmd(nc, in_maps, core_ids=list(range(NCORE)))


def kernel(**inputs):
    res = _run(inputs)
    outs = [res.results[c]["xout"] for c in range(NCORE)]
    return _unshard_x(outs)
